# version 04
# baseline (speedup 1.0000x reference)
import numpy as np
from contextlib import ExitStack
import concourse.bass as bass
import concourse.mybir as mybir
from concourse.bass_utils import run_bass_kernel_spmd

F32 = mybir.dt.float32
BF16 = mybir.dt.bfloat16
AF = mybir.ActivationFunctionType
ALU = mybir.AluOpType
AX = mybir.AxisListType

B, S, D = 4, 8192, 1024
TOK = 4096
NT = 32
DT = 8
CH = 512
NCH = TOK // CH
NS = CH // 128
SLOT = 2048
NEL = 8
VW = 1026
NUMW = 1025
EPS_ATTN = 1e-6
EPS_LN = 1e-5
GSL = ((0, 512), (512, 1024), (1024, 1025))


def build_nc():
    nc = bass.Bass("TRN2", target_bir_lowering=False)

    def din(name, shape):
        return nc.dram_tensor(name, list(shape), F32, kind="ExternalInput").ap()

    xsh = din("xsh", [TOK, D])
    wqt = din("wqt", [D, D]); wkt = din("wkt", [D, D]); wvt = din("wvt", [D, D])
    bq = din("bq", [1, D]); bk = din("bk", [1, D]); bv = din("bv", [1, D])
    gam = din("gam", [1, 1])
    phiwt_d = din("phiwt", [D, SLOT]); phib = din("phib", [1, SLOT])
    ew = din("ew", [NEL, D, D]); eb = din("eb", [NEL, D])
    eye_d = din("eye", [128, 128])
    causal_d = din("causal", [128, 128])
    iotac_d = din("iota_col", [128, 1])
    iotar_d = din("iota_row", [1, 1056])
    flag_d = din("flag", [1, 1])
    out_d = nc.dram_tensor("out", [TOK, D], F32, kind="ExternalOutput").ap()

    sp_xn = nc.dram_tensor("sp_xn", [NT, 128, D], BF16).ap()
    sp_E = nc.dram_tensor("sp_E", [NT, 128, SLOT], BF16).ap()
    sp_dsm = nc.dram_tensor("sp_dsm", [NT, 128, SLOT], BF16).ap()
    sp_q = nc.dram_tensor("sp_q", [NCH, 128, DT * 512], BF16).ap()
    sp_k = nc.dram_tensor("sp_k", [NCH, 128, DT * 512], BF16).ap()
    sp_kh = nc.dram_tensor("sp_kh", [NCH, 128, NS * D], BF16).ap()
    sp_v = nc.dram_tensor("sp_v", [NCH, 128, NS * VW], BF16).ap()
    sp_ya = nc.dram_tensor("sp_ya", [NT, 128, D], F32).ap()
    lng_dr = nc.dram_tensor("lng_dr", [1, 1], F32).ap()
    ar_in = nc.dram_tensor("cc_ar_in", [SLOT, D], F32).ap()
    rs_out = nc.dram_tensor("cc_rs_out", [256, D], F32).ap()
    st_in = nc.dram_tensor("cc_st_in", [DT, 128, VW], BF16).ap()
    st_out = nc.dram_tensor("cc_st_out", [2, DT, 128, VW], BF16).ap()
    yt_in = nc.dram_tensor("cc_yt_in", [256, D], BF16).ap()
    yt_out = nc.dram_tensor("cc_yt_out", [SLOT, D], BF16).ap()

    with ExitStack() as stack:
        def sbt(name, shape, dt_):
            return stack.enter_context(nc.sbuf_tensor(name, list(shape), dt_))
        eye_f = sbt("eye_f", [128, 128], F32)
        eye_b = sbt("eye_b", [128, 128], BF16)
        causal = sbt("sb_causal", [128, 128], F32)
        iotac = sbt("sb_iotac", [128, 1], F32)
        iotaI = sbt("iotaI", [128, 512], F32)
        gam_s = sbt("gam_s", [1, 1], F32)
        lng = sbt("lng", [1, 1], F32)
        lng_bc = sbt("lng_bc", [128, 1], F32)
        nlng_bc = sbt("nlng_bc", [128, 1], F32)
        fcol = sbt("fcol", [128, 1], F32)
        g512c = sbt("g512c", [128, 1], F32)
        ccs = sbt("ccs", [128, 2], F32)
        asc = sbt("asc", [128, 4], F32)
        ksc = sbt("ksc", [128, 4], F32)
        bcols = sbt("bcols", [128, 8], F32)
        gIbf = sbt("gIbf", [128, 128], BF16)
        gammaI = sbt("gammaI", [128, 512], F32)
        ones_r = sbt("ones_r", [1, 512], BF16)
        stat = sbt("stat", [128, 16], F32)
        negmax = sbt("negmax", [128, 2], F32)
        denD = sbt("denD", [128, 128], F32)
        denC = sbt("denC", [128, 64], F32)
        att_sb = sbt("att_sb", [128, 1280], BF16)
        ycol = sbt("ycol", [128, 4], F32)
        A1 = sbt("A1", [128, 32768], BF16)
        A2 = sbt("A2", [128, 49664], BF16)
        A3 = sbt("A3", [128, 16416], BF16)
        PS1 = stack.enter_context(nc.psum_tensor("PS1", [128, 2048], F32))
        PS2 = stack.enter_context(nc.psum_tensor("PS2", [128, 2048], F32))

        A1b = A1[:]
        A1f = A1[:].bitcast(F32)
        A2f = A2[:].bitcast(F32)
        A3f = A3[:].bitcast(F32)

        bq_r = A2[0:1, 45056:46080]
        bk_r = A2[0:1, 46080:47104]
        bv_r = A2[0:1, 47104:48128]
        phib_r = A3[0:1, 14336:16384]
        ebrow = A2[0:1, 47120:49168]

        xnT = A1b
        xn4 = A1b
        SINIT_O = 16500
        YA_O = 12544
        W3_O = 0
        PHIT_O = 24576
        CB = 12288 + NS * VW
        CB_O = [0, CB]
        DSM_O = 40960
        EW_O = [0, 8192]
        XO_F = 16392
        SQ_F = 20480
        SQ6_F = 21512
        TANH_O = 36880
        XTT_O = 38928
        YTS_O = 40976
        CSM_O = [0, 2048]
        CST_O = [4096, 6144]
        YAB_F = 4096
        OUT_F = 6144
        X_F = [0, 1024]
        XN_O = [4096, 5120]
        E_O = [6144, 8192]
        DS_O = [10240, 12288]
        P3Q, P3K, P3H, P3V = 0, 4096, 8192, 12288
        ST_O = [0, DT * VW]

        def sm(name):
            return nc.alloc_semaphore(name)

        def sm2(name):
            return [nc.alloc_semaphore(name + "0"), nc.alloc_semaphore(name + "1")]

        q_c = sm("q_c")
        g_eye = sm("g_eye"); g_bias = sm("g_bias"); g_w = sm("g_w"); g_phi = sm("g_phi")
        s_pro = sm("s_pro"); v_pro = sm("v_pro")
        q_x = sm2("q_x"); s1a = sm("s1a"); v1s = sm("v1s"); s1b = sm("s1b")
        t1 = sm("t1"); v1e = sm("v1e"); q_xn = sm2("q_xn")
        t2 = sm("t2"); v2m = sm("v2m"); s2e = sm("s2e"); v2d = sm("v2d")
        q_E = sm2("q_E"); q_D = sm2("q_D")
        t3 = sm("t3"); t3h = sm("t3h"); v3q = sm("v3q"); s3k = sm("s3k")
        v3v = sm("v3v"); s3h = sm("s3h")
        q_3q = sm("q_3q"); q_3k = sm("q_3k"); q_3h = sm("q_3h"); q_3v = sm("q_3v")
        q_r4 = sm("q_r4"); g_d4 = sm2("g_d4"); t4 = sm("t4"); v4 = sm("v4")
        q_4 = sm2("q_4"); cc1 = sm("cc1")
        q_5 = sm2("q_5"); v5i = sm("v5i"); t5u = sm("t5u"); sv5u = sm("sv5u")
        t5a = sm("t5a"); v5a = sm("v5a"); t5y = sm("t5y"); v5y = sm("v5y")
        s5y = sm("s5y"); q_ya = sm2("q_ya"); q_st = sm("q_st"); cc2 = sm("cc2")
        q_si = sm("q_si"); v5c = sm("v5c"); t5c = sm("t5c"); v5n = sm("v5n")
        q_rs = sm("q_rs"); g_ew = sm2("g_ew"); v6 = sm("v6"); s6 = sm("s6")
        t6 = sm("t6"); v6e = sm("v6e"); q_yt = sm2("q_yt"); cc3 = sm("cc3")
        q_ytl = sm("q_ytl")
        g_7 = sm2("g_7"); q_7 = sm2("q_7"); v7c = sm("v7c"); t7t = sm("t7t")
        v7t = sm("v7t"); t7y = sm("t7y"); v7o = sm("v7o"); q_out = sm2("q_out")
        vd = sm("vd"); sd = sm("sd")
        q_t1 = sm2("q_t1"); gd = sm("gd")
        q_t7 = sm2("q_t7"); gd7 = sm("gd7"); v7a = sm("v7a"); q_y7 = sm2("q_y7")

        def psb(which, lo, hi, bf=False):
            t_ = PS1 if which == 0 else PS2
            ap = t_[:, lo:hi]
            return ap.bitcast(BF16) if bf else ap

        with nc.Block() as blk:

            @blk.tensor
            def _(t):
                t.wait_ge(g_eye, 16)
                t.wait_ge(g_phi, 16)
                t.wait_ge(g_bias, 16 * 4)
                t.wait_ge(v_pro, 13)
                for i in range(NT):
                    p = i % 2
                    t.wait_ge(q_t1[p], 16 * (i // 2 + 1))
                    if i >= 2:
                        t.wait_ge(s2e, i - 1)
                        t.wait_ge(v2m, i - 1)
                    log = (PS1 if p == 0 else PS2)[:, 0:2048]
                    for step in range(DT):
                        lhs = xnT[:, step * 4096 + i * 128: step * 4096 + (i + 1) * 128]
                        for cs in range(4):
                            t.matmul(log[:, cs * 512:(cs + 1) * 512], lhs,
                                     A2[:, PHIT_O + step * 2048 + cs * 512:
                                        PHIT_O + step * 2048 + (cs + 1) * 512],
                                     start=(step == 0), stop=False)
                    for cs in range(4):
                        ins = t.matmul(log[:, cs * 512:(cs + 1) * 512], ones_r[0:1, 0:128],
                                       phib_r[0:1, cs * 512:(cs + 1) * 512],
                                       start=False, stop=True)
                        if cs == 3:
                            ins.then_inc(t2, 1)

                t.wait_ge(g_w, 16 * 3)
                t.wait_ge(s2e, NT)
                t.wait_ge(v2m, NT)
                for c in range(NCH):
                    for j in range(16):
                        proj = 0 if j < 8 else 1
                        e = j % 8
                        pq = j % 2
                        if j >= 2:
                            jp = j - 2
                            if jp < 8:
                                t.wait_ge(v3q, c * 8 + jp + 1)
                            else:
                                t.wait_ge(s3k, c * 8 + (jp - 8) + 1)
                        elif c > 0:
                            t.wait_ge(s3k, (c - 1) * 8 + 6 + j + 1)
                        woff = W3_O + proj * 8192
                        brow = bq_r if proj == 0 else bk_r
                        ps = PS1[:, 512 * pq:512 * (pq + 1)]
                        for step in range(DT):
                            t.matmul(ps, A2[:, woff + step * 1024 + e * 128:
                                            woff + step * 1024 + (e + 1) * 128],
                                     xnT[:, step * 4096 + c * 512:
                                         step * 4096 + (c + 1) * 512],
                                     start=(step == 0), stop=False)
                        t.matmul(ps, brow[0:1, e * 128:(e + 1) * 128],
                                 ones_r[0:1, 0:512], start=False,
                                 stop=True).then_inc(t3, 1)
                    for g in range(8):
                        sbt_, cs = g // 2, g % 2
                        pv = g % 2
                        if g >= 2:
                            t.wait_ge(v3v, c * 8 + (g - 2) + 1)
                        elif c > 0:
                            t.wait_ge(v3v, (c - 1) * 8 + 6 + g + 1)
                        ps = PS1[:, 1024 + 512 * pv:1024 + 512 * (pv + 1)]
                        for step in range(DT):
                            t.matmul(ps, xnT[:, step * 4096 + c * 512 + sbt_ * 128:
                                             step * 4096 + c * 512 + (sbt_ + 1) * 128],
                                     A2[:, W3_O + 16384 + step * 1024 + cs * 512:
                                        W3_O + 16384 + step * 1024 + (cs + 1) * 512],
                                     start=(step == 0), stop=False)
                        t.matmul(ps, ones_r[0:1, 0:128],
                                 bv_r[0:1, cs * 512:(cs + 1) * 512],
                                 start=False, stop=True).then_inc(t3, 1)
                    t.wait_ge(s3k, 8 * (c + 1))
                    for sbt_ in range(NS):
                        ptk = sbt_ % 2
                        if c * 4 + sbt_ >= 2:
                            t.wait_ge(s3h, c * 4 + sbt_ - 1)
                        trb = psb(1, 512 * ptk, 512 * (ptk + 1), bf=True)
                        for dtl in range(DT):
                            ins = t.transpose(trb[:, dtl * 128:(dtl + 1) * 128],
                                              A3[:, P3K + dtl * 512 + sbt_ * 128:
                                                 P3K + dtl * 512 + (sbt_ + 1) * 128],
                                              eye_b[:])
                            if dtl == DT - 1:
                                ins.then_inc(t3h, 1)

                t.wait_ge(v3q, 64); t.wait_ge(s3k, 64); t.wait_ge(v3v, 64)
                t.wait_ge(q_r4, 16)
                for sl in range(16):
                    p = sl % 2
                    t.wait_ge(g_d4[p], 16 * (sl // 2 + 1))
                    if sl >= 2:
                        t.wait_ge(v4, sl - 1)
                    ps = PS1[:, 1024 * p:1024 * (p + 1)]
                    dsm_b = A2[:, DSM_O + 4096 * p:DSM_O + 4096 * (p + 1)]
                    for tt in range(NT):
                        for cs in range(2):
                            ins = t.matmul(ps[:, cs * 512:(cs + 1) * 512],
                                           dsm_b[:, tt * 128:(tt + 1) * 128],
                                           xn4[:, tt * 1024 + cs * 512:
                                               tt * 1024 + (cs + 1) * 512],
                                           start=(tt == 0), stop=(tt == NT - 1))
                            if tt == NT - 1 and cs == 1:
                                ins.then_inc(t4, 1)

                t.wait_ge(v4, 16)
                t.wait_ge(s3h, 32)
                t.wait_ge(v5i, 2)
                t.wait_ge(v_pro, 12)
                for c in range(NCH):
                    bc = CB_O[c % 2]
                    sp = ST_O[c % 2]
                    t.wait_ge(q_5[c % 2], 16 * 4 * (c // 2 + 1))
                    if c in (1, 2):
                        t.wait_ge(v5y, 4 * c)
                    elif c >= 3:
                        t.wait_ge(s5y, 4 * (c - 2))
                    for dk in range(DT):
                        pu = dk % 2
                        if c * 8 + dk >= 2:
                            t.wait_ge(sv5u, c * 8 + dk - 1)
                        ps = PS1 if pu == 0 else PS2
                        for lo, hi in GSL:
                            t.matmul(ps[:, lo:hi], gIbf[:],
                                     A3[:, sp + dk * VW + lo: sp + dk * VW + hi],
                                     start=True, stop=False)
                        for sbt_ in range(NS):
                            lhs = A2[:, bc + 8192 + sbt_ * 1024 + dk * 128:
                                     bc + 8192 + sbt_ * 1024 + (dk + 1) * 128]
                            for gi, (lo, hi) in enumerate(GSL):
                                ins = t.matmul(ps[:, lo:hi], lhs,
                                               A2[:, bc + 12288 + sbt_ * VW + lo:
                                                  bc + 12288 + sbt_ * VW + hi],
                                               start=False, stop=(sbt_ == NS - 1))
                                if sbt_ == NS - 1 and gi == 2:
                                    ins.then_inc(t5u, 1)
                    for b_ in range(NS):
                        for a_ in range(b_ + 1):
                            blkid = (b_ * (b_ + 1)) // 2 + a_
                            pa = blkid % 2
                            if c * 10 + blkid >= 2:
                                t.wait_ge(v5a, c * 10 + blkid - 1)
                            ps = (PS1 if pa == 0 else PS2)[:, 1536:1664]
                            for step in range(DT):
                                ins = t.matmul(ps, A2[:, bc + 4096 + step * 512 + a_ * 128:
                                                      bc + 4096 + step * 512 + (a_ + 1) * 128],
                                               A2[:, bc + step * 512 + b_ * 128:
                                                  bc + step * 512 + (b_ + 1) * 128],
                                               start=(step == 0), stop=(step == DT - 1))
                                if step == DT - 1:
                                    ins.then_inc(t5a, 1)
                    t.wait_ge(sv5u, 8 * (c + 1))
                    for b_ in range(NS):
                        py = b_ % 2
                        yidx = c * NS + b_
                        if yidx >= 2:
                            prev = yidx - 2
                            if prev < 2 * NS:
                                t.wait_ge(v5y, prev + 1)
                            else:
                                t.wait_ge(s5y, prev - 2 * NS + 1)
                        ps = PS1 if py == 0 else PS2
                        for step in range(DT):
                            lhs = A2[:, bc + step * 512 + b_ * 128:
                                     bc + step * 512 + (b_ + 1) * 128]
                            for lo, hi in GSL:
                                t.matmul(ps[:, lo:hi], lhs,
                                         A3[:, sp + step * VW + lo: sp + step * VW + hi],
                                         start=(step == 0), stop=False)
                        for a_ in range(b_ + 1):
                            blkid = (b_ * (b_ + 1)) // 2 + a_
                            t.wait_ge(v5a, c * 10 + blkid + 1)
                            for gi, (lo, hi) in enumerate(GSL):
                                ins = t.matmul(ps[:, lo:hi],
                                               att_sb[:, blkid * 128:(blkid + 1) * 128],
                                               A2[:, bc + 12288 + a_ * VW + lo:
                                                  bc + 12288 + a_ * VW + hi],
                                               start=False, stop=(a_ == b_))
                                if a_ == b_ and gi == 2:
                                    ins.then_inc(t5y, 1)

                t.wait_ge(t5y, NS * NCH)
                t.wait_ge(v5a, 10 * NCH)
                for i6 in range(2):
                    t.wait_ge(s6, 2 + i6 + 1)
                    trb = psb(i6, 1536, 2048, bf=True)
                    for dtl in range(DT):
                        ins = t.transpose(trb[:, dtl * 128:(dtl + 1) * 128],
                                          A2[:, TANH_O + i6 * 1024 + dtl * 128:
                                             TANH_O + i6 * 1024 + (dtl + 1) * 128],
                                          eye_b[:])
                        if dtl == DT - 1:
                            ins.then_inc(t6, 1)
                for n in range(NEL):
                    pe = n % 2
                    t.wait_ge(g_ew[pe], 16 * 2 * (n // 2 + 1))
                    t.wait_ge(v6e, 2)
                    if n >= 1:
                        t.wait_ge(v6e, 2 + 2 * n)
                    for cs in range(2):
                        ps = psb(cs, 1536, 2048)
                        for step in range(DT):
                            t.matmul(ps[0:32, :],
                                     A2[:, XTT_O + step * 256 + n * 32:
                                        XTT_O + step * 256 + n * 32 + 32],
                                     A3[:, EW_O[pe] + step * 1024 + cs * 512:
                                        EW_O[pe] + step * 1024 + (cs + 1) * 512],
                                     start=(step == 0), stop=False)
                        ins = t.matmul(ps[0:32, :],
                                       ones_r[0:1, 0:32],
                                       ebrow[0:1, pe * D + cs * 512:pe * D + (cs + 1) * 512],
                                       start=False, stop=True)
                        if cs == 1:
                            ins.then_inc(t6, 1)

                t.wait_ge(q_si, 16)
                t.wait_ge(v5c, 2)
                for cc in range(2):
                    for b_ in range(NS):
                        idx = cc * NS + b_
                        pyc = idx % 2
                        if idx < 2:
                            t.wait_ge(s5y, 24)
                        else:
                            t.wait_ge(v5n, idx - 1)
                        ps = PS1 if pyc == 0 else PS2
                        for step in range(DT):
                            lhs = A2[:, CB_O[cc] + step * 512 + b_ * 128:
                                     CB_O[cc] + step * 512 + (b_ + 1) * 128]
                            for gi, (lo, hi) in enumerate(GSL):
                                ins = t.matmul(ps[:, lo:hi], lhs,
                                               A1b[:, SINIT_O + step * VW + lo:
                                                   SINIT_O + step * VW + hi],
                                               start=(step == 0), stop=(step == DT - 1))
                                if step == DT - 1 and gi == 2:
                                    ins.then_inc(t5c, 1)

                t.wait_ge(q_ytl, 16)
                t.wait_ge(v6e, 18)
                t.wait_ge(v5n, 8)
                for i in range(NT):
                    p = i % 2
                    t.wait_ge(q_t7[p], 32 * (i // 2 + 1))
                    if i >= 2:
                        t.wait_ge(v7a, i - 1)
                    ps = PS1[:, 1024 * p:1024 * (p + 1)]
                    for sl in range(16):
                        for cs in range(2):
                            ins = t.matmul(ps[:, cs * 512:(cs + 1) * 512],
                                           A2[:, CST_O[p] + sl * 128:
                                              CST_O[p] + (sl + 1) * 128],
                                           A1b[:, sl * 1024 + cs * 512:
                                               sl * 1024 + (cs + 1) * 512],
                                           start=(sl == 0), stop=(sl == 15))
                            if sl == 15 and cs == 1:
                                ins.then_inc(t7y, 1)

            @blk.vector
            def _(v):
                nv = [0]

                def V(inst):
                    inst.then_inc(vd, 1)
                    nv[0] += 1

                def VB():
                    v.wait_ge(vd, nv[0])

                v.wait_ge(q_c, 16 * 8)
                v.tensor_scalar_mul(nlng_bc[:], lng_bc[:], -1.0).then_inc(v_pro, 1)
                for a_ in range(4):
                    v.tensor_scalar_mul(bcols[:, a_:a_ + 1], lng_bc[:],
                                        float(-(128 * a_ + 1))).then_inc(v_pro, 1)
                for s_ in range(4):
                    v.tensor_scalar_mul(bcols[:, 4 + s_:5 + s_], lng_bc[:],
                                        float(511 - 128 * s_)).then_inc(v_pro, 1)
                v.wait_ge(s_pro, 3)
                v.tensor_mul(ccs[:, 1:2], fcol[:], g512c[:]).then_inc(v_pro, 1)
                v.tensor_copy(ccs[:, 0:1], fcol[:]).then_inc(v_pro, 1)
                v.tensor_scalar_mul(gIbf[:], eye_f[:], g512c[:]).then_inc(v_pro, 1)
                v.memset(ones_r[:], 1.0).then_inc(v_pro, 1)

                for i in range(NT):
                    p = i % 2
                    v.wait_ge(q_x[p], 16 * (i // 2 + 1))
                    xt = A3f[:, X_F[p]:X_F[p] + 1024]
                    V(v.reduce_sum(stat[:, p * 8:p * 8 + 1], xt, axis=AX.X))
                    v.wait_ge(s1a, i + 1)
                    VB()
                    V(v.tensor_scalar_mul(stat[:, p * 8:p * 8 + 1],
                                          stat[:, p * 8:p * 8 + 1], 1.0 / D))
                    V(v.tensor_scalar_mul(stat[:, p * 8 + 2:p * 8 + 3],
                                          stat[:, p * 8 + 1:p * 8 + 2], 1.0 / D))
                    VB()
                    V(v.tensor_mul(stat[:, p * 8 + 3:p * 8 + 4],
                                   stat[:, p * 8:p * 8 + 1], stat[:, p * 8:p * 8 + 1]))
                    VB()
                    V(v.tensor_sub(stat[:, p * 8 + 2:p * 8 + 3],
                                   stat[:, p * 8 + 2:p * 8 + 3],
                                   stat[:, p * 8 + 3:p * 8 + 4]))
                    VB()
                    V(v.tensor_scalar_add(stat[:, p * 8 + 2:p * 8 + 3],
                                          stat[:, p * 8 + 2:p * 8 + 3], EPS_LN))
                    VB()
                    v.reciprocal(stat[:, p * 8 + 3:p * 8 + 4],
                                 stat[:, p * 8 + 2:p * 8 + 3]).then_inc(v1s, 1)

                v.wait_ge(t2, 1)
                v.reduce_max(negmax[:, 0:1], PS1[:, 0:2048], axis=AX.X,
                             negate=True).then_inc(v2m, 1)
                for i in range(NT):
                    p = i % 2
                    if i + 1 < NT:
                        pn = (i + 1) % 2
                        v.wait_ge(t2, i + 2)
                        v.reduce_max(negmax[:, pn:pn + 1],
                                     (PS1 if pn == 0 else PS2)[:, 0:2048],
                                     axis=AX.X, negate=True).then_inc(v2m, 1)
                    v.wait_ge(s2e, i + 1)
                    if i >= 2:
                        v.wait_ge(v2d, i - 1)
                    E3 = A3[:, E_O[p]:E_O[p] + 2048].rearrange("p (n q) -> p n q", q=32)
                    V(v.reduce_sum(denD[:, 64 * p:64 * p + 64], E3, axis=AX.X))
                    VB()
                    v.reciprocal(denD[:, 64 * p:64 * p + 64],
                                 denD[:, 64 * p:64 * p + 64]).then_inc(gd, 1)

                v.wait_ge(s_pro, 11)
                for c in range(NCH):
                    for e in range(DT):
                        pq = e % 2
                        v.wait_ge(t3, c * 24 + e + 1)
                        if c >= 1 and e < 2:
                            v.wait_ge(q_3q, 16 * c)
                        ps = PS1[:, 512 * pq:512 * (pq + 1)]
                        v.scalar_tensor_tensor(
                            A3[:, P3Q + e * 512:P3Q + (e + 1) * 512], ps, 0.0,
                            gammaI[:], ALU.max, ALU.mult).then_inc(v3q, 1)
                    if c == 0:
                        v.wait_ge(q_D[0], 16 * 16)
                        v.wait_ge(q_D[1], 16 * 16)
                        v.wait_ge(q_E[0], 16 * 16)
                        v.wait_ge(q_E[1], 16 * 16)
                        for sbt_ in range(NS):
                            V(v.memset(A3[:, P3V + sbt_ * VW + 1024:
                                          P3V + sbt_ * VW + 1025], 1.0))
                            V(v.memset(A3[:, P3V + sbt_ * VW + 1025:
                                          P3V + sbt_ * VW + 1026], 0.0))
                    for g in range(8):
                        sbt_, cs = g // 2, g % 2
                        pv = g % 2
                        v.wait_ge(t3, c * 24 + 16 + g + 1)
                        if c >= 1 and g < 2:
                            v.wait_ge(q_3v, 16 * c)
                        ps = PS1[:, 1024 + 512 * pv:1024 + 512 * (pv + 1)]
                        v.tensor_copy(A3[:, P3V + sbt_ * VW + cs * 512:
                                         P3V + sbt_ * VW + cs * 512 + 512],
                                      ps).then_inc(v3v, 1)

                for sl in range(16):
                    p = sl % 2
                    v.wait_ge(t4, sl + 1)
                    if sl >= 2:
                        v.wait_ge(q_4[p], 16 * ((sl - 2) // 2 + 1))
                    ps = PS1[:, 1024 * p:1024 * (p + 1)]
                    v.tensor_copy(A3f[:, 1024 * p:1024 * (p + 1)], ps).then_inc(v4, 1)

                v.wait_ge(q_4[0], 16 * 8)
                v.wait_ge(q_4[1], 16 * 8)
                v.memset(A3[:, ST_O[0]:ST_O[0] + DT * VW], 0.0).then_inc(v5i, 1)
                v.memset(A3[:, ST_O[1]:ST_O[1] + DT * VW], 0.0).then_inc(v5i, 1)

                for c in range(NCH):
                    for b_ in range(NS):
                        for a_ in range(b_ + 1):
                            blkid = (b_ * (b_ + 1)) // 2 + a_
                            v.wait_ge(t5a, c * 10 + blkid + 1)
                            if c >= 1:
                                v.wait_ge(t5y, 4 * (c - 1) + b_ + 1)
                            pa = blkid % 2
                            ps = (PS1 if pa == 0 else PS2)[:, 1536:1664]
                            dst = att_sb[:, blkid * 128:(blkid + 1) * 128]
                            if a_ == b_:
                                v.scalar_tensor_tensor(dst, ps, asc[:, a_:a_ + 1],
                                                       causal[:], ALU.mult,
                                                       ALU.mult).then_inc(v5a, 1)
                            else:
                                v.tensor_scalar_mul(dst, ps,
                                                    asc[:, a_:a_ + 1]).then_inc(v5a, 1)
                    for b_ in range(NS):
                        py = b_ % 2
                        yidx = c * NS + b_
                        v.wait_ge(t5y, yidx + 1)
                        ps = PS1 if py == 0 else PS2
                        if c < 2:
                            v.tensor_copy(A1f[:, yidx * NUMW:(yidx + 1) * NUMW],
                                          ps[:, 0:NUMW]).then_inc(v5y, 1)
                        else:
                            if yidx >= 10:
                                v.wait_ge(s5y, yidx - 9)
                            V(v.tensor_scalar_add(ycol[:, py:py + 1], ps[:, 1024:1025],
                                                  EPS_ATTN))
                            VB()
                            v.reciprocal(ycol[:, 2 + py:3 + py],
                                         ycol[:, py:py + 1]).then_inc(v5y, 1)

                v.wait_ge(q_rs, 16)
                for i6 in range(2):
                    xo = A2f[:, XO_F + 1024 * i6:XO_F + 1024 * (i6 + 1)]
                    v.tensor_scalar(xo, xo, -33000.0, 65000.0,
                                    ALU.max, ALU.min).then_inc(v6, 1)
                    v.wait_ge(v6, 2 * i6 + 1)
                    V(v.reduce_sum(stat[:, 8 * i6:8 * i6 + 1], xo, axis=AX.X))
                    v.wait_ge(s6, i6 + 1)
                    VB()
                    V(v.tensor_scalar_mul(stat[:, 8 * i6:8 * i6 + 1],
                                          stat[:, 8 * i6:8 * i6 + 1], 1.0 / D))
                    V(v.tensor_scalar_mul(stat[:, 8 * i6 + 2:8 * i6 + 3],
                                          stat[:, 8 * i6 + 1:8 * i6 + 2], 1.0 / D))
                    VB()
                    V(v.tensor_mul(stat[:, 8 * i6 + 3:8 * i6 + 4],
                                   stat[:, 8 * i6:8 * i6 + 1], stat[:, 8 * i6:8 * i6 + 1]))
                    VB()
                    V(v.tensor_sub(stat[:, 8 * i6 + 2:8 * i6 + 3],
                                   stat[:, 8 * i6 + 2:8 * i6 + 3],
                                   stat[:, 8 * i6 + 3:8 * i6 + 4]))
                    VB()
                    V(v.tensor_scalar_add(stat[:, 8 * i6 + 2:8 * i6 + 3],
                                          stat[:, 8 * i6 + 2:8 * i6 + 3], EPS_LN))
                    VB()
                    v.reciprocal(stat[:, 8 * i6 + 3:8 * i6 + 4],
                                 stat[:, 8 * i6 + 2:8 * i6 + 3]).then_inc(v6, 1)
                for i6 in range(2):
                    v.wait_ge(t6, i6 + 1)
                    trb = psb(i6, 1536, 2048, bf=True)
                    src_ = trb.rearrange("p (t c) -> p t c", c=128)
                    dst = A2[:, XTT_O:XTT_O + 2048].rearrange(
                        "p (t c) -> p t c", c=256)[:, :, i6 * 128:(i6 + 1) * 128]
                    v.tensor_copy(dst, src_).then_inc(v6e, 1)
                for n in range(NEL):
                    pe = n % 2
                    v.wait_ge(t6, 2 + n + 1)
                    if n >= 2:
                        v.wait_ge(q_yt[pe], 16 * ((n - 2) // 2 + 1))
                    for cs in range(2):
                        ps = psb(cs, 1536, 2048)[0:32, :]
                        v.tensor_copy(A2[0:32, YTS_O + pe * D + cs * 512:
                                         YTS_O + pe * D + (cs + 1) * 512],
                                      ps).then_inc(v6e, 1)

                for cc in range(2):
                    v.wait_ge(q_5[cc], 16 * 17)
                    v.wait_ge(v_pro, 11)
                    qb = A2[:, CB_O[cc]:CB_O[cc] + 4096]
                    v.tensor_scalar_mul(qb, qb, ccs[:, cc:cc + 1]).then_inc(v5c, 1)
                for cc in range(2):
                    for b_ in range(NS):
                        idx = cc * NS + b_
                        pyc = idx % 2
                        v.wait_ge(t5c, idx + 1)
                        v.wait_ge(s5y, 23 + idx)
                        ps = PS1 if pyc == 0 else PS2
                        nh = A1f[:, idx * NUMW:(idx + 1) * NUMW]
                        V(v.tensor_add(nh, nh, ps[:, 0:NUMW]))
                        VB()
                        V(v.tensor_scalar_add(ycol[:, pyc:pyc + 1], nh[:, 1024:1025],
                                              EPS_ATTN))
                        VB()
                        v.reciprocal(ycol[:, 2 + pyc:3 + pyc],
                                     ycol[:, pyc:pyc + 1]).then_inc(v5n, 1)

                for i in range(NT):
                    p = i % 2
                    v.wait_ge(g_7[p], 16 * (i // 2 + 1))
                    E3 = A2[:, CSM_O[p]:CSM_O[p] + 2048].rearrange("p (n q) -> p n q", q=32)
                    E3t = E3.transpose([0, 2, 1])
                    V(v.reduce_sum(denC[:, 32 * p:32 * p + 32], E3t, axis=AX.X))
                    VB()
                    V(v.reciprocal(denC[:, 32 * p:32 * p + 32], denC[:, 32 * p:32 * p + 32]))
                    rC = denC[:, 32 * p:32 * p + 32].unsqueeze(1).broadcast_to([128, 64, 32])
                    VB()
                    v.tensor_mul(E3, E3, rC).then_inc(v7c, 1)
                    v.wait_ge(t7y, i + 1)
                    v.wait_ge(q_7[p], 16 * (i // 2 + 1))
                    if i >= 2:
                        v.wait_ge(q_out[p], 16 * ((i - 2) // 2 + 1))
                    ps = PS1[:, 1024 * p:1024 * (p + 1)]
                    ob = A2f[:, OUT_F + 1024 * p:OUT_F + 1024 * (p + 1)]
                    v.wait_ge(q_y7[p], 16 * (i // 2 + 1))
                    V(v.tensor_add(ob, ps, A3f[:, X_F[p]:X_F[p] + 1024]))
                    VB()
                    v.tensor_add(ob, ob, A2f[:, YAB_F + 1024 * p:YAB_F + 1024 * (p + 1)]
                                 ).then_inc(v7a, 1)

            @blk.scalar
            def _(s):
                ns = [0]

                def SC(inst):
                    inst.then_inc(sd, 1)
                    ns[0] += 1

                def SB():
                    s.wait_ge(sd, ns[0])

                s.wait_ge(q_c, 16 * 6)
                s.activation(lng[:], gam_s[:], AF.Ln).then_inc(s_pro, 1)
                s.wait_ge(q_c, 16 * 8)
                s.activation(gammaI[:], iotaI[:], AF.Exp,
                             scale=lng_bc[:]).then_inc(s_pro, 1)
                s.activation(g512c[:], lng_bc[:], AF.Exp, scale=512.0).then_inc(s_pro, 1)
                s.wait_ge(v_pro, 9)
                for a_ in range(4):
                    s.activation(asc[:, a_:a_ + 1], iotac[:], AF.Exp,
                                 scale=nlng_bc[:], bias=bcols[:, a_:a_ + 1]).then_inc(s_pro, 1)
                for s_ in range(4):
                    s.activation(ksc[:, s_:s_ + 1], iotac[:], AF.Exp,
                                 scale=nlng_bc[:], bias=bcols[:, 4 + s_:5 + s_]).then_inc(s_pro, 1)

                for i in range(NT):
                    p = i % 2
                    s.wait_ge(q_x[p], 16 * (i // 2 + 1))
                    xt = A3f[:, X_F[p]:X_F[p] + 1024]
                    s.activation(A2f[:, SQ_F + 1024 * p:SQ_F + 1024 * (p + 1)], xt,
                                 AF.Square,
                                 accum_out=stat[:, p * 8 + 1:p * 8 + 2]).then_inc(s1a, 1)
                    s.wait_ge(v1s, i + 1)
                    SC(s.activation(stat[:, p * 8 + 4:p * 8 + 5],
                                    stat[:, p * 8 + 3:p * 8 + 4], AF.Sqrt))
                    SB()
                    SC(s.mul(stat[:, p * 8 + 5:p * 8 + 6], stat[:, p * 8:p * 8 + 1],
                             stat[:, p * 8 + 4:p * 8 + 5]))
                    SB()
                    SC(s.mul(stat[:, p * 8 + 5:p * 8 + 6],
                             stat[:, p * 8 + 5:p * 8 + 6], -1.0))
                    if i >= 2:
                        s.wait_ge(q_xn[p], 16 * ((i - 2) // 2 + 1))
                        s.wait_ge(q_t1[p], 16 * ((i - 2) // 2 + 1))
                    SB()
                    s.activation(A3[:, XN_O[p]:XN_O[p] + 1024], xt, AF.Identity,
                                 scale=stat[:, p * 8 + 4:p * 8 + 5],
                                 bias=stat[:, p * 8 + 5:p * 8 + 6]).then_inc(s1b, 1)

                for i in range(NT):
                    p = i % 2
                    s.wait_ge(v2m, i + 1)
                    if i >= 2:
                        s.wait_ge(v2d, i - 1)
                        s.wait_ge(q_E[p], 16 * ((i - 2) // 2 + 1))
                    log = (PS1 if p == 0 else PS2)[:, 0:2048]
                    s.activation(A3[:, E_O[p]:E_O[p] + 2048], log, AF.Exp,
                                 bias=negmax[:, p:p + 1]).then_inc(s2e, 1)
                    s.wait_ge(v2d, i + 1)
                    s.dma_start(out=sp_dsm[i], in_=A3[:, DS_O[p]:DS_O[p] + 2048]
                                ).then_inc(q_D[p], 16)

                s.wait_ge(q_E[0], 16 * 16)
                s.wait_ge(q_E[1], 16 * 16)
                s.wait_ge(q_xn[0], 16 * 16)
                s.wait_ge(q_xn[1], 16 * 16)
                s.wait_ge(s_pro, 11)
                for c in range(NCH):
                    for e in range(DT):
                        pq = e % 2
                        s.wait_ge(t3, c * 24 + 8 + e + 1)
                        if c >= 1 and e < 2:
                            s.wait_ge(q_3k, 16 * c)
                        ps = PS1[:, 512 * pq:512 * (pq + 1)]
                        s.activation(A3[:, P3K + e * 512:P3K + (e + 1) * 512], ps,
                                     AF.Relu).then_inc(s3k, 1)
                    for sbt_ in range(NS):
                        ptk = sbt_ % 2
                        s.wait_ge(t3h, c * 4 + sbt_ + 1)
                        if c >= 1 and sbt_ < 2:
                            s.wait_ge(q_3h, 16 * c)
                        trb = psb(1, 512 * ptk, 512 * (ptk + 1), bf=True)
                        s.mul(A3[:, P3H + sbt_ * 1024:P3H + (sbt_ + 1) * 1024], trb,
                              ksc[:, sbt_:sbt_ + 1]).then_inc(s3h, 1)

                for c in range(NCH):
                    sq_ = ST_O[(c + 1) % 2]
                    for dk in range(DT):
                        pu = dk % 2
                        s.wait_ge(t5u, c * 8 + dk + 1)
                        ps = PS1 if pu == 0 else PS2
                        s.copy(A3[:, sq_ + dk * VW:sq_ + dk * VW + NUMW],
                               ps[:, 0:NUMW]).then_inc(sv5u, 1)
                    if c >= 2:
                        for b_ in range(NS):
                            py = b_ % 2
                            yidx = c * NS + b_
                            s.wait_ge(v5y, yidx + 1)
                            if yidx >= 10:
                                s.wait_ge(q_ya[py], 16 * ((yidx - 10) // 2 + 1))
                            ps = PS1 if py == 0 else PS2
                            s.mul(A1f[:, YA_O + 1024 * py:YA_O + 1024 * (py + 1)],
                                  ps[:, 0:1024], ycol[:, 2 + py:3 + py]).then_inc(s5y, 1)

                for i6 in range(2):
                    s.wait_ge(v6, 2 * i6 + 1)
                    xo = A2f[:, XO_F + 1024 * i6:XO_F + 1024 * (i6 + 1)]
                    s.activation(A2f[:, SQ6_F + 1024 * i6:SQ6_F + 1024 * (i6 + 1)], xo,
                                 AF.Square,
                                 accum_out=stat[:, 8 * i6 + 1:8 * i6 + 2]).then_inc(s6, 1)
                for i6 in range(2):
                    s.wait_ge(v6, 2 * i6 + 2)
                    xo = A2f[:, XO_F + 1024 * i6:XO_F + 1024 * (i6 + 1)]
                    SC(s.activation(stat[:, 8 * i6 + 4:8 * i6 + 5],
                                    stat[:, 8 * i6 + 3:8 * i6 + 4], AF.Sqrt))
                    SB()
                    SC(s.mul(stat[:, 8 * i6 + 5:8 * i6 + 6], stat[:, 8 * i6:8 * i6 + 1],
                             stat[:, 8 * i6 + 4:8 * i6 + 5]))
                    SB()
                    SC(s.mul(stat[:, 8 * i6 + 5:8 * i6 + 6],
                             stat[:, 8 * i6 + 5:8 * i6 + 6], -1.0))
                    SB()
                    s.activation(A2[:, TANH_O + i6 * 1024:TANH_O + (i6 + 1) * 1024], xo,
                                 AF.Tanh, scale=stat[:, 8 * i6 + 4:8 * i6 + 5],
                                 bias=stat[:, 8 * i6 + 5:8 * i6 + 6]).then_inc(s6, 1)

                for cc in range(2):
                    for b_ in range(NS):
                        idx = cc * NS + b_
                        pyc = idx % 2
                        s.wait_ge(v5n, idx + 1)
                        if idx < 2:
                            s.wait_ge(q_ya[pyc], 16 * 12)
                        else:
                            s.wait_ge(q_ya[pyc], 16 * (12 + (idx - 2) // 2 + 1))
                        nh = A1f[:, idx * NUMW:(idx + 1) * NUMW]
                        s.mul(A1f[:, YA_O + 1024 * pyc:YA_O + 1024 * (pyc + 1)],
                              nh[:, 0:1024], ycol[:, 2 + pyc:3 + pyc]).then_inc(s5y, 1)

                for i in range(NT):
                    p = i % 2
                    s.wait_ge(v7c, i + 1)
                    if i >= 2:
                        s.wait_ge(t7y, i - 1)
                    for half in range(2):
                        src_ = A2[:, CSM_O[p] + half * 1024:CSM_O[p] + (half + 1) * 1024
                                  ].rearrange("p (t c) -> p t c", c=128)
                        dst = A2[:, CST_O[p] + half * 1024:CST_O[p] + (half + 1) * 1024
                                 ].rearrange("p (t c) -> p t c", c=128)
                        s.dma_start_transpose(dst, src_).then_inc(q_t7[p], 16)


            @blk.sync
            def _(sy):
                sy.dma_start(out=eye_f[:], in_=eye_d).then_inc(q_c, 16)
                sy.dma_start(out=causal[:], in_=causal_d).then_inc(q_c, 16)
                sy.dma_start(out=iotac[:], in_=iotac_d).then_inc(q_c, 16)
                sy.dma_start(out=gam_s[:], in_=gam).then_inc(q_c, 16)
                sy.dma_start(out=fcol[:], in_=flag_d[0:1, 0:1].partition_broadcast(128)
                             ).then_inc(q_c, 16)
                sy.dma_start(out=iotaI[:], in_=iotar_d[0:1, 1:513].partition_broadcast(128)
                             ).then_inc(q_c, 16)
                sy.wait_ge(s_pro, 1)
                sy.dma_start(out=lng_dr, in_=lng[:]).then_inc(q_c, 16)
                sy.wait_ge(q_c, 16 * 7)
                sy.dma_start(out=lng_bc[:], in_=lng_dr[0:1, 0:1].partition_broadcast(128)
                             ).then_inc(q_c, 16)

                for i in range(NT):
                    p = i % 2
                    if i >= 2:
                        sy.wait_ge(s1a, i - 1)
                        sy.wait_ge(s1b, i - 1)
                        sy.wait_ge(v1s, i - 1)
                    sy.dma_start(out=A3f[:, X_F[p]:X_F[p] + 1024],
                                 in_=xsh[i * 128:(i + 1) * 128, :]).then_inc(q_x[p], 16)
                    if i >= 1:
                        pp = (i - 1) % 2
                        sy.wait_ge(s1b, i)
                        sy.dma_start(out=sp_xn[i - 1], in_=A3[:, XN_O[pp]:XN_O[pp] + 1024]
                                     ).then_inc(q_xn[pp], 16)
                        sy.dma_start_transpose(
                            A1b.rearrange("p (t c) -> p t c", c=4096
                                          )[:, :, (i - 1) * 128:i * 128],
                            A3[:, XN_O[pp]:XN_O[pp] + 1024].rearrange(
                                "p (t c) -> p t c", c=128)).then_inc(q_t1[pp], 16)
                sy.wait_ge(s1b, NT)
                pp = (NT - 1) % 2
                sy.dma_start(out=sp_xn[NT - 1], in_=A3[:, XN_O[pp]:XN_O[pp] + 1024]
                             ).then_inc(q_xn[pp], 16)
                sy.dma_start_transpose(
                    A1b.rearrange("p (t c) -> p t c", c=4096
                                  )[:, :, (NT - 1) * 128:NT * 128],
                    A3[:, XN_O[pp]:XN_O[pp] + 1024].rearrange(
                        "p (t c) -> p t c", c=128)).then_inc(q_t1[pp], 16)

                for i in range(NT):
                    p = i % 2
                    sy.wait_ge(s2e, i + 1)
                    sy.dma_start(out=sp_E[i], in_=A3[:, E_O[p]:E_O[p] + 2048]
                                 ).then_inc(q_E[p], 16)

                for c in range(NCH):
                    sy.wait_ge(v3q, 8 * (c + 1))
                    sy.dma_start(out=sp_q[c], in_=A3[:, P3Q:P3Q + 4096]).then_inc(q_3q, 16)
                    sy.wait_ge(s3k, 8 * (c + 1))
                    sy.dma_start(out=sp_k[c], in_=A3[:, P3K:P3K + 4096]).then_inc(q_3k, 16)
                    sy.wait_ge(s3h, 4 * (c + 1))
                    sy.dma_start(out=sp_kh[c], in_=A3[:, P3H:P3H + 4096]).then_inc(q_3h, 16)
                    sy.wait_ge(v3v, 8 * (c + 1))
                    sy.dma_start(out=sp_v[c], in_=A3[:, P3V:P3V + NS * VW]).then_inc(q_3v, 16)

                sy.wait_ge(q_3q, 16 * 8); sy.wait_ge(q_3k, 16 * 8)
                sy.wait_ge(q_3h, 16 * 8); sy.wait_ge(q_3v, 16 * 8)
                sy.wait_ge(q_xn[0], 16 * 16); sy.wait_ge(q_xn[1], 16 * 16)
                sy.wait_ge(t3h, 4 * NCH)
                sy.dma_start(out=xn4[:, 0:32768].rearrange("p (t c) -> p t c", c=1024),
                             in_=sp_xn.transpose([1, 0, 2])).then_inc(q_r4, 16)
                for sl in range(16):
                    p = sl % 2
                    sy.wait_ge(v4, sl + 1)
                    sy.dma_start(out=ar_in[sl * 128:(sl + 1) * 128, :],
                                 in_=A3f[:, 1024 * p:1024 * (p + 1)]).then_inc(q_4[p], 16)

                def _ya_store(c_):
                    for b_ in range(NS):
                        yidx = c_ * NS + b_
                        py = b_ % 2
                        sy.wait_ge(s5y, (yidx - 2 * NS) + 1)
                        sy.dma_start(out=sp_ya[yidx],
                                     in_=A1f[:, YA_O + 1024 * py:YA_O + 1024 * (py + 1)]
                                     ).then_inc(q_ya[py], 16)
                for c in range(NCH):
                    bc = CB_O[c % 2]
                    if c >= 2:
                        sy.wait_ge(t5y, NS * (c - 1))
                    sy.dma_start(out=A2[:, bc:bc + 4096], in_=sp_q[c]).then_inc(q_5[c % 2], 16)
                    sy.dma_start(out=A2[:, bc + 4096:bc + 8192],
                                 in_=sp_k[c]).then_inc(q_5[c % 2], 16)
                    sy.dma_start(out=A2[:, bc + 8192:bc + 12288],
                                 in_=sp_kh[c]).then_inc(q_5[c % 2], 16)
                    sy.dma_start(out=A2[:, bc + 12288:bc + 12288 + NS * VW],
                                 in_=sp_v[c]).then_inc(q_5[c % 2], 16)
                    if 2 <= c - 1:
                        _ya_store(c - 1)
                _ya_store(NCH - 1)
                sy.wait_ge(cc1, 1)
                sy.dma_start(out=A2f[:, XO_F:XO_F + 2048].rearrange(
                    "p (t c) -> p t c", c=1024),
                    in_=rs_out.rearrange("(t p) c -> p t c", p=128)).then_inc(q_rs, 16)
                sy.wait_ge(sv5u, 8 * NCH)
                sy.dma_start(out=st_in.transpose([1, 0, 2]),
                             in_=A3[:, ST_O[0]:ST_O[0] + DT * VW].rearrange(
                                 "p (t c) -> p t c", c=VW)).then_inc(q_st, 16)
                for cc in range(2):
                    sy.wait_ge(t5y, NS * NCH)
                    sy.dma_start(out=A2[:, CB_O[cc]:CB_O[cc] + 4096],
                                 in_=sp_q[cc]).then_inc(q_5[cc], 16)
                sy.wait_ge(cc2, 1)
                sy.dma_start(out=A1b[:, SINIT_O:SINIT_O + DT * VW].rearrange(
                    "p (t c) -> p t c", c=VW),
                    in_=st_out[0].transpose([1, 0, 2])).then_inc(q_si, 16)
                for n in range(NEL):
                    pe = n % 2
                    sy.wait_ge(v6e, 2 + 2 * (n + 1))
                    sy.dma_start(out=yt_in[n * 32:(n + 1) * 32, :],
                                 in_=A2[0:32, YTS_O + pe * D:YTS_O + (pe + 1) * D]
                                 ).then_inc(q_yt[pe], 16)
                for cc in range(2):
                    for b_ in range(NS):
                        idx = cc * NS + b_
                        pyc = idx % 2
                        sy.wait_ge(s5y, NS * (NCH - 2) + idx + 1)
                        sy.dma_start(out=sp_ya[idx],
                                     in_=A1f[:, YA_O + 1024 * pyc:YA_O + 1024 * (pyc + 1)]
                                     ).then_inc(q_ya[pyc], 16)
                sy.wait_ge(cc3, 1)
                sy.wait_ge(s5y, 32)
                sy.dma_start(out=A1b[:, 0:16384].rearrange("p (t c) -> p t c", c=1024),
                             in_=yt_out.rearrange("(t p) c -> p t c", p=128)
                             ).then_inc(q_ytl, 16)

                sy.wait_ge(q_st, 16)
                sy.wait_ge(t6, 10)
                def _out_store(j):
                    pj = j % 2
                    sy.wait_ge(v7a, j + 1)
                    sy.dma_start(out=out_d[j * 128:(j + 1) * 128, :],
                                 in_=A2f[:, OUT_F + 1024 * pj:OUT_F + 1024 * (pj + 1)]
                                 ).then_inc(q_out[pj], 16)
                for i in range(NT):
                    p = i % 2
                    if i >= 2:
                        sy.wait_ge(v7a, i - 1)
                    sy.dma_start(out=A3f[:, X_F[p]:X_F[p] + 1024],
                                 in_=xsh[i * 128:(i + 1) * 128, :]).then_inc(q_7[p], 16)
                    if i >= 2:
                        _out_store(i - 2)
                _out_store(NT - 2)
                _out_store(NT - 1)
                sy.wait_ge(q_out[0], 16 * 16)
                sy.wait_ge(q_out[1], 16 * 16)

            @blk.gpsimd
            def _(g):
                g.dma_start(out=eye_b[:], in_=eye_d).then_inc(g_eye, 16)
                g.dma_start(out=bq_r, in_=bq).then_inc(g_bias, 16)
                g.dma_start(out=bk_r, in_=bk).then_inc(g_bias, 16)
                g.dma_start(out=bv_r, in_=bv).then_inc(g_bias, 16)
                g.dma_start(out=phib_r, in_=phib).then_inc(g_bias, 16)
                for wi, wsrc in enumerate((wqt, wkt, wvt)):
                    g.dma_start(out=A2[:, W3_O + wi * 8192:W3_O + (wi + 1) * 8192
                                       ].rearrange("p (t c) -> p t c", c=1024),
                                in_=wsrc.rearrange("(t p) c -> p t c", p=128)
                                ).then_inc(g_w, 16)
                g.dma_start(out=A2[:, PHIT_O:PHIT_O + 16384].rearrange(
                    "p (t c) -> p t c", c=2048),
                    in_=phiwt_d.rearrange("(t p) c -> p t c", p=128)).then_inc(g_phi, 16)

                for i in range(NT):
                    p = i % 2
                    g.wait_ge(gd, i + 1)
                    if i >= 2:
                        g.wait_ge(q_D[p], 16 * ((i - 2) // 2 + 1))
                    E3g = A3[:, E_O[p]:E_O[p] + 2048].rearrange("p (n q) -> p n q", q=32)
                    d3g = A3[:, DS_O[p]:DS_O[p] + 2048].rearrange("p (n q) -> p n q", q=32)
                    rDg = denD[:, 64 * p:64 * p + 64].unsqueeze(2).broadcast_to(
                        [128, 64, 32])
                    g.tensor_mul(d3g, E3g, rDg).then_inc(v2d, 1)

                g.wait_ge(q_D[0], 16 * 16)
                g.wait_ge(q_D[1], 16 * 16)
                g.wait_ge(t3h, 4 * NCH)
                for sl in range(16):
                    p = sl % 2
                    if sl >= 2:
                        g.wait_ge(t4, sl - 1)
                    src = sp_dsm.transpose([1, 0, 2])[:, :, sl * 128:(sl + 1) * 128]
                    g.dma_start(out=A2[:, DSM_O + 4096 * p:DSM_O + 4096 * (p + 1)
                                       ].rearrange("p (t c) -> p t c", c=128),
                                in_=src).then_inc(g_d4[p], 16)
                g.wait_ge(q_4[0], 16 * 8)
                g.wait_ge(q_4[1], 16 * 8)
                g.collective_compute("ReduceScatter", ALU.add,
                                     replica_groups=[list(range(8))],
                                     ins=[ar_in.opt()], outs=[rs_out.opt()]
                                     ).then_inc(cc1, 1)
                g.wait_ge(q_st, 16)
                g.collective_compute("AllGather", ALU.bypass,
                                     replica_groups=[[0, 1], [2, 3], [4, 5], [6, 7]],
                                     ins=[st_in.opt()], outs=[st_out.opt()]
                                     ).then_inc(cc2, 1)
                for n in range(NEL):
                    pe = n % 2
                    if n < 2:
                        g.wait_ge(q_st, 16)
                        g.wait_ge(t5y, NS * NCH)
                    else:
                        g.wait_ge(t6, 2 + (n - 2) + 1)
                    g.dma_start(out=A3[:, EW_O[pe]:EW_O[pe] + 8192].rearrange(
                        "p (t c) -> p t c", c=1024),
                        in_=ew[n].rearrange("(t p) c -> p t c", p=128)).then_inc(g_ew[pe], 16)
                    g.dma_start(out=ebrow[0:1, pe * D:(pe + 1) * D],
                                in_=eb[n:n + 1, :]).then_inc(g_ew[pe], 16)
                g.wait_ge(q_yt[0], 16 * 4)
                g.wait_ge(q_yt[1], 16 * 4)
                g.collective_compute("AllGather", ALU.bypass,
                                     replica_groups=[list(range(8))],
                                     ins=[yt_in.opt()], outs=[yt_out.opt()]
                                     ).then_inc(cc3, 1)
                g.wait_ge(t6, 10)
                g.wait_ge(t5c, 8)
                g.wait_ge(q_ya[0], 16 * 16)
                g.wait_ge(q_ya[1], 16 * 16)

                def _g_accum(j):
                    pj = j % 2
                    if j >= 2:
                        g.wait_ge(v7a, j - 1)
                    g.dma_start(out=A2f[:, YAB_F + 1024 * pj:YAB_F + 1024 * (pj + 1)],
                                in_=sp_ya[j]).then_inc(q_y7[pj], 16)

                for i in range(NT):
                    p = i % 2
                    if i >= 2:
                        g.wait_ge(q_t7[p], 32 * ((i - 2) // 2 + 1))
                    g.dma_start(out=A2[:, CSM_O[p]:CSM_O[p] + 2048],
                                in_=sp_E[i]).then_inc(g_7[p], 16)
                    _g_accum(i)

    return nc


def kernel(L0_attn_x_seq, L0_attn_Wq, L0_attn_bq, L0_attn_Wk, L0_attn_bk,
           L0_attn_Wv, L0_attn_bv, L0_attn_gamma, L0_ffn_phi_w, L0_ffn_phi_b,
           L0_ffn_expert_w, L0_ffn_expert_b):
    x = np.asarray(L0_attn_x_seq)
    nc = build_nc()
    eye = np.eye(128, dtype=np.float32)
    causal = np.ascontiguousarray(np.tril(np.ones((128, 128), dtype=np.float32)).T)
    iota_col = np.arange(128, dtype=np.float32).reshape(128, 1)
    iota_row = np.arange(1056, dtype=np.float32).reshape(1, 1056)
    wqt = np.ascontiguousarray(np.asarray(L0_attn_Wq).T)
    wkt = np.ascontiguousarray(np.asarray(L0_attn_Wk).T)
    wvt = np.ascontiguousarray(np.asarray(L0_attn_Wv).T)
    phiwt = np.ascontiguousarray(np.asarray(L0_ffn_phi_w).T)
    in_maps = []
    for c in range(8):
        b, h = c // 2, c % 2
        in_maps.append({
            "xsh": x[b][h * TOK:(h + 1) * TOK],
            "wqt": wqt, "wkt": wkt, "wvt": wvt,
            "bq": np.asarray(L0_attn_bq).reshape(1, D),
            "bk": np.asarray(L0_attn_bk).reshape(1, D),
            "bv": np.asarray(L0_attn_bv).reshape(1, D),
            "gam": np.asarray(L0_attn_gamma, dtype=np.float32).reshape(1, 1),
            "phiwt": phiwt,
            "phib": np.asarray(L0_ffn_phi_b).reshape(1, SLOT),
            "ew": np.ascontiguousarray(np.asarray(L0_ffn_expert_w)[c * NEL:(c + 1) * NEL]),
            "eb": np.ascontiguousarray(np.asarray(L0_ffn_expert_b)[c * NEL:(c + 1) * NEL]),
            "eye": eye, "causal": causal,
            "iota_col": iota_col, "iota_row": iota_row,
            "flag": np.array([[float(h)]], dtype=np.float32),
        })
    res = run_bass_kernel_spmd(nc, in_maps, core_ids=list(range(8)))
    full = np.empty((B, S, D), dtype=np.float32)
    for c in range(8):
        b, h = c // 2, c % 2
        full[b, h * TOK:(h + 1) * TOK] = res.results[c]["out"]
    return full


# version 05
# speedup vs baseline: 1.1084x; 1.1084x over previous
import numpy as np
from contextlib import ExitStack
import concourse.bass as bass
import concourse.mybir as mybir
from concourse.bass_utils import run_bass_kernel_spmd

F32 = mybir.dt.float32
BF16 = mybir.dt.bfloat16
AF = mybir.ActivationFunctionType
ALU = mybir.AluOpType
AX = mybir.AxisListType

B, S, D = 4, 8192, 1024
TOK = 4096
NT = 32
DT = 8
CH = 512
NCH = TOK // CH
NS = CH // 128
SLOT = 2048
NEL = 8
VW = 1026
NUMW = 1025
EPS_ATTN = 1e-6
EPS_LN = 1e-5
GSL = ((0, 512), (512, 1024), (1024, 1025))


def build_nc():
    nc = bass.Bass("TRN2", target_bir_lowering=False)

    def din(name, shape):
        return nc.dram_tensor(name, list(shape), F32, kind="ExternalInput").ap()

    xsh = din("xsh", [TOK, D])
    wqt = din("wqt", [D, D]); wkt = din("wkt", [D, D]); wvt = din("wvt", [D, D])
    bq = din("bq", [1, D]); bk = din("bk", [1, D]); bv = din("bv", [1, D])
    gam = din("gam", [1, 1])
    phiwt_d = din("phiwt", [D, SLOT]); phib = din("phib", [1, SLOT])
    ew = din("ew", [NEL, D, D]); eb = din("eb", [NEL, D])
    eye_d = din("eye", [128, 128])
    causal_d = din("causal", [128, 128])
    iotac_d = din("iota_col", [128, 1])
    iotar_d = din("iota_row", [1, 1056])
    flag_d = din("flag", [1, 1])
    out_d = nc.dram_tensor("out", [TOK, D], F32, kind="ExternalOutput").ap()

    sp_xn = nc.dram_tensor("sp_xn", [NT, 128, D], BF16).ap()
    sp_E = nc.dram_tensor("sp_E", [NT, 128, SLOT], BF16).ap()
    sp_dsm = nc.dram_tensor("sp_dsm", [NT, 128, SLOT], BF16).ap()
    sp_q = nc.dram_tensor("sp_q", [NCH, 128, DT * 512], BF16).ap()
    sp_k = nc.dram_tensor("sp_k", [NCH, 128, DT * 512], BF16).ap()
    sp_kh = nc.dram_tensor("sp_kh", [NCH, 128, NS * D], BF16).ap()
    sp_v = nc.dram_tensor("sp_v", [NCH, 128, NS * VW], BF16).ap()
    sp_ya = nc.dram_tensor("sp_ya", [NT, 128, D], F32).ap()
    lng_dr = nc.dram_tensor("lng_dr", [1, 1], F32).ap()
    ar_in = nc.dram_tensor("cc_ar_in", [SLOT, D], F32).ap()
    rs_out = nc.dram_tensor("cc_rs_out", [256, D], F32).ap()
    st_in = nc.dram_tensor("cc_st_in", [DT, 128, VW], BF16).ap()
    st_out = nc.dram_tensor("cc_st_out", [2, DT, 128, VW], BF16).ap()
    yt_in = nc.dram_tensor("cc_yt_in", [256, D], BF16).ap()
    yt_out = nc.dram_tensor("cc_yt_out", [SLOT, D], BF16).ap()

    with ExitStack() as stack:
        def sbt(name, shape, dt_):
            return stack.enter_context(nc.sbuf_tensor(name, list(shape), dt_))
        eye_f = sbt("eye_f", [128, 128], F32)
        eye_b = sbt("eye_b", [128, 128], BF16)
        causal = sbt("sb_causal", [128, 128], F32)
        iotac = sbt("sb_iotac", [128, 1], F32)
        iotaI = sbt("iotaI", [128, 512], F32)
        gam_s = sbt("gam_s", [1, 1], F32)
        lng = sbt("lng", [1, 1], F32)
        lng_bc = sbt("lng_bc", [128, 1], F32)
        nlng_bc = sbt("nlng_bc", [128, 1], F32)
        fcol = sbt("fcol", [128, 1], F32)
        g512c = sbt("g512c", [128, 1], F32)
        ccs = sbt("ccs", [128, 2], F32)
        asc = sbt("asc", [128, 4], F32)
        ksc = sbt("ksc", [128, 4], F32)
        bcols = sbt("bcols", [128, 8], F32)
        gIbf = sbt("gIbf", [128, 128], BF16)
        gammaI = sbt("gammaI", [128, 512], F32)
        ones_r = sbt("ones_r", [1, 512], BF16)
        stat = sbt("stat", [128, 16], F32)
        negmax = sbt("negmax", [128, 2], F32)
        denD = sbt("denD", [128, 128], F32)
        denC = sbt("denC", [128, 64], F32)
        att_sb = sbt("att_sb", [128, 1280], BF16)
        ycol = sbt("ycol", [128, 4], F32)
        A1 = sbt("A1", [128, 32768], BF16)
        A2 = sbt("A2", [128, 49664], BF16)
        A3 = sbt("A3", [128, 16416], BF16)
        PS1 = stack.enter_context(nc.psum_tensor("PS1", [128, 2048], F32))
        PS2 = stack.enter_context(nc.psum_tensor("PS2", [128, 2048], F32))

        A1b = A1[:]
        A1f = A1[:].bitcast(F32)
        A2f = A2[:].bitcast(F32)
        A3f = A3[:].bitcast(F32)

        bq_r = A2[0:1, 45056:46080]
        bk_r = A2[0:1, 46080:47104]
        bv_r = A2[0:1, 47104:48128]
        phib_r = A3[0:1, 14336:16384]
        ebrow = A2[0:1, 47120:49168]

        xnT = A1b
        xn4 = A1b
        SINIT_O = 16500
        YA_O = 12544
        W3_O = 0
        PHIT_O = 24576
        CB = 12288 + NS * VW
        CB_O = [0, CB]
        DSM_O = 40960
        EW_O = [0, 8192]
        XO_F = 16392
        SQ_F = 20480
        SQ6_F = 21512
        TANH_O = 36880
        XTT_O = 38928
        YTS_O = 40976
        CSM_O = [0, 2048]
        CST_O = [4096, 6144]
        YAB_F = 4096
        OUT_F = 6144
        X_F = [0, 1024]
        XN_O = [4096, 5120]
        E_O = [6144, 8192]
        DS_O = [10240, 12288]
        P3Q, P3K, P3H, P3V = 0, 4096, 8192, 12288
        ST_O = [0, DT * VW]

        def sm(name):
            return nc.alloc_semaphore(name)

        def sm2(name):
            return [nc.alloc_semaphore(name + "0"), nc.alloc_semaphore(name + "1")]

        q_c = sm("q_c")
        g_eye = sm("g_eye"); g_bias = sm("g_bias"); g_w = sm("g_w"); g_phi = sm("g_phi")
        s_pro = sm("s_pro"); v_pro = sm("v_pro")
        q_x = sm2("q_x"); s1a = sm("s1a"); v1s = sm("v1s"); s1b = sm("s1b")
        t1 = sm("t1"); v1e = sm("v1e"); q_xn = sm2("q_xn")
        t2 = sm("t2"); v2m = sm("v2m"); s2e = sm("s2e"); v2d = sm("v2d")
        q_E = sm2("q_E"); q_D = sm2("q_D")
        t3 = sm("t3"); t3h = sm("t3h"); v3q = sm("v3q"); s3k = sm("s3k")
        v3v = sm("v3v"); s3h = sm("s3h")
        q_3q = sm("q_3q"); q_3k = sm("q_3k"); q_3h = sm("q_3h"); q_3v = sm("q_3v")
        q_r4 = sm("q_r4"); g_d4 = sm2("g_d4"); t4 = sm("t4"); v4 = sm("v4")
        q_4 = sm2("q_4"); cc1 = sm("cc1")
        q_5 = sm2("q_5"); v5i = sm("v5i"); t5u = sm("t5u"); sv5u = sm("sv5u")
        t5a = sm("t5a"); v5a = sm("v5a"); t5y = sm("t5y"); v5y = sm("v5y")
        s5y = sm("s5y"); q_ya = sm2("q_ya"); q_st = sm("q_st"); cc2 = sm("cc2")
        q_si = sm("q_si"); v5c = sm("v5c"); t5c = sm("t5c"); v5n = sm("v5n")
        q_rs = sm("q_rs"); g_ew = sm2("g_ew"); v6 = sm("v6"); s6 = sm("s6")
        t6 = sm("t6"); v6e = sm("v6e"); q_yt = sm2("q_yt"); cc3 = sm("cc3")
        q_ytl = sm("q_ytl")
        g_7 = sm2("g_7"); q_7 = sm2("q_7"); v7c = sm("v7c"); t7t = sm("t7t")
        v7t = sm("v7t"); t7y = sm("t7y"); v7o = sm("v7o"); q_out = sm2("q_out")
        vd = sm("vd"); sd = sm("sd")
        q_t1 = sm2("q_t1"); gd = sm("gd")
        q_t7 = sm2("q_t7"); gd7 = sm("gd7"); v7a = sm("v7a"); q_y7 = sm2("q_y7"); v7x = sm("v7x")

        def psb(which, lo, hi, bf=False):
            t_ = PS1 if which == 0 else PS2
            ap = t_[:, lo:hi]
            return ap.bitcast(BF16) if bf else ap

        with nc.Block() as blk:

            @blk.tensor
            def _(t):
                t.wait_ge(g_eye, 16)
                t.wait_ge(g_phi, 16)
                t.wait_ge(g_bias, 16 * 4)
                t.wait_ge(v_pro, 13)
                for i in range(NT):
                    p = i % 2
                    t.wait_ge(q_t1[p], 16 * (i // 2 + 1))
                    if i >= 2:
                        t.wait_ge(s2e, i - 1)
                        t.wait_ge(v2m, i - 1)
                    log = (PS1 if p == 0 else PS2)[:, 0:2048]
                    for step in range(DT):
                        lhs = xnT[:, step * 4096 + i * 128: step * 4096 + (i + 1) * 128]
                        for cs in range(4):
                            t.matmul(log[:, cs * 512:(cs + 1) * 512], lhs,
                                     A2[:, PHIT_O + step * 2048 + cs * 512:
                                        PHIT_O + step * 2048 + (cs + 1) * 512],
                                     start=(step == 0), stop=False)
                    for cs in range(4):
                        ins = t.matmul(log[:, cs * 512:(cs + 1) * 512], ones_r[0:1, 0:128],
                                       phib_r[0:1, cs * 512:(cs + 1) * 512],
                                       start=False, stop=True)
                        if cs == 3:
                            ins.then_inc(t2, 1)

                t.wait_ge(g_w, 16 * 3)
                t.wait_ge(s2e, NT)
                t.wait_ge(v2m, NT)
                for c in range(NCH):
                    for j in range(16):
                        proj = 0 if j < 8 else 1
                        e = j % 8
                        pq = j % 2
                        if j >= 2:
                            jp = j - 2
                            if jp < 8:
                                t.wait_ge(v3q, c * 8 + jp + 1)
                            else:
                                t.wait_ge(s3k, c * 8 + (jp - 8) + 1)
                        elif c > 0:
                            t.wait_ge(s3k, (c - 1) * 8 + 6 + j + 1)
                        woff = W3_O + proj * 8192
                        brow = bq_r if proj == 0 else bk_r
                        ps = PS1[:, 512 * pq:512 * (pq + 1)]
                        for step in range(DT):
                            t.matmul(ps, A2[:, woff + step * 1024 + e * 128:
                                            woff + step * 1024 + (e + 1) * 128],
                                     xnT[:, step * 4096 + c * 512:
                                         step * 4096 + (c + 1) * 512],
                                     start=(step == 0), stop=False)
                        t.matmul(ps, brow[0:1, e * 128:(e + 1) * 128],
                                 ones_r[0:1, 0:512], start=False,
                                 stop=True).then_inc(t3, 1)
                    for g in range(8):
                        sbt_, cs = g // 2, g % 2
                        pv = g % 2
                        if g >= 2:
                            t.wait_ge(v3v, c * 8 + (g - 2) + 1)
                        elif c > 0:
                            t.wait_ge(v3v, (c - 1) * 8 + 6 + g + 1)
                        ps = PS1[:, 1024 + 512 * pv:1024 + 512 * (pv + 1)]
                        for step in range(DT):
                            t.matmul(ps, xnT[:, step * 4096 + c * 512 + sbt_ * 128:
                                             step * 4096 + c * 512 + (sbt_ + 1) * 128],
                                     A2[:, W3_O + 16384 + step * 1024 + cs * 512:
                                        W3_O + 16384 + step * 1024 + (cs + 1) * 512],
                                     start=(step == 0), stop=False)
                        t.matmul(ps, ones_r[0:1, 0:128],
                                 bv_r[0:1, cs * 512:(cs + 1) * 512],
                                 start=False, stop=True).then_inc(t3, 1)
                    t.wait_ge(s3k, 8 * (c + 1))
                    for sbt_ in range(NS):
                        ptk = sbt_ % 2
                        if c * 4 + sbt_ >= 2:
                            t.wait_ge(s3h, c * 4 + sbt_ - 1)
                        trb = psb(1, 512 * ptk, 512 * (ptk + 1), bf=True)
                        for dtl in range(DT):
                            ins = t.transpose(trb[:, dtl * 128:(dtl + 1) * 128],
                                              A3[:, P3K + dtl * 512 + sbt_ * 128:
                                                 P3K + dtl * 512 + (sbt_ + 1) * 128],
                                              eye_b[:])
                            if dtl == DT - 1:
                                ins.then_inc(t3h, 1)

                t.wait_ge(v3q, 64); t.wait_ge(s3k, 64); t.wait_ge(v3v, 64)
                t.wait_ge(q_r4, 16)
                for sl in range(16):
                    p = sl % 2
                    t.wait_ge(g_d4[p], 16 * (sl // 2 + 1))
                    if sl >= 2:
                        t.wait_ge(v4, sl - 1)
                    ps = PS1[:, 1024 * p:1024 * (p + 1)]
                    dsm_b = A2[:, DSM_O + 4096 * p:DSM_O + 4096 * (p + 1)]
                    for tt in range(NT):
                        for cs in range(2):
                            ins = t.matmul(ps[:, cs * 512:(cs + 1) * 512],
                                           dsm_b[:, tt * 128:(tt + 1) * 128],
                                           xn4[:, tt * 1024 + cs * 512:
                                               tt * 1024 + (cs + 1) * 512],
                                           start=(tt == 0), stop=(tt == NT - 1))
                            if tt == NT - 1 and cs == 1:
                                ins.then_inc(t4, 1)

                t.wait_ge(v4, 16)
                t.wait_ge(s3h, 32)
                t.wait_ge(v5i, 2)
                t.wait_ge(v_pro, 12)
                for c in range(NCH):
                    bc = CB_O[c % 2]
                    sp = ST_O[c % 2]
                    t.wait_ge(q_5[c % 2], 16 * 4 * (c // 2 + 1))
                    if c in (1, 2):
                        t.wait_ge(v5y, 4 * c)
                    elif c >= 3:
                        t.wait_ge(s5y, 4 * (c - 2))
                    for dk in range(DT):
                        pu = dk % 2
                        if c * 8 + dk >= 2:
                            t.wait_ge(sv5u, c * 8 + dk - 1)
                        ps = PS1 if pu == 0 else PS2
                        for lo, hi in GSL:
                            t.matmul(ps[:, lo:hi], gIbf[:],
                                     A3[:, sp + dk * VW + lo: sp + dk * VW + hi],
                                     start=True, stop=False)
                        for sbt_ in range(NS):
                            lhs = A2[:, bc + 8192 + sbt_ * 1024 + dk * 128:
                                     bc + 8192 + sbt_ * 1024 + (dk + 1) * 128]
                            for gi, (lo, hi) in enumerate(GSL):
                                ins = t.matmul(ps[:, lo:hi], lhs,
                                               A2[:, bc + 12288 + sbt_ * VW + lo:
                                                  bc + 12288 + sbt_ * VW + hi],
                                               start=False, stop=(sbt_ == NS - 1))
                                if sbt_ == NS - 1 and gi == 2:
                                    ins.then_inc(t5u, 1)
                    for b_ in range(NS):
                        for a_ in range(b_ + 1):
                            blkid = (b_ * (b_ + 1)) // 2 + a_
                            pa = blkid % 2
                            if c * 10 + blkid >= 2:
                                t.wait_ge(v5a, c * 10 + blkid - 1)
                            ps = (PS1 if pa == 0 else PS2)[:, 1536:1664]
                            for step in range(DT):
                                ins = t.matmul(ps, A2[:, bc + 4096 + step * 512 + a_ * 128:
                                                      bc + 4096 + step * 512 + (a_ + 1) * 128],
                                               A2[:, bc + step * 512 + b_ * 128:
                                                  bc + step * 512 + (b_ + 1) * 128],
                                               start=(step == 0), stop=(step == DT - 1))
                                if step == DT - 1:
                                    ins.then_inc(t5a, 1)
                    t.wait_ge(sv5u, 8 * (c + 1))
                    for b_ in range(NS):
                        py = b_ % 2
                        yidx = c * NS + b_
                        if yidx >= 2:
                            prev = yidx - 2
                            if prev < 2 * NS:
                                t.wait_ge(v5y, prev + 1)
                            else:
                                t.wait_ge(s5y, prev - 2 * NS + 1)
                        ps = PS1 if py == 0 else PS2
                        for step in range(DT):
                            lhs = A2[:, bc + step * 512 + b_ * 128:
                                     bc + step * 512 + (b_ + 1) * 128]
                            for lo, hi in GSL:
                                t.matmul(ps[:, lo:hi], lhs,
                                         A3[:, sp + step * VW + lo: sp + step * VW + hi],
                                         start=(step == 0), stop=False)
                        for a_ in range(b_ + 1):
                            blkid = (b_ * (b_ + 1)) // 2 + a_
                            t.wait_ge(v5a, c * 10 + blkid + 1)
                            for gi, (lo, hi) in enumerate(GSL):
                                ins = t.matmul(ps[:, lo:hi],
                                               att_sb[:, blkid * 128:(blkid + 1) * 128],
                                               A2[:, bc + 12288 + a_ * VW + lo:
                                                  bc + 12288 + a_ * VW + hi],
                                               start=False, stop=(a_ == b_))
                                if a_ == b_ and gi == 2:
                                    ins.then_inc(t5y, 1)

                t.wait_ge(t5y, NS * NCH)
                t.wait_ge(v5a, 10 * NCH)
                for i6 in range(2):
                    t.wait_ge(s6, 2 + i6 + 1)
                    trb = psb(i6, 1536, 2048, bf=True)
                    for dtl in range(DT):
                        ins = t.transpose(trb[:, dtl * 128:(dtl + 1) * 128],
                                          A2[:, TANH_O + i6 * 1024 + dtl * 128:
                                             TANH_O + i6 * 1024 + (dtl + 1) * 128],
                                          eye_b[:])
                        if dtl == DT - 1:
                            ins.then_inc(t6, 1)
                for n in range(NEL):
                    pe = n % 2
                    t.wait_ge(g_ew[pe], 16 * 2 * (n // 2 + 1))
                    t.wait_ge(v6e, 2)
                    if n >= 1:
                        t.wait_ge(v6e, 2 + 2 * n)
                    for cs in range(2):
                        ps = psb(cs, 1536, 2048)
                        for step in range(DT):
                            t.matmul(ps[0:32, :],
                                     A2[:, XTT_O + step * 256 + n * 32:
                                        XTT_O + step * 256 + n * 32 + 32],
                                     A3[:, EW_O[pe] + step * 1024 + cs * 512:
                                        EW_O[pe] + step * 1024 + (cs + 1) * 512],
                                     start=(step == 0), stop=False)
                        ins = t.matmul(ps[0:32, :],
                                       ones_r[0:1, 0:32],
                                       ebrow[0:1, pe * D + cs * 512:pe * D + (cs + 1) * 512],
                                       start=False, stop=True)
                        if cs == 1:
                            ins.then_inc(t6, 1)

                t.wait_ge(q_si, 16)
                t.wait_ge(v5c, 2)
                for cc in range(2):
                    for b_ in range(NS):
                        idx = cc * NS + b_
                        pyc = idx % 2
                        if idx < 2:
                            t.wait_ge(s5y, 24)
                        else:
                            t.wait_ge(v5n, idx - 1)
                        ps = PS1 if pyc == 0 else PS2
                        for step in range(DT):
                            lhs = A2[:, CB_O[cc] + step * 512 + b_ * 128:
                                     CB_O[cc] + step * 512 + (b_ + 1) * 128]
                            for gi, (lo, hi) in enumerate(GSL):
                                ins = t.matmul(ps[:, lo:hi], lhs,
                                               A1b[:, SINIT_O + step * VW + lo:
                                                   SINIT_O + step * VW + hi],
                                               start=(step == 0), stop=(step == DT - 1))
                                if step == DT - 1 and gi == 2:
                                    ins.then_inc(t5c, 1)

                t.wait_ge(q_ytl, 16)
                t.wait_ge(v6e, 18)
                t.wait_ge(v5n, 8)
                for i in range(NT):
                    p = i % 2
                    t.wait_ge(q_t7[p], 32 * (i // 2 + 1))
                    if i >= 2:
                        t.wait_ge(v7x, i - 1)
                    ps = PS1[:, 1024 * p:1024 * (p + 1)]
                    for sl in range(16):
                        for cs in range(2):
                            ins = t.matmul(ps[:, cs * 512:(cs + 1) * 512],
                                           A2[:, CST_O[p] + sl * 128:
                                              CST_O[p] + (sl + 1) * 128],
                                           A1b[:, sl * 1024 + cs * 512:
                                               sl * 1024 + (cs + 1) * 512],
                                           start=(sl == 0), stop=(sl == 15))
                            if sl == 15 and cs == 1:
                                ins.then_inc(t7y, 1)

            @blk.vector
            def _(v):
                nv = [0]

                def V(inst):
                    inst.then_inc(vd, 1)
                    nv[0] += 1

                def VB():
                    v.wait_ge(vd, nv[0])

                v.wait_ge(q_c, 16 * 8)
                v.tensor_scalar_mul(nlng_bc[:], lng_bc[:], -1.0).then_inc(v_pro, 1)
                for a_ in range(4):
                    v.tensor_scalar_mul(bcols[:, a_:a_ + 1], lng_bc[:],
                                        float(-(128 * a_ + 1))).then_inc(v_pro, 1)
                for s_ in range(4):
                    v.tensor_scalar_mul(bcols[:, 4 + s_:5 + s_], lng_bc[:],
                                        float(511 - 128 * s_)).then_inc(v_pro, 1)
                v.wait_ge(s_pro, 3)
                v.tensor_mul(ccs[:, 1:2], fcol[:], g512c[:]).then_inc(v_pro, 1)
                v.tensor_copy(ccs[:, 0:1], fcol[:]).then_inc(v_pro, 1)
                v.tensor_scalar_mul(gIbf[:], eye_f[:], g512c[:]).then_inc(v_pro, 1)
                v.memset(ones_r[:], 1.0).then_inc(v_pro, 1)

                for i in range(NT + 2):
                    if i < NT:
                        p = i % 2
                        v.wait_ge(q_x[p], 16 * (i // 2 + 1))
                        xt = A3f[:, X_F[p]:X_F[p] + 1024]
                        V(v.reduce_sum(stat[:, p * 8:p * 8 + 1], xt, axis=AX.X))
                        v.wait_ge(s1a, i + 1)
                        VB()
                        V(v.tensor_scalar_mul(stat[:, p * 8:p * 8 + 1],
                                              stat[:, p * 8:p * 8 + 1], 1.0 / D))
                        V(v.tensor_scalar_mul(stat[:, p * 8 + 2:p * 8 + 3],
                                              stat[:, p * 8 + 1:p * 8 + 2], 1.0 / D))
                        VB()
                        V(v.tensor_mul(stat[:, p * 8 + 3:p * 8 + 4],
                                       stat[:, p * 8:p * 8 + 1],
                                       stat[:, p * 8:p * 8 + 1]))
                        VB()
                        V(v.tensor_sub(stat[:, p * 8 + 2:p * 8 + 3],
                                       stat[:, p * 8 + 2:p * 8 + 3],
                                       stat[:, p * 8 + 3:p * 8 + 4]))
                        VB()
                        V(v.tensor_scalar_add(stat[:, p * 8 + 2:p * 8 + 3],
                                              stat[:, p * 8 + 2:p * 8 + 3], EPS_LN))
                        VB()
                        v.reciprocal(stat[:, p * 8 + 3:p * 8 + 4],
                                     stat[:, p * 8 + 2:p * 8 + 3]).then_inc(v1s, 1)
                    if 1 <= i <= NT:
                        j = i - 1
                        pj = j % 2
                        v.wait_ge(t2, j + 1)
                        v.reduce_max(negmax[:, pj:pj + 1],
                                     (PS1 if pj == 0 else PS2)[:, 0:2048],
                                     axis=AX.X, negate=True).then_inc(v2m, 1)
                    if i >= 2:
                        j = i - 2
                        pj = j % 2
                        v.wait_ge(s2e, j + 1)
                        if j >= 2:
                            v.wait_ge(v2d, j - 1)
                        E3 = A3[:, E_O[pj]:E_O[pj] + 2048].rearrange(
                            "p (n q) -> p n q", q=32)
                        V(v.reduce_sum(denD[:, 64 * pj:64 * pj + 64], E3, axis=AX.X))
                        VB()
                        v.reciprocal(denD[:, 64 * pj:64 * pj + 64],
                                     denD[:, 64 * pj:64 * pj + 64]).then_inc(gd, 1)

                v.wait_ge(s_pro, 11)
                for c in range(NCH):
                    for e in range(DT):
                        pq = e % 2
                        v.wait_ge(t3, c * 24 + e + 1)
                        if c >= 1 and e < 2:
                            v.wait_ge(q_3q, 16 * c)
                        ps = PS1[:, 512 * pq:512 * (pq + 1)]
                        v.scalar_tensor_tensor(
                            A3[:, P3Q + e * 512:P3Q + (e + 1) * 512], ps, 0.0,
                            gammaI[:], ALU.max, ALU.mult).then_inc(v3q, 1)
                    if c == 0:
                        v.wait_ge(q_D[0], 16 * 16)
                        v.wait_ge(q_D[1], 16 * 16)
                        v.wait_ge(q_E[0], 16 * 16)
                        v.wait_ge(q_E[1], 16 * 16)
                        for sbt_ in range(NS):
                            V(v.memset(A3[:, P3V + sbt_ * VW + 1024:
                                          P3V + sbt_ * VW + 1025], 1.0))
                            V(v.memset(A3[:, P3V + sbt_ * VW + 1025:
                                          P3V + sbt_ * VW + 1026], 0.0))
                    for g in range(8):
                        sbt_, cs = g // 2, g % 2
                        pv = g % 2
                        v.wait_ge(t3, c * 24 + 16 + g + 1)
                        if c >= 1 and g < 2:
                            v.wait_ge(q_3v, 16 * c)
                        ps = PS1[:, 1024 + 512 * pv:1024 + 512 * (pv + 1)]
                        v.tensor_copy(A3[:, P3V + sbt_ * VW + cs * 512:
                                         P3V + sbt_ * VW + cs * 512 + 512],
                                      ps).then_inc(v3v, 1)

                for sl in range(16):
                    p = sl % 2
                    v.wait_ge(t4, sl + 1)
                    if sl >= 2:
                        v.wait_ge(q_4[p], 16 * ((sl - 2) // 2 + 1))
                    ps = PS1[:, 1024 * p:1024 * (p + 1)]
                    v.tensor_copy(A3f[:, 1024 * p:1024 * (p + 1)], ps).then_inc(v4, 1)

                v.wait_ge(q_4[0], 16 * 8)
                v.wait_ge(q_4[1], 16 * 8)
                v.memset(A3[:, ST_O[0]:ST_O[0] + DT * VW], 0.0).then_inc(v5i, 1)
                v.memset(A3[:, ST_O[1]:ST_O[1] + DT * VW], 0.0).then_inc(v5i, 1)

                for c in range(NCH):
                    for b_ in range(NS):
                        for a_ in range(b_ + 1):
                            blkid = (b_ * (b_ + 1)) // 2 + a_
                            v.wait_ge(t5a, c * 10 + blkid + 1)
                            if c >= 1:
                                v.wait_ge(t5y, 4 * (c - 1) + b_ + 1)
                            pa = blkid % 2
                            ps = (PS1 if pa == 0 else PS2)[:, 1536:1664]
                            dst = att_sb[:, blkid * 128:(blkid + 1) * 128]
                            if a_ == b_:
                                v.scalar_tensor_tensor(dst, ps, asc[:, a_:a_ + 1],
                                                       causal[:], ALU.mult,
                                                       ALU.mult).then_inc(v5a, 1)
                            else:
                                v.tensor_scalar_mul(dst, ps,
                                                    asc[:, a_:a_ + 1]).then_inc(v5a, 1)
                    for b_ in range(NS):
                        py = b_ % 2
                        yidx = c * NS + b_
                        v.wait_ge(t5y, yidx + 1)
                        ps = PS1 if py == 0 else PS2
                        if c < 2:
                            v.tensor_copy(A1f[:, yidx * NUMW:(yidx + 1) * NUMW],
                                          ps[:, 0:NUMW]).then_inc(v5y, 1)
                        else:
                            if yidx >= 10:
                                v.wait_ge(s5y, yidx - 9)
                            V(v.tensor_scalar_add(ycol[:, py:py + 1], ps[:, 1024:1025],
                                                  EPS_ATTN))
                            VB()
                            v.reciprocal(ycol[:, 2 + py:3 + py],
                                         ycol[:, py:py + 1]).then_inc(v5y, 1)

                v.wait_ge(q_rs, 16)
                for i6 in range(2):
                    xo = A2f[:, XO_F + 1024 * i6:XO_F + 1024 * (i6 + 1)]
                    v.tensor_scalar(xo, xo, -33000.0, 65000.0,
                                    ALU.max, ALU.min).then_inc(v6, 1)
                    v.wait_ge(v6, 2 * i6 + 1)
                    V(v.reduce_sum(stat[:, 8 * i6:8 * i6 + 1], xo, axis=AX.X))
                    v.wait_ge(s6, i6 + 1)
                    VB()
                    V(v.tensor_scalar_mul(stat[:, 8 * i6:8 * i6 + 1],
                                          stat[:, 8 * i6:8 * i6 + 1], 1.0 / D))
                    V(v.tensor_scalar_mul(stat[:, 8 * i6 + 2:8 * i6 + 3],
                                          stat[:, 8 * i6 + 1:8 * i6 + 2], 1.0 / D))
                    VB()
                    V(v.tensor_mul(stat[:, 8 * i6 + 3:8 * i6 + 4],
                                   stat[:, 8 * i6:8 * i6 + 1], stat[:, 8 * i6:8 * i6 + 1]))
                    VB()
                    V(v.tensor_sub(stat[:, 8 * i6 + 2:8 * i6 + 3],
                                   stat[:, 8 * i6 + 2:8 * i6 + 3],
                                   stat[:, 8 * i6 + 3:8 * i6 + 4]))
                    VB()
                    V(v.tensor_scalar_add(stat[:, 8 * i6 + 2:8 * i6 + 3],
                                          stat[:, 8 * i6 + 2:8 * i6 + 3], EPS_LN))
                    VB()
                    v.reciprocal(stat[:, 8 * i6 + 3:8 * i6 + 4],
                                 stat[:, 8 * i6 + 2:8 * i6 + 3]).then_inc(v6, 1)
                for i6 in range(2):
                    v.wait_ge(t6, i6 + 1)
                    trb = psb(i6, 1536, 2048, bf=True)
                    src_ = trb.rearrange("p (t c) -> p t c", c=128)
                    dst = A2[:, XTT_O:XTT_O + 2048].rearrange(
                        "p (t c) -> p t c", c=256)[:, :, i6 * 128:(i6 + 1) * 128]
                    v.tensor_copy(dst, src_).then_inc(v6e, 1)
                for n in range(NEL):
                    pe = n % 2
                    v.wait_ge(t6, 2 + n + 1)
                    if n >= 2:
                        v.wait_ge(q_yt[pe], 16 * ((n - 2) // 2 + 1))
                    for cs in range(2):
                        ps = psb(cs, 1536, 2048)[0:32, :]
                        v.tensor_copy(A2[0:32, YTS_O + pe * D + cs * 512:
                                         YTS_O + pe * D + (cs + 1) * 512],
                                      ps).then_inc(v6e, 1)

                for cc in range(2):
                    v.wait_ge(q_5[cc], 16 * 17)
                    v.wait_ge(v_pro, 11)
                    qb = A2[:, CB_O[cc]:CB_O[cc] + 4096]
                    v.tensor_scalar_mul(qb, qb, ccs[:, cc:cc + 1]).then_inc(v5c, 1)
                for cc in range(2):
                    for b_ in range(NS):
                        idx = cc * NS + b_
                        pyc = idx % 2
                        v.wait_ge(t5c, idx + 1)
                        v.wait_ge(s5y, 23 + idx)
                        ps = PS1 if pyc == 0 else PS2
                        nh = A1f[:, idx * NUMW:(idx + 1) * NUMW]
                        V(v.tensor_add(nh, nh, ps[:, 0:NUMW]))
                        VB()
                        V(v.tensor_scalar_add(ycol[:, pyc:pyc + 1], nh[:, 1024:1025],
                                              EPS_ATTN))
                        VB()
                        v.reciprocal(ycol[:, 2 + pyc:3 + pyc],
                                     ycol[:, pyc:pyc + 1]).then_inc(v5n, 1)

                def _v7_add(j):
                    pj = j % 2
                    v.wait_ge(t7y, j + 1)
                    v.wait_ge(q_7[pj], 16 * (j // 2 + 1))
                    if j >= 2:
                        v.wait_ge(q_out[pj], 16 * ((j - 2) // 2 + 1))
                        v.wait_ge(v7a, j - 1)
                    ps = PS1[:, 1024 * pj:1024 * (pj + 1)]
                    ob = A2f[:, OUT_F + 1024 * pj:OUT_F + 1024 * (pj + 1)]
                    v.tensor_add(ob, ps, A3f[:, X_F[pj]:X_F[pj] + 1024]).then_inc(v7x, 1)

                for i in range(NT):
                    p = i % 2
                    v.wait_ge(g_7[p], 16 * (i // 2 + 1))
                    E3 = A2[:, CSM_O[p]:CSM_O[p] + 2048].rearrange("p (n q) -> p n q", q=32)
                    E3t = E3.transpose([0, 2, 1])
                    V(v.reduce_sum(denC[:, 32 * p:32 * p + 32], E3t, axis=AX.X))
                    VB()
                    V(v.reciprocal(denC[:, 32 * p:32 * p + 32], denC[:, 32 * p:32 * p + 32]))
                    rC = denC[:, 32 * p:32 * p + 32].unsqueeze(1).broadcast_to([128, 64, 32])
                    VB()
                    v.tensor_mul(E3, E3, rC).then_inc(v7c, 1)
                    if i >= 1:
                        _v7_add(i - 1)
                _v7_add(NT - 1)

            @blk.scalar
            def _(s):
                ns = [0]

                def SC(inst):
                    inst.then_inc(sd, 1)
                    ns[0] += 1

                def SB():
                    s.wait_ge(sd, ns[0])

                s.wait_ge(q_c, 16 * 6)
                s.activation(lng[:], gam_s[:], AF.Ln).then_inc(s_pro, 1)
                s.wait_ge(q_c, 16 * 8)
                s.activation(gammaI[:], iotaI[:], AF.Exp,
                             scale=lng_bc[:]).then_inc(s_pro, 1)
                s.activation(g512c[:], lng_bc[:], AF.Exp, scale=512.0).then_inc(s_pro, 1)
                s.wait_ge(v_pro, 9)
                for a_ in range(4):
                    s.activation(asc[:, a_:a_ + 1], iotac[:], AF.Exp,
                                 scale=nlng_bc[:], bias=bcols[:, a_:a_ + 1]).then_inc(s_pro, 1)
                for s_ in range(4):
                    s.activation(ksc[:, s_:s_ + 1], iotac[:], AF.Exp,
                                 scale=nlng_bc[:], bias=bcols[:, 4 + s_:5 + s_]).then_inc(s_pro, 1)

                for i in range(NT + 3):
                    if i < NT:
                        p = i % 2
                        s.wait_ge(q_x[p], 16 * (i // 2 + 1))
                        xt = A3f[:, X_F[p]:X_F[p] + 1024]
                        s.activation(A2f[:, SQ_F + 1024 * p:SQ_F + 1024 * (p + 1)], xt,
                                     AF.Square,
                                     accum_out=stat[:, p * 8 + 1:p * 8 + 2]
                                     ).then_inc(s1a, 1)
                        s.wait_ge(v1s, i + 1)
                        SC(s.activation(stat[:, p * 8 + 4:p * 8 + 5],
                                        stat[:, p * 8 + 3:p * 8 + 4], AF.Sqrt))
                        SB()
                        SC(s.mul(stat[:, p * 8 + 5:p * 8 + 6], stat[:, p * 8:p * 8 + 1],
                                 stat[:, p * 8 + 4:p * 8 + 5]))
                        SB()
                        SC(s.mul(stat[:, p * 8 + 5:p * 8 + 6],
                                 stat[:, p * 8 + 5:p * 8 + 6], -1.0))
                        if i >= 2:
                            s.wait_ge(q_xn[p], 16 * ((i - 2) // 2 + 1))
                            s.wait_ge(q_t1[p], 16 * ((i - 2) // 2 + 1))
                        SB()
                        s.activation(A3[:, XN_O[p]:XN_O[p] + 1024], xt, AF.Identity,
                                     scale=stat[:, p * 8 + 4:p * 8 + 5],
                                     bias=stat[:, p * 8 + 5:p * 8 + 6]).then_inc(s1b, 1)
                    if 2 <= i < NT + 2:
                        j = i - 2
                        pj = j % 2
                        s.wait_ge(v2m, j + 1)
                        if j >= 2:
                            s.wait_ge(v2d, j - 1)
                            s.wait_ge(q_E[pj], 16 * ((j - 2) // 2 + 1))
                        log = (PS1 if pj == 0 else PS2)[:, 0:2048]
                        s.activation(A3[:, E_O[pj]:E_O[pj] + 2048], log, AF.Exp,
                                     bias=negmax[:, pj:pj + 1]).then_inc(s2e, 1)
                    if i >= 3:
                        j = i - 3
                        pj = j % 2
                        s.wait_ge(v2d, j + 1)
                        s.dma_start(out=sp_dsm[j], in_=A3[:, DS_O[pj]:DS_O[pj] + 2048]
                                    ).then_inc(q_D[pj], 16)

                s.wait_ge(q_E[0], 16 * 16)
                s.wait_ge(q_E[1], 16 * 16)
                s.wait_ge(q_xn[0], 16 * 16)
                s.wait_ge(q_xn[1], 16 * 16)
                s.wait_ge(s_pro, 11)
                for c in range(NCH):
                    for e in range(DT):
                        pq = e % 2
                        s.wait_ge(t3, c * 24 + 8 + e + 1)
                        if c >= 1 and e < 2:
                            s.wait_ge(q_3k, 16 * c)
                        ps = PS1[:, 512 * pq:512 * (pq + 1)]
                        s.activation(A3[:, P3K + e * 512:P3K + (e + 1) * 512], ps,
                                     AF.Relu).then_inc(s3k, 1)
                    for sbt_ in range(NS):
                        ptk = sbt_ % 2
                        s.wait_ge(t3h, c * 4 + sbt_ + 1)
                        if c >= 1 and sbt_ < 2:
                            s.wait_ge(q_3h, 16 * c)
                        trb = psb(1, 512 * ptk, 512 * (ptk + 1), bf=True)
                        s.mul(A3[:, P3H + sbt_ * 1024:P3H + (sbt_ + 1) * 1024], trb,
                              ksc[:, sbt_:sbt_ + 1]).then_inc(s3h, 1)

                for c in range(NCH):
                    sq_ = ST_O[(c + 1) % 2]
                    for dk in range(DT):
                        pu = dk % 2
                        s.wait_ge(t5u, c * 8 + dk + 1)
                        ps = PS1 if pu == 0 else PS2
                        s.copy(A3[:, sq_ + dk * VW:sq_ + dk * VW + NUMW],
                               ps[:, 0:NUMW]).then_inc(sv5u, 1)
                    if c >= 2:
                        for b_ in range(NS):
                            py = b_ % 2
                            yidx = c * NS + b_
                            s.wait_ge(v5y, yidx + 1)
                            if yidx >= 10:
                                s.wait_ge(q_ya[py], 16 * ((yidx - 10) // 2 + 1))
                            ps = PS1 if py == 0 else PS2
                            s.mul(A1f[:, YA_O + 1024 * py:YA_O + 1024 * (py + 1)],
                                  ps[:, 0:1024], ycol[:, 2 + py:3 + py]).then_inc(s5y, 1)

                for i6 in range(2):
                    s.wait_ge(v6, 2 * i6 + 1)
                    xo = A2f[:, XO_F + 1024 * i6:XO_F + 1024 * (i6 + 1)]
                    s.activation(A2f[:, SQ6_F + 1024 * i6:SQ6_F + 1024 * (i6 + 1)], xo,
                                 AF.Square,
                                 accum_out=stat[:, 8 * i6 + 1:8 * i6 + 2]).then_inc(s6, 1)
                for i6 in range(2):
                    s.wait_ge(v6, 2 * i6 + 2)
                    xo = A2f[:, XO_F + 1024 * i6:XO_F + 1024 * (i6 + 1)]
                    SC(s.activation(stat[:, 8 * i6 + 4:8 * i6 + 5],
                                    stat[:, 8 * i6 + 3:8 * i6 + 4], AF.Sqrt))
                    SB()
                    SC(s.mul(stat[:, 8 * i6 + 5:8 * i6 + 6], stat[:, 8 * i6:8 * i6 + 1],
                             stat[:, 8 * i6 + 4:8 * i6 + 5]))
                    SB()
                    SC(s.mul(stat[:, 8 * i6 + 5:8 * i6 + 6],
                             stat[:, 8 * i6 + 5:8 * i6 + 6], -1.0))
                    SB()
                    s.activation(A2[:, TANH_O + i6 * 1024:TANH_O + (i6 + 1) * 1024], xo,
                                 AF.Tanh, scale=stat[:, 8 * i6 + 4:8 * i6 + 5],
                                 bias=stat[:, 8 * i6 + 5:8 * i6 + 6]).then_inc(s6, 1)

                for cc in range(2):
                    for b_ in range(NS):
                        idx = cc * NS + b_
                        pyc = idx % 2
                        s.wait_ge(v5n, idx + 1)
                        if idx < 2:
                            s.wait_ge(q_ya[pyc], 16 * 12)
                        else:
                            s.wait_ge(q_ya[pyc], 16 * (12 + (idx - 2) // 2 + 1))
                        nh = A1f[:, idx * NUMW:(idx + 1) * NUMW]
                        s.mul(A1f[:, YA_O + 1024 * pyc:YA_O + 1024 * (pyc + 1)],
                              nh[:, 0:1024], ycol[:, 2 + pyc:3 + pyc]).then_inc(s5y, 1)

                for i in range(NT):
                    p = i % 2
                    s.wait_ge(v7c, i + 1)
                    if i >= 2:
                        s.wait_ge(t7y, i - 1)
                    for half in range(2):
                        src_ = A2[:, CSM_O[p] + half * 1024:CSM_O[p] + (half + 1) * 1024
                                  ].rearrange("p (t c) -> p t c", c=128)
                        dst = A2[:, CST_O[p] + half * 1024:CST_O[p] + (half + 1) * 1024
                                 ].rearrange("p (t c) -> p t c", c=128)
                        s.dma_start_transpose(dst, src_).then_inc(q_t7[p], 16)


            @blk.sync
            def _(sy):
                sy.dma_start(out=eye_f[:], in_=eye_d).then_inc(q_c, 16)
                sy.dma_start(out=causal[:], in_=causal_d).then_inc(q_c, 16)
                sy.dma_start(out=iotac[:], in_=iotac_d).then_inc(q_c, 16)
                sy.dma_start(out=gam_s[:], in_=gam).then_inc(q_c, 16)
                sy.dma_start(out=fcol[:], in_=flag_d[0:1, 0:1].partition_broadcast(128)
                             ).then_inc(q_c, 16)
                sy.dma_start(out=iotaI[:], in_=iotar_d[0:1, 1:513].partition_broadcast(128)
                             ).then_inc(q_c, 16)
                sy.wait_ge(s_pro, 1)
                sy.dma_start(out=lng_dr, in_=lng[:]).then_inc(q_c, 16)
                sy.wait_ge(q_c, 16 * 7)
                sy.dma_start(out=lng_bc[:], in_=lng_dr[0:1, 0:1].partition_broadcast(128)
                             ).then_inc(q_c, 16)

                for i in range(NT + 3):
                    if i < NT:
                        p = i % 2
                        if i >= 2:
                            sy.wait_ge(s1a, i - 1)
                            sy.wait_ge(s1b, i - 1)
                            sy.wait_ge(v1s, i - 1)
                        sy.dma_start(out=A3f[:, X_F[p]:X_F[p] + 1024],
                                     in_=xsh[i * 128:(i + 1) * 128, :]).then_inc(q_x[p], 16)
                    if 1 <= i <= NT:
                        j = i - 1
                        pp = j % 2
                        sy.wait_ge(s1b, j + 1)
                        sy.dma_start(out=sp_xn[j], in_=A3[:, XN_O[pp]:XN_O[pp] + 1024]
                                     ).then_inc(q_xn[pp], 16)
                        sy.dma_start_transpose(
                            A1b.rearrange("p (t c) -> p t c", c=4096
                                          )[:, :, j * 128:(j + 1) * 128],
                            A3[:, XN_O[pp]:XN_O[pp] + 1024].rearrange(
                                "p (t c) -> p t c", c=128)).then_inc(q_t1[pp], 16)
                    if i >= 3:
                        j = i - 3
                        pj = j % 2
                        sy.wait_ge(s2e, j + 1)
                        sy.dma_start(out=sp_E[j], in_=A3[:, E_O[pj]:E_O[pj] + 2048]
                                     ).then_inc(q_E[pj], 16)

                for c in range(NCH):
                    sy.wait_ge(v3q, 8 * (c + 1))
                    sy.dma_start(out=sp_q[c], in_=A3[:, P3Q:P3Q + 4096]).then_inc(q_3q, 16)
                    sy.wait_ge(s3k, 8 * (c + 1))
                    sy.dma_start(out=sp_k[c], in_=A3[:, P3K:P3K + 4096]).then_inc(q_3k, 16)
                    sy.wait_ge(s3h, 4 * (c + 1))
                    sy.dma_start(out=sp_kh[c], in_=A3[:, P3H:P3H + 4096]).then_inc(q_3h, 16)
                    sy.wait_ge(v3v, 8 * (c + 1))
                    sy.dma_start(out=sp_v[c], in_=A3[:, P3V:P3V + NS * VW]).then_inc(q_3v, 16)

                sy.wait_ge(q_3q, 16 * 8); sy.wait_ge(q_3k, 16 * 8)
                sy.wait_ge(q_3h, 16 * 8); sy.wait_ge(q_3v, 16 * 8)
                sy.wait_ge(q_xn[0], 16 * 16); sy.wait_ge(q_xn[1], 16 * 16)
                sy.wait_ge(t3h, 4 * NCH)
                sy.dma_start(out=xn4[:, 0:32768].rearrange("p (t c) -> p t c", c=1024),
                             in_=sp_xn.transpose([1, 0, 2])).then_inc(q_r4, 16)
                for sl in range(16):
                    p = sl % 2
                    sy.wait_ge(v4, sl + 1)
                    sy.dma_start(out=ar_in[sl * 128:(sl + 1) * 128, :],
                                 in_=A3f[:, 1024 * p:1024 * (p + 1)]).then_inc(q_4[p], 16)

                def _ya_store(c_):
                    for b_ in range(NS):
                        yidx = c_ * NS + b_
                        py = b_ % 2
                        sy.wait_ge(s5y, (yidx - 2 * NS) + 1)
                        sy.dma_start(out=sp_ya[yidx],
                                     in_=A1f[:, YA_O + 1024 * py:YA_O + 1024 * (py + 1)]
                                     ).then_inc(q_ya[py], 16)
                for c in range(NCH):
                    bc = CB_O[c % 2]
                    if c >= 2:
                        sy.wait_ge(t5y, NS * (c - 1))
                    sy.dma_start(out=A2[:, bc:bc + 4096], in_=sp_q[c]).then_inc(q_5[c % 2], 16)
                    sy.dma_start(out=A2[:, bc + 4096:bc + 8192],
                                 in_=sp_k[c]).then_inc(q_5[c % 2], 16)
                    sy.dma_start(out=A2[:, bc + 8192:bc + 12288],
                                 in_=sp_kh[c]).then_inc(q_5[c % 2], 16)
                    sy.dma_start(out=A2[:, bc + 12288:bc + 12288 + NS * VW],
                                 in_=sp_v[c]).then_inc(q_5[c % 2], 16)
                    if 2 <= c - 1:
                        _ya_store(c - 1)
                _ya_store(NCH - 1)
                sy.wait_ge(cc1, 1)
                sy.dma_start(out=A2f[:, XO_F:XO_F + 2048].rearrange(
                    "p (t c) -> p t c", c=1024),
                    in_=rs_out.rearrange("(t p) c -> p t c", p=128)).then_inc(q_rs, 16)
                sy.wait_ge(sv5u, 8 * NCH)
                sy.dma_start(out=st_in.transpose([1, 0, 2]),
                             in_=A3[:, ST_O[0]:ST_O[0] + DT * VW].rearrange(
                                 "p (t c) -> p t c", c=VW)).then_inc(q_st, 16)
                for cc in range(2):
                    sy.wait_ge(t5y, NS * NCH)
                    sy.dma_start(out=A2[:, CB_O[cc]:CB_O[cc] + 4096],
                                 in_=sp_q[cc]).then_inc(q_5[cc], 16)
                sy.wait_ge(cc2, 1)
                sy.dma_start(out=A1b[:, SINIT_O:SINIT_O + DT * VW].rearrange(
                    "p (t c) -> p t c", c=VW),
                    in_=st_out[0].transpose([1, 0, 2])).then_inc(q_si, 16)
                for n in range(NEL):
                    pe = n % 2
                    sy.wait_ge(v6e, 2 + 2 * (n + 1))
                    sy.dma_start(out=yt_in[n * 32:(n + 1) * 32, :],
                                 in_=A2[0:32, YTS_O + pe * D:YTS_O + (pe + 1) * D]
                                 ).then_inc(q_yt[pe], 16)
                for cc in range(2):
                    for b_ in range(NS):
                        idx = cc * NS + b_
                        pyc = idx % 2
                        sy.wait_ge(s5y, NS * (NCH - 2) + idx + 1)
                        sy.dma_start(out=sp_ya[idx],
                                     in_=A1f[:, YA_O + 1024 * pyc:YA_O + 1024 * (pyc + 1)]
                                     ).then_inc(q_ya[pyc], 16)
                sy.wait_ge(cc3, 1)
                sy.wait_ge(s5y, 32)
                sy.dma_start(out=A1b[:, 0:16384].rearrange("p (t c) -> p t c", c=1024),
                             in_=yt_out.rearrange("(t p) c -> p t c", p=128)
                             ).then_inc(q_ytl, 16)

                sy.wait_ge(q_st, 16)
                sy.wait_ge(t6, 10)
                def _out_store(j):
                    pj = j % 2
                    sy.wait_ge(v7a, j + 1)
                    sy.dma_start(out=out_d[j * 128:(j + 1) * 128, :],
                                 in_=A2f[:, OUT_F + 1024 * pj:OUT_F + 1024 * (pj + 1)]
                                 ).then_inc(q_out[pj], 16)
                for i in range(NT):
                    p = i % 2
                    if i >= 2:
                        sy.wait_ge(v7x, i - 1)
                    sy.dma_start(out=A3f[:, X_F[p]:X_F[p] + 1024],
                                 in_=xsh[i * 128:(i + 1) * 128, :]).then_inc(q_7[p], 16)
                    if i >= 2:
                        _out_store(i - 2)
                _out_store(NT - 2)
                _out_store(NT - 1)
                sy.wait_ge(q_out[0], 16 * 16)
                sy.wait_ge(q_out[1], 16 * 16)

            @blk.gpsimd
            def _(g):
                g.dma_start(out=eye_b[:], in_=eye_d).then_inc(g_eye, 16)
                g.dma_start(out=bq_r, in_=bq).then_inc(g_bias, 16)
                g.dma_start(out=bk_r, in_=bk).then_inc(g_bias, 16)
                g.dma_start(out=bv_r, in_=bv).then_inc(g_bias, 16)
                g.dma_start(out=phib_r, in_=phib).then_inc(g_bias, 16)
                for wi, wsrc in enumerate((wqt, wkt, wvt)):
                    g.dma_start(out=A2[:, W3_O + wi * 8192:W3_O + (wi + 1) * 8192
                                       ].rearrange("p (t c) -> p t c", c=1024),
                                in_=wsrc.rearrange("(t p) c -> p t c", p=128)
                                ).then_inc(g_w, 16)
                g.dma_start(out=A2[:, PHIT_O:PHIT_O + 16384].rearrange(
                    "p (t c) -> p t c", c=2048),
                    in_=phiwt_d.rearrange("(t p) c -> p t c", p=128)).then_inc(g_phi, 16)

                for i in range(NT):
                    p = i % 2
                    g.wait_ge(gd, i + 1)
                    if i >= 2:
                        g.wait_ge(q_D[p], 16 * ((i - 2) // 2 + 1))
                    E3g = A3[:, E_O[p]:E_O[p] + 2048].rearrange("p (n q) -> p n q", q=32)
                    d3g = A3[:, DS_O[p]:DS_O[p] + 2048].rearrange("p (n q) -> p n q", q=32)
                    rDg = denD[:, 64 * p:64 * p + 64].unsqueeze(2).broadcast_to(
                        [128, 64, 32])
                    g.tensor_mul(d3g, E3g, rDg).then_inc(v2d, 1)

                g.wait_ge(q_D[0], 16 * 16)
                g.wait_ge(q_D[1], 16 * 16)
                g.wait_ge(t3h, 4 * NCH)
                for sl in range(16):
                    p = sl % 2
                    if sl >= 2:
                        g.wait_ge(t4, sl - 1)
                    src = sp_dsm.transpose([1, 0, 2])[:, :, sl * 128:(sl + 1) * 128]
                    g.dma_start(out=A2[:, DSM_O + 4096 * p:DSM_O + 4096 * (p + 1)
                                       ].rearrange("p (t c) -> p t c", c=128),
                                in_=src).then_inc(g_d4[p], 16)
                g.wait_ge(q_4[0], 16 * 8)
                g.wait_ge(q_4[1], 16 * 8)
                g.collective_compute("ReduceScatter", ALU.add,
                                     replica_groups=[list(range(8))],
                                     ins=[ar_in.opt()], outs=[rs_out.opt()]
                                     ).then_inc(cc1, 1)
                g.wait_ge(q_st, 16)
                g.collective_compute("AllGather", ALU.bypass,
                                     replica_groups=[[0, 1], [2, 3], [4, 5], [6, 7]],
                                     ins=[st_in.opt()], outs=[st_out.opt()]
                                     ).then_inc(cc2, 1)
                for n in range(NEL):
                    pe = n % 2
                    if n < 2:
                        g.wait_ge(q_st, 16)
                        g.wait_ge(t5y, NS * NCH)
                    else:
                        g.wait_ge(t6, 2 + (n - 2) + 1)
                    g.dma_start(out=A3[:, EW_O[pe]:EW_O[pe] + 8192].rearrange(
                        "p (t c) -> p t c", c=1024),
                        in_=ew[n].rearrange("(t p) c -> p t c", p=128)).then_inc(g_ew[pe], 16)
                    g.dma_start(out=ebrow[0:1, pe * D:(pe + 1) * D],
                                in_=eb[n:n + 1, :]).then_inc(g_ew[pe], 16)
                g.wait_ge(q_yt[0], 16 * 4)
                g.wait_ge(q_yt[1], 16 * 4)
                g.collective_compute("AllGather", ALU.bypass,
                                     replica_groups=[list(range(8))],
                                     ins=[yt_in.opt()], outs=[yt_out.opt()]
                                     ).then_inc(cc3, 1)
                g.wait_ge(t6, 10)
                g.wait_ge(t5c, 8)
                g.wait_ge(q_ya[0], 16 * 16)
                g.wait_ge(q_ya[1], 16 * 16)

                def _g_add2(j):
                    pj = j % 2
                    g.wait_ge(v7x, j + 1)
                    g.wait_ge(q_y7[pj], 16 * (j // 2 + 1))
                    ob = A2f[:, OUT_F + 1024 * pj:OUT_F + 1024 * (pj + 1)]
                    g.tensor_add(ob, ob,
                                 A2f[:, YAB_F + 1024 * pj:YAB_F + 1024 * (pj + 1)]
                                 ).then_inc(v7a, 1)

                for i in range(NT):
                    p = i % 2
                    if i >= 2:
                        g.wait_ge(q_t7[p], 32 * ((i - 2) // 2 + 1))
                    g.dma_start(out=A2[:, CSM_O[p]:CSM_O[p] + 2048],
                                in_=sp_E[i]).then_inc(g_7[p], 16)
                    if i >= 2:
                        g.wait_ge(v7a, i - 1)
                    g.dma_start(out=A2f[:, YAB_F + 1024 * p:YAB_F + 1024 * (p + 1)],
                                in_=sp_ya[i]).then_inc(q_y7[p], 16)
                    if i >= 1:
                        _g_add2(i - 1)
                _g_add2(NT - 1)

    return nc


def kernel(L0_attn_x_seq, L0_attn_Wq, L0_attn_bq, L0_attn_Wk, L0_attn_bk,
           L0_attn_Wv, L0_attn_bv, L0_attn_gamma, L0_ffn_phi_w, L0_ffn_phi_b,
           L0_ffn_expert_w, L0_ffn_expert_b):
    x = np.asarray(L0_attn_x_seq)
    nc = build_nc()
    eye = np.eye(128, dtype=np.float32)
    causal = np.ascontiguousarray(np.tril(np.ones((128, 128), dtype=np.float32)).T)
    iota_col = np.arange(128, dtype=np.float32).reshape(128, 1)
    iota_row = np.arange(1056, dtype=np.float32).reshape(1, 1056)
    wqt = np.ascontiguousarray(np.asarray(L0_attn_Wq).T)
    wkt = np.ascontiguousarray(np.asarray(L0_attn_Wk).T)
    wvt = np.ascontiguousarray(np.asarray(L0_attn_Wv).T)
    phiwt = np.ascontiguousarray(np.asarray(L0_ffn_phi_w).T)
    in_maps = []
    for c in range(8):
        b, h = c // 2, c % 2
        in_maps.append({
            "xsh": x[b][h * TOK:(h + 1) * TOK],
            "wqt": wqt, "wkt": wkt, "wvt": wvt,
            "bq": np.asarray(L0_attn_bq).reshape(1, D),
            "bk": np.asarray(L0_attn_bk).reshape(1, D),
            "bv": np.asarray(L0_attn_bv).reshape(1, D),
            "gam": np.asarray(L0_attn_gamma, dtype=np.float32).reshape(1, 1),
            "phiwt": phiwt,
            "phib": np.asarray(L0_ffn_phi_b).reshape(1, SLOT),
            "ew": np.ascontiguousarray(np.asarray(L0_ffn_expert_w)[c * NEL:(c + 1) * NEL]),
            "eb": np.ascontiguousarray(np.asarray(L0_ffn_expert_b)[c * NEL:(c + 1) * NEL]),
            "eye": eye, "causal": causal,
            "iota_col": iota_col, "iota_row": iota_row,
            "flag": np.array([[float(h)]], dtype=np.float32),
        })
    res = run_bass_kernel_spmd(nc, in_maps, core_ids=list(range(8)))
    full = np.empty((B, S, D), dtype=np.float32)
    for c in range(8):
        b, h = c // 2, c % 2
        full[b, h * TOK:(h + 1) * TOK] = res.results[c]["out"]
    return full


# version 06
# speedup vs baseline: 1.1589x; 1.0456x over previous
import numpy as np
from contextlib import ExitStack
import concourse.bass as bass
import concourse.mybir as mybir
from concourse.bass_utils import run_bass_kernel_spmd

F32 = mybir.dt.float32
BF16 = mybir.dt.bfloat16
AF = mybir.ActivationFunctionType
ALU = mybir.AluOpType
AX = mybir.AxisListType

B, S, D = 4, 8192, 1024
TOK = 4096
NT = 32
DT = 8
CH = 512
NCH = TOK // CH
NS = CH // 128
SLOT = 2048
NEL = 8
VW = 1026
NUMW = 1025
EPS_ATTN = 1e-6
EPS_LN = 1e-5
GSL = ((0, 512), (512, 1024), (1024, 1025))


def build_nc():
    nc = bass.Bass("TRN2", target_bir_lowering=False)

    def din(name, shape):
        return nc.dram_tensor(name, list(shape), F32, kind="ExternalInput").ap()

    xsh = din("xsh", [TOK, D])
    wqt = din("wqt", [D, D]); wkt = din("wkt", [D, D]); wvt = din("wvt", [D, D])
    bq = din("bq", [1, D]); bk = din("bk", [1, D]); bv = din("bv", [1, D])
    gam = din("gam", [1, 1])
    phiwt_d = din("phiwt", [D, SLOT]); phib = din("phib", [1, SLOT])
    ew = din("ew", [NEL, D, D]); eb = din("eb", [NEL, D])
    eye_d = din("eye", [128, 128])
    causal_d = din("causal", [128, 128])
    iotac_d = din("iota_col", [128, 1])
    iotar_d = din("iota_row", [1, 1056])
    flag_d = din("flag", [1, 1])
    out_d = nc.dram_tensor("out", [TOK, D], F32, kind="ExternalOutput").ap()

    sp_xn = nc.dram_tensor("sp_xn", [NT, 128, D], BF16).ap()
    sp_E = nc.dram_tensor("sp_E", [NT, 128, SLOT], BF16).ap()
    sp_dsm = nc.dram_tensor("sp_dsm", [NT, 128, SLOT], BF16).ap()
    sp_q = nc.dram_tensor("sp_q", [NCH, 128, DT * 512], BF16).ap()
    sp_k = nc.dram_tensor("sp_k", [NCH, 128, DT * 512], BF16).ap()
    sp_kh = nc.dram_tensor("sp_kh", [NCH, 128, NS * D], BF16).ap()
    sp_v = nc.dram_tensor("sp_v", [NCH, 128, NS * VW], BF16).ap()
    sp_ya = nc.dram_tensor("sp_ya", [NT, 128, D], F32).ap()
    lng_dr = nc.dram_tensor("lng_dr", [1, 1], F32).ap()
    ar_in = nc.dram_tensor("cc_ar_in", [SLOT, D], F32).ap()
    rs_out = nc.dram_tensor("cc_rs_out", [256, D], F32).ap()
    st_in = nc.dram_tensor("cc_st_in", [DT, 128, VW], BF16).ap()
    st_out = nc.dram_tensor("cc_st_out", [2, DT, 128, VW], BF16).ap()
    yt_in = nc.dram_tensor("cc_yt_in", [256, D], BF16).ap()
    yt_out = nc.dram_tensor("cc_yt_out", [SLOT, D], BF16).ap()

    with ExitStack() as stack:
        def sbt(name, shape, dt_):
            return stack.enter_context(nc.sbuf_tensor(name, list(shape), dt_))
        eye_f = sbt("eye_f", [128, 128], F32)
        eye_b = sbt("eye_b", [128, 128], BF16)
        causal = sbt("sb_causal", [128, 128], F32)
        iotac = sbt("sb_iotac", [128, 1], F32)
        iotaI = sbt("iotaI", [128, 512], F32)
        gam_s = sbt("gam_s", [1, 1], F32)
        lng = sbt("lng", [1, 1], F32)
        lng_bc = sbt("lng_bc", [128, 1], F32)
        nlng_bc = sbt("nlng_bc", [128, 1], F32)
        fcol = sbt("fcol", [128, 1], F32)
        g512c = sbt("g512c", [128, 1], F32)
        ccs = sbt("ccs", [128, 2], F32)
        asc = sbt("asc", [128, 4], F32)
        ksc = sbt("ksc", [128, 4], F32)
        bcols = sbt("bcols", [128, 8], F32)
        gIbf = sbt("gIbf", [128, 128], BF16)
        gammaI = sbt("gammaI", [128, 512], F32)
        ones_r = sbt("ones_r", [1, 512], BF16)
        stat = sbt("stat", [128, 16], F32)
        negmax = sbt("negmax", [128, 2], F32)
        denD = sbt("denD", [128, 128], F32)
        denC = sbt("denC", [128, 64], F32)
        att_sb = sbt("att_sb", [128, 1280], BF16)
        ycol = sbt("ycol", [128, 4], F32)
        A1 = sbt("A1", [128, 32768], BF16)
        A2 = sbt("A2", [128, 49664], BF16)
        A3 = sbt("A3", [128, 16416], BF16)
        PS1 = stack.enter_context(nc.psum_tensor("PS1", [128, 2048], F32))
        PS2 = stack.enter_context(nc.psum_tensor("PS2", [128, 2048], F32))

        A1b = A1[:]
        A1f = A1[:].bitcast(F32)
        A2f = A2[:].bitcast(F32)
        A3f = A3[:].bitcast(F32)

        bq_r = A2[0:1, 45056:46080]
        bk_r = A2[0:1, 46080:47104]
        bv_r = A2[0:1, 47104:48128]
        phib_r = A3[0:1, 14336:16384]
        ebrow = A2[0:1, 47120:49168]

        xnT = A1b
        xn4 = A1b
        SINIT_O = 16500
        YA_O = 12544
        W3_O = 0
        PHIT_O = 24576
        CB = 12288 + NS * VW
        CB_O = [0, CB]
        DSM_O = 40960
        EW_O = [0, 8192]
        XO_F = 16392
        SQ_F = 20480
        SQ6_F = 21512
        XS_F = 21512
        XC_F = 14592
        TANH_O = 36880
        XTT_O = 38928
        YTS_O = 40976
        CSM_O = [0, 2048]
        CST_O = [4096, 6144]
        YAB_F = 4096
        OUT_F = 6144
        X_F = [0, 1024]
        XN_O = [4096, 5120]
        E_O = [6144, 8192]
        DS_O = [10240, 12288]
        P3Q, P3K, P3H, P3V = 0, 4096, 8192, 12288
        ST_O = [0, DT * VW]

        def sm(name):
            return nc.alloc_semaphore(name)

        def sm2(name):
            return [nc.alloc_semaphore(name + "0"), nc.alloc_semaphore(name + "1")]

        q_c = sm("q_c")
        g_eye = sm("g_eye"); g_bias = sm("g_bias"); g_w = sm("g_w"); g_phi = sm("g_phi")
        s_pro = sm("s_pro"); v_pro = sm("v_pro")
        q_x = sm2("q_x"); s1a = sm("s1a"); v1s = sm("v1s"); s1b = sm("s1b")
        t1 = sm("t1"); v1e = sm("v1e"); q_xn = sm2("q_xn")
        t2 = sm("t2"); v2m = sm("v2m"); s2e = sm("s2e"); v2d = sm("v2d")
        q_E = sm2("q_E"); q_D = sm2("q_D")
        t3 = sm("t3"); t3h = sm("t3h"); v3q = sm("v3q"); s3k = sm("s3k")
        v3v = sm("v3v"); s3h = sm("s3h")
        q_3q = sm("q_3q"); q_3k = sm("q_3k"); q_3h = sm("q_3h"); q_3v = sm("q_3v")
        q_r4 = sm("q_r4"); g_d4 = sm2("g_d4"); t4 = sm("t4"); v4 = sm("v4")
        q_4 = sm2("q_4"); cc1 = sm("cc1")
        q_5 = sm2("q_5"); v5i = sm("v5i"); t5u = sm("t5u"); sv5u = sm("sv5u")
        t5a = sm("t5a"); v5a = sm("v5a"); t5y = sm("t5y"); v5y = sm("v5y")
        s5y = sm("s5y"); q_ya = sm2("q_ya"); q_st = sm("q_st"); cc2 = sm("cc2")
        q_si = sm("q_si"); v5c = sm("v5c"); t5c = sm("t5c"); v5n = sm("v5n")
        q_rs = sm("q_rs"); g_ew = sm2("g_ew"); v6 = sm("v6"); s6 = sm("s6")
        t6 = sm("t6"); v6e = sm("v6e"); q_yt = sm2("q_yt"); cc3 = sm("cc3")
        q_ytl = sm("q_ytl")
        g_7 = sm2("g_7"); q_7 = sm2("q_7"); v7c = sm("v7c"); t7t = sm("t7t")
        v7t = sm("v7t"); t7y = sm("t7y"); v7o = sm("v7o"); q_out = sm2("q_out")
        vd = sm("vd"); sd = sm("sd")
        q_t1 = sm2("q_t1"); gd = sm("gd")
        q_t7 = sm2("q_t7"); gd7 = sm("gd7"); v7a = sm("v7a"); q_y7 = sm2("q_y7"); v7x = sm("v7x")
        q_x5 = sm2("q_x5"); q_xc = sm("q_xc")

        def psb(which, lo, hi, bf=False):
            t_ = PS1 if which == 0 else PS2
            ap = t_[:, lo:hi]
            return ap.bitcast(BF16) if bf else ap

        with nc.Block() as blk:

            @blk.tensor
            def _(t):
                t.wait_ge(g_eye, 16)
                t.wait_ge(g_phi, 16)
                t.wait_ge(g_bias, 16 * 4)
                t.wait_ge(v_pro, 13)
                for i in range(NT):
                    p = i % 2
                    t.wait_ge(q_t1[p], 16 * (i // 2 + 1))
                    if i >= 2:
                        t.wait_ge(s2e, i - 1)
                    log = (PS1 if p == 0 else PS2)[:, 0:2048]
                    for step in range(DT):
                        lhs = xnT[:, step * 4096 + i * 128: step * 4096 + (i + 1) * 128]
                        for cs in range(4):
                            t.matmul(log[:, cs * 512:(cs + 1) * 512], lhs,
                                     A2[:, PHIT_O + step * 2048 + cs * 512:
                                        PHIT_O + step * 2048 + (cs + 1) * 512],
                                     start=(step == 0), stop=False)
                    for cs in range(4):
                        ins = t.matmul(log[:, cs * 512:(cs + 1) * 512], ones_r[0:1, 0:128],
                                       phib_r[0:1, cs * 512:(cs + 1) * 512],
                                       start=False, stop=True)
                        if cs == 3:
                            ins.then_inc(t2, 1)

                t.wait_ge(g_w, 16 * 3)
                t.wait_ge(s2e, NT)
                for c in range(NCH):
                    for j in range(16):
                        proj = 0 if j < 8 else 1
                        e = j % 8
                        pq = j % 2
                        if j >= 2:
                            jp = j - 2
                            if jp < 8:
                                t.wait_ge(v3q, c * 8 + jp + 1)
                            else:
                                t.wait_ge(s3k, c * 8 + (jp - 8) + 1)
                        elif c > 0:
                            t.wait_ge(s3k, (c - 1) * 8 + 6 + j + 1)
                        woff = W3_O + proj * 8192
                        brow = bq_r if proj == 0 else bk_r
                        ps = PS1[:, 512 * pq:512 * (pq + 1)]
                        for step in range(DT):
                            t.matmul(ps, A2[:, woff + step * 1024 + e * 128:
                                            woff + step * 1024 + (e + 1) * 128],
                                     xnT[:, step * 4096 + c * 512:
                                         step * 4096 + (c + 1) * 512],
                                     start=(step == 0), stop=False)
                        t.matmul(ps, brow[0:1, e * 128:(e + 1) * 128],
                                 ones_r[0:1, 0:512], start=False,
                                 stop=True).then_inc(t3, 1)
                    for g in range(8):
                        sbt_, cs = g // 2, g % 2
                        pv = g % 2
                        if g >= 2:
                            t.wait_ge(v3v, c * 8 + (g - 2) + 1)
                        elif c > 0:
                            t.wait_ge(v3v, (c - 1) * 8 + 6 + g + 1)
                        ps = PS1[:, 1024 + 512 * pv:1024 + 512 * (pv + 1)]
                        for step in range(DT):
                            t.matmul(ps, xnT[:, step * 4096 + c * 512 + sbt_ * 128:
                                             step * 4096 + c * 512 + (sbt_ + 1) * 128],
                                     A2[:, W3_O + 16384 + step * 1024 + cs * 512:
                                        W3_O + 16384 + step * 1024 + (cs + 1) * 512],
                                     start=(step == 0), stop=False)
                        t.matmul(ps, ones_r[0:1, 0:128],
                                 bv_r[0:1, cs * 512:(cs + 1) * 512],
                                 start=False, stop=True).then_inc(t3, 1)
                    t.wait_ge(s3k, 8 * (c + 1))
                    for sbt_ in range(NS):
                        ptk = sbt_ % 2
                        if c * 4 + sbt_ >= 2:
                            t.wait_ge(s3h, c * 4 + sbt_ - 1)
                        trb = psb(1, 512 * ptk, 512 * (ptk + 1), bf=True)
                        for dtl in range(DT):
                            ins = t.transpose(trb[:, dtl * 128:(dtl + 1) * 128],
                                              A3[:, P3K + dtl * 512 + sbt_ * 128:
                                                 P3K + dtl * 512 + (sbt_ + 1) * 128],
                                              eye_b[:])
                            if dtl == DT - 1:
                                ins.then_inc(t3h, 1)

                t.wait_ge(v3q, 64); t.wait_ge(s3k, 64); t.wait_ge(v3v, 64)
                t.wait_ge(q_r4, 16)
                for sl in range(16):
                    p = sl % 2
                    t.wait_ge(g_d4[p], 16 * (sl // 2 + 1))
                    if sl >= 2:
                        t.wait_ge(v4, sl - 1)
                    ps = PS1[:, 1024 * p:1024 * (p + 1)]
                    dsm_b = A2[:, DSM_O + 4096 * p:DSM_O + 4096 * (p + 1)]
                    for tt in range(NT):
                        for cs in range(2):
                            ins = t.matmul(ps[:, cs * 512:(cs + 1) * 512],
                                           dsm_b[:, tt * 128:(tt + 1) * 128],
                                           xn4[:, tt * 1024 + cs * 512:
                                               tt * 1024 + (cs + 1) * 512],
                                           start=(tt == 0), stop=(tt == NT - 1))
                            if tt == NT - 1 and cs == 1:
                                ins.then_inc(t4, 1)

                t.wait_ge(v4, 16)
                t.wait_ge(s3h, 32)
                t.wait_ge(v5i, 2)
                t.wait_ge(v_pro, 12)
                for c in range(NCH):
                    bc = CB_O[c % 2]
                    sp = ST_O[c % 2]
                    t.wait_ge(q_5[c % 2], 16 * 4 * (c // 2 + 1))
                    if c in (1, 2):
                        t.wait_ge(v5y, 4 * c)
                    elif c >= 3:
                        t.wait_ge(s5y, 4 * (c - 2))
                    for dk in range(DT):
                        pu = dk % 2
                        if c * 8 + dk >= 2:
                            t.wait_ge(sv5u, c * 8 + dk - 1)
                        ps = PS1 if pu == 0 else PS2
                        for lo, hi in GSL:
                            t.matmul(ps[:, lo:hi], gIbf[:],
                                     A3[:, sp + dk * VW + lo: sp + dk * VW + hi],
                                     start=True, stop=False)
                        for sbt_ in range(NS):
                            lhs = A2[:, bc + 8192 + sbt_ * 1024 + dk * 128:
                                     bc + 8192 + sbt_ * 1024 + (dk + 1) * 128]
                            for gi, (lo, hi) in enumerate(GSL):
                                ins = t.matmul(ps[:, lo:hi], lhs,
                                               A2[:, bc + 12288 + sbt_ * VW + lo:
                                                  bc + 12288 + sbt_ * VW + hi],
                                               start=False, stop=(sbt_ == NS - 1))
                                if sbt_ == NS - 1 and gi == 2:
                                    ins.then_inc(t5u, 1)
                    for b_ in range(NS):
                        for a_ in range(b_ + 1):
                            blkid = (b_ * (b_ + 1)) // 2 + a_
                            pa = blkid % 2
                            if c * 10 + blkid >= 2:
                                t.wait_ge(v5a, c * 10 + blkid - 1)
                            ps = (PS1 if pa == 0 else PS2)[:, 1536:1664]
                            for step in range(DT):
                                ins = t.matmul(ps, A2[:, bc + 4096 + step * 512 + a_ * 128:
                                                      bc + 4096 + step * 512 + (a_ + 1) * 128],
                                               A2[:, bc + step * 512 + b_ * 128:
                                                  bc + step * 512 + (b_ + 1) * 128],
                                               start=(step == 0), stop=(step == DT - 1))
                                if step == DT - 1:
                                    ins.then_inc(t5a, 1)
                    t.wait_ge(sv5u, 8 * (c + 1))
                    for b_ in range(NS):
                        py = b_ % 2
                        yidx = c * NS + b_
                        if yidx >= 2:
                            prev = yidx - 2
                            if prev < 2 * NS:
                                t.wait_ge(v5y, prev + 1)
                            else:
                                t.wait_ge(s5y, prev - 2 * NS + 1)
                        ps = PS1 if py == 0 else PS2
                        for step in range(DT):
                            lhs = A2[:, bc + step * 512 + b_ * 128:
                                     bc + step * 512 + (b_ + 1) * 128]
                            for lo, hi in GSL:
                                t.matmul(ps[:, lo:hi], lhs,
                                         A3[:, sp + step * VW + lo: sp + step * VW + hi],
                                         start=(step == 0), stop=False)
                        for a_ in range(b_ + 1):
                            blkid = (b_ * (b_ + 1)) // 2 + a_
                            t.wait_ge(v5a, c * 10 + blkid + 1)
                            for gi, (lo, hi) in enumerate(GSL):
                                ins = t.matmul(ps[:, lo:hi],
                                               att_sb[:, blkid * 128:(blkid + 1) * 128],
                                               A2[:, bc + 12288 + a_ * VW + lo:
                                                  bc + 12288 + a_ * VW + hi],
                                               start=False, stop=(a_ == b_))
                                if a_ == b_ and gi == 2:
                                    ins.then_inc(t5y, 1)

                t.wait_ge(t5y, NS * NCH)
                t.wait_ge(v5a, 10 * NCH)
                for i6 in range(2):
                    t.wait_ge(s6, 2 + i6 + 1)
                    trb = psb(i6, 1536, 2048, bf=True)
                    for dtl in range(DT):
                        ins = t.transpose(trb[:, dtl * 128:(dtl + 1) * 128],
                                          A2[:, TANH_O + i6 * 1024 + dtl * 128:
                                             TANH_O + i6 * 1024 + (dtl + 1) * 128],
                                          eye_b[:])
                        if dtl == DT - 1:
                            ins.then_inc(t6, 1)
                for n in range(NEL):
                    pe = n % 2
                    t.wait_ge(g_ew[pe], 16 * 2 * (n // 2 + 1))
                    t.wait_ge(v6e, 2)
                    if n >= 1:
                        t.wait_ge(v6e, 2 + 2 * n)
                    for cs in range(2):
                        ps = psb(cs, 1536, 2048)
                        for step in range(DT):
                            t.matmul(ps[0:32, :],
                                     A2[:, XTT_O + step * 256 + n * 32:
                                        XTT_O + step * 256 + n * 32 + 32],
                                     A3[:, EW_O[pe] + step * 1024 + cs * 512:
                                        EW_O[pe] + step * 1024 + (cs + 1) * 512],
                                     start=(step == 0), stop=False)
                        ins = t.matmul(ps[0:32, :],
                                       ones_r[0:1, 0:32],
                                       ebrow[0:1, pe * D + cs * 512:pe * D + (cs + 1) * 512],
                                       start=False, stop=True)
                        if cs == 1:
                            ins.then_inc(t6, 1)

                t.wait_ge(q_si, 16)
                t.wait_ge(v5c, 2)
                for cc in range(2):
                    for b_ in range(NS):
                        idx = cc * NS + b_
                        pyc = idx % 2
                        if idx < 2:
                            t.wait_ge(s5y, 24)
                        else:
                            t.wait_ge(v5n, idx - 1)
                        ps = PS1 if pyc == 0 else PS2
                        for step in range(DT):
                            lhs = A2[:, CB_O[cc] + step * 512 + b_ * 128:
                                     CB_O[cc] + step * 512 + (b_ + 1) * 128]
                            for gi, (lo, hi) in enumerate(GSL):
                                ins = t.matmul(ps[:, lo:hi], lhs,
                                               A1b[:, SINIT_O + step * VW + lo:
                                                   SINIT_O + step * VW + hi],
                                               start=(step == 0), stop=(step == DT - 1))
                                if step == DT - 1 and gi == 2:
                                    ins.then_inc(t5c, 1)

                t.wait_ge(q_ytl, 16)
                t.wait_ge(v6e, 18)
                t.wait_ge(v5n, 8)
                for i in range(NT):
                    p = i % 2
                    t.wait_ge(q_t7[p], 32 * (i // 2 + 1))
                    if i >= 2:
                        t.wait_ge(v7x, i - 1)
                    ps = PS1[:, 1024 * p:1024 * (p + 1)]
                    for sl in range(16):
                        for cs in range(2):
                            ins = t.matmul(ps[:, cs * 512:(cs + 1) * 512],
                                           A2[:, CST_O[p] + sl * 128:
                                              CST_O[p] + (sl + 1) * 128],
                                           A1b[:, sl * 1024 + cs * 512:
                                               sl * 1024 + (cs + 1) * 512],
                                           start=(sl == 0), stop=(sl == 15))
                            if sl == 15 and cs == 1:
                                ins.then_inc(t7y, 1)

            @blk.vector
            def _(v):
                nv = [0]

                def V(inst):
                    inst.then_inc(vd, 1)
                    nv[0] += 1

                def VB():
                    v.wait_ge(vd, nv[0])

                v.wait_ge(q_c, 16 * 8)
                v.tensor_scalar_mul(nlng_bc[:], lng_bc[:], -1.0).then_inc(v_pro, 1)
                for a_ in range(4):
                    v.tensor_scalar_mul(bcols[:, a_:a_ + 1], lng_bc[:],
                                        float(-(128 * a_ + 1))).then_inc(v_pro, 1)
                for s_ in range(4):
                    v.tensor_scalar_mul(bcols[:, 4 + s_:5 + s_], lng_bc[:],
                                        float(511 - 128 * s_)).then_inc(v_pro, 1)
                v.wait_ge(s_pro, 3)
                v.tensor_mul(ccs[:, 1:2], fcol[:], g512c[:]).then_inc(v_pro, 1)
                v.tensor_copy(ccs[:, 0:1], fcol[:]).then_inc(v_pro, 1)
                v.tensor_scalar_mul(gIbf[:], eye_f[:], g512c[:]).then_inc(v_pro, 1)
                v.memset(ones_r[:], 1.0).then_inc(v_pro, 1)

                for i in range(NT + 2):
                    if i < NT:
                        p = i % 2
                        v.wait_ge(q_x[p], 16 * (i // 2 + 1))
                        xt = A3f[:, X_F[p]:X_F[p] + 1024]
                        V(v.reduce_sum(stat[:, p * 8:p * 8 + 1], xt, axis=AX.X))
                        v.wait_ge(s1a, i + 1)
                        VB()
                        V(v.tensor_scalar_mul(stat[:, p * 8:p * 8 + 1],
                                              stat[:, p * 8:p * 8 + 1], 1.0 / D))
                        V(v.tensor_scalar_mul(stat[:, p * 8 + 2:p * 8 + 3],
                                              stat[:, p * 8 + 1:p * 8 + 2], 1.0 / D))
                        VB()
                        V(v.tensor_mul(stat[:, p * 8 + 3:p * 8 + 4],
                                       stat[:, p * 8:p * 8 + 1],
                                       stat[:, p * 8:p * 8 + 1]))
                        VB()
                        V(v.tensor_sub(stat[:, p * 8 + 2:p * 8 + 3],
                                       stat[:, p * 8 + 2:p * 8 + 3],
                                       stat[:, p * 8 + 3:p * 8 + 4]))
                        VB()
                        V(v.tensor_scalar_add(stat[:, p * 8 + 2:p * 8 + 3],
                                              stat[:, p * 8 + 2:p * 8 + 3], EPS_LN))
                        VB()
                        v.reciprocal(stat[:, p * 8 + 3:p * 8 + 4],
                                     stat[:, p * 8 + 2:p * 8 + 3]).then_inc(v1s, 1)
                    if i >= 2:
                        j = i - 2
                        pj = j % 2
                        v.wait_ge(s2e, j + 1)
                        if j >= 2:
                            v.wait_ge(v2d, j - 1)
                        E3 = A3[:, E_O[pj]:E_O[pj] + 2048].rearrange(
                            "p (n q) -> p n q", q=32)
                        V(v.reduce_sum(denD[:, 64 * pj:64 * pj + 64], E3, axis=AX.X))
                        VB()
                        v.reciprocal(denD[:, 64 * pj:64 * pj + 64],
                                     denD[:, 64 * pj:64 * pj + 64]).then_inc(gd, 1)

                v.wait_ge(s_pro, 11)
                for c in range(NCH):
                    for e in range(DT):
                        pq = e % 2
                        v.wait_ge(t3, c * 24 + e + 1)
                        if c >= 1 and e < 2:
                            v.wait_ge(q_3q, 16 * c)
                        ps = PS1[:, 512 * pq:512 * (pq + 1)]
                        v.scalar_tensor_tensor(
                            A3[:, P3Q + e * 512:P3Q + (e + 1) * 512], ps, 0.0,
                            gammaI[:], ALU.max, ALU.mult).then_inc(v3q, 1)
                    if c == 0:
                        v.wait_ge(q_D[0], 16 * 16)
                        v.wait_ge(q_D[1], 16 * 16)
                        v.wait_ge(q_E[0], 16 * 16)
                        v.wait_ge(q_E[1], 16 * 16)
                        for sbt_ in range(NS):
                            V(v.memset(A3[:, P3V + sbt_ * VW + 1024:
                                          P3V + sbt_ * VW + 1025], 1.0))
                            V(v.memset(A3[:, P3V + sbt_ * VW + 1025:
                                          P3V + sbt_ * VW + 1026], 0.0))
                    for g in range(8):
                        sbt_, cs = g // 2, g % 2
                        pv = g % 2
                        v.wait_ge(t3, c * 24 + 16 + g + 1)
                        if c >= 1 and g < 2:
                            v.wait_ge(q_3v, 16 * c)
                        ps = PS1[:, 1024 + 512 * pv:1024 + 512 * (pv + 1)]
                        v.tensor_copy(A3[:, P3V + sbt_ * VW + cs * 512:
                                         P3V + sbt_ * VW + cs * 512 + 512],
                                      ps).then_inc(v3v, 1)

                for sl in range(16):
                    p = sl % 2
                    v.wait_ge(t4, sl + 1)
                    if sl >= 2:
                        v.wait_ge(q_4[p], 16 * ((sl - 2) // 2 + 1))
                    ps = PS1[:, 1024 * p:1024 * (p + 1)]
                    v.tensor_copy(A3f[:, 1024 * p:1024 * (p + 1)], ps).then_inc(v4, 1)

                v.wait_ge(q_4[0], 16 * 8)
                v.wait_ge(q_4[1], 16 * 8)
                v.memset(A3[:, ST_O[0]:ST_O[0] + DT * VW], 0.0).then_inc(v5i, 1)
                v.memset(A3[:, ST_O[1]:ST_O[1] + DT * VW], 0.0).then_inc(v5i, 1)

                for c in range(NCH):
                    for b_ in range(NS):
                        for a_ in range(b_ + 1):
                            blkid = (b_ * (b_ + 1)) // 2 + a_
                            v.wait_ge(t5a, c * 10 + blkid + 1)
                            if c >= 1:
                                v.wait_ge(t5y, 4 * (c - 1) + b_ + 1)
                            pa = blkid % 2
                            ps = (PS1 if pa == 0 else PS2)[:, 1536:1664]
                            dst = att_sb[:, blkid * 128:(blkid + 1) * 128]
                            if a_ == b_:
                                v.scalar_tensor_tensor(dst, ps, asc[:, a_:a_ + 1],
                                                       causal[:], ALU.mult,
                                                       ALU.mult).then_inc(v5a, 1)
                            else:
                                v.tensor_scalar_mul(dst, ps,
                                                    asc[:, a_:a_ + 1]).then_inc(v5a, 1)
                    for b_ in range(NS):
                        py = b_ % 2
                        yidx = c * NS + b_
                        v.wait_ge(t5y, yidx + 1)
                        ps = PS1 if py == 0 else PS2
                        if c < 2:
                            v.tensor_copy(A1f[:, yidx * NUMW:(yidx + 1) * NUMW],
                                          ps[:, 0:NUMW]).then_inc(v5y, 1)
                        else:
                            if yidx >= 10:
                                v.wait_ge(s5y, yidx - 9)
                            V(v.tensor_scalar_add(ycol[:, py:py + 1], ps[:, 1024:1025],
                                                  EPS_ATTN))
                            VB()
                            v.reciprocal(ycol[:, 2 + py:3 + py],
                                         ycol[:, py:py + 1]).then_inc(v5y, 1)
                            v.wait_ge(q_x5[py], 16 * ((yidx - 8) // 2 + 1))
                            v.wait_ge(v5y, yidx + 1)
                            if yidx >= 10:
                                v.wait_ge(q_ya[py], 16 * ((yidx - 10) // 2 + 1))
                            VB()
                            v.scalar_tensor_tensor(
                                A1f[:, YA_O + 1024 * py:YA_O + 1024 * (py + 1)],
                                ps[:, 0:1024], ycol[:, 2 + py:3 + py],
                                A2f[:, XS_F + 1024 * py:XS_F + 1024 * (py + 1)],
                                ALU.mult, ALU.add).then_inc(s5y, 1)

                v.wait_ge(q_rs, 16)
                for i6 in range(2):
                    xo = A2f[:, XO_F + 1024 * i6:XO_F + 1024 * (i6 + 1)]
                    v.tensor_scalar(xo, xo, -33000.0, 65000.0,
                                    ALU.max, ALU.min).then_inc(v6, 1)
                    v.wait_ge(v6, 2 * i6 + 1)
                    V(v.reduce_sum(stat[:, 8 * i6:8 * i6 + 1], xo, axis=AX.X))
                    v.wait_ge(s6, i6 + 1)
                    VB()
                    V(v.tensor_scalar_mul(stat[:, 8 * i6:8 * i6 + 1],
                                          stat[:, 8 * i6:8 * i6 + 1], 1.0 / D))
                    V(v.tensor_scalar_mul(stat[:, 8 * i6 + 2:8 * i6 + 3],
                                          stat[:, 8 * i6 + 1:8 * i6 + 2], 1.0 / D))
                    VB()
                    V(v.tensor_mul(stat[:, 8 * i6 + 3:8 * i6 + 4],
                                   stat[:, 8 * i6:8 * i6 + 1], stat[:, 8 * i6:8 * i6 + 1]))
                    VB()
                    V(v.tensor_sub(stat[:, 8 * i6 + 2:8 * i6 + 3],
                                   stat[:, 8 * i6 + 2:8 * i6 + 3],
                                   stat[:, 8 * i6 + 3:8 * i6 + 4]))
                    VB()
                    V(v.tensor_scalar_add(stat[:, 8 * i6 + 2:8 * i6 + 3],
                                          stat[:, 8 * i6 + 2:8 * i6 + 3], EPS_LN))
                    VB()
                    v.reciprocal(stat[:, 8 * i6 + 3:8 * i6 + 4],
                                 stat[:, 8 * i6 + 2:8 * i6 + 3]).then_inc(v6, 1)
                for i6 in range(2):
                    v.wait_ge(t6, i6 + 1)
                    trb = psb(i6, 1536, 2048, bf=True)
                    src_ = trb.rearrange("p (t c) -> p t c", c=128)
                    dst = A2[:, XTT_O:XTT_O + 2048].rearrange(
                        "p (t c) -> p t c", c=256)[:, :, i6 * 128:(i6 + 1) * 128]
                    v.tensor_copy(dst, src_).then_inc(v6e, 1)
                for n in range(NEL):
                    pe = n % 2
                    v.wait_ge(t6, 2 + n + 1)
                    if n >= 2:
                        v.wait_ge(q_yt[pe], 16 * ((n - 2) // 2 + 1))
                    for cs in range(2):
                        ps = psb(cs, 1536, 2048)[0:32, :]
                        v.tensor_copy(A2[0:32, YTS_O + pe * D + cs * 512:
                                         YTS_O + pe * D + (cs + 1) * 512],
                                      ps).then_inc(v6e, 1)

                for cc in range(2):
                    v.wait_ge(q_5[cc], 16 * 17)
                    v.wait_ge(v_pro, 11)
                    qb = A2[:, CB_O[cc]:CB_O[cc] + 4096]
                    v.tensor_scalar_mul(qb, qb, ccs[:, cc:cc + 1]).then_inc(v5c, 1)
                for cc in range(2):
                    for b_ in range(NS):
                        idx = cc * NS + b_
                        pyc = idx % 2
                        v.wait_ge(t5c, idx + 1)
                        v.wait_ge(s5y, 23 + idx)
                        ps = PS1 if pyc == 0 else PS2
                        nh = A1f[:, idx * NUMW:(idx + 1) * NUMW]
                        V(v.tensor_add(nh, nh, ps[:, 0:NUMW]))
                        VB()
                        V(v.tensor_scalar_add(ycol[:, pyc:pyc + 1], nh[:, 1024:1025],
                                              EPS_ATTN))
                        VB()
                        v.reciprocal(ycol[:, 2 + pyc:3 + pyc],
                                     ycol[:, pyc:pyc + 1]).then_inc(v5n, 1)
                        v.wait_ge(q_xc, 16 * (idx + 1))
                        v.wait_ge(v5n, idx + 1)
                        if idx < 2:
                            v.wait_ge(q_ya[pyc], 16 * 12)
                        else:
                            v.wait_ge(q_ya[pyc], 16 * (12 + (idx - 2) // 2 + 1))
                        VB()
                        v.scalar_tensor_tensor(
                            A1f[:, YA_O + 1024 * pyc:YA_O + 1024 * (pyc + 1)],
                            nh[:, 0:1024], ycol[:, 2 + pyc:3 + pyc],
                            A1f[:, XC_F:XC_F + 1024],
                            ALU.mult, ALU.add).then_inc(s5y, 1)

                def _v7_add(j):
                    pj = j % 2
                    v.wait_ge(t7y, j + 1)
                    v.wait_ge(q_y7[pj], 16 * (j // 2 + 1))
                    if j >= 2:
                        v.wait_ge(q_out[pj], 16 * ((j - 2) // 2 + 1))
                    ps = PS1[:, 1024 * pj:1024 * (pj + 1)]
                    ob = A2f[:, OUT_F + 1024 * pj:OUT_F + 1024 * (pj + 1)]
                    v.tensor_add(ob, ps,
                                 A2f[:, YAB_F + 1024 * pj:YAB_F + 1024 * (pj + 1)]
                                 ).then_inc(v7x, 1)

                for i in range(NT):
                    p = i % 2
                    v.wait_ge(g_7[p], 16 * (i // 2 + 1))
                    E3 = A2[:, CSM_O[p]:CSM_O[p] + 2048].rearrange("p (n q) -> p n q", q=32)
                    E3t = E3.transpose([0, 2, 1])
                    V(v.reduce_sum(denC[:, 32 * p:32 * p + 32], E3t, axis=AX.X))
                    VB()
                    V(v.reciprocal(denC[:, 32 * p:32 * p + 32], denC[:, 32 * p:32 * p + 32]))
                    rC = denC[:, 32 * p:32 * p + 32].unsqueeze(1).broadcast_to([128, 64, 32])
                    VB()
                    v.tensor_mul(E3, E3, rC).then_inc(v7c, 1)
                    if i >= 1:
                        _v7_add(i - 1)
                _v7_add(NT - 1)

            @blk.scalar
            def _(s):
                ns = [0]

                def SC(inst):
                    inst.then_inc(sd, 1)
                    ns[0] += 1

                def SB():
                    s.wait_ge(sd, ns[0])

                s.wait_ge(q_c, 16 * 6)
                s.activation(lng[:], gam_s[:], AF.Ln).then_inc(s_pro, 1)
                s.wait_ge(q_c, 16 * 8)
                s.activation(gammaI[:], iotaI[:], AF.Exp,
                             scale=lng_bc[:]).then_inc(s_pro, 1)
                s.activation(g512c[:], lng_bc[:], AF.Exp, scale=512.0).then_inc(s_pro, 1)
                s.wait_ge(v_pro, 9)
                for a_ in range(4):
                    s.activation(asc[:, a_:a_ + 1], iotac[:], AF.Exp,
                                 scale=nlng_bc[:], bias=bcols[:, a_:a_ + 1]).then_inc(s_pro, 1)
                for s_ in range(4):
                    s.activation(ksc[:, s_:s_ + 1], iotac[:], AF.Exp,
                                 scale=nlng_bc[:], bias=bcols[:, 4 + s_:5 + s_]).then_inc(s_pro, 1)

                for i in range(NT + 3):
                    if i < NT:
                        p = i % 2
                        s.wait_ge(q_x[p], 16 * (i // 2 + 1))
                        xt = A3f[:, X_F[p]:X_F[p] + 1024]
                        s.activation(A2f[:, SQ_F + 1024 * p:SQ_F + 1024 * (p + 1)], xt,
                                     AF.Square,
                                     accum_out=stat[:, p * 8 + 1:p * 8 + 2]
                                     ).then_inc(s1a, 1)
                        s.wait_ge(v1s, i + 1)
                        SC(s.activation(stat[:, p * 8 + 4:p * 8 + 5],
                                        stat[:, p * 8 + 3:p * 8 + 4], AF.Sqrt))
                        SB()
                        SC(s.mul(stat[:, p * 8 + 5:p * 8 + 6], stat[:, p * 8:p * 8 + 1],
                                 stat[:, p * 8 + 4:p * 8 + 5]))
                        SB()
                        SC(s.mul(stat[:, p * 8 + 5:p * 8 + 6],
                                 stat[:, p * 8 + 5:p * 8 + 6], -1.0))
                        if i >= 2:
                            s.wait_ge(q_xn[p], 16 * ((i - 2) // 2 + 1))
                            s.wait_ge(q_t1[p], 16 * ((i - 2) // 2 + 1))
                        SB()
                        s.activation(A3[:, XN_O[p]:XN_O[p] + 1024], xt, AF.Identity,
                                     scale=stat[:, p * 8 + 4:p * 8 + 5],
                                     bias=stat[:, p * 8 + 5:p * 8 + 6]).then_inc(s1b, 1)
                    if 2 <= i < NT + 2:
                        j = i - 2
                        pj = j % 2
                        s.wait_ge(t2, j + 1)
                        if j >= 2:
                            s.wait_ge(v2d, j - 1)
                            s.wait_ge(q_E[pj], 16 * ((j - 2) // 2 + 1))
                        log = (PS1 if pj == 0 else PS2)[:, 0:2048]
                        s.activation(A3[:, E_O[pj]:E_O[pj] + 2048], log,
                                     AF.Exp).then_inc(s2e, 1)
                    if i >= 3:
                        j = i - 3
                        pj = j % 2
                        s.wait_ge(v2d, j + 1)
                        s.dma_start(out=sp_dsm[j], in_=A3[:, DS_O[pj]:DS_O[pj] + 2048]
                                    ).then_inc(q_D[pj], 16)

                s.wait_ge(q_E[0], 16 * 16)
                s.wait_ge(q_E[1], 16 * 16)
                s.wait_ge(q_xn[0], 16 * 16)
                s.wait_ge(q_xn[1], 16 * 16)
                s.wait_ge(s_pro, 11)
                for c in range(NCH):
                    for e in range(DT):
                        pq = e % 2
                        s.wait_ge(t3, c * 24 + 8 + e + 1)
                        if c >= 1 and e < 2:
                            s.wait_ge(q_3k, 16 * c)
                        ps = PS1[:, 512 * pq:512 * (pq + 1)]
                        s.activation(A3[:, P3K + e * 512:P3K + (e + 1) * 512], ps,
                                     AF.Relu).then_inc(s3k, 1)
                    for sbt_ in range(NS):
                        ptk = sbt_ % 2
                        s.wait_ge(t3h, c * 4 + sbt_ + 1)
                        if c >= 1 and sbt_ < 2:
                            s.wait_ge(q_3h, 16 * c)
                        trb = psb(1, 512 * ptk, 512 * (ptk + 1), bf=True)
                        s.mul(A3[:, P3H + sbt_ * 1024:P3H + (sbt_ + 1) * 1024], trb,
                              ksc[:, sbt_:sbt_ + 1]).then_inc(s3h, 1)

                for c in range(NCH):
                    sq_ = ST_O[(c + 1) % 2]
                    for dk in range(DT):
                        pu = dk % 2
                        s.wait_ge(t5u, c * 8 + dk + 1)
                        ps = PS1 if pu == 0 else PS2
                        s.copy(A3[:, sq_ + dk * VW:sq_ + dk * VW + NUMW],
                               ps[:, 0:NUMW]).then_inc(sv5u, 1)

                for i6 in range(2):
                    s.wait_ge(v6, 2 * i6 + 1)
                    xo = A2f[:, XO_F + 1024 * i6:XO_F + 1024 * (i6 + 1)]
                    s.activation(A2f[:, SQ6_F + 1024 * i6:SQ6_F + 1024 * (i6 + 1)], xo,
                                 AF.Square,
                                 accum_out=stat[:, 8 * i6 + 1:8 * i6 + 2]).then_inc(s6, 1)
                for i6 in range(2):
                    s.wait_ge(v6, 2 * i6 + 2)
                    xo = A2f[:, XO_F + 1024 * i6:XO_F + 1024 * (i6 + 1)]
                    SC(s.activation(stat[:, 8 * i6 + 4:8 * i6 + 5],
                                    stat[:, 8 * i6 + 3:8 * i6 + 4], AF.Sqrt))
                    SB()
                    SC(s.mul(stat[:, 8 * i6 + 5:8 * i6 + 6], stat[:, 8 * i6:8 * i6 + 1],
                             stat[:, 8 * i6 + 4:8 * i6 + 5]))
                    SB()
                    SC(s.mul(stat[:, 8 * i6 + 5:8 * i6 + 6],
                             stat[:, 8 * i6 + 5:8 * i6 + 6], -1.0))
                    SB()
                    s.activation(A2[:, TANH_O + i6 * 1024:TANH_O + (i6 + 1) * 1024], xo,
                                 AF.Tanh, scale=stat[:, 8 * i6 + 4:8 * i6 + 5],
                                 bias=stat[:, 8 * i6 + 5:8 * i6 + 6]).then_inc(s6, 1)


                for i in range(NT):
                    p = i % 2
                    s.wait_ge(v7c, i + 1)
                    if i >= 2:
                        s.wait_ge(t7y, i - 1)
                    for half in range(2):
                        src_ = A2[:, CSM_O[p] + half * 1024:CSM_O[p] + (half + 1) * 1024
                                  ].rearrange("p (t c) -> p t c", c=128)
                        dst = A2[:, CST_O[p] + half * 1024:CST_O[p] + (half + 1) * 1024
                                 ].rearrange("p (t c) -> p t c", c=128)
                        s.dma_start_transpose(dst, src_).then_inc(q_t7[p], 16)


            @blk.sync
            def _(sy):
                sy.dma_start(out=eye_f[:], in_=eye_d).then_inc(q_c, 16)
                sy.dma_start(out=causal[:], in_=causal_d).then_inc(q_c, 16)
                sy.dma_start(out=iotac[:], in_=iotac_d).then_inc(q_c, 16)
                sy.dma_start(out=gam_s[:], in_=gam).then_inc(q_c, 16)
                sy.dma_start(out=fcol[:], in_=flag_d[0:1, 0:1].partition_broadcast(128)
                             ).then_inc(q_c, 16)
                sy.dma_start(out=iotaI[:], in_=iotar_d[0:1, 1:513].partition_broadcast(128)
                             ).then_inc(q_c, 16)
                sy.wait_ge(s_pro, 1)
                sy.dma_start(out=lng_dr, in_=lng[:]).then_inc(q_c, 16)
                sy.wait_ge(q_c, 16 * 7)
                sy.dma_start(out=lng_bc[:], in_=lng_dr[0:1, 0:1].partition_broadcast(128)
                             ).then_inc(q_c, 16)

                for i in range(NT + 3):
                    if i < NT:
                        p = i % 2
                        if i >= 2:
                            sy.wait_ge(s1a, i - 1)
                            sy.wait_ge(s1b, i - 1)
                            sy.wait_ge(v1s, i - 1)
                        sy.dma_start(out=A3f[:, X_F[p]:X_F[p] + 1024],
                                     in_=xsh[i * 128:(i + 1) * 128, :]).then_inc(q_x[p], 16)
                    if 1 <= i <= NT:
                        j = i - 1
                        pp = j % 2
                        sy.wait_ge(s1b, j + 1)
                        sy.dma_start(out=sp_xn[j], in_=A3[:, XN_O[pp]:XN_O[pp] + 1024]
                                     ).then_inc(q_xn[pp], 16)
                        sy.dma_start_transpose(
                            A1b.rearrange("p (t c) -> p t c", c=4096
                                          )[:, :, j * 128:(j + 1) * 128],
                            A3[:, XN_O[pp]:XN_O[pp] + 1024].rearrange(
                                "p (t c) -> p t c", c=128)).then_inc(q_t1[pp], 16)
                    if i >= 3:
                        j = i - 3
                        pj = j % 2
                        sy.wait_ge(s2e, j + 1)
                        sy.dma_start(out=sp_E[j], in_=A3[:, E_O[pj]:E_O[pj] + 2048]
                                     ).then_inc(q_E[pj], 16)

                for c in range(NCH):
                    sy.wait_ge(v3q, 8 * (c + 1))
                    sy.dma_start(out=sp_q[c], in_=A3[:, P3Q:P3Q + 4096]).then_inc(q_3q, 16)
                    sy.wait_ge(s3k, 8 * (c + 1))
                    sy.dma_start(out=sp_k[c], in_=A3[:, P3K:P3K + 4096]).then_inc(q_3k, 16)
                    sy.wait_ge(s3h, 4 * (c + 1))
                    sy.dma_start(out=sp_kh[c], in_=A3[:, P3H:P3H + 4096]).then_inc(q_3h, 16)
                    sy.wait_ge(v3v, 8 * (c + 1))
                    sy.dma_start(out=sp_v[c], in_=A3[:, P3V:P3V + NS * VW]).then_inc(q_3v, 16)

                sy.wait_ge(q_3q, 16 * 8); sy.wait_ge(q_3k, 16 * 8)
                sy.wait_ge(q_3h, 16 * 8); sy.wait_ge(q_3v, 16 * 8)
                sy.wait_ge(q_xn[0], 16 * 16); sy.wait_ge(q_xn[1], 16 * 16)
                sy.wait_ge(t3h, 4 * NCH)
                sy.dma_start(out=xn4[:, 0:32768].rearrange("p (t c) -> p t c", c=1024),
                             in_=sp_xn.transpose([1, 0, 2])).then_inc(q_r4, 16)
                for sl in range(16):
                    p = sl % 2
                    sy.wait_ge(v4, sl + 1)
                    sy.dma_start(out=ar_in[sl * 128:(sl + 1) * 128, :],
                                 in_=A3f[:, 1024 * p:1024 * (p + 1)]).then_inc(q_4[p], 16)

                def _x5_load(c_):
                    for b_ in range(NS):
                        yidx = c_ * NS + b_
                        py = b_ % 2
                        if yidx >= 10:
                            sy.wait_ge(s5y, yidx - 9)
                        sy.dma_start(out=A2f[:, XS_F + 1024 * py:XS_F + 1024 * (py + 1)],
                                     in_=xsh[yidx * 128:(yidx + 1) * 128, :]
                                     ).then_inc(q_x5[py], 16)

                def _ya_store(c_):
                    for b_ in range(NS):
                        yidx = c_ * NS + b_
                        py = b_ % 2
                        sy.wait_ge(s5y, (yidx - 2 * NS) + 1)
                        sy.dma_start(out=sp_ya[yidx],
                                     in_=A1f[:, YA_O + 1024 * py:YA_O + 1024 * (py + 1)]
                                     ).then_inc(q_ya[py], 16)
                for c in range(NCH):
                    bc = CB_O[c % 2]
                    if c >= 2:
                        sy.wait_ge(t5y, NS * (c - 1))
                    sy.dma_start(out=A2[:, bc:bc + 4096], in_=sp_q[c]).then_inc(q_5[c % 2], 16)
                    sy.dma_start(out=A2[:, bc + 4096:bc + 8192],
                                 in_=sp_k[c]).then_inc(q_5[c % 2], 16)
                    sy.dma_start(out=A2[:, bc + 8192:bc + 12288],
                                 in_=sp_kh[c]).then_inc(q_5[c % 2], 16)
                    sy.dma_start(out=A2[:, bc + 12288:bc + 12288 + NS * VW],
                                 in_=sp_v[c]).then_inc(q_5[c % 2], 16)
                    if 2 <= c - 1:
                        _ya_store(c - 1)
                    if c >= 2:
                        _x5_load(c)
                _ya_store(NCH - 1)
                sy.wait_ge(cc1, 1)
                sy.dma_start(out=A2f[:, XO_F:XO_F + 2048].rearrange(
                    "p (t c) -> p t c", c=1024),
                    in_=rs_out.rearrange("(t p) c -> p t c", p=128)).then_inc(q_rs, 16)
                sy.wait_ge(sv5u, 8 * NCH)
                sy.dma_start(out=st_in.transpose([1, 0, 2]),
                             in_=A3[:, ST_O[0]:ST_O[0] + DT * VW].rearrange(
                                 "p (t c) -> p t c", c=VW)).then_inc(q_st, 16)
                for cc in range(2):
                    sy.wait_ge(t5y, NS * NCH)
                    sy.dma_start(out=A2[:, CB_O[cc]:CB_O[cc] + 4096],
                                 in_=sp_q[cc]).then_inc(q_5[cc], 16)
                sy.wait_ge(cc2, 1)
                sy.dma_start(out=A1b[:, SINIT_O:SINIT_O + DT * VW].rearrange(
                    "p (t c) -> p t c", c=VW),
                    in_=st_out[0].transpose([1, 0, 2])).then_inc(q_si, 16)
                for n in range(NEL):
                    pe = n % 2
                    sy.wait_ge(v6e, 2 + 2 * (n + 1))
                    sy.dma_start(out=yt_in[n * 32:(n + 1) * 32, :],
                                 in_=A2[0:32, YTS_O + pe * D:YTS_O + (pe + 1) * D]
                                 ).then_inc(q_yt[pe], 16)
                for cc in range(2):
                    for b_ in range(NS):
                        idx = cc * NS + b_
                        pyc = idx % 2
                        if idx >= 1:
                            sy.wait_ge(s5y, NS * (NCH - 2) + idx)
                        sy.dma_start(out=A1f[:, XC_F:XC_F + 1024],
                                     in_=xsh[idx * 128:(idx + 1) * 128, :]
                                     ).then_inc(q_xc, 16)
                        sy.wait_ge(s5y, NS * (NCH - 2) + idx + 1)
                        sy.dma_start(out=sp_ya[idx],
                                     in_=A1f[:, YA_O + 1024 * pyc:YA_O + 1024 * (pyc + 1)]
                                     ).then_inc(q_ya[pyc], 16)
                sy.wait_ge(cc3, 1)
                sy.wait_ge(s5y, 32)
                sy.dma_start(out=A1b[:, 0:16384].rearrange("p (t c) -> p t c", c=1024),
                             in_=yt_out.rearrange("(t p) c -> p t c", p=128)
                             ).then_inc(q_ytl, 16)

                sy.wait_ge(q_st, 16)
                sy.wait_ge(t6, 10)
                def _out_store(j):
                    pj = j % 2
                    sy.wait_ge(v7x, j + 1)
                    sy.dma_start(out=out_d[j * 128:(j + 1) * 128, :],
                                 in_=A2f[:, OUT_F + 1024 * pj:OUT_F + 1024 * (pj + 1)]
                                 ).then_inc(q_out[pj], 16)
                for i in range(NT):
                    _out_store(i)
                sy.wait_ge(q_out[0], 16 * 16)
                sy.wait_ge(q_out[1], 16 * 16)

            @blk.gpsimd
            def _(g):
                g.dma_start(out=eye_b[:], in_=eye_d).then_inc(g_eye, 16)
                g.dma_start(out=bq_r, in_=bq).then_inc(g_bias, 16)
                g.dma_start(out=bk_r, in_=bk).then_inc(g_bias, 16)
                g.dma_start(out=bv_r, in_=bv).then_inc(g_bias, 16)
                g.dma_start(out=phib_r, in_=phib).then_inc(g_bias, 16)
                g.dma_start(out=A2[:, PHIT_O:PHIT_O + 16384].rearrange(
                    "p (t c) -> p t c", c=2048),
                    in_=phiwt_d.rearrange("(t p) c -> p t c", p=128)).then_inc(g_phi, 16)
                for wi, wsrc in enumerate((wqt, wkt, wvt)):
                    g.dma_start(out=A2[:, W3_O + wi * 8192:W3_O + (wi + 1) * 8192
                                       ].rearrange("p (t c) -> p t c", c=1024),
                                in_=wsrc.rearrange("(t p) c -> p t c", p=128)
                                ).then_inc(g_w, 16)

                for i in range(NT):
                    p = i % 2
                    g.wait_ge(gd, i + 1)
                    if i >= 2:
                        g.wait_ge(q_D[p], 16 * ((i - 2) // 2 + 1))
                    E3g = A3[:, E_O[p]:E_O[p] + 2048].rearrange("p (n q) -> p n q", q=32)
                    d3g = A3[:, DS_O[p]:DS_O[p] + 2048].rearrange("p (n q) -> p n q", q=32)
                    rDg = denD[:, 64 * p:64 * p + 64].unsqueeze(2).broadcast_to(
                        [128, 64, 32])
                    g.tensor_mul(d3g, E3g, rDg).then_inc(v2d, 1)

                g.wait_ge(q_D[0], 16 * 16)
                g.wait_ge(q_D[1], 16 * 16)
                g.wait_ge(t3h, 4 * NCH)
                for sl in range(16):
                    p = sl % 2
                    if sl >= 2:
                        g.wait_ge(t4, sl - 1)
                    src = sp_dsm.transpose([1, 0, 2])[:, :, sl * 128:(sl + 1) * 128]
                    g.dma_start(out=A2[:, DSM_O + 4096 * p:DSM_O + 4096 * (p + 1)
                                       ].rearrange("p (t c) -> p t c", c=128),
                                in_=src).then_inc(g_d4[p], 16)
                g.wait_ge(q_4[0], 16 * 8)
                g.wait_ge(q_4[1], 16 * 8)
                g.collective_compute("ReduceScatter", ALU.add,
                                     replica_groups=[list(range(8))],
                                     ins=[ar_in.opt()], outs=[rs_out.opt()]
                                     ).then_inc(cc1, 1)
                g.wait_ge(q_st, 16)
                g.collective_compute("AllGather", ALU.bypass,
                                     replica_groups=[[0, 1], [2, 3], [4, 5], [6, 7]],
                                     ins=[st_in.opt()], outs=[st_out.opt()]
                                     ).then_inc(cc2, 1)
                for n in range(NEL):
                    pe = n % 2
                    if n < 2:
                        g.wait_ge(q_st, 16)
                        g.wait_ge(t5y, NS * NCH)
                    else:
                        g.wait_ge(t6, 2 + (n - 2) + 1)
                    g.dma_start(out=A3[:, EW_O[pe]:EW_O[pe] + 8192].rearrange(
                        "p (t c) -> p t c", c=1024),
                        in_=ew[n].rearrange("(t p) c -> p t c", p=128)).then_inc(g_ew[pe], 16)
                    g.dma_start(out=ebrow[0:1, pe * D:(pe + 1) * D],
                                in_=eb[n:n + 1, :]).then_inc(g_ew[pe], 16)
                g.wait_ge(q_yt[0], 16 * 4)
                g.wait_ge(q_yt[1], 16 * 4)
                g.collective_compute("AllGather", ALU.bypass,
                                     replica_groups=[list(range(8))],
                                     ins=[yt_in.opt()], outs=[yt_out.opt()]
                                     ).then_inc(cc3, 1)
                g.wait_ge(t6, 10)
                g.wait_ge(t5c, 8)
                g.wait_ge(q_ya[0], 16 * 16)
                g.wait_ge(q_ya[1], 16 * 16)

                for i in range(NT):
                    p = i % 2
                    if i >= 2:
                        g.wait_ge(q_t7[p], 32 * ((i - 2) // 2 + 1))
                    g.dma_start(out=A2[:, CSM_O[p]:CSM_O[p] + 2048],
                                in_=sp_E[i]).then_inc(g_7[p], 16)
                    if i >= 2:
                        g.wait_ge(v7x, i - 1)
                    g.dma_start(out=A2f[:, YAB_F + 1024 * p:YAB_F + 1024 * (p + 1)],
                                in_=sp_ya[i]).then_inc(q_y7[p], 16)

    return nc


def kernel(L0_attn_x_seq, L0_attn_Wq, L0_attn_bq, L0_attn_Wk, L0_attn_bk,
           L0_attn_Wv, L0_attn_bv, L0_attn_gamma, L0_ffn_phi_w, L0_ffn_phi_b,
           L0_ffn_expert_w, L0_ffn_expert_b):
    x = np.asarray(L0_attn_x_seq)
    nc = build_nc()
    eye = np.eye(128, dtype=np.float32)
    causal = np.ascontiguousarray(np.tril(np.ones((128, 128), dtype=np.float32)).T)
    iota_col = np.arange(128, dtype=np.float32).reshape(128, 1)
    iota_row = np.arange(1056, dtype=np.float32).reshape(1, 1056)
    wqt = np.ascontiguousarray(np.asarray(L0_attn_Wq).T)
    wkt = np.ascontiguousarray(np.asarray(L0_attn_Wk).T)
    wvt = np.ascontiguousarray(np.asarray(L0_attn_Wv).T)
    phiwt = np.ascontiguousarray(np.asarray(L0_ffn_phi_w).T)
    in_maps = []
    for c in range(8):
        b, h = c // 2, c % 2
        in_maps.append({
            "xsh": x[b][h * TOK:(h + 1) * TOK],
            "wqt": wqt, "wkt": wkt, "wvt": wvt,
            "bq": np.asarray(L0_attn_bq).reshape(1, D),
            "bk": np.asarray(L0_attn_bk).reshape(1, D),
            "bv": np.asarray(L0_attn_bv).reshape(1, D),
            "gam": np.asarray(L0_attn_gamma, dtype=np.float32).reshape(1, 1),
            "phiwt": phiwt,
            "phib": np.asarray(L0_ffn_phi_b).reshape(1, SLOT),
            "ew": np.ascontiguousarray(np.asarray(L0_ffn_expert_w)[c * NEL:(c + 1) * NEL]),
            "eb": np.ascontiguousarray(np.asarray(L0_ffn_expert_b)[c * NEL:(c + 1) * NEL]),
            "eye": eye, "causal": causal,
            "iota_col": iota_col, "iota_row": iota_row,
            "flag": np.array([[float(h)]], dtype=np.float32),
        })
    res = run_bass_kernel_spmd(nc, in_maps, core_ids=list(range(8)))
    full = np.empty((B, S, D), dtype=np.float32)
    for c in range(8):
        b, h = c // 2, c % 2
        full[b, h * TOK:(h + 1) * TOK] = res.results[c]["out"]
    return full


# version 09
# speedup vs baseline: 1.1592x; 1.0002x over previous
import numpy as np
from contextlib import ExitStack
import concourse.bass as bass
import concourse.mybir as mybir
from concourse.bass_utils import run_bass_kernel_spmd

F32 = mybir.dt.float32
BF16 = mybir.dt.bfloat16
AF = mybir.ActivationFunctionType
ALU = mybir.AluOpType
AX = mybir.AxisListType

B, S, D = 4, 8192, 1024
TOK = 4096
NT = 32
DT = 8
CH = 512
NCH = TOK // CH
NS = CH // 128
SLOT = 2048
NEL = 8
VW = 1026
NUMW = 1025
EPS_ATTN = 1e-6
EPS_LN = 1e-5
GSL = ((0, 512), (512, 1024), (1024, 1025))


def build_nc():
    nc = bass.Bass("TRN2", target_bir_lowering=False)

    def din(name, shape):
        return nc.dram_tensor(name, list(shape), F32, kind="ExternalInput").ap()

    xsh = din("xsh", [TOK, D])
    wqt = din("wqt", [D, D]); wkt = din("wkt", [D, D]); wvt = din("wvt", [D, D])
    bq = din("bq", [1, D]); bk = din("bk", [1, D]); bv = din("bv", [1, D])
    gam = din("gam", [1, 1])
    phiwt_d = din("phiwt", [D, SLOT]); phib = din("phib", [1, SLOT])
    ew = din("ew", [NEL, D, D]); eb = din("eb", [NEL, D])
    eye_d = din("eye", [128, 128])
    causal_d = din("causal", [128, 128])
    iotac_d = din("iota_col", [128, 1])
    iotar_d = din("iota_row", [1, 1056])
    flag_d = din("flag", [1, 1])
    out_d = nc.dram_tensor("out", [TOK, D], F32, kind="ExternalOutput").ap()

    sp_xn = nc.dram_tensor("sp_xn", [NT, 128, D], BF16).ap()
    sp_E = nc.dram_tensor("sp_E", [NT, 128, SLOT], BF16).ap()
    sp_dsm = nc.dram_tensor("sp_dsm", [NT, 128, SLOT], BF16).ap()
    sp_q = nc.dram_tensor("sp_q", [NCH, 128, DT * 512], BF16).ap()
    sp_k = nc.dram_tensor("sp_k", [NCH, 128, DT * 512], BF16).ap()
    sp_kh = nc.dram_tensor("sp_kh", [NCH, 128, NS * D], BF16).ap()
    sp_v = nc.dram_tensor("sp_v", [NCH, 128, NS * VW], BF16).ap()
    sp_ya = nc.dram_tensor("sp_ya", [NT, 128, D], F32).ap()
    lng_dr = nc.dram_tensor("lng_dr", [1, 1], F32).ap()
    ar_in = nc.dram_tensor("cc_ar_in", [SLOT, D], F32).ap()
    rs_out = nc.dram_tensor("cc_rs_out", [256, D], F32).ap()
    st_in = nc.dram_tensor("cc_st_in", [DT, 128, VW], BF16).ap()
    st_out = nc.dram_tensor("cc_st_out", [2, DT, 128, VW], BF16).ap()
    yt_in = nc.dram_tensor("cc_yt_in", [256, D], BF16).ap()
    yt_out = nc.dram_tensor("cc_yt_out", [SLOT, D], BF16).ap()

    with ExitStack() as stack:
        def sbt(name, shape, dt_):
            return stack.enter_context(nc.sbuf_tensor(name, list(shape), dt_))
        eye_f = sbt("eye_f", [128, 128], F32)
        eye_b = sbt("eye_b", [128, 128], BF16)
        causal = sbt("sb_causal", [128, 128], F32)
        iotac = sbt("sb_iotac", [128, 1], F32)
        iotaI = sbt("iotaI", [128, 512], F32)
        gam_s = sbt("gam_s", [1, 1], F32)
        lng = sbt("lng", [1, 1], F32)
        lng_bc = sbt("lng_bc", [128, 1], F32)
        nlng_bc = sbt("nlng_bc", [128, 1], F32)
        fcol = sbt("fcol", [128, 1], F32)
        g512c = sbt("g512c", [128, 1], F32)
        ccs = sbt("ccs", [128, 2], F32)
        asc = sbt("asc", [128, 4], F32)
        ksc = sbt("ksc", [128, 4], F32)
        bcols = sbt("bcols", [128, 8], F32)
        gIbf = sbt("gIbf", [128, 128], BF16)
        gammaI = sbt("gammaI", [128, 512], F32)
        ones_r = sbt("ones_r", [1, 512], BF16)
        stat = sbt("stat", [128, 16], F32)
        negmax = sbt("negmax", [128, 2], F32)
        denD = sbt("denD", [128, 128], F32)
        denC = sbt("denC", [128, 64], F32)
        att_sb = sbt("att_sb", [128, 2048], BF16)
        ycol = sbt("ycol", [128, 4], F32)
        A1 = sbt("A1", [128, 32768], BF16)
        A2 = sbt("A2", [128, 49664], BF16)
        A3 = sbt("A3", [128, 16416], BF16)
        PS1 = stack.enter_context(nc.psum_tensor("PS1", [128, 2048], F32))
        PS2 = stack.enter_context(nc.psum_tensor("PS2", [128, 2048], F32))

        A1b = A1[:]
        A1f = A1[:].bitcast(F32)
        A2f = A2[:].bitcast(F32)
        A3f = A3[:].bitcast(F32)

        bq_r = A2[0:1, 45056:46080]
        bk_r = A2[0:1, 46080:47104]
        bv_r = A2[0:1, 47104:48128]
        phib_r = A3[0:1, 14336:16384]
        ebrow = A2[0:1, 47120:49168]

        xnT = A1b
        xn4 = A1b
        SINIT_O = 16500
        YA_O = 12544
        W3_O = 0
        PHIT_O = 24576
        CB = 12288 + NS * VW
        CB_O = [0, CB]
        DSM_O = 40960
        EW_O = [4096, 20488]
        CSM_O2 = [12288, 14336]
        CST_O2 = [28688, 30736]
        YAB2_F = 0
        OUT2_F = 2048
        XO_F = 16392
        SQ_F = 20480
        SQ6_F = 21512
        XS_F = 21512
        XC_F = 14592
        TANH_O = 36880
        XTT_O = 38928
        YTS_O = 40976
        CSM_O = [0, 2048]
        CST_O = [4096, 6144]
        YAB_F = 4096
        OUT_F = 6144
        X_F = [0, 1024]
        XN_O = [4096, 5120]
        E_O = [6144, 8192]
        DS_O = [10240, 12288]
        P3Q, P3K, P3H, P3V = 0, 4096, 8192, 12288
        ST_O = [0, DT * VW]

        def sm(name):
            return nc.alloc_semaphore(name)

        def sm2(name):
            return [nc.alloc_semaphore(name + "0"), nc.alloc_semaphore(name + "1")]

        q_c = sm("q_c")
        g_eye = sm("g_eye"); g_bias = sm("g_bias"); g_w = sm("g_w"); g_phi = sm("g_phi")
        s_pro = sm("s_pro"); v_pro = sm("v_pro")
        q_x = sm2("q_x"); s1a = sm("s1a"); v1s = sm("v1s"); s1b = sm("s1b")
        t1 = sm("t1"); v1e = sm("v1e"); q_xn = sm2("q_xn")
        t2 = sm("t2"); v2m = sm("v2m"); s2e = sm("s2e"); v2d = sm("v2d")
        q_E = sm2("q_E"); q_D = sm2("q_D")
        t3 = sm("t3"); t3h = sm("t3h"); v3q = sm("v3q"); s3k = sm("s3k")
        v3v = sm("v3v"); s3h = sm("s3h")
        q_3q = sm("q_3q"); q_3k = sm("q_3k"); q_3h = sm("q_3h"); q_3v = sm("q_3v")
        q_r4 = sm("q_r4"); g_d4 = sm2("g_d4"); t4 = sm("t4"); v4 = sm("v4")
        q_4 = sm2("q_4"); cc1 = sm("cc1")
        q_5 = sm2("q_5"); v5i = sm("v5i"); t5u = sm("t5u"); sv5u = sm("sv5u")
        t5a = sm("t5a"); v5a = sm("v5a"); t5y = sm("t5y"); v5y = sm("v5y")
        s5y = sm("s5y"); q_ya = sm2("q_ya"); q_st = sm("q_st"); cc2 = sm("cc2")
        q_si = sm("q_si"); v5c = sm("v5c"); t5c = sm("t5c"); v5n = sm("v5n")
        q_rs = sm("q_rs"); g_ew = sm2("g_ew"); v6 = sm("v6"); s6 = sm("s6")
        t6 = sm("t6"); v6e = sm("v6e"); q_yt = sm2("q_yt"); cc3 = sm("cc3")
        q_ytl = sm("q_ytl")
        g_7 = sm2("g_7"); q_7 = sm2("q_7"); v7c = sm("v7c"); t7t = sm("t7t")
        v7t = sm("v7t"); t7y = sm("t7y"); v7o = sm("v7o"); q_out = sm2("q_out")
        vd = sm("vd"); sd = sm("sd")
        q_t1 = sm2("q_t1"); gd = sm("gd")
        q_t7 = sm2("q_t7"); gd7 = sm("gd7"); v7a = sm("v7a"); q_y7 = sm2("q_y7"); v7x = sm("v7x")
        q_x5 = sm2("q_x5"); q_xc = sm("q_xc")

        def psb(which, lo, hi, bf=False):
            t_ = PS1 if which == 0 else PS2
            ap = t_[:, lo:hi]
            return ap.bitcast(BF16) if bf else ap

        with nc.Block() as blk:

            @blk.tensor
            def _(t):
                t.wait_ge(g_eye, 16)
                t.wait_ge(g_phi, 16)
                t.wait_ge(g_bias, 16 * 4)
                t.wait_ge(v_pro, 13)
                for i in range(NT):
                    p = i % 2
                    t.wait_ge(q_t1[p], 16 * (i // 2 + 1))
                    if i >= 2:
                        t.wait_ge(s2e, i - 1)
                    log = (PS1 if p == 0 else PS2)[:, 0:2048]
                    for step in range(DT):
                        lhs = xnT[:, step * 4096 + i * 128: step * 4096 + (i + 1) * 128]
                        for cs in range(4):
                            t.matmul(log[:, cs * 512:(cs + 1) * 512], lhs,
                                     A2[:, PHIT_O + step * 2048 + cs * 512:
                                        PHIT_O + step * 2048 + (cs + 1) * 512],
                                     start=(step == 0), stop=False)
                    for cs in range(4):
                        ins = t.matmul(log[:, cs * 512:(cs + 1) * 512], ones_r[0:1, 0:128],
                                       phib_r[0:1, cs * 512:(cs + 1) * 512],
                                       start=False, stop=True)
                        if cs == 3:
                            ins.then_inc(t2, 1)

                t.wait_ge(g_w, 16 * 3)
                t.wait_ge(s2e, NT)
                for c in range(NCH):
                    for j in range(16):
                        proj = 0 if j < 8 else 1
                        e = j % 8
                        pq = j % 2
                        if j >= 2:
                            jp = j - 2
                            if jp < 8:
                                t.wait_ge(v3q, c * 8 + jp + 1)
                            else:
                                t.wait_ge(s3k, c * 8 + (jp - 8) + 1)
                        elif c > 0:
                            t.wait_ge(s3k, (c - 1) * 8 + 6 + j + 1)
                        woff = W3_O + proj * 8192
                        brow = bq_r if proj == 0 else bk_r
                        ps = PS1[:, 512 * pq:512 * (pq + 1)]
                        for step in range(DT):
                            t.matmul(ps, A2[:, woff + step * 1024 + e * 128:
                                            woff + step * 1024 + (e + 1) * 128],
                                     xnT[:, step * 4096 + c * 512:
                                         step * 4096 + (c + 1) * 512],
                                     start=(step == 0), stop=False)
                        t.matmul(ps, brow[0:1, e * 128:(e + 1) * 128],
                                 ones_r[0:1, 0:512], start=False,
                                 stop=True).then_inc(t3, 1)
                    for g in range(8):
                        sbt_, cs = g // 2, g % 2
                        pv = g % 2
                        if g >= 2:
                            t.wait_ge(v3v, c * 8 + (g - 2) + 1)
                        elif c > 0:
                            t.wait_ge(v3v, (c - 1) * 8 + 6 + g + 1)
                        ps = PS1[:, 1024 + 512 * pv:1024 + 512 * (pv + 1)]
                        for step in range(DT):
                            t.matmul(ps, xnT[:, step * 4096 + c * 512 + sbt_ * 128:
                                             step * 4096 + c * 512 + (sbt_ + 1) * 128],
                                     A2[:, W3_O + 16384 + step * 1024 + cs * 512:
                                        W3_O + 16384 + step * 1024 + (cs + 1) * 512],
                                     start=(step == 0), stop=False)
                        t.matmul(ps, ones_r[0:1, 0:128],
                                 bv_r[0:1, cs * 512:(cs + 1) * 512],
                                 start=False, stop=True).then_inc(t3, 1)
                    t.wait_ge(s3k, 8 * (c + 1))
                    for sbt_ in range(NS):
                        ptk = sbt_ % 2
                        if c * 4 + sbt_ >= 2:
                            t.wait_ge(s3h, c * 4 + sbt_ - 1)
                        trb = psb(1, 512 * ptk, 512 * (ptk + 1), bf=True)
                        for dtl in range(DT):
                            ins = t.transpose(trb[:, dtl * 128:(dtl + 1) * 128],
                                              A3[:, P3K + dtl * 512 + sbt_ * 128:
                                                 P3K + dtl * 512 + (sbt_ + 1) * 128],
                                              eye_b[:])
                            if dtl == DT - 1:
                                ins.then_inc(t3h, 1)

                t.wait_ge(v3q, 64); t.wait_ge(s3k, 64); t.wait_ge(v3v, 64)
                t.wait_ge(q_r4, 16)
                for sl in range(16):
                    p = sl % 2
                    t.wait_ge(g_d4[p], 16 * (sl // 2 + 1))
                    if sl >= 2:
                        t.wait_ge(v4, sl - 1)
                    ps = PS1[:, 1024 * p:1024 * (p + 1)]
                    dsm_b = A2[:, DSM_O + 4096 * p:DSM_O + 4096 * (p + 1)]
                    for tt in range(NT):
                        for cs in range(2):
                            ins = t.matmul(ps[:, cs * 512:(cs + 1) * 512],
                                           dsm_b[:, tt * 128:(tt + 1) * 128],
                                           xn4[:, tt * 1024 + cs * 512:
                                               tt * 1024 + (cs + 1) * 512],
                                           start=(tt == 0), stop=(tt == NT - 1))
                            if tt == NT - 1 and cs == 1:
                                ins.then_inc(t4, 1)

                t.wait_ge(v4, 16)
                t.wait_ge(s3h, 32)
                t.wait_ge(v5i, 2)
                t.wait_ge(v_pro, 12)
                for c in range(NCH):
                    bc = CB_O[c % 2]
                    sp = ST_O[c % 2]
                    t.wait_ge(q_5[c % 2], 16 * 4 * (c // 2 + 1))
                    if c in (1, 2):
                        t.wait_ge(v5y, 4 * c)
                    elif c >= 3:
                        t.wait_ge(s5y, 4 * (c - 2))
                    for dk in range(DT):
                        pu = dk % 2
                        if c * 8 + dk >= 2:
                            t.wait_ge(sv5u, c * 8 + dk - 1)
                        ps = PS1 if pu == 0 else PS2
                        for lo, hi in GSL:
                            t.matmul(ps[:, lo:hi], gIbf[:],
                                     A3[:, sp + dk * VW + lo: sp + dk * VW + hi],
                                     start=True, stop=False)
                        for sbt_ in range(NS):
                            lhs = A2[:, bc + 8192 + sbt_ * 1024 + dk * 128:
                                     bc + 8192 + sbt_ * 1024 + (dk + 1) * 128]
                            for gi, (lo, hi) in enumerate(GSL):
                                ins = t.matmul(ps[:, lo:hi], lhs,
                                               A2[:, bc + 12288 + sbt_ * VW + lo:
                                                  bc + 12288 + sbt_ * VW + hi],
                                               start=False, stop=(sbt_ == NS - 1))
                                if sbt_ == NS - 1 and gi == 2:
                                    ins.then_inc(t5u, 1)
                    for a_ in range(NS):
                        pa = a_ % 2
                        if c * 4 + a_ >= 2:
                            t.wait_ge(v5a, c * 4 + a_ - 1)
                        ps = (PS1 if pa == 0 else PS2)[:, 1536:2048]
                        for step in range(DT):
                            ins = t.matmul(ps, A2[:, bc + 4096 + step * 512 + a_ * 128:
                                                  bc + 4096 + step * 512 + (a_ + 1) * 128],
                                           A2[:, bc + step * 512:bc + (step + 1) * 512],
                                           start=(step == 0), stop=(step == DT - 1))
                            if step == DT - 1:
                                ins.then_inc(t5a, 1)
                    t.wait_ge(sv5u, 8 * (c + 1))
                    for b_ in range(NS):
                        py = b_ % 2
                        yidx = c * NS + b_
                        if yidx >= 2:
                            prev = yidx - 2
                            if prev < 2 * NS:
                                t.wait_ge(v5y, prev + 1)
                            else:
                                t.wait_ge(s5y, prev - 2 * NS + 1)
                        ps = PS1 if py == 0 else PS2
                        for step in range(DT):
                            lhs = A2[:, bc + step * 512 + b_ * 128:
                                     bc + step * 512 + (b_ + 1) * 128]
                            for lo, hi in GSL:
                                t.matmul(ps[:, lo:hi], lhs,
                                         A3[:, sp + step * VW + lo: sp + step * VW + hi],
                                         start=(step == 0), stop=False)
                        for a_ in range(b_ + 1):
                            t.wait_ge(v5a, c * 4 + a_ + 1)
                            for gi, (lo, hi) in enumerate(GSL):
                                ins = t.matmul(ps[:, lo:hi],
                                               att_sb[:, a_ * 512 + b_ * 128:
                                                  a_ * 512 + (b_ + 1) * 128],
                                               A2[:, bc + 12288 + a_ * VW + lo:
                                                  bc + 12288 + a_ * VW + hi],
                                               start=False, stop=(a_ == b_))
                                if a_ == b_ and gi == 2:
                                    ins.then_inc(t5y, 1)

                t.wait_ge(t5y, NS * NCH)
                t.wait_ge(v5a, 4 * NCH)
                for i6 in range(2):
                    t.wait_ge(s6, 2 + i6 + 1)
                    trb = psb(i6, 1536, 2048, bf=True)
                    for dtl in range(DT):
                        ins = t.transpose(trb[:, dtl * 128:(dtl + 1) * 128],
                                          A2[:, TANH_O + i6 * 1024 + dtl * 128:
                                             TANH_O + i6 * 1024 + (dtl + 1) * 128],
                                          eye_b[:])
                        if dtl == DT - 1:
                            ins.then_inc(t6, 1)
                for n in range(NEL):
                    pe = n % 2
                    t.wait_ge(g_ew[pe], 16 * 2 * (n // 2 + 1))
                    t.wait_ge(v6e, 2)
                    if n >= 1:
                        t.wait_ge(v6e, 2 + 2 * n)
                    for cs in range(2):
                        ps = psb(cs, 1536, 2048)
                        for step in range(DT):
                            t.matmul(ps[0:32, :],
                                     A2[:, XTT_O + step * 256 + n * 32:
                                        XTT_O + step * 256 + n * 32 + 32],
                                     A2[:, EW_O[pe] + step * 1024 + cs * 512:
                                        EW_O[pe] + step * 1024 + (cs + 1) * 512],
                                     start=(step == 0), stop=False)
                        ins = t.matmul(ps[0:32, :],
                                       ones_r[0:1, 0:32],
                                       ebrow[0:1, pe * D + cs * 512:pe * D + (cs + 1) * 512],
                                       start=False, stop=True)
                        if cs == 1:
                            ins.then_inc(t6, 1)

                t.wait_ge(q_si, 16)
                t.wait_ge(v5c, 2)
                for cc in range(2):
                    for b_ in range(NS):
                        idx = cc * NS + b_
                        pyc = idx % 2
                        if idx < 2:
                            t.wait_ge(s5y, 24)
                        else:
                            t.wait_ge(v5n, idx - 1)
                        ps = PS1 if pyc == 0 else PS2
                        for step in range(DT):
                            lhs = A2[:, CB_O[cc] + step * 512 + b_ * 128:
                                     CB_O[cc] + step * 512 + (b_ + 1) * 128]
                            for gi, (lo, hi) in enumerate(GSL):
                                ins = t.matmul(ps[:, lo:hi], lhs,
                                               A1b[:, SINIT_O + step * VW + lo:
                                                   SINIT_O + step * VW + hi],
                                               start=(step == 0), stop=(step == DT - 1))
                                if step == DT - 1 and gi == 2:
                                    ins.then_inc(t5c, 1)

                t.wait_ge(q_ytl, 16)
                t.wait_ge(v6e, 18)
                t.wait_ge(v5n, 8)
                for i in range(NT):
                    p = i % 2
                    t.wait_ge(q_t7[p], 32 * (i // 2 + 1))
                    if i >= 2:
                        t.wait_ge(v7x, i - 1)
                    ps = PS1[:, 1024 * p:1024 * (p + 1)]
                    for sl in range(16):
                        for cs in range(2):
                            ins = t.matmul(ps[:, cs * 512:(cs + 1) * 512],
                                           A2[:, CST_O2[p] + sl * 128:
                                              CST_O2[p] + (sl + 1) * 128],
                                           A1b[:, sl * 1024 + cs * 512:
                                               sl * 1024 + (cs + 1) * 512],
                                           start=(sl == 0), stop=(sl == 15))
                            if sl == 15 and cs == 1:
                                ins.then_inc(t7y, 1)

            @blk.vector
            def _(v):
                nv = [0]

                def V(inst):
                    inst.then_inc(vd, 1)
                    nv[0] += 1

                def VB():
                    v.wait_ge(vd, nv[0])

                v.wait_ge(q_c, 16 * 8)
                v.tensor_scalar_mul(nlng_bc[:], lng_bc[:], -1.0).then_inc(v_pro, 1)
                for a_ in range(4):
                    v.tensor_scalar_mul(bcols[:, a_:a_ + 1], lng_bc[:],
                                        float(-(128 * a_ + 1))).then_inc(v_pro, 1)
                for s_ in range(4):
                    v.tensor_scalar_mul(bcols[:, 4 + s_:5 + s_], lng_bc[:],
                                        float(511 - 128 * s_)).then_inc(v_pro, 1)
                v.wait_ge(s_pro, 3)
                v.tensor_mul(ccs[:, 1:2], fcol[:], g512c[:]).then_inc(v_pro, 1)
                v.tensor_copy(ccs[:, 0:1], fcol[:]).then_inc(v_pro, 1)
                v.tensor_scalar_mul(gIbf[:], eye_f[:], g512c[:]).then_inc(v_pro, 1)
                v.memset(ones_r[:], 1.0).then_inc(v_pro, 1)

                for i in range(NT + 2):
                    if i < NT:
                        p = i % 2
                        v.wait_ge(q_x[p], 16 * (i // 2 + 1))
                        xt = A3f[:, X_F[p]:X_F[p] + 1024]
                        V(v.reduce_sum(stat[:, p * 8:p * 8 + 1], xt, axis=AX.X))
                        v.wait_ge(s1a, i + 1)
                        VB()
                        V(v.tensor_scalar_mul(stat[:, p * 8:p * 8 + 1],
                                              stat[:, p * 8:p * 8 + 1], 1.0 / D))
                        V(v.tensor_scalar_mul(stat[:, p * 8 + 2:p * 8 + 3],
                                              stat[:, p * 8 + 1:p * 8 + 2], 1.0 / D))
                        VB()
                        V(v.tensor_mul(stat[:, p * 8 + 3:p * 8 + 4],
                                       stat[:, p * 8:p * 8 + 1],
                                       stat[:, p * 8:p * 8 + 1]))
                        VB()
                        V(v.tensor_sub(stat[:, p * 8 + 2:p * 8 + 3],
                                       stat[:, p * 8 + 2:p * 8 + 3],
                                       stat[:, p * 8 + 3:p * 8 + 4]))
                        VB()
                        V(v.tensor_scalar_add(stat[:, p * 8 + 2:p * 8 + 3],
                                              stat[:, p * 8 + 2:p * 8 + 3], EPS_LN))
                        VB()
                        v.reciprocal(stat[:, p * 8 + 3:p * 8 + 4],
                                     stat[:, p * 8 + 2:p * 8 + 3]).then_inc(v1s, 1)
                    if i >= 2:
                        j = i - 2
                        pj = j % 2
                        v.wait_ge(s2e, j + 1)
                        if j >= 2:
                            v.wait_ge(v2d, j - 1)
                        E3 = A3[:, E_O[pj]:E_O[pj] + 2048].rearrange(
                            "p (n q) -> p n q", q=32)
                        V(v.reduce_sum(denD[:, 64 * pj:64 * pj + 64], E3, axis=AX.X))
                        VB()
                        v.reciprocal(denD[:, 64 * pj:64 * pj + 64],
                                     denD[:, 64 * pj:64 * pj + 64]).then_inc(gd, 1)

                v.wait_ge(s_pro, 11)
                for c in range(NCH):
                    for e in range(DT):
                        pq = e % 2
                        v.wait_ge(t3, c * 24 + e + 1)
                        if c >= 1 and e < 2:
                            v.wait_ge(q_3q, 16 * c)
                        ps = PS1[:, 512 * pq:512 * (pq + 1)]
                        v.scalar_tensor_tensor(
                            A3[:, P3Q + e * 512:P3Q + (e + 1) * 512], ps, 0.0,
                            gammaI[:], ALU.max, ALU.mult).then_inc(v3q, 1)
                    if c == 0:
                        v.wait_ge(q_D[0], 16 * 16)
                        v.wait_ge(q_D[1], 16 * 16)
                        v.wait_ge(q_E[0], 16 * 16)
                        v.wait_ge(q_E[1], 16 * 16)
                        for sbt_ in range(NS):
                            V(v.memset(A3[:, P3V + sbt_ * VW + 1024:
                                          P3V + sbt_ * VW + 1025], 1.0))
                            V(v.memset(A3[:, P3V + sbt_ * VW + 1025:
                                          P3V + sbt_ * VW + 1026], 0.0))
                    for g in range(8):
                        sbt_, cs = g // 2, g % 2
                        pv = g % 2
                        v.wait_ge(t3, c * 24 + 16 + g + 1)
                        if c >= 1 and g < 2:
                            v.wait_ge(q_3v, 16 * c)
                        ps = PS1[:, 1024 + 512 * pv:1024 + 512 * (pv + 1)]
                        v.tensor_copy(A3[:, P3V + sbt_ * VW + cs * 512:
                                         P3V + sbt_ * VW + cs * 512 + 512],
                                      ps).then_inc(v3v, 1)

                for sl in range(16):
                    p = sl % 2
                    v.wait_ge(t4, sl + 1)
                    if sl >= 2:
                        v.wait_ge(q_4[p], 16 * ((sl - 2) // 2 + 1))
                    ps = PS1[:, 1024 * p:1024 * (p + 1)]
                    v.tensor_copy(A3f[:, 1024 * p:1024 * (p + 1)], ps).then_inc(v4, 1)

                v.wait_ge(q_4[0], 16 * 8)
                v.wait_ge(q_4[1], 16 * 8)
                v.memset(A3[:, ST_O[0]:ST_O[0] + DT * VW], 0.0).then_inc(v5i, 1)
                v.memset(A3[:, ST_O[1]:ST_O[1] + DT * VW], 0.0).then_inc(v5i, 1)

                for c in range(NCH):
                    for a_ in range(NS):
                        v.wait_ge(t5a, c * 4 + a_ + 1)
                        if c >= 1:
                            v.wait_ge(t5y, 4 * c)
                        pa = a_ % 2
                        ps = (PS1 if pa == 0 else PS2)[:, 1536:2048]
                        row = att_sb[:, a_ * 512:(a_ + 1) * 512]
                        V(v.tensor_scalar_mul(row, ps, asc[:, a_:a_ + 1]))
                        VB()
                        v.scalar_tensor_tensor(
                            att_sb[:, a_ * 512 + a_ * 128:a_ * 512 + (a_ + 1) * 128],
                            ps[:, a_ * 128:(a_ + 1) * 128], asc[:, a_:a_ + 1],
                            causal[:], ALU.mult, ALU.mult).then_inc(v5a, 1)
                    for b_ in range(NS):
                        py = b_ % 2
                        yidx = c * NS + b_
                        v.wait_ge(t5y, yidx + 1)
                        ps = PS1 if py == 0 else PS2
                        if c < 2:
                            v.tensor_copy(A1f[:, yidx * NUMW:(yidx + 1) * NUMW],
                                          ps[:, 0:NUMW]).then_inc(v5y, 1)
                        else:
                            if yidx >= 10:
                                v.wait_ge(s5y, yidx - 9)
                            V(v.tensor_scalar_add(ycol[:, py:py + 1], ps[:, 1024:1025],
                                                  EPS_ATTN))
                            VB()
                            v.reciprocal(ycol[:, 2 + py:3 + py],
                                         ycol[:, py:py + 1]).then_inc(v5y, 1)
                            v.wait_ge(q_x5[py], 16 * ((yidx - 8) // 2 + 1))
                            v.wait_ge(v5y, yidx + 1)
                            if yidx >= 10:
                                v.wait_ge(q_ya[py], 16 * ((yidx - 10) // 2 + 1))
                            VB()
                            v.scalar_tensor_tensor(
                                A1f[:, YA_O + 1024 * py:YA_O + 1024 * (py + 1)],
                                ps[:, 0:1024], ycol[:, 2 + py:3 + py],
                                A2f[:, XS_F + 1024 * py:XS_F + 1024 * (py + 1)],
                                ALU.mult, ALU.add).then_inc(s5y, 1)

                v.wait_ge(q_rs, 16)
                for i6 in range(2):
                    xo = A2f[:, XO_F + 1024 * i6:XO_F + 1024 * (i6 + 1)]
                    v.tensor_scalar(xo, xo, -33000.0, 65000.0,
                                    ALU.max, ALU.min).then_inc(v6, 1)
                    v.wait_ge(v6, 2 * i6 + 1)
                    V(v.reduce_sum(stat[:, 8 * i6:8 * i6 + 1], xo, axis=AX.X))
                    v.wait_ge(s6, i6 + 1)
                    VB()
                    V(v.tensor_scalar_mul(stat[:, 8 * i6:8 * i6 + 1],
                                          stat[:, 8 * i6:8 * i6 + 1], 1.0 / D))
                    V(v.tensor_scalar_mul(stat[:, 8 * i6 + 2:8 * i6 + 3],
                                          stat[:, 8 * i6 + 1:8 * i6 + 2], 1.0 / D))
                    VB()
                    V(v.tensor_mul(stat[:, 8 * i6 + 3:8 * i6 + 4],
                                   stat[:, 8 * i6:8 * i6 + 1], stat[:, 8 * i6:8 * i6 + 1]))
                    VB()
                    V(v.tensor_sub(stat[:, 8 * i6 + 2:8 * i6 + 3],
                                   stat[:, 8 * i6 + 2:8 * i6 + 3],
                                   stat[:, 8 * i6 + 3:8 * i6 + 4]))
                    VB()
                    V(v.tensor_scalar_add(stat[:, 8 * i6 + 2:8 * i6 + 3],
                                          stat[:, 8 * i6 + 2:8 * i6 + 3], EPS_LN))
                    VB()
                    v.reciprocal(stat[:, 8 * i6 + 3:8 * i6 + 4],
                                 stat[:, 8 * i6 + 2:8 * i6 + 3]).then_inc(v6, 1)
                for i6 in range(2):
                    v.wait_ge(t6, i6 + 1)
                    trb = psb(i6, 1536, 2048, bf=True)
                    src_ = trb.rearrange("p (t c) -> p t c", c=128)
                    dst = A2[:, XTT_O:XTT_O + 2048].rearrange(
                        "p (t c) -> p t c", c=256)[:, :, i6 * 128:(i6 + 1) * 128]
                    v.tensor_copy(dst, src_).then_inc(v6e, 1)
                for n in range(NEL):
                    pe = n % 2
                    v.wait_ge(t6, 2 + n + 1)
                    if n >= 2:
                        v.wait_ge(q_yt[pe], 16 * ((n - 2) // 2 + 1))
                    for cs in range(2):
                        ps = psb(cs, 1536, 2048)[0:32, :]
                        v.tensor_copy(A2[0:32, YTS_O + pe * D + cs * 512:
                                         YTS_O + pe * D + (cs + 1) * 512],
                                      ps).then_inc(v6e, 1)

                for cc in range(2):
                    v.wait_ge(q_5[cc], 16 * 17)
                    v.wait_ge(v_pro, 11)
                    qb = A2[:, CB_O[cc]:CB_O[cc] + 4096]
                    v.tensor_scalar_mul(qb, qb, ccs[:, cc:cc + 1]).then_inc(v5c, 1)
                for cc in range(2):
                    for b_ in range(NS):
                        idx = cc * NS + b_
                        pyc = idx % 2
                        v.wait_ge(t5c, idx + 1)
                        v.wait_ge(s5y, 23 + idx)
                        ps = PS1 if pyc == 0 else PS2
                        nh = A1f[:, idx * NUMW:(idx + 1) * NUMW]
                        V(v.tensor_add(nh, nh, ps[:, 0:NUMW]))
                        VB()
                        V(v.tensor_scalar_add(ycol[:, pyc:pyc + 1], nh[:, 1024:1025],
                                              EPS_ATTN))
                        VB()
                        v.reciprocal(ycol[:, 2 + pyc:3 + pyc],
                                     ycol[:, pyc:pyc + 1]).then_inc(v5n, 1)
                        v.wait_ge(q_xc, 16 * (idx + 1))
                        v.wait_ge(v5n, idx + 1)
                        if idx < 2:
                            v.wait_ge(q_ya[pyc], 16 * 12)
                        else:
                            v.wait_ge(q_ya[pyc], 16 * (12 + (idx - 2) // 2 + 1))
                        VB()
                        v.scalar_tensor_tensor(
                            A1f[:, YA_O + 1024 * pyc:YA_O + 1024 * (pyc + 1)],
                            nh[:, 0:1024], ycol[:, 2 + pyc:3 + pyc],
                            A1f[:, XC_F:XC_F + 1024],
                            ALU.mult, ALU.add).then_inc(s5y, 1)

                def _v7_add(j):
                    pj = j % 2
                    v.wait_ge(t7y, j + 1)
                    v.wait_ge(q_y7[pj], 16 * (j // 2 + 1))
                    if j >= 2:
                        v.wait_ge(q_out[pj], 16 * ((j - 2) // 2 + 1))
                    ps = PS1[:, 1024 * pj:1024 * (pj + 1)]
                    ob = A3f[:, OUT2_F + 1024 * pj:OUT2_F + 1024 * (pj + 1)]
                    v.tensor_add(ob, ps,
                                 A3f[:, YAB2_F + 1024 * pj:YAB2_F + 1024 * (pj + 1)]
                                 ).then_inc(v7x, 1)

                for i in range(NT):
                    p = i % 2
                    v.wait_ge(g_7[p], 16 * (i // 2 + 1))
                    E3 = A2[:, CSM_O2[p]:CSM_O2[p] + 2048].rearrange("p (n q) -> p n q", q=32)
                    E3t = E3.transpose([0, 2, 1])
                    V(v.reduce_sum(denC[:, 32 * p:32 * p + 32], E3t, axis=AX.X))
                    VB()
                    V(v.reciprocal(denC[:, 32 * p:32 * p + 32], denC[:, 32 * p:32 * p + 32]))
                    rC = denC[:, 32 * p:32 * p + 32].unsqueeze(1).broadcast_to([128, 64, 32])
                    VB()
                    v.tensor_mul(E3, E3, rC).then_inc(v7c, 1)
                    if i >= 1:
                        _v7_add(i - 1)
                _v7_add(NT - 1)

            @blk.scalar
            def _(s):
                ns = [0]

                def SC(inst):
                    inst.then_inc(sd, 1)
                    ns[0] += 1

                def SB():
                    s.wait_ge(sd, ns[0])

                s.wait_ge(q_c, 16 * 6)
                s.activation(lng[:], gam_s[:], AF.Ln).then_inc(s_pro, 1)
                s.wait_ge(q_c, 16 * 8)
                s.activation(gammaI[:], iotaI[:], AF.Exp,
                             scale=lng_bc[:]).then_inc(s_pro, 1)
                s.activation(g512c[:], lng_bc[:], AF.Exp, scale=512.0).then_inc(s_pro, 1)
                s.wait_ge(v_pro, 9)
                for a_ in range(4):
                    s.activation(asc[:, a_:a_ + 1], iotac[:], AF.Exp,
                                 scale=nlng_bc[:], bias=bcols[:, a_:a_ + 1]).then_inc(s_pro, 1)
                for s_ in range(4):
                    s.activation(ksc[:, s_:s_ + 1], iotac[:], AF.Exp,
                                 scale=nlng_bc[:], bias=bcols[:, 4 + s_:5 + s_]).then_inc(s_pro, 1)

                for i in range(NT + 3):
                    if i < NT:
                        p = i % 2
                        s.wait_ge(q_x[p], 16 * (i // 2 + 1))
                        xt = A3f[:, X_F[p]:X_F[p] + 1024]
                        s.activation(A2f[:, SQ_F + 1024 * p:SQ_F + 1024 * (p + 1)], xt,
                                     AF.Square,
                                     accum_out=stat[:, p * 8 + 1:p * 8 + 2]
                                     ).then_inc(s1a, 1)
                        s.wait_ge(v1s, i + 1)
                        SC(s.activation(stat[:, p * 8 + 4:p * 8 + 5],
                                        stat[:, p * 8 + 3:p * 8 + 4], AF.Sqrt))
                        SB()
                        SC(s.mul(stat[:, p * 8 + 5:p * 8 + 6], stat[:, p * 8:p * 8 + 1],
                                 stat[:, p * 8 + 4:p * 8 + 5]))
                        SB()
                        SC(s.mul(stat[:, p * 8 + 5:p * 8 + 6],
                                 stat[:, p * 8 + 5:p * 8 + 6], -1.0))
                        if i >= 2:
                            s.wait_ge(q_xn[p], 16 * ((i - 2) // 2 + 1))
                            s.wait_ge(q_t1[p], 16 * ((i - 2) // 2 + 1))
                        SB()
                        s.activation(A3[:, XN_O[p]:XN_O[p] + 1024], xt, AF.Identity,
                                     scale=stat[:, p * 8 + 4:p * 8 + 5],
                                     bias=stat[:, p * 8 + 5:p * 8 + 6]).then_inc(s1b, 1)
                    if 2 <= i < NT + 2:
                        j = i - 2
                        pj = j % 2
                        s.wait_ge(t2, j + 1)
                        if j >= 2:
                            s.wait_ge(v2d, j - 1)
                            s.wait_ge(q_E[pj], 16 * ((j - 2) // 2 + 1))
                        log = (PS1 if pj == 0 else PS2)[:, 0:2048]
                        s.activation(A3[:, E_O[pj]:E_O[pj] + 2048], log,
                                     AF.Exp).then_inc(s2e, 1)
                    if i >= 3:
                        j = i - 3
                        pj = j % 2
                        s.dma_start(out=sp_E[j], in_=A3[:, E_O[pj]:E_O[pj] + 2048]
                                    ).then_inc(q_E[pj], 16)
                        s.wait_ge(v2d, j + 1)
                        s.dma_start(out=sp_dsm[j], in_=A3[:, DS_O[pj]:DS_O[pj] + 2048]
                                    ).then_inc(q_D[pj], 16)

                s.wait_ge(q_E[0], 16 * 16)
                s.wait_ge(q_E[1], 16 * 16)
                s.wait_ge(q_xn[0], 16 * 16)
                s.wait_ge(q_xn[1], 16 * 16)
                s.wait_ge(s_pro, 11)
                for c in range(NCH):
                    for e in range(DT):
                        pq = e % 2
                        s.wait_ge(t3, c * 24 + 8 + e + 1)
                        if c >= 1 and e < 2:
                            s.wait_ge(q_3k, 16 * c)
                        ps = PS1[:, 512 * pq:512 * (pq + 1)]
                        s.activation(A3[:, P3K + e * 512:P3K + (e + 1) * 512], ps,
                                     AF.Relu).then_inc(s3k, 1)
                    for sbt_ in range(NS):
                        ptk = sbt_ % 2
                        s.wait_ge(t3h, c * 4 + sbt_ + 1)
                        if c >= 1 and sbt_ < 2:
                            s.wait_ge(q_3h, 16 * c)
                        trb = psb(1, 512 * ptk, 512 * (ptk + 1), bf=True)
                        s.mul(A3[:, P3H + sbt_ * 1024:P3H + (sbt_ + 1) * 1024], trb,
                              ksc[:, sbt_:sbt_ + 1]).then_inc(s3h, 1)

                for c in range(NCH):
                    sq_ = ST_O[(c + 1) % 2]
                    for dk in range(DT):
                        pu = dk % 2
                        s.wait_ge(t5u, c * 8 + dk + 1)
                        ps = PS1 if pu == 0 else PS2
                        s.copy(A3[:, sq_ + dk * VW:sq_ + dk * VW + NUMW],
                               ps[:, 0:NUMW]).then_inc(sv5u, 1)

                for i6 in range(2):
                    s.wait_ge(v6, 2 * i6 + 1)
                    xo = A2f[:, XO_F + 1024 * i6:XO_F + 1024 * (i6 + 1)]
                    s.activation(A2f[:, SQ6_F + 1024 * i6:SQ6_F + 1024 * (i6 + 1)], xo,
                                 AF.Square,
                                 accum_out=stat[:, 8 * i6 + 1:8 * i6 + 2]).then_inc(s6, 1)
                for i6 in range(2):
                    s.wait_ge(v6, 2 * i6 + 2)
                    xo = A2f[:, XO_F + 1024 * i6:XO_F + 1024 * (i6 + 1)]
                    SC(s.activation(stat[:, 8 * i6 + 4:8 * i6 + 5],
                                    stat[:, 8 * i6 + 3:8 * i6 + 4], AF.Sqrt))
                    SB()
                    SC(s.mul(stat[:, 8 * i6 + 5:8 * i6 + 6], stat[:, 8 * i6:8 * i6 + 1],
                             stat[:, 8 * i6 + 4:8 * i6 + 5]))
                    SB()
                    SC(s.mul(stat[:, 8 * i6 + 5:8 * i6 + 6],
                             stat[:, 8 * i6 + 5:8 * i6 + 6], -1.0))
                    SB()
                    s.activation(A2[:, TANH_O + i6 * 1024:TANH_O + (i6 + 1) * 1024], xo,
                                 AF.Tanh, scale=stat[:, 8 * i6 + 4:8 * i6 + 5],
                                 bias=stat[:, 8 * i6 + 5:8 * i6 + 6]).then_inc(s6, 1)


                s.wait_ge(t5y, NS * NCH)
                for i in range(NT):
                    p = i % 2
                    s.wait_ge(v7c, i + 1)
                    if i >= 2:
                        s.wait_ge(t7y, i - 1)
                    for half in range(2):
                        src_ = A2[:, CSM_O2[p] + half * 1024:CSM_O2[p] + (half + 1) * 1024
                                  ].rearrange("p (t c) -> p t c", c=128)
                        dst = A2[:, CST_O2[p] + half * 1024:CST_O2[p] + (half + 1) * 1024
                                 ].rearrange("p (t c) -> p t c", c=128)
                        s.dma_start_transpose(dst, src_).then_inc(q_t7[p], 16)


            @blk.sync
            def _(sy):
                sy.dma_start(out=eye_f[:], in_=eye_d).then_inc(q_c, 16)
                sy.dma_start(out=causal[:], in_=causal_d).then_inc(q_c, 16)
                sy.dma_start(out=iotac[:], in_=iotac_d).then_inc(q_c, 16)
                sy.dma_start(out=gam_s[:], in_=gam).then_inc(q_c, 16)
                sy.dma_start(out=fcol[:], in_=flag_d[0:1, 0:1].partition_broadcast(128)
                             ).then_inc(q_c, 16)
                sy.dma_start(out=iotaI[:], in_=iotar_d[0:1, 1:513].partition_broadcast(128)
                             ).then_inc(q_c, 16)
                sy.wait_ge(s_pro, 1)
                sy.dma_start(out=lng_dr, in_=lng[:]).then_inc(q_c, 16)
                sy.wait_ge(q_c, 16 * 7)
                sy.dma_start(out=lng_bc[:], in_=lng_dr[0:1, 0:1].partition_broadcast(128)
                             ).then_inc(q_c, 16)

                for i in range(NT + 3):
                    if i < NT:
                        p = i % 2
                        if i >= 2:
                            sy.wait_ge(s1a, i - 1)
                            sy.wait_ge(s1b, i - 1)
                            sy.wait_ge(v1s, i - 1)
                        sy.dma_start(out=A3f[:, X_F[p]:X_F[p] + 1024],
                                     in_=xsh[i * 128:(i + 1) * 128, :]).then_inc(q_x[p], 16)
                    if 1 <= i <= NT:
                        j = i - 1
                        pp = j % 2
                        sy.wait_ge(s1b, j + 1)
                        sy.dma_start_transpose(
                            A1b.rearrange("p (t c) -> p t c", c=4096
                                          )[:, :, j * 128:(j + 1) * 128],
                            A3[:, XN_O[pp]:XN_O[pp] + 1024].rearrange(
                                "p (t c) -> p t c", c=128)).then_inc(q_t1[pp], 16)
                        sy.dma_start(out=sp_xn[j], in_=A3[:, XN_O[pp]:XN_O[pp] + 1024]
                                     ).then_inc(q_xn[pp], 16)

                for c in range(NCH):
                    sy.wait_ge(v3q, 8 * (c + 1))
                    sy.dma_start(out=sp_q[c], in_=A3[:, P3Q:P3Q + 4096]).then_inc(q_3q, 16)
                    sy.wait_ge(s3k, 8 * (c + 1))
                    sy.dma_start(out=sp_k[c], in_=A3[:, P3K:P3K + 4096]).then_inc(q_3k, 16)
                    sy.wait_ge(s3h, 4 * (c + 1))
                    sy.dma_start(out=sp_kh[c], in_=A3[:, P3H:P3H + 4096]).then_inc(q_3h, 16)
                    sy.wait_ge(v3v, 8 * (c + 1))
                    sy.dma_start(out=sp_v[c], in_=A3[:, P3V:P3V + NS * VW]).then_inc(q_3v, 16)

                sy.wait_ge(q_3q, 16 * 8); sy.wait_ge(q_3k, 16 * 8)
                sy.wait_ge(q_3h, 16 * 8); sy.wait_ge(q_3v, 16 * 8)
                sy.wait_ge(q_xn[0], 16 * 16); sy.wait_ge(q_xn[1], 16 * 16)
                sy.wait_ge(t3h, 4 * NCH)
                sy.dma_start(out=xn4[:, 0:32768].rearrange("p (t c) -> p t c", c=1024),
                             in_=sp_xn.transpose([1, 0, 2])).then_inc(q_r4, 16)
                for sl in range(16):
                    p = sl % 2
                    sy.wait_ge(v4, sl + 1)
                    sy.dma_start(out=ar_in[sl * 128:(sl + 1) * 128, :],
                                 in_=A3f[:, 1024 * p:1024 * (p + 1)]).then_inc(q_4[p], 16)

                def _x5_load(c_):
                    for b_ in range(NS):
                        yidx = c_ * NS + b_
                        py = b_ % 2
                        if yidx >= 10:
                            sy.wait_ge(s5y, yidx - 9)
                        sy.dma_start(out=A2f[:, XS_F + 1024 * py:XS_F + 1024 * (py + 1)],
                                     in_=xsh[yidx * 128:(yidx + 1) * 128, :]
                                     ).then_inc(q_x5[py], 16)

                def _ya_store(c_):
                    for b_ in range(NS):
                        yidx = c_ * NS + b_
                        py = b_ % 2
                        sy.wait_ge(s5y, (yidx - 2 * NS) + 1)
                        sy.dma_start(out=sp_ya[yidx],
                                     in_=A1f[:, YA_O + 1024 * py:YA_O + 1024 * (py + 1)]
                                     ).then_inc(q_ya[py], 16)
                for c in range(NCH):
                    bc = CB_O[c % 2]
                    if c >= 2:
                        sy.wait_ge(t5y, NS * (c - 1))
                    sy.dma_start(out=A2[:, bc:bc + 4096], in_=sp_q[c]).then_inc(q_5[c % 2], 16)
                    sy.dma_start(out=A2[:, bc + 4096:bc + 8192],
                                 in_=sp_k[c]).then_inc(q_5[c % 2], 16)
                    sy.dma_start(out=A2[:, bc + 8192:bc + 12288],
                                 in_=sp_kh[c]).then_inc(q_5[c % 2], 16)
                    sy.dma_start(out=A2[:, bc + 12288:bc + 12288 + NS * VW],
                                 in_=sp_v[c]).then_inc(q_5[c % 2], 16)
                    if 2 <= c - 1:
                        _ya_store(c - 1)
                    if c >= 2:
                        _x5_load(c)
                    if c == 4:
                        sy.wait_ge(cc1, 1)
                        sy.dma_start(out=A2f[:, XO_F:XO_F + 2048].rearrange(
                            "p (t c) -> p t c", c=1024),
                            in_=rs_out.rearrange("(t p) c -> p t c", p=128)
                            ).then_inc(q_rs, 16)
                _ya_store(NCH - 1)
                sy.wait_ge(sv5u, 8 * NCH)
                sy.dma_start(out=st_in.transpose([1, 0, 2]),
                             in_=A3[:, ST_O[0]:ST_O[0] + DT * VW].rearrange(
                                 "p (t c) -> p t c", c=VW)).then_inc(q_st, 16)
                for cc in range(2):
                    sy.wait_ge(t5y, NS * NCH)
                    sy.dma_start(out=A2[:, CB_O[cc]:CB_O[cc] + 4096],
                                 in_=sp_q[cc]).then_inc(q_5[cc], 16)
                sy.wait_ge(cc2, 1)
                sy.dma_start(out=A1b[:, SINIT_O:SINIT_O + DT * VW].rearrange(
                    "p (t c) -> p t c", c=VW),
                    in_=st_out[0].transpose([1, 0, 2])).then_inc(q_si, 16)
                for n in range(NEL):
                    pe = n % 2
                    sy.wait_ge(v6e, 2 + 2 * (n + 1))
                    sy.dma_start(out=yt_in[n * 32:(n + 1) * 32, :],
                                 in_=A2[0:32, YTS_O + pe * D:YTS_O + (pe + 1) * D]
                                 ).then_inc(q_yt[pe], 16)
                for cc in range(2):
                    for b_ in range(NS):
                        idx = cc * NS + b_
                        pyc = idx % 2
                        if idx >= 1:
                            sy.wait_ge(s5y, NS * (NCH - 2) + idx)
                        sy.dma_start(out=A1f[:, XC_F:XC_F + 1024],
                                     in_=xsh[idx * 128:(idx + 1) * 128, :]
                                     ).then_inc(q_xc, 16)
                        sy.wait_ge(s5y, NS * (NCH - 2) + idx + 1)
                        sy.dma_start(out=sp_ya[idx],
                                     in_=A1f[:, YA_O + 1024 * pyc:YA_O + 1024 * (pyc + 1)]
                                     ).then_inc(q_ya[pyc], 16)
                sy.wait_ge(cc3, 1)
                sy.wait_ge(s5y, 32)
                sy.dma_start(out=A1b[:, 0:16384].rearrange("p (t c) -> p t c", c=1024),
                             in_=yt_out.rearrange("(t p) c -> p t c", p=128)
                             ).then_inc(q_ytl, 16)

                sy.wait_ge(q_st, 16)
                sy.wait_ge(t6, 10)
                def _out_store(j):
                    pj = j % 2
                    sy.wait_ge(v7x, j + 1)
                    sy.dma_start(out=out_d[j * 128:(j + 1) * 128, :],
                                 in_=A3f[:, OUT2_F + 1024 * pj:OUT2_F + 1024 * (pj + 1)]
                                 ).then_inc(q_out[pj], 16)
                for i in range(NT):
                    _out_store(i)
                sy.wait_ge(q_out[0], 16 * 16)
                sy.wait_ge(q_out[1], 16 * 16)

            @blk.gpsimd
            def _(g):
                g.dma_start(out=eye_b[:], in_=eye_d).then_inc(g_eye, 16)
                g.dma_start(out=bq_r, in_=bq).then_inc(g_bias, 16)
                g.dma_start(out=bk_r, in_=bk).then_inc(g_bias, 16)
                g.dma_start(out=bv_r, in_=bv).then_inc(g_bias, 16)
                g.dma_start(out=phib_r, in_=phib).then_inc(g_bias, 16)
                g.dma_start(out=A2[:, PHIT_O:PHIT_O + 16384].rearrange(
                    "p (t c) -> p t c", c=2048),
                    in_=phiwt_d.rearrange("(t p) c -> p t c", p=128)).then_inc(g_phi, 16)
                for wi, wsrc in enumerate((wqt, wkt, wvt)):
                    g.dma_start(out=A2[:, W3_O + wi * 8192:W3_O + (wi + 1) * 8192
                                       ].rearrange("p (t c) -> p t c", c=1024),
                                in_=wsrc.rearrange("(t p) c -> p t c", p=128)
                                ).then_inc(g_w, 16)

                for i in range(NT):
                    p = i % 2
                    g.wait_ge(gd, i + 1)
                    if i >= 2:
                        g.wait_ge(q_D[p], 16 * ((i - 2) // 2 + 1))
                    E3g = A3[:, E_O[p]:E_O[p] + 2048].rearrange("p (n q) -> p n q", q=32)
                    d3g = A3[:, DS_O[p]:DS_O[p] + 2048].rearrange("p (n q) -> p n q", q=32)
                    rDg = denD[:, 64 * p:64 * p + 64].unsqueeze(2).broadcast_to(
                        [128, 64, 32])
                    g.tensor_mul(d3g, E3g, rDg).then_inc(v2d, 1)

                g.wait_ge(q_D[0], 16 * 16)
                g.wait_ge(q_D[1], 16 * 16)
                g.wait_ge(t3h, 4 * NCH)
                for sl in range(16):
                    p = sl % 2
                    if sl >= 2:
                        g.wait_ge(t4, sl - 1)
                    src = sp_dsm.transpose([1, 0, 2])[:, :, sl * 128:(sl + 1) * 128]
                    g.dma_start(out=A2[:, DSM_O + 4096 * p:DSM_O + 4096 * (p + 1)
                                       ].rearrange("p (t c) -> p t c", c=128),
                                in_=src).then_inc(g_d4[p], 16)
                g.wait_ge(q_4[0], 16 * 8)
                g.wait_ge(q_4[1], 16 * 8)
                g.collective_compute("ReduceScatter", ALU.add,
                                     replica_groups=[list(range(8))],
                                     ins=[ar_in.opt()], outs=[rs_out.opt()]
                                     ).then_inc(cc1, 1)
                g.wait_ge(q_st, 16)
                g.collective_compute("AllGather", ALU.bypass,
                                     replica_groups=[[0, 1], [2, 3], [4, 5], [6, 7]],
                                     ins=[st_in.opt()], outs=[st_out.opt()]
                                     ).then_inc(cc2, 1)
                for n in range(NEL):
                    pe = n % 2
                    if n == 0:
                        g.wait_ge(t5y, NS * (NCH - 1))
                    elif n == 1:
                        g.wait_ge(t5y, NS * NCH)
                    else:
                        g.wait_ge(t6, 2 + (n - 2) + 1)
                    g.dma_start(out=A2[:, EW_O[pe]:EW_O[pe] + 8192].rearrange(
                        "p (t c) -> p t c", c=1024),
                        in_=ew[n].rearrange("(t p) c -> p t c", p=128)).then_inc(g_ew[pe], 16)
                    g.dma_start(out=ebrow[0:1, pe * D:(pe + 1) * D],
                                in_=eb[n:n + 1, :]).then_inc(g_ew[pe], 16)
                g.wait_ge(q_yt[0], 16 * 4)
                g.wait_ge(q_yt[1], 16 * 4)
                g.collective_compute("AllGather", ALU.bypass,
                                     replica_groups=[list(range(8))],
                                     ins=[yt_in.opt()], outs=[yt_out.opt()]
                                     ).then_inc(cc3, 1)
                g.wait_ge(t5y, NS * NCH)
                g.wait_ge(q_st, 16)

                for i in range(NT):
                    p = i % 2
                    if i >= 2:
                        g.wait_ge(q_t7[p], 32 * ((i - 2) // 2 + 1))
                    g.dma_start(out=A2[:, CSM_O2[p]:CSM_O2[p] + 2048],
                                in_=sp_E[i]).then_inc(g_7[p], 16)
                    if i >= 2:
                        g.wait_ge(v7x, i - 1)
                    if i < 8:
                        g.wait_ge(q_ya[i % 2], 16 * (12 + i // 2 + 1))
                    else:
                        g.wait_ge(q_ya[i % 2], 16 * ((i - 8) // 2 + 1))
                    g.dma_start(out=A3f[:, YAB2_F + 1024 * p:YAB2_F + 1024 * (p + 1)],
                                in_=sp_ya[i]).then_inc(q_y7[p], 16)

    return nc


def kernel(L0_attn_x_seq, L0_attn_Wq, L0_attn_bq, L0_attn_Wk, L0_attn_bk,
           L0_attn_Wv, L0_attn_bv, L0_attn_gamma, L0_ffn_phi_w, L0_ffn_phi_b,
           L0_ffn_expert_w, L0_ffn_expert_b):
    x = np.asarray(L0_attn_x_seq)
    nc = build_nc()
    eye = np.eye(128, dtype=np.float32)
    causal = np.ascontiguousarray(np.tril(np.ones((128, 128), dtype=np.float32)).T)
    iota_col = np.arange(128, dtype=np.float32).reshape(128, 1)
    iota_row = np.arange(1056, dtype=np.float32).reshape(1, 1056)
    wqt = np.ascontiguousarray(np.asarray(L0_attn_Wq).T)
    wkt = np.ascontiguousarray(np.asarray(L0_attn_Wk).T)
    wvt = np.ascontiguousarray(np.asarray(L0_attn_Wv).T)
    phiwt = np.ascontiguousarray(np.asarray(L0_ffn_phi_w).T)
    in_maps = []
    for c in range(8):
        b, h = c // 2, c % 2
        in_maps.append({
            "xsh": x[b][h * TOK:(h + 1) * TOK],
            "wqt": wqt, "wkt": wkt, "wvt": wvt,
            "bq": np.asarray(L0_attn_bq).reshape(1, D),
            "bk": np.asarray(L0_attn_bk).reshape(1, D),
            "bv": np.asarray(L0_attn_bv).reshape(1, D),
            "gam": np.asarray(L0_attn_gamma, dtype=np.float32).reshape(1, 1),
            "phiwt": phiwt,
            "phib": np.asarray(L0_ffn_phi_b).reshape(1, SLOT),
            "ew": np.ascontiguousarray(np.asarray(L0_ffn_expert_w)[c * NEL:(c + 1) * NEL]),
            "eb": np.ascontiguousarray(np.asarray(L0_ffn_expert_b)[c * NEL:(c + 1) * NEL]),
            "eye": eye, "causal": causal,
            "iota_col": iota_col, "iota_row": iota_row,
            "flag": np.array([[float(h)]], dtype=np.float32),
        })
    res = run_bass_kernel_spmd(nc, in_maps, core_ids=list(range(8)))
    full = np.empty((B, S, D), dtype=np.float32)
    for c in range(8):
        b, h = c // 2, c % 2
        full[b, h * TOK:(h + 1) * TOK] = res.results[c]["out"]
    return full


# version 10
# speedup vs baseline: 1.2266x; 1.0581x over previous
import numpy as np
from contextlib import ExitStack
import concourse.bass as bass
import concourse.mybir as mybir
from concourse.bass_utils import run_bass_kernel_spmd

F32 = mybir.dt.float32
BF16 = mybir.dt.bfloat16
AF = mybir.ActivationFunctionType
ALU = mybir.AluOpType
AX = mybir.AxisListType

B, S, D = 4, 8192, 1024
TOK = 4096
NT = 32
DT = 8
CH = 512
NCH = TOK // CH
NS = CH // 128
SLOT = 2048
NEL = 8
VW = 1026
NUMW = 1025
EPS_ATTN = 1e-6
EPS_LN = 1e-5
GSL = ((0, 512), (512, 1024), (1024, 1025))


def build_nc():
    nc = bass.Bass("TRN2", target_bir_lowering=False)

    def din(name, shape):
        return nc.dram_tensor(name, list(shape), F32, kind="ExternalInput").ap()

    xsh = din("xsh", [TOK, D])
    wqt = din("wqt", [D, D]); wkt = din("wkt", [D, D]); wvt = din("wvt", [D, D])
    bq = din("bq", [1, D]); bk = din("bk", [1, D]); bv = din("bv", [1, D])
    gam = din("gam", [1, 1])
    phiwt_d = din("phiwt", [D, SLOT]); phib = din("phib", [1, SLOT])
    ew = din("ew", [NEL, D, D]); eb = din("eb", [NEL, D])
    eye_d = din("eye", [128, 128])
    causal_d = din("causal", [128, 128])
    iotac_d = din("iota_col", [128, 1])
    iotar_d = din("iota_row", [1, 1056])
    flag_d = din("flag", [1, 1])
    out_d = nc.dram_tensor("out", [TOK, D], F32, kind="ExternalOutput").ap()

    sp_xn = nc.dram_tensor("sp_xn", [NT, 128, D], BF16).ap()
    sp_E = nc.dram_tensor("sp_E", [NT, 128, SLOT], BF16).ap()
    sp_dsm = nc.dram_tensor("sp_dsm", [NT, 128, SLOT], BF16).ap()
    sp_q = nc.dram_tensor("sp_q", [NCH, 128, DT * 512], BF16).ap()
    sp_k = nc.dram_tensor("sp_k", [NCH, 128, DT * 512], BF16).ap()
    sp_kh = nc.dram_tensor("sp_kh", [NCH, 128, NS * D], BF16).ap()
    sp_v = nc.dram_tensor("sp_v", [NCH, 128, NS * VW], BF16).ap()
    sp_ya = nc.dram_tensor("sp_ya", [NT, 128, D], F32).ap()
    lng_dr = nc.dram_tensor("lng_dr", [1, 1], F32).ap()
    ar_in = nc.dram_tensor("cc_ar_in", [SLOT, D], F32).ap()
    rs_out = nc.dram_tensor("cc_rs_out", [256, D], F32).ap()
    st_in = nc.dram_tensor("cc_st_in", [DT, 128, VW], BF16).ap()
    st_out2 = nc.dram_tensor("cc_st_out2", [DT, 128, VW], BF16).ap()
    yt_in = nc.dram_tensor("cc_yt_in", [256, D], BF16).ap()
    yt_out = nc.dram_tensor("cc_yt_out", [SLOT, D], BF16).ap()

    with ExitStack() as stack:
        def sbt(name, shape, dt_):
            return stack.enter_context(nc.sbuf_tensor(name, list(shape), dt_))
        eye_f = sbt("eye_f", [128, 128], F32)
        eye_b = sbt("eye_b", [128, 128], BF16)
        causal = sbt("sb_causal", [128, 128], F32)
        iotac = sbt("sb_iotac", [128, 1], F32)
        iotaI = sbt("iotaI", [128, 512], F32)
        gam_s = sbt("gam_s", [1, 1], F32)
        lng = sbt("lng", [1, 1], F32)
        lng_bc = sbt("lng_bc", [128, 1], F32)
        nlng_bc = sbt("nlng_bc", [128, 1], F32)
        fcol = sbt("fcol", [128, 1], F32)
        g512c = sbt("g512c", [128, 1], F32)
        ccs = sbt("ccs", [128, 2], F32)
        asc = sbt("asc", [128, 4], F32)
        ksc = sbt("ksc", [128, 4], F32)
        bcols = sbt("bcols", [128, 8], F32)
        gIbf = sbt("gIbf", [128, 128], BF16)
        gammaI = sbt("gammaI", [128, 512], F32)
        ones_r = sbt("ones_r", [1, 512], BF16)
        stat = sbt("stat", [128, 16], F32)
        negmax = sbt("negmax", [128, 2], F32)
        denD = sbt("denD", [128, 128], F32)
        denC = sbt("denC", [128, 64], F32)
        nfcol = sbt("nfcol", [128, 1], F32)
        att_sb = sbt("att_sb", [128, 2048], BF16)
        ycol = sbt("ycol", [128, 4], F32)
        A1 = sbt("A1", [128, 32768], BF16)
        A2 = sbt("A2", [128, 49664], BF16)
        A3 = sbt("A3", [128, 16416], BF16)
        PS1 = stack.enter_context(nc.psum_tensor("PS1", [128, 2048], F32))
        PS2 = stack.enter_context(nc.psum_tensor("PS2", [128, 2048], F32))

        A1b = A1[:]
        A1f = A1[:].bitcast(F32)
        A2f = A2[:].bitcast(F32)
        A3f = A3[:].bitcast(F32)

        bq_r = A2[0:1, 45056:46080]
        bk_r = A2[0:1, 46080:47104]
        bv_r = A2[0:1, 47104:48128]
        phib_r = A3[0:1, 14336:16384]
        ebrow = A2[0:1, 47120:49168]

        xnT = A1b
        xn4 = A1b
        SINIT_O = 16500
        YA_O = 12544
        W3_O = 0
        PHIT_O = 24576
        CB = 12288 + NS * VW
        CB_O = [0, CB]
        DSM_O = 40960
        EW_O = [4096, 20488]
        CSM_O2 = [12288, 14336]
        CST_O2 = [28688, 30736]
        YTL_O = [4096, 20488]
        P7ORD = list(range(8, 32)) + list(range(8))
        YAB2_F = 0
        OUT2_F = 2048
        XO_F = 16392
        SQ_F = 20480
        SQ6_F = 21512
        XS_F = 21512
        XC_F = 14592
        TANH_O = 36880
        XTT_O = 38928
        YTS_O = 40976
        CSM_O = [0, 2048]
        CST_O = [4096, 6144]
        YAB_F = 4096
        OUT_F = 6144
        X_F = [0, 1024]
        XN_O = [4096, 5120]
        E_O = [6144, 8192]
        DS_O = [10240, 12288]
        P3Q, P3K, P3H, P3V = 0, 4096, 8192, 12288
        ST_O = [0, DT * VW]

        def sm(name):
            return nc.alloc_semaphore(name)

        def sm2(name):
            return [nc.alloc_semaphore(name + "0"), nc.alloc_semaphore(name + "1")]

        q_c = sm("q_c")
        g_eye = sm("g_eye"); g_bias = sm("g_bias"); g_w = sm("g_w"); g_phi = sm("g_phi")
        s_pro = sm("s_pro"); v_pro = sm("v_pro")
        q_x = sm2("q_x"); s1a = sm("s1a"); v1s = sm("v1s"); s1b = sm("s1b")
        t1 = sm("t1"); v1e = sm("v1e"); q_xn = sm2("q_xn")
        t2 = sm("t2"); v2m = sm("v2m"); s2e = sm("s2e"); v2d = sm("v2d")
        q_E = sm2("q_E"); q_D = sm2("q_D")
        t3 = sm("t3"); t3h = sm("t3h"); v3q = sm("v3q"); s3k = sm("s3k")
        v3v = sm("v3v"); s3h = sm("s3h")
        q_3q = sm("q_3q"); q_3k = sm("q_3k"); q_3h = sm("q_3h"); q_3v = sm("q_3v")
        q_r4 = sm("q_r4"); g_d4 = sm2("g_d4"); t4 = sm("t4"); v4 = sm("v4")
        q_4 = sm2("q_4"); cc1 = sm("cc1")
        q_5 = sm2("q_5"); v5i = sm("v5i"); t5u = sm("t5u"); sv5u = sm("sv5u")
        t5a = sm("t5a"); v5a = sm("v5a"); t5y = sm("t5y"); v5y = sm("v5y")
        s5y = sm("s5y"); q_ya = sm2("q_ya"); q_st = sm("q_st"); cc2 = sm("cc2")
        q_si = sm("q_si"); v5c = sm("v5c"); t5c = sm("t5c"); v5n = sm("v5n")
        q_rs = sm("q_rs"); g_ew = sm2("g_ew"); v6 = sm("v6"); s6 = sm("s6")
        t6 = sm("t6"); v6e = sm("v6e"); q_yt = sm2("q_yt"); cc3 = sm("cc3")
        q_ytl = sm("q_ytl")
        g_7 = sm2("g_7"); q_7 = sm2("q_7"); v7c = sm("v7c"); t7t = sm("t7t")
        v7t = sm("v7t"); t7y = sm("t7y"); v7o = sm("v7o"); q_out = sm2("q_out")
        vd = sm("vd"); sd = sm("sd")
        q_t1 = sm2("q_t1"); gd = sm("gd")
        q_t7 = sm2("q_t7"); gd7 = sm("gd7"); v7a = sm("v7a"); q_y7 = sm2("q_y7"); v7x = sm("v7x")
        q_x5 = sm2("q_x5"); q_xc = sm("q_xc")

        def csm_ap(k):
            m = k % 4
            if m < 2:
                return A2[:, CSM_O2[m]:CSM_O2[m] + 2048]
            off = 8208 + (m - 2) * 2048
            return A3[:, off:off + 2048]

        def psb(which, lo, hi, bf=False):
            t_ = PS1 if which == 0 else PS2
            ap = t_[:, lo:hi]
            return ap.bitcast(BF16) if bf else ap

        with nc.Block() as blk:

            @blk.tensor
            def _(t):
                t.wait_ge(g_eye, 16)
                t.wait_ge(g_phi, 16)
                t.wait_ge(g_bias, 16 * 4)
                t.wait_ge(v_pro, 13)
                for i in range(NT):
                    p = i % 2
                    t.wait_ge(q_t1[p], 16 * (i // 2 + 1))
                    if i >= 2:
                        t.wait_ge(s2e, i - 1)
                    log = (PS1 if p == 0 else PS2)[:, 0:2048]
                    for step in range(DT):
                        lhs = xnT[:, step * 4096 + i * 128: step * 4096 + (i + 1) * 128]
                        for cs in range(4):
                            t.matmul(log[:, cs * 512:(cs + 1) * 512], lhs,
                                     A2[:, PHIT_O + step * 2048 + cs * 512:
                                        PHIT_O + step * 2048 + (cs + 1) * 512],
                                     start=(step == 0), stop=False)
                    for cs in range(4):
                        ins = t.matmul(log[:, cs * 512:(cs + 1) * 512], ones_r[0:1, 0:128],
                                       phib_r[0:1, cs * 512:(cs + 1) * 512],
                                       start=False, stop=True)
                        if cs == 3:
                            ins.then_inc(t2, 1)

                t.wait_ge(g_w, 16 * 12)
                t.wait_ge(s2e, NT)
                for c in range(NCH):
                    for j in range(16):
                        proj = 0 if j < 8 else 1
                        e = j % 8
                        pq = j % 2
                        if j >= 2:
                            jp = j - 2
                            if jp < 8:
                                t.wait_ge(v3q, c * 8 + jp + 1)
                            else:
                                t.wait_ge(s3k, c * 8 + (jp - 8) + 1)
                        elif c > 0:
                            t.wait_ge(s3k, (c - 1) * 8 + 6 + j + 1)
                        woff = W3_O + proj * 8192
                        brow = bq_r if proj == 0 else bk_r
                        ps = PS1[:, 512 * pq:512 * (pq + 1)]
                        for step in range(DT):
                            t.matmul(ps, A2[:, woff + step * 1024 + e * 128:
                                            woff + step * 1024 + (e + 1) * 128],
                                     xnT[:, step * 4096 + c * 512:
                                         step * 4096 + (c + 1) * 512],
                                     start=(step == 0), stop=False)
                        t.matmul(ps, brow[0:1, e * 128:(e + 1) * 128],
                                 ones_r[0:1, 0:512], start=False,
                                 stop=True).then_inc(t3, 1)
                    for g in range(8):
                        sbt_, cs = g // 2, g % 2
                        pv = g % 2
                        if g >= 2:
                            t.wait_ge(v3v, c * 8 + (g - 2) + 1)
                        elif c > 0:
                            t.wait_ge(v3v, (c - 1) * 8 + 6 + g + 1)
                        ps = PS1[:, 1024 + 512 * pv:1024 + 512 * (pv + 1)]
                        for step in range(DT):
                            t.matmul(ps, xnT[:, step * 4096 + c * 512 + sbt_ * 128:
                                             step * 4096 + c * 512 + (sbt_ + 1) * 128],
                                     A2[:, W3_O + 16384 + step * 1024 + cs * 512:
                                        W3_O + 16384 + step * 1024 + (cs + 1) * 512],
                                     start=(step == 0), stop=False)
                        t.matmul(ps, ones_r[0:1, 0:128],
                                 bv_r[0:1, cs * 512:(cs + 1) * 512],
                                 start=False, stop=True).then_inc(t3, 1)
                    t.wait_ge(s3k, 8 * (c + 1))
                    for sbt_ in range(NS):
                        ptk = sbt_ % 2
                        if c * 4 + sbt_ >= 2:
                            t.wait_ge(s3h, c * 4 + sbt_ - 1)
                        trb = psb(1, 512 * ptk, 512 * (ptk + 1), bf=True)
                        for dtl in range(DT):
                            ins = t.transpose(trb[:, dtl * 128:(dtl + 1) * 128],
                                              A3[:, P3K + dtl * 512 + sbt_ * 128:
                                                 P3K + dtl * 512 + (sbt_ + 1) * 128],
                                              eye_b[:])
                            if dtl == DT - 1:
                                ins.then_inc(t3h, 1)

                t.wait_ge(v3q, 64); t.wait_ge(s3k, 64); t.wait_ge(v3v, 64)
                t.wait_ge(q_r4, 16)
                for sl in range(16):
                    p = sl % 2
                    t.wait_ge(g_d4[p], 16 * (sl // 2 + 1))
                    if sl >= 2:
                        t.wait_ge(v4, sl - 1)
                    ps = PS1[:, 1024 * p:1024 * (p + 1)]
                    dsm_b = A2[:, DSM_O + 4096 * p:DSM_O + 4096 * (p + 1)]
                    for tt in range(NT):
                        for cs in range(2):
                            ins = t.matmul(ps[:, cs * 512:(cs + 1) * 512],
                                           dsm_b[:, tt * 128:(tt + 1) * 128],
                                           xn4[:, tt * 1024 + cs * 512:
                                               tt * 1024 + (cs + 1) * 512],
                                           start=(tt == 0), stop=(tt == NT - 1))
                            if tt == NT - 1 and cs == 1:
                                ins.then_inc(t4, 1)

                t.wait_ge(v4, 16)
                t.wait_ge(s3h, 32)
                t.wait_ge(v5i, 2)
                t.wait_ge(v_pro, 12)
                for c in range(NCH):
                    bc = CB_O[c % 2]
                    sp = ST_O[c % 2]
                    t.wait_ge(q_5[c % 2], 16 * 4 * (c // 2 + 1))
                    if c in (1, 2):
                        t.wait_ge(v5y, 4 * c)
                    elif c >= 3:
                        t.wait_ge(s5y, 4 * (c - 2))
                    for dk in range(DT):
                        pu = dk % 2
                        if c * 8 + dk >= 2:
                            t.wait_ge(sv5u, c * 8 + dk - 1)
                        ps = PS1 if pu == 0 else PS2
                        for lo, hi in GSL:
                            t.matmul(ps[:, lo:hi], gIbf[:],
                                     A3[:, sp + dk * VW + lo: sp + dk * VW + hi],
                                     start=True, stop=False)
                        for sbt_ in range(NS):
                            lhs = A2[:, bc + 8192 + sbt_ * 1024 + dk * 128:
                                     bc + 8192 + sbt_ * 1024 + (dk + 1) * 128]
                            for gi, (lo, hi) in enumerate(GSL):
                                ins = t.matmul(ps[:, lo:hi], lhs,
                                               A2[:, bc + 12288 + sbt_ * VW + lo:
                                                  bc + 12288 + sbt_ * VW + hi],
                                               start=False, stop=(sbt_ == NS - 1))
                                if sbt_ == NS - 1 and gi == 2:
                                    ins.then_inc(t5u, 1)
                    for a_ in range(NS):
                        pa = a_ % 2
                        if c * 4 + a_ >= 2:
                            t.wait_ge(v5a, c * 4 + a_ - 1)
                        ps = (PS1 if pa == 0 else PS2)[:, 1536:2048]
                        for step in range(DT):
                            ins = t.matmul(ps, A2[:, bc + 4096 + step * 512 + a_ * 128:
                                                  bc + 4096 + step * 512 + (a_ + 1) * 128],
                                           A2[:, bc + step * 512:bc + (step + 1) * 512],
                                           start=(step == 0), stop=(step == DT - 1))
                            if step == DT - 1:
                                ins.then_inc(t5a, 1)
                    t.wait_ge(sv5u, 8 * (c + 1))
                    for b_ in range(NS):
                        py = b_ % 2
                        yidx = c * NS + b_
                        if yidx >= 2:
                            prev = yidx - 2
                            if prev < 2 * NS:
                                t.wait_ge(v5y, prev + 1)
                            else:
                                t.wait_ge(s5y, prev - 2 * NS + 1)
                        ps = PS1 if py == 0 else PS2
                        for step in range(DT):
                            lhs = A2[:, bc + step * 512 + b_ * 128:
                                     bc + step * 512 + (b_ + 1) * 128]
                            for lo, hi in GSL:
                                t.matmul(ps[:, lo:hi], lhs,
                                         A3[:, sp + step * VW + lo: sp + step * VW + hi],
                                         start=(step == 0), stop=False)
                        for a_ in range(b_ + 1):
                            t.wait_ge(v5a, c * 4 + a_ + 1)
                            for gi, (lo, hi) in enumerate(GSL):
                                ins = t.matmul(ps[:, lo:hi],
                                               att_sb[:, a_ * 512 + b_ * 128:
                                                  a_ * 512 + (b_ + 1) * 128],
                                               A2[:, bc + 12288 + a_ * VW + lo:
                                                  bc + 12288 + a_ * VW + hi],
                                               start=False, stop=(a_ == b_))
                                if a_ == b_ and gi == 2:
                                    ins.then_inc(t5y, 1)

                t.wait_ge(t5y, NS * NCH)
                t.wait_ge(v5a, 4 * NCH)
                for i6 in range(2):
                    t.wait_ge(s6, 2 + i6 + 1)
                    trb = psb(i6, 1536, 2048, bf=True)
                    for dtl in range(DT):
                        ins = t.transpose(trb[:, dtl * 128:(dtl + 1) * 128],
                                          A2[:, TANH_O + i6 * 1024 + dtl * 128:
                                             TANH_O + i6 * 1024 + (dtl + 1) * 128],
                                          eye_b[:])
                        if dtl == DT - 1:
                            ins.then_inc(t6, 1)
                for n in range(NEL):
                    pe = n % 2
                    t.wait_ge(g_ew[pe], 16 * 2 * (n // 2 + 1))
                    t.wait_ge(v6e, 2)
                    if n >= 1:
                        t.wait_ge(v6e, 2 + 2 * n)
                    for cs in range(2):
                        ps = psb(cs, 1536, 2048)
                        for step in range(DT):
                            t.matmul(ps[0:32, :],
                                     A2[:, XTT_O + step * 256 + n * 32:
                                        XTT_O + step * 256 + n * 32 + 32],
                                     A2[:, EW_O[pe] + step * 1024 + cs * 512:
                                        EW_O[pe] + step * 1024 + (cs + 1) * 512],
                                     start=(step == 0), stop=False)
                        ins = t.matmul(ps[0:32, :],
                                       ones_r[0:1, 0:32],
                                       ebrow[0:1, pe * D + cs * 512:pe * D + (cs + 1) * 512],
                                       start=False, stop=True)
                        if cs == 1:
                            ins.then_inc(t6, 1)

                t.wait_ge(q_ytl, 32)
                t.wait_ge(v6e, 18)
                for k, i in enumerate(P7ORD):
                    p = k % 2
                    if 16 <= k < 24:
                        idx = k - 16
                        cc = idx // NS
                        b_ = idx % NS
                        if idx == 0:
                            t.wait_ge(q_si, 16)
                            t.wait_ge(v5c, 2)
                        else:
                            t.wait_ge(v5n, idx)
                        for step in range(DT):
                            lhs = A2[:, CB_O[cc] + step * 512 + b_ * 128:
                                     CB_O[cc] + step * 512 + (b_ + 1) * 128]
                            for gi, (lo, hi) in enumerate(GSL):
                                ins = t.matmul(PS2[:, lo:hi], lhs,
                                               A1b[:, SINIT_O + step * VW + lo:
                                                   SINIT_O + step * VW + hi],
                                               start=(step == 0), stop=(step == DT - 1))
                                if step == DT - 1 and gi == 2:
                                    ins.then_inc(t5c, 1)
                    t.wait_ge(q_t7[p], 32 * (k // 2 + 1))
                    if k >= 2:
                        t.wait_ge(v7x, k - 1)
                    ps = PS1[:, 1024 * p:1024 * (p + 1)]
                    for sl in range(16):
                        yoff = (YTL_O[0] + sl * 1024 if sl < 8
                                else YTL_O[1] + (sl - 8) * 1024)
                        for cs in range(2):
                            ins = t.matmul(ps[:, cs * 512:(cs + 1) * 512],
                                           A2[:, CST_O2[p] + sl * 128:
                                              CST_O2[p] + (sl + 1) * 128],
                                           A2[:, yoff + cs * 512:
                                              yoff + (cs + 1) * 512],
                                           start=(sl == 0), stop=(sl == 15))
                            if sl == 15 and cs == 1:
                                ins.then_inc(t7y, 1)

            @blk.vector
            def _(v):
                nv = [0]

                def V(inst):
                    inst.then_inc(vd, 1)
                    nv[0] += 1

                def VB():
                    v.wait_ge(vd, nv[0])

                v.wait_ge(q_c, 16 * 8)
                v.tensor_scalar_mul(nlng_bc[:], lng_bc[:], -1.0).then_inc(v_pro, 1)
                for a_ in range(4):
                    v.tensor_scalar_mul(bcols[:, a_:a_ + 1], lng_bc[:],
                                        float(-(128 * a_ + 1))).then_inc(v_pro, 1)
                for s_ in range(4):
                    v.tensor_scalar_mul(bcols[:, 4 + s_:5 + s_], lng_bc[:],
                                        float(511 - 128 * s_)).then_inc(v_pro, 1)
                v.wait_ge(s_pro, 3)
                v.tensor_mul(ccs[:, 1:2], fcol[:], g512c[:]).then_inc(v_pro, 1)
                v.tensor_copy(ccs[:, 0:1], fcol[:]).then_inc(v_pro, 1)
                v.tensor_scalar_mul(gIbf[:], eye_f[:], g512c[:]).then_inc(v_pro, 1)
                v.memset(ones_r[:], 1.0).then_inc(v_pro, 1)
                v.tensor_scalar_mul(nfcol[:], fcol[:], -1.0).then_inc(v_pro, 1)
                v.wait_ge(v_pro, 14)
                v.tensor_scalar_add(nfcol[:], nfcol[:], 1.0).then_inc(v_pro, 1)

                for i in range(NT + 2):
                    if i < NT:
                        p = i % 2
                        v.wait_ge(q_x[p], 16 * (i // 2 + 1))
                        xt = A3f[:, X_F[p]:X_F[p] + 1024]
                        V(v.reduce_sum(stat[:, p * 8:p * 8 + 1], xt, axis=AX.X))
                        v.wait_ge(s1a, i + 1)
                        VB()
                        V(v.tensor_scalar_mul(stat[:, p * 8:p * 8 + 1],
                                              stat[:, p * 8:p * 8 + 1], 1.0 / D))
                        V(v.tensor_scalar_mul(stat[:, p * 8 + 2:p * 8 + 3],
                                              stat[:, p * 8 + 1:p * 8 + 2], 1.0 / D))
                        VB()
                        V(v.tensor_mul(stat[:, p * 8 + 3:p * 8 + 4],
                                       stat[:, p * 8:p * 8 + 1],
                                       stat[:, p * 8:p * 8 + 1]))
                        VB()
                        V(v.tensor_sub(stat[:, p * 8 + 2:p * 8 + 3],
                                       stat[:, p * 8 + 2:p * 8 + 3],
                                       stat[:, p * 8 + 3:p * 8 + 4]))
                        VB()
                        V(v.tensor_scalar_add(stat[:, p * 8 + 2:p * 8 + 3],
                                              stat[:, p * 8 + 2:p * 8 + 3], EPS_LN))
                        VB()
                        v.reciprocal(stat[:, p * 8 + 3:p * 8 + 4],
                                     stat[:, p * 8 + 2:p * 8 + 3]).then_inc(v1s, 1)
                    if i >= 2:
                        j = i - 2
                        pj = j % 2
                        v.wait_ge(s2e, j + 1)
                        if j >= 2:
                            v.wait_ge(v2d, j - 1)
                        E3 = A3[:, E_O[pj]:E_O[pj] + 2048].rearrange(
                            "p (n q) -> p n q", q=32)
                        V(v.reduce_sum(denD[:, 64 * pj:64 * pj + 64], E3, axis=AX.X))
                        VB()
                        v.reciprocal(denD[:, 64 * pj:64 * pj + 64],
                                     denD[:, 64 * pj:64 * pj + 64]).then_inc(gd, 1)

                v.wait_ge(v2d, NT)
                v.wait_ge(q_D[0], 16 * 16)
                v.wait_ge(q_D[1], 16 * 16)
                v.wait_ge(s_pro, 11)
                for c in range(NCH):
                    for e in range(DT):
                        pq = e % 2
                        v.wait_ge(t3, c * 24 + e + 1)
                        if c >= 1 and e < 2:
                            v.wait_ge(q_3q, 16 * c)
                        ps = PS1[:, 512 * pq:512 * (pq + 1)]
                        v.scalar_tensor_tensor(
                            A3[:, P3Q + e * 512:P3Q + (e + 1) * 512], ps, 0.0,
                            gammaI[:], ALU.max, ALU.mult).then_inc(v3q, 1)
                    if c == 0:
                        v.wait_ge(q_D[0], 16 * 16)
                        v.wait_ge(q_D[1], 16 * 16)
                        v.wait_ge(q_E[0], 16 * 16)
                        v.wait_ge(q_E[1], 16 * 16)
                        for sbt_ in range(NS):
                            V(v.memset(A3[:, P3V + sbt_ * VW + 1024:
                                          P3V + sbt_ * VW + 1025], 1.0))
                            V(v.memset(A3[:, P3V + sbt_ * VW + 1025:
                                          P3V + sbt_ * VW + 1026], 0.0))
                    for g in range(8):
                        sbt_, cs = g // 2, g % 2
                        pv = g % 2
                        v.wait_ge(t3, c * 24 + 16 + g + 1)
                        if c >= 1 and g < 2:
                            v.wait_ge(q_3v, 16 * c)
                        ps = PS1[:, 1024 + 512 * pv:1024 + 512 * (pv + 1)]
                        v.tensor_copy(A3[:, P3V + sbt_ * VW + cs * 512:
                                         P3V + sbt_ * VW + cs * 512 + 512],
                                      ps).then_inc(v3v, 1)

                for sl in range(16):
                    p = sl % 2
                    v.wait_ge(t4, sl + 1)
                    if sl >= 2:
                        v.wait_ge(q_4[p], 16 * ((sl - 2) // 2 + 1))
                    ps = PS1[:, 1024 * p:1024 * (p + 1)]
                    v.tensor_copy(A3f[:, 1024 * p:1024 * (p + 1)], ps).then_inc(v4, 1)

                v.wait_ge(q_4[0], 16 * 8)
                v.wait_ge(q_4[1], 16 * 8)
                v.memset(A3[:, ST_O[0]:ST_O[0] + DT * VW], 0.0).then_inc(v5i, 1)
                v.memset(A3[:, ST_O[1]:ST_O[1] + DT * VW], 0.0).then_inc(v5i, 1)

                for c in range(NCH):
                    for a_ in range(NS):
                        v.wait_ge(t5a, c * 4 + a_ + 1)
                        if c >= 1:
                            v.wait_ge(t5y, 4 * c)
                        pa = a_ % 2
                        ps = (PS1 if pa == 0 else PS2)[:, 1536:2048]
                        row = att_sb[:, a_ * 512:(a_ + 1) * 512]
                        V(v.tensor_scalar_mul(row, ps, asc[:, a_:a_ + 1]))
                        VB()
                        v.scalar_tensor_tensor(
                            att_sb[:, a_ * 512 + a_ * 128:a_ * 512 + (a_ + 1) * 128],
                            ps[:, a_ * 128:(a_ + 1) * 128], asc[:, a_:a_ + 1],
                            causal[:], ALU.mult, ALU.mult).then_inc(v5a, 1)
                    for b_ in range(NS):
                        py = b_ % 2
                        yidx = c * NS + b_
                        v.wait_ge(t5y, yidx + 1)
                        ps = PS1 if py == 0 else PS2
                        if c < 2:
                            v.tensor_copy(A1f[:, yidx * NUMW:(yidx + 1) * NUMW],
                                          ps[:, 0:NUMW]).then_inc(v5y, 1)
                        else:
                            if yidx >= 10:
                                v.wait_ge(s5y, yidx - 9)
                            V(v.tensor_scalar_add(ycol[:, py:py + 1], ps[:, 1024:1025],
                                                  EPS_ATTN))
                            VB()
                            v.reciprocal(ycol[:, 2 + py:3 + py],
                                         ycol[:, py:py + 1]).then_inc(v5y, 1)
                            v.wait_ge(q_x5[py], 16 * ((yidx - 8) // 2 + 1))
                            v.wait_ge(v5y, yidx + 1)
                            if yidx >= 10:
                                v.wait_ge(q_ya[py], 16 * ((yidx - 10) // 2 + 1))
                            VB()
                            v.scalar_tensor_tensor(
                                A1f[:, YA_O + 1024 * py:YA_O + 1024 * (py + 1)],
                                ps[:, 0:1024], ycol[:, 2 + py:3 + py],
                                A2f[:, XS_F + 1024 * py:XS_F + 1024 * (py + 1)],
                                ALU.mult, ALU.add).then_inc(s5y, 1)

                v.wait_ge(q_rs, 16)
                for i6 in range(2):
                    xo = A2f[:, XO_F + 1024 * i6:XO_F + 1024 * (i6 + 1)]
                    v.tensor_scalar(xo, xo, -33000.0, 65000.0,
                                    ALU.max, ALU.min).then_inc(v6, 1)
                    v.wait_ge(v6, 2 * i6 + 1)
                    V(v.reduce_sum(stat[:, 8 * i6:8 * i6 + 1], xo, axis=AX.X))
                    v.wait_ge(s6, i6 + 1)
                    VB()
                    V(v.tensor_scalar_mul(stat[:, 8 * i6:8 * i6 + 1],
                                          stat[:, 8 * i6:8 * i6 + 1], 1.0 / D))
                    V(v.tensor_scalar_mul(stat[:, 8 * i6 + 2:8 * i6 + 3],
                                          stat[:, 8 * i6 + 1:8 * i6 + 2], 1.0 / D))
                    VB()
                    V(v.tensor_mul(stat[:, 8 * i6 + 3:8 * i6 + 4],
                                   stat[:, 8 * i6:8 * i6 + 1], stat[:, 8 * i6:8 * i6 + 1]))
                    VB()
                    V(v.tensor_sub(stat[:, 8 * i6 + 2:8 * i6 + 3],
                                   stat[:, 8 * i6 + 2:8 * i6 + 3],
                                   stat[:, 8 * i6 + 3:8 * i6 + 4]))
                    VB()
                    V(v.tensor_scalar_add(stat[:, 8 * i6 + 2:8 * i6 + 3],
                                          stat[:, 8 * i6 + 2:8 * i6 + 3], EPS_LN))
                    VB()
                    v.reciprocal(stat[:, 8 * i6 + 3:8 * i6 + 4],
                                 stat[:, 8 * i6 + 2:8 * i6 + 3]).then_inc(v6, 1)
                for i6 in range(2):
                    v.wait_ge(t6, i6 + 1)
                    trb = psb(i6, 1536, 2048, bf=True)
                    src_ = trb.rearrange("p (t c) -> p t c", c=128)
                    dst = A2[:, XTT_O:XTT_O + 2048].rearrange(
                        "p (t c) -> p t c", c=256)[:, :, i6 * 128:(i6 + 1) * 128]
                    v.tensor_copy(dst, src_).then_inc(v6e, 1)
                for n in range(NEL):
                    pe = n % 2
                    v.wait_ge(t6, 2 + n + 1)
                    if n >= 2:
                        v.wait_ge(q_yt[pe], 16 * ((n - 2) // 2 + 1))
                    for cs in range(2):
                        ps = psb(cs, 1536, 2048)[0:32, :]
                        v.tensor_copy(A2[0:32, YTS_O + pe * D + cs * 512:
                                         YTS_O + pe * D + (cs + 1) * 512],
                                      ps).then_inc(v6e, 1)

                for cc in range(2):
                    v.wait_ge(q_5[cc], 16 * 17)
                    v.wait_ge(v_pro, 11)
                    qb = A2[:, CB_O[cc]:CB_O[cc] + 4096]
                    v.tensor_scalar_mul(qb, qb, ccs[:, cc:cc + 1]).then_inc(v5c, 1)

                for k in range(NT + 2):
                    if k < NT:
                        p = k % 2
                        v.wait_ge(g_7[p], 16 * (k // 2 + 1))
                        E3 = csm_ap(k).rearrange("p (n q) -> p n q", q=32)
                        E3t = E3.transpose([0, 2, 1])
                        V(v.reduce_sum(denC[:, 32 * p:32 * p + 32], E3t, axis=AX.X))
                        VB()
                        V(v.reciprocal(denC[:, 32 * p:32 * p + 32],
                                       denC[:, 32 * p:32 * p + 32]))
                        rC = denC[:, 32 * p:32 * p + 32].unsqueeze(1).broadcast_to(
                            [128, 64, 32])
                        VB()
                        v.tensor_mul(E3, E3, rC).then_inc(v7c, 1)
                    if 16 <= k < 24:
                        idx = k - 16
                        pyc = idx % 2
                        v.wait_ge(t5c, idx + 1)
                        nh = A1f[:, idx * NUMW:(idx + 1) * NUMW]
                        V(v.tensor_add(nh, nh, PS2[:, 0:NUMW]))
                        VB()
                        V(v.tensor_scalar_add(ycol[:, pyc:pyc + 1], nh[:, 1024:1025],
                                              EPS_ATTN))
                        VB()
                        v.reciprocal(ycol[:, 2 + pyc:3 + pyc],
                                     ycol[:, pyc:pyc + 1]).then_inc(v5n, 1)
                        v.wait_ge(q_xc, 16 * (idx + 1))
                        v.wait_ge(v5n, idx + 1)
                        if idx < 2:
                            v.wait_ge(q_ya[pyc], 16 * 12)
                        else:
                            v.wait_ge(q_ya[pyc], 16 * (12 + (idx - 2) // 2 + 1))
                        VB()
                        v.scalar_tensor_tensor(
                            A1f[:, YA_O + 1024 * pyc:YA_O + 1024 * (pyc + 1)],
                            nh[:, 0:1024], ycol[:, 2 + pyc:3 + pyc],
                            A1f[:, XC_F:XC_F + 1024],
                            ALU.mult, ALU.add).then_inc(s5y, 1)
                    if k >= 2:
                        j = k - 2
                        pj = j % 2
                        v.wait_ge(t7y, j + 1)
                        v.wait_ge(q_y7[pj], 16 * (j // 2 + 1))
                        if j >= 2:
                            v.wait_ge(q_out[pj], 16 * ((j - 2) // 2 + 1))
                        ps = PS1[:, 1024 * pj:1024 * (pj + 1)]
                        ob = A3f[:, OUT2_F + 1024 * pj:OUT2_F + 1024 * (pj + 1)]
                        v.tensor_add(ob, ps,
                                     A3f[:, YAB2_F + 1024 * pj:
                                         YAB2_F + 1024 * (pj + 1)]
                                     ).then_inc(v7x, 1)

            @blk.scalar
            def _(s):
                ns = [0]

                def SC(inst):
                    inst.then_inc(sd, 1)
                    ns[0] += 1

                def SB():
                    s.wait_ge(sd, ns[0])

                s.dma_start(out=A3f[:, X_F[0]:X_F[0] + 1024],
                            in_=xsh[0:128, :]).then_inc(q_x[0], 16)
                s.dma_start(out=A3f[:, X_F[1]:X_F[1] + 1024],
                            in_=xsh[128:256, :]).then_inc(q_x[1], 16)
                s.wait_ge(q_c, 16 * 6)
                s.activation(lng[:], gam_s[:], AF.Ln).then_inc(s_pro, 1)
                s.wait_ge(q_c, 16 * 8)
                s.activation(gammaI[:], iotaI[:], AF.Exp,
                             scale=lng_bc[:]).then_inc(s_pro, 1)
                s.activation(g512c[:], lng_bc[:], AF.Exp, scale=512.0).then_inc(s_pro, 1)
                s.wait_ge(v_pro, 9)
                for a_ in range(4):
                    s.activation(asc[:, a_:a_ + 1], iotac[:], AF.Exp,
                                 scale=nlng_bc[:], bias=bcols[:, a_:a_ + 1]).then_inc(s_pro, 1)
                for s_ in range(4):
                    s.activation(ksc[:, s_:s_ + 1], iotac[:], AF.Exp,
                                 scale=nlng_bc[:], bias=bcols[:, 4 + s_:5 + s_]).then_inc(s_pro, 1)

                for i in range(NT + 3):
                    if 2 <= i < NT + 2:
                        j = i - 2
                        pj = j % 2
                        s.wait_ge(t2, j + 1)
                        if j >= 2:
                            s.wait_ge(v2d, j - 1)
                            s.wait_ge(q_E[pj], 16 * ((j - 2) // 2 + 1))
                        log = (PS1 if pj == 0 else PS2)[:, 0:2048]
                        s.activation(A3[:, E_O[pj]:E_O[pj] + 2048], log,
                                     AF.Exp).then_inc(s2e, 1)
                    if i >= 3:
                        j = i - 3
                        pj = j % 2
                        s.wait_ge(s2e, j + 1)
                        s.dma_start(out=sp_E[j], in_=A3[:, E_O[pj]:E_O[pj] + 2048]
                                    ).then_inc(q_E[pj], 16)
                    if i < NT:
                        p = i % 2
                        s.wait_ge(q_x[p], 16 * (i // 2 + 1))
                        xt = A3f[:, X_F[p]:X_F[p] + 1024]
                        s.activation(A2f[:, SQ_F + 1024 * p:SQ_F + 1024 * (p + 1)], xt,
                                     AF.Square,
                                     accum_out=stat[:, p * 8 + 1:p * 8 + 2]
                                     ).then_inc(s1a, 1)
                        s.wait_ge(v1s, i + 1)
                        SC(s.activation(stat[:, p * 8 + 4:p * 8 + 5],
                                        stat[:, p * 8 + 3:p * 8 + 4], AF.Sqrt))
                        SB()
                        SC(s.mul(stat[:, p * 8 + 5:p * 8 + 6], stat[:, p * 8:p * 8 + 1],
                                 stat[:, p * 8 + 4:p * 8 + 5]))
                        SB()
                        SC(s.mul(stat[:, p * 8 + 5:p * 8 + 6],
                                 stat[:, p * 8 + 5:p * 8 + 6], -1.0))
                        if i >= 2:
                            s.wait_ge(q_xn[p], 16 * ((i - 2) // 2 + 1))
                            s.wait_ge(q_t1[p], 16 * ((i - 2) // 2 + 1))
                        SB()
                        s.activation(A3[:, XN_O[p]:XN_O[p] + 1024], xt, AF.Identity,
                                     scale=stat[:, p * 8 + 4:p * 8 + 5],
                                     bias=stat[:, p * 8 + 5:p * 8 + 6]).then_inc(s1b, 1)
                        if i + 2 < NT:
                            s.wait_ge(s1b, i + 1)
                            s.dma_start(out=A3f[:, X_F[p]:X_F[p] + 1024],
                                        in_=xsh[(i + 2) * 128:(i + 3) * 128, :]
                                        ).then_inc(q_x[p], 16)

                s.wait_ge(q_E[0], 16 * 16)
                s.wait_ge(q_E[1], 16 * 16)
                s.wait_ge(q_xn[0], 16 * 16)
                s.wait_ge(q_xn[1], 16 * 16)
                s.wait_ge(v2d, NT)
                s.wait_ge(q_D[0], 16 * 16)
                s.wait_ge(q_D[1], 16 * 16)
                s.wait_ge(s_pro, 11)
                for c in range(NCH):
                    for e in range(DT):
                        pq = e % 2
                        s.wait_ge(t3, c * 24 + 8 + e + 1)
                        if c >= 1 and e < 2:
                            s.wait_ge(q_3k, 16 * c)
                        ps = PS1[:, 512 * pq:512 * (pq + 1)]
                        s.activation(A3[:, P3K + e * 512:P3K + (e + 1) * 512], ps,
                                     AF.Relu).then_inc(s3k, 1)
                    for sbt_ in range(NS):
                        ptk = sbt_ % 2
                        s.wait_ge(t3h, c * 4 + sbt_ + 1)
                        if c >= 1 and sbt_ < 2:
                            s.wait_ge(q_3h, 16 * c)
                        trb = psb(1, 512 * ptk, 512 * (ptk + 1), bf=True)
                        s.mul(A3[:, P3H + sbt_ * 1024:P3H + (sbt_ + 1) * 1024], trb,
                              ksc[:, sbt_:sbt_ + 1]).then_inc(s3h, 1)

                for c in range(NCH):
                    sq_ = ST_O[(c + 1) % 2]
                    if c == NCH - 1:
                        s.wait_ge(v_pro, 15)
                    for dk in range(DT):
                        pu = dk % 2
                        s.wait_ge(t5u, c * 8 + dk + 1)
                        ps = PS1 if pu == 0 else PS2
                        if c == NCH - 1:
                            s.activation(A3[:, sq_ + dk * VW:sq_ + dk * VW + NUMW],
                                         ps[:, 0:NUMW], AF.Identity,
                                         scale=nfcol[:]).then_inc(sv5u, 1)
                        else:
                            s.copy(A3[:, sq_ + dk * VW:sq_ + dk * VW + NUMW],
                                   ps[:, 0:NUMW]).then_inc(sv5u, 1)

                for i6 in range(2):
                    s.wait_ge(v6, 2 * i6 + 1)
                    xo = A2f[:, XO_F + 1024 * i6:XO_F + 1024 * (i6 + 1)]
                    s.activation(A2f[:, SQ6_F + 1024 * i6:SQ6_F + 1024 * (i6 + 1)], xo,
                                 AF.Square,
                                 accum_out=stat[:, 8 * i6 + 1:8 * i6 + 2]).then_inc(s6, 1)
                for i6 in range(2):
                    s.wait_ge(v6, 2 * i6 + 2)
                    xo = A2f[:, XO_F + 1024 * i6:XO_F + 1024 * (i6 + 1)]
                    SC(s.activation(stat[:, 8 * i6 + 4:8 * i6 + 5],
                                    stat[:, 8 * i6 + 3:8 * i6 + 4], AF.Sqrt))
                    SB()
                    SC(s.mul(stat[:, 8 * i6 + 5:8 * i6 + 6], stat[:, 8 * i6:8 * i6 + 1],
                             stat[:, 8 * i6 + 4:8 * i6 + 5]))
                    SB()
                    SC(s.mul(stat[:, 8 * i6 + 5:8 * i6 + 6],
                             stat[:, 8 * i6 + 5:8 * i6 + 6], -1.0))
                    SB()
                    s.activation(A2[:, TANH_O + i6 * 1024:TANH_O + (i6 + 1) * 1024], xo,
                                 AF.Tanh, scale=stat[:, 8 * i6 + 4:8 * i6 + 5],
                                 bias=stat[:, 8 * i6 + 5:8 * i6 + 6]).then_inc(s6, 1)


                s.wait_ge(cc3, 1)
                s.wait_ge(t6, 10)
                s.dma_start(out=A2[:, YTL_O[1]:YTL_O[1] + 8192].rearrange(
                    "p (t c) -> p t c", c=1024),
                    in_=yt_out.rearrange("(t p) c -> p t c", p=128)[:, 8:16, :]
                    ).then_inc(q_ytl, 16)
                s.wait_ge(t5y, NS * NCH)
                for k in range(NT):
                    p = k % 2
                    s.wait_ge(v7c, k + 1)
                    if k >= 2:
                        s.wait_ge(t7y, k - 1)
                    csm = csm_ap(k)
                    for half in range(2):
                        src_ = csm[:, half * 1024:(half + 1) * 1024
                                   ].rearrange("p (t c) -> p t c", c=128)
                        dst = A2[:, CST_O2[p] + half * 1024:CST_O2[p] + (half + 1) * 1024
                                 ].rearrange("p (t c) -> p t c", c=128)
                        s.dma_start_transpose(dst, src_).then_inc(q_t7[p], 16)


            @blk.sync
            def _(sy):
                sy.dma_start(out=eye_f[:], in_=eye_d).then_inc(q_c, 16)
                sy.dma_start(out=causal[:], in_=causal_d).then_inc(q_c, 16)
                sy.dma_start(out=iotac[:], in_=iotac_d).then_inc(q_c, 16)
                sy.dma_start(out=gam_s[:], in_=gam).then_inc(q_c, 16)
                sy.dma_start(out=fcol[:], in_=flag_d[0:1, 0:1].partition_broadcast(128)
                             ).then_inc(q_c, 16)
                sy.dma_start(out=iotaI[:], in_=iotar_d[0:1, 1:513].partition_broadcast(128)
                             ).then_inc(q_c, 16)
                sy.wait_ge(s_pro, 1)
                sy.dma_start(out=lng_dr, in_=lng[:]).then_inc(q_c, 16)
                sy.wait_ge(q_c, 16 * 7)
                sy.dma_start(out=lng_bc[:], in_=lng_dr[0:1, 0:1].partition_broadcast(128)
                             ).then_inc(q_c, 16)

                for j in range(NT):
                    pp = j % 2
                    sy.wait_ge(s1b, j + 1)
                    sy.dma_start_transpose(
                        A1b.rearrange("p (t c) -> p t c", c=4096
                                      )[:, :, j * 128:(j + 1) * 128],
                        A3[:, XN_O[pp]:XN_O[pp] + 1024].rearrange(
                            "p (t c) -> p t c", c=128)).then_inc(q_t1[pp], 16)
                    sy.dma_start(out=sp_xn[j], in_=A3[:, XN_O[pp]:XN_O[pp] + 1024]
                                 ).then_inc(q_xn[pp], 16)

                for c in range(NCH):
                    sy.wait_ge(v3q, 8 * (c + 1))
                    sy.dma_start(out=sp_q[c], in_=A3[:, P3Q:P3Q + 4096]).then_inc(q_3q, 16)
                    sy.wait_ge(s3k, 8 * (c + 1))
                    sy.dma_start(out=sp_k[c], in_=A3[:, P3K:P3K + 4096]).then_inc(q_3k, 16)
                    sy.wait_ge(s3h, 4 * (c + 1))
                    sy.dma_start(out=sp_kh[c], in_=A3[:, P3H:P3H + 4096]).then_inc(q_3h, 16)
                    sy.wait_ge(v3v, 8 * (c + 1))
                    sy.dma_start(out=sp_v[c], in_=A3[:, P3V:P3V + NS * VW]).then_inc(q_3v, 16)

                sy.wait_ge(q_3q, 16 * 8); sy.wait_ge(q_3k, 16 * 8)
                sy.wait_ge(q_3h, 16 * 8); sy.wait_ge(q_3v, 16 * 8)
                sy.wait_ge(q_xn[0], 16 * 16); sy.wait_ge(q_xn[1], 16 * 16)
                sy.wait_ge(t3h, 4 * NCH)
                sy.dma_start(out=xn4[:, 0:32768].rearrange("p (t c) -> p t c", c=1024),
                             in_=sp_xn.transpose([1, 0, 2])).then_inc(q_r4, 16)
                for sl in range(16):
                    p = sl % 2
                    sy.wait_ge(v4, sl + 1)
                    sy.dma_start(out=ar_in[sl * 128:(sl + 1) * 128, :],
                                 in_=A3f[:, 1024 * p:1024 * (p + 1)]).then_inc(q_4[p], 16)

                def _x5_load(c_):
                    for b_ in range(NS):
                        yidx = c_ * NS + b_
                        py = b_ % 2
                        if yidx >= 10:
                            sy.wait_ge(s5y, yidx - 9)
                        sy.dma_start(out=A2f[:, XS_F + 1024 * py:XS_F + 1024 * (py + 1)],
                                     in_=xsh[yidx * 128:(yidx + 1) * 128, :]
                                     ).then_inc(q_x5[py], 16)

                def _ya_store(c_):
                    for b_ in range(NS):
                        yidx = c_ * NS + b_
                        py = b_ % 2
                        sy.wait_ge(s5y, (yidx - 2 * NS) + 1)
                        sy.dma_start(out=sp_ya[yidx],
                                     in_=A1f[:, YA_O + 1024 * py:YA_O + 1024 * (py + 1)]
                                     ).then_inc(q_ya[py], 16)
                for c in range(NCH):
                    bc = CB_O[c % 2]
                    if c >= 2:
                        sy.wait_ge(t5y, NS * (c - 1))
                    sy.dma_start(out=A2[:, bc:bc + 4096], in_=sp_q[c]).then_inc(q_5[c % 2], 16)
                    sy.dma_start(out=A2[:, bc + 4096:bc + 8192],
                                 in_=sp_k[c]).then_inc(q_5[c % 2], 16)
                    sy.dma_start(out=A2[:, bc + 8192:bc + 12288],
                                 in_=sp_kh[c]).then_inc(q_5[c % 2], 16)
                    sy.dma_start(out=A2[:, bc + 12288:bc + 12288 + NS * VW],
                                 in_=sp_v[c]).then_inc(q_5[c % 2], 16)
                    if 2 <= c - 1:
                        _ya_store(c - 1)
                    if c >= 2:
                        _x5_load(c)
                    if c == 4:
                        sy.wait_ge(cc1, 1)
                        sy.dma_start(out=A2f[:, XO_F:XO_F + 2048].rearrange(
                            "p (t c) -> p t c", c=1024),
                            in_=rs_out.rearrange("(t p) c -> p t c", p=128)
                            ).then_inc(q_rs, 16)
                _ya_store(NCH - 1)
                sy.wait_ge(sv5u, 8 * NCH)
                sy.dma_start(out=st_in.transpose([1, 0, 2]),
                             in_=A3[:, ST_O[0]:ST_O[0] + DT * VW].rearrange(
                                 "p (t c) -> p t c", c=VW)).then_inc(q_st, 16)
                for cc in range(2):
                    sy.wait_ge(t5y, NS * NCH)
                    sy.dma_start(out=A2[:, CB_O[cc]:CB_O[cc] + 4096],
                                 in_=sp_q[cc]).then_inc(q_5[cc], 16)
                for n in range(NEL):
                    pe = n % 2
                    sy.wait_ge(v6e, 2 + 2 * (n + 1))
                    sy.dma_start(out=yt_in[n * 32:(n + 1) * 32, :],
                                 in_=A2[0:32, YTS_O + pe * D:YTS_O + (pe + 1) * D]
                                 ).then_inc(q_yt[pe], 16)
                sy.wait_ge(cc3, 1)
                sy.wait_ge(t6, 10)
                sy.dma_start(out=A2[:, YTL_O[0]:YTL_O[0] + 8192].rearrange(
                    "p (t c) -> p t c", c=1024),
                    in_=yt_out.rearrange("(t p) c -> p t c", p=128)[:, 0:8, :]
                    ).then_inc(q_ytl, 16)

                for k, i in enumerate(P7ORD):
                    pj = k % 2
                    if k == 12:
                        sy.wait_ge(cc2, 1)
                        sy.dma_start(out=A1b[:, SINIT_O:SINIT_O + DT * VW].rearrange(
                            "p (t c) -> p t c", c=VW),
                            in_=st_out2.transpose([1, 0, 2])).then_inc(q_si, 16)
                    if 14 <= k < 22:
                        idx = k - 14
                        if idx >= 1:
                            sy.wait_ge(s5y, NS * (NCH - 2) + idx)
                        sy.dma_start(out=A1f[:, XC_F:XC_F + 1024],
                                     in_=xsh[idx * 128:(idx + 1) * 128, :]
                                     ).then_inc(q_xc, 16)
                    if 15 <= k < 23:
                        idx = k - 15
                        pyc = idx % 2
                        sy.wait_ge(s5y, NS * (NCH - 2) + idx + 1)
                        sy.dma_start(out=sp_ya[idx],
                                     in_=A1f[:, YA_O + 1024 * pyc:YA_O + 1024 * (pyc + 1)]
                                     ).then_inc(q_ya[pyc], 16)
                    sy.wait_ge(v7x, k + 1)
                    sy.dma_start(out=out_d[i * 128:(i + 1) * 128, :],
                                 in_=A3f[:, OUT2_F + 1024 * pj:OUT2_F + 1024 * (pj + 1)]
                                 ).then_inc(q_out[pj], 16)
                sy.wait_ge(q_out[0], 16 * 16)
                sy.wait_ge(q_out[1], 16 * 16)

            @blk.gpsimd
            def _(g):
                g.dma_start(out=eye_b[:], in_=eye_d).then_inc(g_eye, 16)
                g.dma_start(out=bq_r, in_=bq).then_inc(g_bias, 16)
                g.dma_start(out=bk_r, in_=bk).then_inc(g_bias, 16)
                g.dma_start(out=bv_r, in_=bv).then_inc(g_bias, 16)
                g.dma_start(out=phib_r, in_=phib).then_inc(g_bias, 16)
                g.dma_start(out=A2[:, PHIT_O:PHIT_O + 16384].rearrange(
                    "p (t c) -> p t c", c=2048),
                    in_=phiwt_d.rearrange("(t p) c -> p t c", p=128)).then_inc(g_phi, 16)

                for i in range(NT):
                    p = i % 2
                    g.wait_ge(gd, i + 1)
                    if i >= 2:
                        g.wait_ge(q_D[p], 16 * ((i - 2) // 2 + 1))
                    E3g = A3[:, E_O[p]:E_O[p] + 2048].rearrange("p (n q) -> p n q", q=32)
                    d3g = A3[:, DS_O[p]:DS_O[p] + 2048].rearrange("p (n q) -> p n q", q=32)
                    rDg = denD[:, 64 * p:64 * p + 64].unsqueeze(2).broadcast_to(
                        [128, 64, 32])
                    g.tensor_mul(d3g, E3g, rDg).then_inc(v2d, 1)
                    g.wait_ge(v2d, i + 1)
                    g.dma_start(out=sp_dsm[i], in_=A3[:, DS_O[p]:DS_O[p] + 2048]
                                ).then_inc(q_D[p], 16)
                    if 4 <= i < 28 and i % 2 == 0:
                        pidx = (i - 4) // 2
                        wi, qq = pidx // 4, pidx % 4
                        wsrc = (wqt, wkt, wvt)[wi]
                        g.dma_start(out=A2[:, W3_O + wi * 8192 + qq * 2048:
                                           W3_O + wi * 8192 + (qq + 1) * 2048
                                           ].rearrange("p (t c) -> p t c", c=1024),
                                    in_=wsrc.rearrange("(t p) c -> p t c", p=128
                                                       )[:, qq * 2:(qq + 1) * 2, :]
                                    ).then_inc(g_w, 16)

                g.wait_ge(q_D[0], 16 * 16)
                g.wait_ge(q_D[1], 16 * 16)
                g.wait_ge(t3h, 4 * NCH)
                for sl in range(16):
                    p = sl % 2
                    if sl >= 2:
                        g.wait_ge(t4, sl - 1)
                    src = sp_dsm.transpose([1, 0, 2])[:, :, sl * 128:(sl + 1) * 128]
                    g.dma_start(out=A2[:, DSM_O + 4096 * p:DSM_O + 4096 * (p + 1)
                                       ].rearrange("p (t c) -> p t c", c=128),
                                in_=src).then_inc(g_d4[p], 16)
                g.wait_ge(q_4[0], 16 * 8)
                g.wait_ge(q_4[1], 16 * 8)
                g.collective_compute("ReduceScatter", ALU.add,
                                     replica_groups=[list(range(8))],
                                     ins=[ar_in.opt()], outs=[rs_out.opt()]
                                     ).then_inc(cc1, 1)
                for n in range(NEL):
                    pe = n % 2
                    if n == 0:
                        g.wait_ge(t5y, NS * (NCH - 1))
                    elif n == 1:
                        g.wait_ge(t5y, NS * NCH)
                    else:
                        g.wait_ge(t6, 2 + (n - 2) + 1)
                    g.dma_start(out=A2[:, EW_O[pe]:EW_O[pe] + 8192].rearrange(
                        "p (t c) -> p t c", c=1024),
                        in_=ew[n].rearrange("(t p) c -> p t c", p=128)).then_inc(g_ew[pe], 16)
                    g.dma_start(out=ebrow[0:1, pe * D:(pe + 1) * D],
                                in_=eb[n:n + 1, :]).then_inc(g_ew[pe], 16)
                g.wait_ge(q_yt[0], 16 * 4)
                g.wait_ge(q_yt[1], 16 * 4)
                g.collective_compute("AllGather", ALU.bypass,
                                     replica_groups=[list(range(8))],
                                     ins=[yt_in.opt()], outs=[yt_out.opt()]
                                     ).then_inc(cc3, 1)
                g.wait_ge(q_st, 16)
                g.collective_compute("AllReduce", ALU.add,
                                     replica_groups=[[0, 1], [2, 3], [4, 5], [6, 7]],
                                     ins=[st_in.opt()], outs=[st_out2.opt()]
                                     ).then_inc(cc2, 1)
                g.wait_ge(t5y, NS * NCH)
                g.wait_ge(q_st, 16)

                for k in range(NT + 2):
                    if k < NT:
                        p = k % 2
                        if k >= 2:
                            g.wait_ge(g_7[p], 16 * (k // 2))
                        if k >= 4:
                            g.wait_ge(q_t7[p], 32 * ((k - 4) // 2 + 1))
                        g.dma_start(out=csm_ap(k),
                                    in_=sp_E[P7ORD[k]]).then_inc(g_7[p], 16)
                    if k >= 2:
                        j = k - 2
                        pj = j % 2
                        i2 = P7ORD[j]
                        if j >= 2:
                            g.wait_ge(v7x, j - 1)
                        if i2 < 8:
                            g.wait_ge(q_ya[i2 % 2], 16 * (12 + i2 // 2 + 1))
                        else:
                            g.wait_ge(q_ya[i2 % 2], 16 * ((i2 - 8) // 2 + 1))
                        g.dma_start(out=A3f[:, YAB2_F + 1024 * pj:
                                            YAB2_F + 1024 * (pj + 1)],
                                    in_=sp_ya[i2]).then_inc(q_y7[pj], 16)

    return nc


def kernel(L0_attn_x_seq, L0_attn_Wq, L0_attn_bq, L0_attn_Wk, L0_attn_bk,
           L0_attn_Wv, L0_attn_bv, L0_attn_gamma, L0_ffn_phi_w, L0_ffn_phi_b,
           L0_ffn_expert_w, L0_ffn_expert_b):
    x = np.asarray(L0_attn_x_seq)
    nc = build_nc()
    eye = np.eye(128, dtype=np.float32)
    causal = np.ascontiguousarray(np.tril(np.ones((128, 128), dtype=np.float32)).T)
    iota_col = np.arange(128, dtype=np.float32).reshape(128, 1)
    iota_row = np.arange(1056, dtype=np.float32).reshape(1, 1056)
    wqt = np.ascontiguousarray(np.asarray(L0_attn_Wq).T)
    wkt = np.ascontiguousarray(np.asarray(L0_attn_Wk).T)
    wvt = np.ascontiguousarray(np.asarray(L0_attn_Wv).T)
    phiwt = np.ascontiguousarray(np.asarray(L0_ffn_phi_w).T)
    in_maps = []
    for c in range(8):
        b, h = c // 2, c % 2
        in_maps.append({
            "xsh": x[b][h * TOK:(h + 1) * TOK],
            "wqt": wqt, "wkt": wkt, "wvt": wvt,
            "bq": np.asarray(L0_attn_bq).reshape(1, D),
            "bk": np.asarray(L0_attn_bk).reshape(1, D),
            "bv": np.asarray(L0_attn_bv).reshape(1, D),
            "gam": np.asarray(L0_attn_gamma, dtype=np.float32).reshape(1, 1),
            "phiwt": phiwt,
            "phib": np.asarray(L0_ffn_phi_b).reshape(1, SLOT),
            "ew": np.ascontiguousarray(np.asarray(L0_ffn_expert_w)[c * NEL:(c + 1) * NEL]),
            "eb": np.ascontiguousarray(np.asarray(L0_ffn_expert_b)[c * NEL:(c + 1) * NEL]),
            "eye": eye, "causal": causal,
            "iota_col": iota_col, "iota_row": iota_row,
            "flag": np.array([[float(h)]], dtype=np.float32),
        })
    res = run_bass_kernel_spmd(nc, in_maps, core_ids=list(range(8)))
    full = np.empty((B, S, D), dtype=np.float32)
    for c in range(8):
        b, h = c // 2, c % 2
        full[b, h * TOK:(h + 1) * TOK] = res.results[c]["out"]
    return full


# version 13
# speedup vs baseline: 1.2311x; 1.0037x over previous
import numpy as np
from contextlib import ExitStack
import concourse.bass as bass
import concourse.mybir as mybir
from concourse.bass_utils import run_bass_kernel_spmd

F32 = mybir.dt.float32
BF16 = mybir.dt.bfloat16
AF = mybir.ActivationFunctionType
ALU = mybir.AluOpType
AX = mybir.AxisListType

B, S, D = 4, 8192, 1024
TOK = 4096
NT = 32
DT = 8
CH = 512
NCH = TOK // CH
NS = CH // 128
SLOT = 2048
NEL = 8
VW = 1026
NUMW = 1025
EPS_ATTN = 1e-6
EPS_LN = 1e-5
GSL = ((0, 512), (512, 1024), (1024, 1025))


def build_nc():
    nc = bass.Bass("TRN2", target_bir_lowering=False)

    def din(name, shape):
        return nc.dram_tensor(name, list(shape), F32, kind="ExternalInput").ap()

    xsh = din("xsh", [TOK, D])
    wqt = din("wqt", [D, D]); wkt = din("wkt", [D, D]); wvt = din("wvt", [D, D])
    bq = din("bq", [1, D]); bk = din("bk", [1, D]); bv = din("bv", [1, D])
    gam = din("gam", [1, 1])
    phiwt_d = din("phiwt", [D, SLOT]); phib = din("phib", [1, SLOT])
    ew = din("ew", [NEL, D, D]); eb = din("eb", [NEL, D])
    eye_d = din("eye", [128, 128])
    causal_d = din("causal", [128, 128])
    iotac_d = din("iota_col", [128, 1])
    iotar_d = din("iota_row", [1, 1056])
    flag_d = din("flag", [1, 1])
    out_d = nc.dram_tensor("out", [TOK, D], F32, kind="ExternalOutput").ap()

    sp_xn = nc.dram_tensor("sp_xn", [NT, 128, D], BF16).ap()
    sp_E = nc.dram_tensor("sp_E", [NT, 128, SLOT], BF16).ap()
    sp_dsm = nc.dram_tensor("sp_dsm", [NT, 128, SLOT], BF16).ap()
    sp_q = nc.dram_tensor("sp_q", [NCH, 128, DT * 512], BF16).ap()
    sp_k = nc.dram_tensor("sp_k", [NCH, 128, DT * 512], BF16).ap()
    sp_kh = nc.dram_tensor("sp_kh", [NCH, 128, NS * D], BF16).ap()
    sp_v = nc.dram_tensor("sp_v", [NCH, 128, NS * VW], BF16).ap()
    sp_ya = nc.dram_tensor("sp_ya", [NT, 128, D], F32).ap()
    lng_dr = nc.dram_tensor("lng_dr", [1, 1], F32).ap()
    ar_in = nc.dram_tensor("cc_ar_in", [SLOT, D], F32).ap()
    rs_out = nc.dram_tensor("cc_rs_out", [256, D], F32).ap()
    st_in = nc.dram_tensor("cc_st_in", [DT, 128, VW], BF16).ap()
    st_out2 = nc.dram_tensor("cc_st_out2", [DT, 128, VW], BF16).ap()
    yt_in = nc.dram_tensor("cc_yt_in", [256, D], BF16).ap()
    yt_out = nc.dram_tensor("cc_yt_out", [SLOT, D], BF16).ap()

    with ExitStack() as stack:
        def sbt(name, shape, dt_):
            return stack.enter_context(nc.sbuf_tensor(name, list(shape), dt_))
        eye_f = sbt("eye_f", [128, 128], F32)
        eye_b = sbt("eye_b", [128, 128], BF16)
        causal = sbt("sb_causal", [128, 128], F32)
        iotac = sbt("sb_iotac", [128, 1], F32)
        iotaI = sbt("iotaI", [128, 512], F32)
        gam_s = sbt("gam_s", [1, 1], F32)
        lng = sbt("lng", [1, 1], F32)
        lng_bc = sbt("lng_bc", [128, 1], F32)
        nlng_bc = sbt("nlng_bc", [128, 1], F32)
        fcol = sbt("fcol", [128, 1], F32)
        g512c = sbt("g512c", [128, 1], F32)
        ccs = sbt("ccs", [128, 2], F32)
        asc = sbt("asc", [128, 4], F32)
        ksc = sbt("ksc", [128, 4], F32)
        bcols = sbt("bcols", [128, 8], F32)
        gIbf = sbt("gIbf", [128, 128], BF16)
        gammaI = sbt("gammaI", [128, 512], F32)
        ones_r = sbt("ones_r", [1, 512], BF16)
        stat = sbt("stat", [128, 16], F32)
        negmax = sbt("negmax", [128, 2], F32)
        denD = sbt("denD", [128, 128], F32)
        denC = sbt("denC", [128, 64], F32)
        nfcol = sbt("nfcol", [128, 1], F32)
        att_sb = sbt("att_sb", [128, 2048], BF16)
        ycol = sbt("ycol", [128, 4], F32)
        A1 = sbt("A1", [128, 32768], BF16)
        A2 = sbt("A2", [128, 49664], BF16)
        A3 = sbt("A3", [128, 16416], BF16)
        PS1 = stack.enter_context(nc.psum_tensor("PS1", [128, 2048], F32))
        PS2 = stack.enter_context(nc.psum_tensor("PS2", [128, 2048], F32))

        A1b = A1[:]
        A1f = A1[:].bitcast(F32)
        A2f = A2[:].bitcast(F32)
        A3f = A3[:].bitcast(F32)

        bq_r = A2[0:1, 45056:46080]
        bk_r = A2[0:1, 46080:47104]
        bv_r = A2[0:1, 47104:48128]
        phib_r = A3[0:1, 14336:16384]
        ebrow = A2[0:1, 47120:49168]

        xnT = A1b
        xn4 = A1b
        SINIT_O = 16500
        YA_O = 12544
        W3_O = 0
        PHIT_O = 24576
        CB = 12288 + NS * VW
        CB_O = [0, CB]
        DSM_O = 40960
        EW_O = [4096, 20488]
        CSM_O2 = [12288, 14336]
        CST_O2 = [28688, 30736]
        YTL_O = [4096, 20488]
        P7ORD = list(range(8, 32)) + list(range(8))
        YAB2_F = 0
        OUT2_F = 2048
        XO_F = 16392
        SQ_F = 20480
        SQ6_F = 21512
        XS_F = 21512
        XC_F = 14592
        TANH_O = 36880
        XTT_O = 38928
        YTS_O = 40976
        CSM_O = [0, 2048]
        CST_O = [4096, 6144]
        YAB_F = 4096
        OUT_F = 6144
        X_F = [0, 1024]
        XN_O = [4096, 5120]
        E_O = [6144, 8192]
        DS_O = [10240, 12288]
        P3Q, P3K, P3H, P3V = 0, 4096, 8192, 12288
        ST_O = [0, DT * VW]

        def sm(name):
            return nc.alloc_semaphore(name)

        def sm2(name):
            return [nc.alloc_semaphore(name + "0"), nc.alloc_semaphore(name + "1")]

        q_c = sm("q_c")
        g_eye = sm("g_eye"); g_bias = sm("g_bias"); g_w = sm("g_w"); g_phi = sm("g_phi")
        s_pro = sm("s_pro"); v_pro = sm("v_pro")
        q_x = sm2("q_x"); s1a = sm("s1a"); v1s = sm("v1s"); s1b = sm("s1b")
        t1 = sm("t1"); v1e = sm("v1e"); q_xn = sm2("q_xn")
        t2 = sm("t2"); v2m = sm("v2m"); s2e = sm("s2e"); v2d = sm("v2d")
        q_E = sm2("q_E"); q_D = sm2("q_D")
        t3 = sm("t3"); t3h = sm("t3h"); v3q = sm("v3q"); s3k = sm("s3k")
        v3v = sm("v3v"); s3h = sm("s3h")
        q_3q = sm("q_3q"); q_3k = sm("q_3k"); q_3h = sm("q_3h"); q_3v = sm("q_3v")
        q_r4 = sm("q_r4"); g_d4 = sm2("g_d4"); t4 = sm("t4"); v4 = sm("v4")
        q_4 = sm2("q_4"); cc1 = sm("cc1")
        q_5 = sm2("q_5"); v5i = sm("v5i"); t5u = sm("t5u"); sv5u = sm("sv5u")
        t5a = sm("t5a"); v5a = sm("v5a"); t5y = sm("t5y"); v5y = sm("v5y")
        s5y = sm("s5y"); q_ya = sm2("q_ya"); q_st = sm("q_st"); cc2 = sm("cc2")
        q_si = sm("q_si"); v5c = sm("v5c"); t5c = sm("t5c"); v5n = sm("v5n")
        q_rs = sm("q_rs"); g_ew = sm2("g_ew"); v6 = sm("v6"); s6 = sm("s6")
        t6 = sm("t6"); v6e = sm("v6e"); q_yt = sm2("q_yt"); cc3 = sm("cc3")
        q_ytl = sm("q_ytl")
        g_7 = sm2("g_7"); q_7 = sm2("q_7"); v7c = sm("v7c"); t7t = sm("t7t")
        v7t = sm("v7t"); t7y = sm("t7y"); v7o = sm("v7o"); q_out = sm2("q_out")
        vd = sm("vd"); sd = sm("sd")
        q_t1 = sm2("q_t1"); gd = sm("gd")
        q_t7 = sm2("q_t7"); gd7 = sm("gd7"); v7a = sm("v7a"); q_y7 = sm2("q_y7"); v7x = sm("v7x")
        q_x5 = sm2("q_x5"); q_xc = sm("q_xc")

        def csm_ap(k):
            m = k % 4
            if m < 2:
                return A2[:, CSM_O2[m]:CSM_O2[m] + 2048]
            off = 8208 + (m - 2) * 2048
            return A3[:, off:off + 2048]

        def psb(which, lo, hi, bf=False):
            t_ = PS1 if which == 0 else PS2
            ap = t_[:, lo:hi]
            return ap.bitcast(BF16) if bf else ap

        with nc.Block() as blk:

            @blk.tensor
            def _(t):
                t.wait_ge(g_eye, 16)
                t.wait_ge(g_phi, 16)
                t.wait_ge(g_bias, 16 * 4)
                t.wait_ge(v_pro, 13)
                for i in range(NT):
                    p = i % 2
                    t.wait_ge(q_t1[p], 16 * (i // 2 + 1))
                    if i >= 2:
                        t.wait_ge(s2e, i - 1)
                    log = (PS1 if p == 0 else PS2)[:, 0:2048]
                    for step in range(DT):
                        lhs = xnT[:, step * 4096 + i * 128: step * 4096 + (i + 1) * 128]
                        for cs in range(4):
                            t.matmul(log[:, cs * 512:(cs + 1) * 512], lhs,
                                     A2[:, PHIT_O + step * 2048 + cs * 512:
                                        PHIT_O + step * 2048 + (cs + 1) * 512],
                                     start=(step == 0), stop=False)
                    for cs in range(4):
                        ins = t.matmul(log[:, cs * 512:(cs + 1) * 512], ones_r[0:1, 0:128],
                                       phib_r[0:1, cs * 512:(cs + 1) * 512],
                                       start=False, stop=True)
                        if cs == 3:
                            ins.then_inc(t2, 1)

                t.wait_ge(g_w, 16 * 12)
                t.wait_ge(s2e, NT)
                for c in range(NCH):
                    for j in range(16):
                        proj = 0 if j < 8 else 1
                        e = j % 8
                        pq = j % 2
                        if j >= 2:
                            jp = j - 2
                            if jp < 8:
                                t.wait_ge(v3q, c * 8 + jp + 1)
                            else:
                                t.wait_ge(s3k, c * 8 + (jp - 8) + 1)
                        elif c > 0:
                            t.wait_ge(s3k, (c - 1) * 8 + 6 + j + 1)
                        woff = W3_O + proj * 8192
                        brow = bq_r if proj == 0 else bk_r
                        ps = PS1[:, 512 * pq:512 * (pq + 1)]
                        for step in range(DT):
                            t.matmul(ps, A2[:, woff + step * 1024 + e * 128:
                                            woff + step * 1024 + (e + 1) * 128],
                                     xnT[:, step * 4096 + c * 512:
                                         step * 4096 + (c + 1) * 512],
                                     start=(step == 0), stop=False)
                        t.matmul(ps, brow[0:1, e * 128:(e + 1) * 128],
                                 ones_r[0:1, 0:512], start=False,
                                 stop=True).then_inc(t3, 1)
                    for g in range(8):
                        sbt_, cs = g // 2, g % 2
                        pv = g % 2
                        if g >= 2:
                            t.wait_ge(v3v, c * 8 + (g - 2) + 1)
                        elif c > 0:
                            t.wait_ge(v3v, (c - 1) * 8 + 6 + g + 1)
                        ps = PS1[:, 1024 + 512 * pv:1024 + 512 * (pv + 1)]
                        for step in range(DT):
                            t.matmul(ps, xnT[:, step * 4096 + c * 512 + sbt_ * 128:
                                             step * 4096 + c * 512 + (sbt_ + 1) * 128],
                                     A2[:, W3_O + 16384 + step * 1024 + cs * 512:
                                        W3_O + 16384 + step * 1024 + (cs + 1) * 512],
                                     start=(step == 0), stop=False)
                        t.matmul(ps, ones_r[0:1, 0:128],
                                 bv_r[0:1, cs * 512:(cs + 1) * 512],
                                 start=False, stop=True).then_inc(t3, 1)
                    t.wait_ge(s3k, 8 * (c + 1))
                    for sbt_ in range(NS):
                        ptk = sbt_ % 2
                        if c * 4 + sbt_ >= 2:
                            t.wait_ge(s3h, c * 4 + sbt_ - 1)
                        trb = psb(1, 512 * ptk, 512 * (ptk + 1), bf=True)
                        for dtl in range(DT):
                            ins = t.transpose(trb[:, dtl * 128:(dtl + 1) * 128],
                                              A3[:, P3K + dtl * 512 + sbt_ * 128:
                                                 P3K + dtl * 512 + (sbt_ + 1) * 128],
                                              eye_b[:])
                            if dtl == DT - 1:
                                ins.then_inc(t3h, 1)

                t.wait_ge(v3q, 64); t.wait_ge(s3k, 64); t.wait_ge(v3v, 64)
                t.wait_ge(q_r4, 16)
                for sl in range(16):
                    p = sl % 2
                    t.wait_ge(g_d4[p], 16 * (sl // 2 + 1))
                    if sl >= 2:
                        t.wait_ge(v4, sl - 1)
                    ps = PS1[:, 1024 * p:1024 * (p + 1)]
                    dsm_b = A2[:, DSM_O + 4096 * p:DSM_O + 4096 * (p + 1)]
                    for tt in range(NT):
                        for cs in range(2):
                            ins = t.matmul(ps[:, cs * 512:(cs + 1) * 512],
                                           dsm_b[:, tt * 128:(tt + 1) * 128],
                                           xn4[:, tt * 1024 + cs * 512:
                                               tt * 1024 + (cs + 1) * 512],
                                           start=(tt == 0), stop=(tt == NT - 1))
                            if tt == NT - 1 and cs == 1:
                                ins.then_inc(t4, 1)

                t.wait_ge(v4, 16)
                t.wait_ge(s3h, 32)
                t.wait_ge(v5i, 2)
                t.wait_ge(v_pro, 12)
                for c in range(NCH):
                    bc = CB_O[c % 2]
                    sp = ST_O[c % 2]
                    t.wait_ge(q_5[c % 2], 16 * 4 * (c // 2 + 1))
                    if c in (1, 2):
                        t.wait_ge(v5y, 4 * c)
                    elif c >= 3:
                        t.wait_ge(s5y, 4 * (c - 2))
                    for dk in range(DT):
                        pu = dk % 2
                        if c * 8 + dk >= 2:
                            t.wait_ge(sv5u, c * 8 + dk - 1)
                        ps = PS1 if pu == 0 else PS2
                        for lo, hi in GSL:
                            t.matmul(ps[:, lo:hi], gIbf[:],
                                     A3[:, sp + dk * VW + lo: sp + dk * VW + hi],
                                     start=True, stop=False)
                        for sbt_ in range(NS):
                            lhs = A2[:, bc + 8192 + sbt_ * 1024 + dk * 128:
                                     bc + 8192 + sbt_ * 1024 + (dk + 1) * 128]
                            for gi, (lo, hi) in enumerate(GSL):
                                ins = t.matmul(ps[:, lo:hi], lhs,
                                               A2[:, bc + 12288 + sbt_ * VW + lo:
                                                  bc + 12288 + sbt_ * VW + hi],
                                               start=False, stop=(sbt_ == NS - 1))
                                if sbt_ == NS - 1 and gi == 2:
                                    ins.then_inc(t5u, 1)
                    for a_ in range(NS):
                        pa = a_ % 2
                        if c * 4 + a_ >= 2:
                            t.wait_ge(v5a, c * 4 + a_ - 1)
                        ps = (PS1 if pa == 0 else PS2)[:, 1536:2048]
                        for step in range(DT):
                            ins = t.matmul(ps, A2[:, bc + 4096 + step * 512 + a_ * 128:
                                                  bc + 4096 + step * 512 + (a_ + 1) * 128],
                                           A2[:, bc + step * 512:bc + (step + 1) * 512],
                                           start=(step == 0), stop=(step == DT - 1))
                            if step == DT - 1:
                                ins.then_inc(t5a, 1)
                    t.wait_ge(sv5u, 8 * (c + 1))
                    for b_ in range(NS):
                        py = b_ % 2
                        yidx = c * NS + b_
                        if yidx >= 2:
                            prev = yidx - 2
                            if prev < 2 * NS:
                                t.wait_ge(v5y, prev + 1)
                            else:
                                t.wait_ge(s5y, prev - 2 * NS + 1)
                        ps = PS1 if py == 0 else PS2
                        for step in range(DT):
                            lhs = A2[:, bc + step * 512 + b_ * 128:
                                     bc + step * 512 + (b_ + 1) * 128]
                            for lo, hi in GSL:
                                t.matmul(ps[:, lo:hi], lhs,
                                         A3[:, sp + step * VW + lo: sp + step * VW + hi],
                                         start=(step == 0), stop=False)
                        for a_ in range(b_ + 1):
                            t.wait_ge(v5a, c * 4 + a_ + 1)
                            for gi, (lo, hi) in enumerate(GSL):
                                ins = t.matmul(ps[:, lo:hi],
                                               att_sb[:, a_ * 512 + b_ * 128:
                                                  a_ * 512 + (b_ + 1) * 128],
                                               A2[:, bc + 12288 + a_ * VW + lo:
                                                  bc + 12288 + a_ * VW + hi],
                                               start=False, stop=(a_ == b_))
                                if a_ == b_ and gi == 2:
                                    ins.then_inc(t5y, 1)

                t.wait_ge(t5y, NS * NCH)
                t.wait_ge(v5a, 4 * NCH)
                for i6 in range(2):
                    t.wait_ge(s6, 2 + i6 + 1)
                    trb = psb(i6, 1536, 2048, bf=True)
                    for dtl in range(DT):
                        ins = t.transpose(trb[:, dtl * 128:(dtl + 1) * 128],
                                          A2[:, TANH_O + i6 * 1024 + dtl * 128:
                                             TANH_O + i6 * 1024 + (dtl + 1) * 128],
                                          eye_b[:])
                        if dtl == DT - 1:
                            ins.then_inc(t6, 1)
                for n in range(NEL):
                    pe = n % 2
                    t.wait_ge(g_ew[pe], 16 * 2 * (n // 2 + 1))
                    t.wait_ge(v6e, 2)
                    if n >= 1:
                        t.wait_ge(v6e, 2 + 2 * n)
                    for cs in range(2):
                        ps = psb(cs, 1536, 2048)
                        for step in range(DT):
                            t.matmul(ps[0:32, :],
                                     A2[:, XTT_O + step * 256 + n * 32:
                                        XTT_O + step * 256 + n * 32 + 32],
                                     A2[:, EW_O[pe] + step * 1024 + cs * 512:
                                        EW_O[pe] + step * 1024 + (cs + 1) * 512],
                                     start=(step == 0), stop=False)
                        ins = t.matmul(ps[0:32, :],
                                       ones_r[0:1, 0:32],
                                       ebrow[0:1, pe * D + cs * 512:pe * D + (cs + 1) * 512],
                                       start=False, stop=True)
                        if cs == 1:
                            ins.then_inc(t6, 1)

                t.wait_ge(q_ytl, 32)
                t.wait_ge(v6e, 18)
                for k, i in enumerate(P7ORD):
                    p = k % 2
                    if 16 <= k < 24:
                        idx = k - 16
                        cc = idx // NS
                        b_ = idx % NS
                        if idx == 0:
                            t.wait_ge(q_si, 16)
                            t.wait_ge(v5c, 2)
                        else:
                            t.wait_ge(v5n, idx)
                        for step in range(DT):
                            lhs = A2[:, CB_O[cc] + step * 512 + b_ * 128:
                                     CB_O[cc] + step * 512 + (b_ + 1) * 128]
                            for gi, (lo, hi) in enumerate(GSL):
                                ins = t.matmul(PS2[:, lo:hi], lhs,
                                               A1b[:, SINIT_O + step * VW + lo:
                                                   SINIT_O + step * VW + hi],
                                               start=(step == 0), stop=(step == DT - 1))
                                if step == DT - 1 and gi == 2:
                                    ins.then_inc(t5c, 1)
                    t.wait_ge(q_t7[p], 32 * (k // 2 + 1))
                    if k >= 2:
                        t.wait_ge(v7x, k - 1)
                    ps = PS1[:, 1024 * p:1024 * (p + 1)]
                    for sl in range(16):
                        yoff = (YTL_O[0] + sl * 1024 if sl < 8
                                else YTL_O[1] + (sl - 8) * 1024)
                        for cs in range(2):
                            ins = t.matmul(ps[:, cs * 512:(cs + 1) * 512],
                                           A2[:, CST_O2[p] + sl * 128:
                                              CST_O2[p] + (sl + 1) * 128],
                                           A2[:, yoff + cs * 512:
                                              yoff + (cs + 1) * 512],
                                           start=(sl == 0), stop=(sl == 15))
                            if sl == 15 and cs == 1:
                                ins.then_inc(t7y, 1)

            @blk.vector
            def _(v):
                nv = [0]

                def V(inst):
                    inst.then_inc(vd, 1)
                    nv[0] += 1

                def VB():
                    v.wait_ge(vd, nv[0])

                v.wait_ge(q_c, 16 * 8)
                v.tensor_scalar_mul(nlng_bc[:], lng_bc[:], -1.0).then_inc(v_pro, 1)
                for a_ in range(4):
                    v.tensor_scalar_mul(bcols[:, a_:a_ + 1], lng_bc[:],
                                        float(-(128 * a_ + 1))).then_inc(v_pro, 1)
                for s_ in range(4):
                    v.tensor_scalar_mul(bcols[:, 4 + s_:5 + s_], lng_bc[:],
                                        float(511 - 128 * s_)).then_inc(v_pro, 1)
                v.wait_ge(s_pro, 3)
                v.tensor_mul(ccs[:, 1:2], fcol[:], g512c[:]).then_inc(v_pro, 1)
                v.tensor_copy(ccs[:, 0:1], fcol[:]).then_inc(v_pro, 1)
                v.tensor_scalar_mul(gIbf[:], eye_f[:], g512c[:]).then_inc(v_pro, 1)
                v.memset(ones_r[:], 1.0).then_inc(v_pro, 1)
                v.tensor_scalar_mul(nfcol[:], fcol[:], -1.0).then_inc(v_pro, 1)
                v.wait_ge(v_pro, 14)
                v.tensor_scalar_add(nfcol[:], nfcol[:], 1.0).then_inc(v_pro, 1)

                for i in range(NT + 2):
                    if i < NT:
                        p = i % 2
                        v.wait_ge(q_x[p], 16 * (i // 2 + 1))
                        xt = A3f[:, X_F[p]:X_F[p] + 1024]
                        V(v.reduce_sum(stat[:, p * 8:p * 8 + 1], xt, axis=AX.X))
                        v.wait_ge(s1a, i + 1)
                        VB()
                        V(v.tensor_scalar_mul(stat[:, p * 8:p * 8 + 1],
                                              stat[:, p * 8:p * 8 + 1], -1.0 / D))
                        V(v.tensor_scalar_mul(stat[:, p * 8 + 2:p * 8 + 3],
                                              stat[:, p * 8 + 1:p * 8 + 2], 1.0 / D))
                        VB()
                        V(v.tensor_mul(stat[:, p * 8 + 3:p * 8 + 4],
                                       stat[:, p * 8:p * 8 + 1],
                                       stat[:, p * 8:p * 8 + 1]))
                        VB()
                        V(v.tensor_sub(stat[:, p * 8 + 2:p * 8 + 3],
                                       stat[:, p * 8 + 2:p * 8 + 3],
                                       stat[:, p * 8 + 3:p * 8 + 4]))
                        VB()
                        v.tensor_scalar_add(stat[:, p * 8 + 2:p * 8 + 3],
                                            stat[:, p * 8 + 2:p * 8 + 3],
                                            EPS_LN).then_inc(v1s, 1)
                    if i >= 2:
                        j = i - 2
                        pj = j % 2
                        v.wait_ge(s2e, j + 1)
                        if j >= 2:
                            v.wait_ge(v2d, j - 1)
                        E3 = A3[:, E_O[pj]:E_O[pj] + 2048].rearrange(
                            "p (n q) -> p n q", q=32)
                        V(v.reduce_sum(denD[:, 64 * pj:64 * pj + 64], E3, axis=AX.X))
                        VB()
                        v.reciprocal(denD[:, 64 * pj:64 * pj + 64],
                                     denD[:, 64 * pj:64 * pj + 64]).then_inc(gd, 1)

                v.wait_ge(v2d, NT)
                v.wait_ge(q_D[0], 16 * 16)
                v.wait_ge(q_D[1], 16 * 16)
                v.wait_ge(s_pro, 11)
                for c in range(NCH):
                    for e in range(DT):
                        pq = e % 2
                        v.wait_ge(t3, c * 24 + e + 1)
                        if c >= 1 and e < 2:
                            v.wait_ge(q_3q, 16 * c)
                        ps = PS1[:, 512 * pq:512 * (pq + 1)]
                        v.scalar_tensor_tensor(
                            A3[:, P3Q + e * 512:P3Q + (e + 1) * 512], ps, 0.0,
                            gammaI[:], ALU.max, ALU.mult).then_inc(v3q, 1)
                    if c == 0:
                        v.wait_ge(q_D[0], 16 * 16)
                        v.wait_ge(q_D[1], 16 * 16)
                        v.wait_ge(q_E[0], 16 * 16)
                        v.wait_ge(q_E[1], 16 * 16)
                        for sbt_ in range(NS):
                            V(v.memset(A3[:, P3V + sbt_ * VW + 1024:
                                          P3V + sbt_ * VW + 1025], 1.0))
                            V(v.memset(A3[:, P3V + sbt_ * VW + 1025:
                                          P3V + sbt_ * VW + 1026], 0.0))
                    for g in range(8):
                        sbt_, cs = g // 2, g % 2
                        pv = g % 2
                        v.wait_ge(t3, c * 24 + 16 + g + 1)
                        if c >= 1 and g < 2:
                            v.wait_ge(q_3v, 16 * c)
                        ps = PS1[:, 1024 + 512 * pv:1024 + 512 * (pv + 1)]
                        v.tensor_copy(A3[:, P3V + sbt_ * VW + cs * 512:
                                         P3V + sbt_ * VW + cs * 512 + 512],
                                      ps).then_inc(v3v, 1)

                for sl in range(16):
                    p = sl % 2
                    v.wait_ge(t4, sl + 1)
                    if sl >= 2:
                        v.wait_ge(q_4[p], 16 * ((sl - 2) // 2 + 1))
                    ps = PS1[:, 1024 * p:1024 * (p + 1)]
                    v.tensor_copy(A3f[:, 1024 * p:1024 * (p + 1)], ps).then_inc(v4, 1)

                v.wait_ge(q_4[0], 16 * 8)
                v.wait_ge(q_4[1], 16 * 8)
                v.memset(A3[:, ST_O[0]:ST_O[0] + DT * VW], 0.0).then_inc(v5i, 1)
                v.memset(A3[:, ST_O[1]:ST_O[1] + DT * VW], 0.0).then_inc(v5i, 1)

                for c in range(NCH):
                    for a_ in range(NS):
                        v.wait_ge(t5a, c * 4 + a_ + 1)
                        if c >= 1:
                            v.wait_ge(t5y, 4 * c)
                        pa = a_ % 2
                        ps = (PS1 if pa == 0 else PS2)[:, 1536:2048]
                        row = att_sb[:, a_ * 512:(a_ + 1) * 512]
                        V(v.tensor_scalar_mul(row, ps, asc[:, a_:a_ + 1]))
                        VB()
                        v.scalar_tensor_tensor(
                            att_sb[:, a_ * 512 + a_ * 128:a_ * 512 + (a_ + 1) * 128],
                            ps[:, a_ * 128:(a_ + 1) * 128], asc[:, a_:a_ + 1],
                            causal[:], ALU.mult, ALU.mult).then_inc(v5a, 1)
                    for b_ in range(NS):
                        py = b_ % 2
                        yidx = c * NS + b_
                        v.wait_ge(t5y, yidx + 1)
                        ps = PS1 if py == 0 else PS2
                        if c < 2:
                            v.tensor_copy(A1f[:, yidx * NUMW:(yidx + 1) * NUMW],
                                          ps[:, 0:NUMW]).then_inc(v5y, 1)
                        else:
                            if yidx >= 10:
                                v.wait_ge(s5y, yidx - 9)
                            V(v.tensor_scalar_add(ycol[:, py:py + 1], ps[:, 1024:1025],
                                                  EPS_ATTN))
                            VB()
                            v.reciprocal(ycol[:, 2 + py:3 + py],
                                         ycol[:, py:py + 1]).then_inc(v5y, 1)
                            v.wait_ge(q_x5[py], 16 * ((yidx - 8) // 2 + 1))
                            v.wait_ge(v5y, yidx + 1)
                            if yidx >= 10:
                                v.wait_ge(q_ya[py], 16 * ((yidx - 10) // 2 + 1))
                            VB()
                            v.scalar_tensor_tensor(
                                A1f[:, YA_O + 1024 * py:YA_O + 1024 * (py + 1)],
                                ps[:, 0:1024], ycol[:, 2 + py:3 + py],
                                A2f[:, XS_F + 1024 * py:XS_F + 1024 * (py + 1)],
                                ALU.mult, ALU.add).then_inc(s5y, 1)

                v.wait_ge(q_rs, 16)
                for i6 in range(2):
                    xo = A2f[:, XO_F + 1024 * i6:XO_F + 1024 * (i6 + 1)]
                    v.tensor_scalar(xo, xo, -33000.0, 65000.0,
                                    ALU.max, ALU.min).then_inc(v6, 1)
                    v.wait_ge(v6, 2 * i6 + 1)
                    V(v.reduce_sum(stat[:, 8 * i6:8 * i6 + 1], xo, axis=AX.X))
                    v.wait_ge(s6, i6 + 1)
                    VB()
                    V(v.tensor_scalar_mul(stat[:, 8 * i6:8 * i6 + 1],
                                          stat[:, 8 * i6:8 * i6 + 1], 1.0 / D))
                    V(v.tensor_scalar_mul(stat[:, 8 * i6 + 2:8 * i6 + 3],
                                          stat[:, 8 * i6 + 1:8 * i6 + 2], 1.0 / D))
                    VB()
                    V(v.tensor_mul(stat[:, 8 * i6 + 3:8 * i6 + 4],
                                   stat[:, 8 * i6:8 * i6 + 1], stat[:, 8 * i6:8 * i6 + 1]))
                    VB()
                    V(v.tensor_sub(stat[:, 8 * i6 + 2:8 * i6 + 3],
                                   stat[:, 8 * i6 + 2:8 * i6 + 3],
                                   stat[:, 8 * i6 + 3:8 * i6 + 4]))
                    VB()
                    V(v.tensor_scalar_add(stat[:, 8 * i6 + 2:8 * i6 + 3],
                                          stat[:, 8 * i6 + 2:8 * i6 + 3], EPS_LN))
                    VB()
                    v.reciprocal(stat[:, 8 * i6 + 3:8 * i6 + 4],
                                 stat[:, 8 * i6 + 2:8 * i6 + 3]).then_inc(v6, 1)
                for i6 in range(2):
                    v.wait_ge(t6, i6 + 1)
                    trb = psb(i6, 1536, 2048, bf=True)
                    src_ = trb.rearrange("p (t c) -> p t c", c=128)
                    dst = A2[:, XTT_O:XTT_O + 2048].rearrange(
                        "p (t c) -> p t c", c=256)[:, :, i6 * 128:(i6 + 1) * 128]
                    v.tensor_copy(dst, src_).then_inc(v6e, 1)
                for n in range(NEL):
                    pe = n % 2
                    v.wait_ge(t6, 2 + n + 1)
                    if n >= 2:
                        v.wait_ge(q_yt[pe], 16 * ((n - 2) // 2 + 1))
                    for cs in range(2):
                        ps = psb(cs, 1536, 2048)[0:32, :]
                        v.tensor_copy(A2[0:32, YTS_O + pe * D + cs * 512:
                                         YTS_O + pe * D + (cs + 1) * 512],
                                      ps).then_inc(v6e, 1)

                for cc in range(2):
                    v.wait_ge(q_5[cc], 16 * 17)
                    v.wait_ge(v_pro, 11)
                    qb = A2[:, CB_O[cc]:CB_O[cc] + 4096]
                    v.tensor_scalar_mul(qb, qb, ccs[:, cc:cc + 1]).then_inc(v5c, 1)

                for k in range(NT + 2):
                    if k < NT:
                        p = k % 2
                        v.wait_ge(g_7[p], 16 * (k // 2 + 1))
                        E3 = csm_ap(k).rearrange("p (n q) -> p n q", q=32)
                        E3t = E3.transpose([0, 2, 1])
                        V(v.reduce_sum(denC[:, 32 * p:32 * p + 32], E3t, axis=AX.X))
                        VB()
                        V(v.reciprocal(denC[:, 32 * p:32 * p + 32],
                                       denC[:, 32 * p:32 * p + 32]))
                        rC = denC[:, 32 * p:32 * p + 32].unsqueeze(1).broadcast_to(
                            [128, 64, 32])
                        VB()
                        v.tensor_mul(E3, E3, rC).then_inc(v7c, 1)
                    if 16 <= k < 24:
                        idx = k - 16
                        pyc = idx % 2
                        v.wait_ge(t5c, idx + 1)
                        nh = A1f[:, idx * NUMW:(idx + 1) * NUMW]
                        V(v.tensor_add(nh, nh, PS2[:, 0:NUMW]))
                        VB()
                        V(v.tensor_scalar_add(ycol[:, pyc:pyc + 1], nh[:, 1024:1025],
                                              EPS_ATTN))
                        VB()
                        v.reciprocal(ycol[:, 2 + pyc:3 + pyc],
                                     ycol[:, pyc:pyc + 1]).then_inc(v5n, 1)
                        v.wait_ge(q_xc, 16 * (idx + 1))
                        v.wait_ge(v5n, idx + 1)
                        if idx < 2:
                            v.wait_ge(q_ya[pyc], 16 * 12)
                        else:
                            v.wait_ge(q_ya[pyc], 16 * (12 + (idx - 2) // 2 + 1))
                        VB()
                        v.scalar_tensor_tensor(
                            A1f[:, YA_O + 1024 * pyc:YA_O + 1024 * (pyc + 1)],
                            nh[:, 0:1024], ycol[:, 2 + pyc:3 + pyc],
                            A1f[:, XC_F:XC_F + 1024],
                            ALU.mult, ALU.add).then_inc(s5y, 1)
                    if k >= 2:
                        j = k - 2
                        pj = j % 2
                        v.wait_ge(t7y, j + 1)
                        v.wait_ge(q_y7[pj], 16 * (j // 2 + 1))
                        if j >= 2:
                            v.wait_ge(q_out[pj], 16 * ((j - 2) // 2 + 1))
                        ps = PS1[:, 1024 * pj:1024 * (pj + 1)]
                        ob = A3f[:, OUT2_F + 1024 * pj:OUT2_F + 1024 * (pj + 1)]
                        v.tensor_add(ob, ps,
                                     A3f[:, YAB2_F + 1024 * pj:
                                         YAB2_F + 1024 * (pj + 1)]
                                     ).then_inc(v7x, 1)

            @blk.scalar
            def _(s):
                ns = [0]

                def SC(inst):
                    inst.then_inc(sd, 1)
                    ns[0] += 1

                def SB():
                    s.wait_ge(sd, ns[0])

                s.dma_start(out=A3f[:, X_F[0]:X_F[0] + 1024],
                            in_=xsh[0:128, :]).then_inc(q_x[0], 16)
                s.dma_start(out=A3f[:, X_F[1]:X_F[1] + 1024],
                            in_=xsh[128:256, :]).then_inc(q_x[1], 16)
                s.wait_ge(q_c, 16 * 6)
                s.activation(lng[:], gam_s[:], AF.Ln).then_inc(s_pro, 1)
                s.wait_ge(q_c, 16 * 8)
                s.activation(gammaI[:], iotaI[:], AF.Exp,
                             scale=lng_bc[:]).then_inc(s_pro, 1)
                s.activation(g512c[:], lng_bc[:], AF.Exp, scale=512.0).then_inc(s_pro, 1)
                s.wait_ge(v_pro, 9)
                for a_ in range(4):
                    s.activation(asc[:, a_:a_ + 1], iotac[:], AF.Exp,
                                 scale=nlng_bc[:], bias=bcols[:, a_:a_ + 1]).then_inc(s_pro, 1)
                for s_ in range(4):
                    s.activation(ksc[:, s_:s_ + 1], iotac[:], AF.Exp,
                                 scale=nlng_bc[:], bias=bcols[:, 4 + s_:5 + s_]).then_inc(s_pro, 1)

                s.wait_ge(q_x[0], 16)
                s.activation(A2f[:, SQ_F:SQ_F + 1024], A3f[:, X_F[0]:X_F[0] + 1024],
                             AF.Square,
                             accum_out=stat[:, 1:2]).then_inc(s1a, 1)
                for i in range(NT + 3):
                    if 2 <= i < NT + 2:
                        j = i - 2
                        pj = j % 2
                        s.wait_ge(t2, j + 1)
                        if j >= 2:
                            s.wait_ge(v2d, j - 1)
                            s.wait_ge(q_E[pj], 16 * ((j - 2) // 2 + 1))
                        log = (PS1 if pj == 0 else PS2)[:, 0:2048]
                        s.activation(A3[:, E_O[pj]:E_O[pj] + 2048], log,
                                     AF.Exp).then_inc(s2e, 1)
                    if i < NT:
                        p = i % 2
                        s.wait_ge(v1s, i + 1)
                        xt = A3f[:, X_F[p]:X_F[p] + 1024]
                        SC(s.activation(stat[:, p * 8 + 3:p * 8 + 4],
                                        stat[:, p * 8 + 2:p * 8 + 3], AF.Ln))
                        SB()
                        SC(s.activation(stat[:, p * 8 + 4:p * 8 + 5],
                                        stat[:, p * 8 + 3:p * 8 + 4], AF.Exp,
                                        scale=-0.5))
                        SB()
                        SC(s.mul(stat[:, p * 8 + 5:p * 8 + 6], stat[:, p * 8:p * 8 + 1],
                                 stat[:, p * 8 + 4:p * 8 + 5]))
                        if i >= 2:
                            s.wait_ge(q_xn[p], 16 * ((i - 2) // 2 + 1))
                            s.wait_ge(q_t1[p], 16 * ((i - 2) // 2 + 1))
                        SB()
                        s.activation(A3[:, XN_O[p]:XN_O[p] + 1024], xt, AF.Identity,
                                     scale=stat[:, p * 8 + 4:p * 8 + 5],
                                     bias=stat[:, p * 8 + 5:p * 8 + 6]).then_inc(s1b, 1)
                    if i + 1 < NT:
                        p1 = (i + 1) % 2
                        s.wait_ge(q_x[p1], 16 * ((i + 1) // 2 + 1))
                        s.activation(A2f[:, SQ_F + 1024 * p1:SQ_F + 1024 * (p1 + 1)],
                                     A3f[:, X_F[p1]:X_F[p1] + 1024], AF.Square,
                                     accum_out=stat[:, p1 * 8 + 1:p1 * 8 + 2]
                                     ).then_inc(s1a, 1)
                    if i + 2 < NT:
                        p = i % 2
                        s.wait_ge(s1b, i + 1)
                        s.dma_start(out=A3f[:, X_F[p]:X_F[p] + 1024],
                                    in_=xsh[(i + 2) * 128:(i + 3) * 128, :]
                                    ).then_inc(q_x[p], 16)
                    if i >= 3:
                        j = i - 3
                        pj = j % 2
                        s.wait_ge(s2e, j + 1)
                        s.dma_start(out=sp_E[j], in_=A3[:, E_O[pj]:E_O[pj] + 2048]
                                    ).then_inc(q_E[pj], 16)

                s.wait_ge(q_E[0], 16 * 16)
                s.wait_ge(q_E[1], 16 * 16)
                s.wait_ge(q_xn[0], 16 * 16)
                s.wait_ge(q_xn[1], 16 * 16)
                s.wait_ge(v2d, NT)
                s.wait_ge(q_D[0], 16 * 16)
                s.wait_ge(q_D[1], 16 * 16)
                s.wait_ge(s_pro, 11)
                for c in range(NCH):
                    for e in range(DT):
                        pq = e % 2
                        s.wait_ge(t3, c * 24 + 8 + e + 1)
                        if c >= 1 and e < 2:
                            s.wait_ge(q_3k, 16 * c)
                        ps = PS1[:, 512 * pq:512 * (pq + 1)]
                        s.activation(A3[:, P3K + e * 512:P3K + (e + 1) * 512], ps,
                                     AF.Relu).then_inc(s3k, 1)
                    for sbt_ in range(NS):
                        ptk = sbt_ % 2
                        s.wait_ge(t3h, c * 4 + sbt_ + 1)
                        if c >= 1 and sbt_ < 2:
                            s.wait_ge(q_3h, 16 * c)
                        trb = psb(1, 512 * ptk, 512 * (ptk + 1), bf=True)
                        s.mul(A3[:, P3H + sbt_ * 1024:P3H + (sbt_ + 1) * 1024], trb,
                              ksc[:, sbt_:sbt_ + 1]).then_inc(s3h, 1)

                for c in range(NCH):
                    sq_ = ST_O[(c + 1) % 2]
                    if c == NCH - 1:
                        s.wait_ge(v_pro, 15)
                    for dk in range(DT):
                        pu = dk % 2
                        s.wait_ge(t5u, c * 8 + dk + 1)
                        ps = PS1 if pu == 0 else PS2
                        if c == NCH - 1:
                            s.activation(A3[:, sq_ + dk * VW:sq_ + dk * VW + NUMW],
                                         ps[:, 0:NUMW], AF.Identity,
                                         scale=nfcol[:]).then_inc(sv5u, 1)
                        else:
                            s.copy(A3[:, sq_ + dk * VW:sq_ + dk * VW + NUMW],
                                   ps[:, 0:NUMW]).then_inc(sv5u, 1)

                for i6 in range(2):
                    s.wait_ge(v6, 2 * i6 + 1)
                    xo = A2f[:, XO_F + 1024 * i6:XO_F + 1024 * (i6 + 1)]
                    s.activation(A2f[:, SQ6_F + 1024 * i6:SQ6_F + 1024 * (i6 + 1)], xo,
                                 AF.Square,
                                 accum_out=stat[:, 8 * i6 + 1:8 * i6 + 2]).then_inc(s6, 1)
                for i6 in range(2):
                    s.wait_ge(v6, 2 * i6 + 2)
                    xo = A2f[:, XO_F + 1024 * i6:XO_F + 1024 * (i6 + 1)]
                    SC(s.activation(stat[:, 8 * i6 + 4:8 * i6 + 5],
                                    stat[:, 8 * i6 + 3:8 * i6 + 4], AF.Sqrt))
                    SB()
                    SC(s.mul(stat[:, 8 * i6 + 5:8 * i6 + 6], stat[:, 8 * i6:8 * i6 + 1],
                             stat[:, 8 * i6 + 4:8 * i6 + 5]))
                    SB()
                    SC(s.mul(stat[:, 8 * i6 + 5:8 * i6 + 6],
                             stat[:, 8 * i6 + 5:8 * i6 + 6], -1.0))
                    SB()
                    s.activation(A2[:, TANH_O + i6 * 1024:TANH_O + (i6 + 1) * 1024], xo,
                                 AF.Tanh, scale=stat[:, 8 * i6 + 4:8 * i6 + 5],
                                 bias=stat[:, 8 * i6 + 5:8 * i6 + 6]).then_inc(s6, 1)


                s.wait_ge(cc3, 1)
                s.wait_ge(t6, 10)
                s.dma_start(out=A2[:, YTL_O[1]:YTL_O[1] + 8192].rearrange(
                    "p (t c) -> p t c", c=1024),
                    in_=yt_out.rearrange("(t p) c -> p t c", p=128)[:, 8:16, :]
                    ).then_inc(q_ytl, 16)
                s.wait_ge(t5y, NS * NCH)
                for k in range(NT):
                    p = k % 2
                    s.wait_ge(v7c, k + 1)
                    if k >= 2:
                        s.wait_ge(t7y, k - 1)
                    csm = csm_ap(k)
                    for half in range(2):
                        src_ = csm[:, half * 1024:(half + 1) * 1024
                                   ].rearrange("p (t c) -> p t c", c=128)
                        dst = A2[:, CST_O2[p] + half * 1024:CST_O2[p] + (half + 1) * 1024
                                 ].rearrange("p (t c) -> p t c", c=128)
                        s.dma_start_transpose(dst, src_).then_inc(q_t7[p], 16)


            @blk.sync
            def _(sy):
                sy.dma_start(out=eye_f[:], in_=eye_d).then_inc(q_c, 16)
                sy.dma_start(out=causal[:], in_=causal_d).then_inc(q_c, 16)
                sy.dma_start(out=iotac[:], in_=iotac_d).then_inc(q_c, 16)
                sy.dma_start(out=gam_s[:], in_=gam).then_inc(q_c, 16)
                sy.dma_start(out=fcol[:], in_=flag_d[0:1, 0:1].partition_broadcast(128)
                             ).then_inc(q_c, 16)
                sy.dma_start(out=iotaI[:], in_=iotar_d[0:1, 1:513].partition_broadcast(128)
                             ).then_inc(q_c, 16)
                sy.wait_ge(s_pro, 1)
                sy.dma_start(out=lng_dr, in_=lng[:]).then_inc(q_c, 16)
                sy.wait_ge(q_c, 16 * 7)
                sy.dma_start(out=lng_bc[:], in_=lng_dr[0:1, 0:1].partition_broadcast(128)
                             ).then_inc(q_c, 16)

                for j in range(NT):
                    pp = j % 2
                    sy.wait_ge(s1b, j + 1)
                    sy.dma_start_transpose(
                        A1b.rearrange("p (t c) -> p t c", c=4096
                                      )[:, :, j * 128:(j + 1) * 128],
                        A3[:, XN_O[pp]:XN_O[pp] + 1024].rearrange(
                            "p (t c) -> p t c", c=128)).then_inc(q_t1[pp], 16)
                    sy.dma_start(out=sp_xn[j], in_=A3[:, XN_O[pp]:XN_O[pp] + 1024]
                                 ).then_inc(q_xn[pp], 16)

                for c in range(NCH):
                    sy.wait_ge(v3q, 8 * (c + 1))
                    sy.dma_start(out=sp_q[c], in_=A3[:, P3Q:P3Q + 4096]).then_inc(q_3q, 16)
                    sy.wait_ge(s3k, 8 * (c + 1))
                    sy.dma_start(out=sp_k[c], in_=A3[:, P3K:P3K + 4096]).then_inc(q_3k, 16)
                    sy.wait_ge(s3h, 4 * (c + 1))
                    sy.dma_start(out=sp_kh[c], in_=A3[:, P3H:P3H + 4096]).then_inc(q_3h, 16)
                    sy.wait_ge(v3v, 8 * (c + 1))
                    sy.dma_start(out=sp_v[c], in_=A3[:, P3V:P3V + NS * VW]).then_inc(q_3v, 16)

                sy.wait_ge(q_3q, 16 * 8); sy.wait_ge(q_3k, 16 * 8)
                sy.wait_ge(q_3h, 16 * 8); sy.wait_ge(q_3v, 16 * 8)
                sy.wait_ge(q_xn[0], 16 * 16); sy.wait_ge(q_xn[1], 16 * 16)
                sy.wait_ge(t3h, 4 * NCH)
                sy.dma_start(out=xn4[:, 0:32768].rearrange("p (t c) -> p t c", c=1024),
                             in_=sp_xn.transpose([1, 0, 2])).then_inc(q_r4, 16)
                for sl in range(16):
                    p = sl % 2
                    sy.wait_ge(v4, sl + 1)
                    sy.dma_start(out=ar_in[sl * 128:(sl + 1) * 128, :],
                                 in_=A3f[:, 1024 * p:1024 * (p + 1)]).then_inc(q_4[p], 16)

                def _x5_load(c_):
                    for b_ in range(NS):
                        yidx = c_ * NS + b_
                        py = b_ % 2
                        if yidx >= 10:
                            sy.wait_ge(s5y, yidx - 9)
                        sy.dma_start(out=A2f[:, XS_F + 1024 * py:XS_F + 1024 * (py + 1)],
                                     in_=xsh[yidx * 128:(yidx + 1) * 128, :]
                                     ).then_inc(q_x5[py], 16)

                def _ya_store(c_):
                    for b_ in range(NS):
                        yidx = c_ * NS + b_
                        py = b_ % 2
                        sy.wait_ge(s5y, (yidx - 2 * NS) + 1)
                        sy.dma_start(out=sp_ya[yidx],
                                     in_=A1f[:, YA_O + 1024 * py:YA_O + 1024 * (py + 1)]
                                     ).then_inc(q_ya[py], 16)
                for c in range(NCH):
                    bc = CB_O[c % 2]
                    if c >= 2:
                        sy.wait_ge(t5y, NS * (c - 1))
                    sy.dma_start(out=A2[:, bc:bc + 4096], in_=sp_q[c]).then_inc(q_5[c % 2], 16)
                    sy.dma_start(out=A2[:, bc + 4096:bc + 8192],
                                 in_=sp_k[c]).then_inc(q_5[c % 2], 16)
                    sy.dma_start(out=A2[:, bc + 8192:bc + 12288],
                                 in_=sp_kh[c]).then_inc(q_5[c % 2], 16)
                    sy.dma_start(out=A2[:, bc + 12288:bc + 12288 + NS * VW],
                                 in_=sp_v[c]).then_inc(q_5[c % 2], 16)
                    if 2 <= c - 1:
                        _ya_store(c - 1)
                    if c >= 2:
                        _x5_load(c)
                    if c == 4:
                        sy.wait_ge(cc1, 1)
                        sy.dma_start(out=A2f[:, XO_F:XO_F + 2048].rearrange(
                            "p (t c) -> p t c", c=1024),
                            in_=rs_out.rearrange("(t p) c -> p t c", p=128)
                            ).then_inc(q_rs, 16)
                _ya_store(NCH - 1)
                sy.wait_ge(sv5u, 8 * NCH)
                sy.dma_start(out=st_in.transpose([1, 0, 2]),
                             in_=A3[:, ST_O[0]:ST_O[0] + DT * VW].rearrange(
                                 "p (t c) -> p t c", c=VW)).then_inc(q_st, 16)
                for cc in range(2):
                    sy.wait_ge(t5y, NS * NCH)
                    sy.dma_start(out=A2[:, CB_O[cc]:CB_O[cc] + 4096],
                                 in_=sp_q[cc]).then_inc(q_5[cc], 16)
                for n in range(NEL):
                    pe = n % 2
                    sy.wait_ge(v6e, 2 + 2 * (n + 1))
                    sy.dma_start(out=yt_in[n * 32:(n + 1) * 32, :],
                                 in_=A2[0:32, YTS_O + pe * D:YTS_O + (pe + 1) * D]
                                 ).then_inc(q_yt[pe], 16)
                sy.wait_ge(cc3, 1)
                sy.wait_ge(t6, 10)
                sy.dma_start(out=A2[:, YTL_O[0]:YTL_O[0] + 8192].rearrange(
                    "p (t c) -> p t c", c=1024),
                    in_=yt_out.rearrange("(t p) c -> p t c", p=128)[:, 0:8, :]
                    ).then_inc(q_ytl, 16)

                for k, i in enumerate(P7ORD):
                    pj = k % 2
                    if k == 12:
                        sy.wait_ge(cc2, 1)
                        sy.dma_start(out=A1b[:, SINIT_O:SINIT_O + DT * VW].rearrange(
                            "p (t c) -> p t c", c=VW),
                            in_=st_out2.transpose([1, 0, 2])).then_inc(q_si, 16)
                    if 14 <= k < 22:
                        idx = k - 14
                        if idx >= 1:
                            sy.wait_ge(s5y, NS * (NCH - 2) + idx)
                        sy.dma_start(out=A1f[:, XC_F:XC_F + 1024],
                                     in_=xsh[idx * 128:(idx + 1) * 128, :]
                                     ).then_inc(q_xc, 16)
                    if 15 <= k < 23:
                        idx = k - 15
                        pyc = idx % 2
                        sy.wait_ge(s5y, NS * (NCH - 2) + idx + 1)
                        sy.dma_start(out=sp_ya[idx],
                                     in_=A1f[:, YA_O + 1024 * pyc:YA_O + 1024 * (pyc + 1)]
                                     ).then_inc(q_ya[pyc], 16)
                    sy.wait_ge(v7x, k + 1)
                    sy.dma_start(out=out_d[i * 128:(i + 1) * 128, :],
                                 in_=A3f[:, OUT2_F + 1024 * pj:OUT2_F + 1024 * (pj + 1)]
                                 ).then_inc(q_out[pj], 16)
                sy.wait_ge(q_out[0], 16 * 16)
                sy.wait_ge(q_out[1], 16 * 16)

            @blk.gpsimd
            def _(g):
                g.dma_start(out=eye_b[:], in_=eye_d).then_inc(g_eye, 16)
                g.dma_start(out=bq_r, in_=bq).then_inc(g_bias, 16)
                g.dma_start(out=bk_r, in_=bk).then_inc(g_bias, 16)
                g.dma_start(out=bv_r, in_=bv).then_inc(g_bias, 16)
                g.dma_start(out=phib_r, in_=phib).then_inc(g_bias, 16)
                g.dma_start(out=A2[:, PHIT_O:PHIT_O + 16384].rearrange(
                    "p (t c) -> p t c", c=2048),
                    in_=phiwt_d.rearrange("(t p) c -> p t c", p=128)).then_inc(g_phi, 16)

                for i in range(NT):
                    p = i % 2
                    g.wait_ge(gd, i + 1)
                    if i >= 2:
                        g.wait_ge(q_D[p], 16 * ((i - 2) // 2 + 1))
                    E3g = A3[:, E_O[p]:E_O[p] + 2048].rearrange("p (n q) -> p n q", q=32)
                    d3g = A3[:, DS_O[p]:DS_O[p] + 2048].rearrange("p (n q) -> p n q", q=32)
                    rDg = denD[:, 64 * p:64 * p + 64].unsqueeze(2).broadcast_to(
                        [128, 64, 32])
                    g.tensor_mul(d3g, E3g, rDg).then_inc(v2d, 1)
                    g.wait_ge(v2d, i + 1)
                    g.dma_start(out=sp_dsm[i], in_=A3[:, DS_O[p]:DS_O[p] + 2048]
                                ).then_inc(q_D[p], 16)
                    if 4 <= i < 28 and i % 2 == 0:
                        pidx = (i - 4) // 2
                        wi, qq = pidx // 4, pidx % 4
                        wsrc = (wqt, wkt, wvt)[wi]
                        g.dma_start(out=A2[:, W3_O + wi * 8192 + qq * 2048:
                                           W3_O + wi * 8192 + (qq + 1) * 2048
                                           ].rearrange("p (t c) -> p t c", c=1024),
                                    in_=wsrc.rearrange("(t p) c -> p t c", p=128
                                                       )[:, qq * 2:(qq + 1) * 2, :]
                                    ).then_inc(g_w, 16)

                g.wait_ge(q_D[0], 16 * 16)
                g.wait_ge(q_D[1], 16 * 16)
                g.wait_ge(t3h, 4 * NCH)
                for sl in range(16):
                    p = sl % 2
                    if sl >= 2:
                        g.wait_ge(t4, sl - 1)
                    src = sp_dsm.transpose([1, 0, 2])[:, :, sl * 128:(sl + 1) * 128]
                    g.dma_start(out=A2[:, DSM_O + 4096 * p:DSM_O + 4096 * (p + 1)
                                       ].rearrange("p (t c) -> p t c", c=128),
                                in_=src).then_inc(g_d4[p], 16)
                g.wait_ge(q_4[0], 16 * 8)
                g.wait_ge(q_4[1], 16 * 8)
                g.collective_compute("ReduceScatter", ALU.add,
                                     replica_groups=[list(range(8))],
                                     ins=[ar_in.opt()], outs=[rs_out.opt()]
                                     ).then_inc(cc1, 1)
                for n in range(NEL):
                    pe = n % 2
                    if n == 0:
                        g.wait_ge(t5y, NS * (NCH - 1))
                    elif n == 1:
                        g.wait_ge(t5y, NS * NCH)
                    else:
                        g.wait_ge(t6, 2 + (n - 2) + 1)
                    g.dma_start(out=A2[:, EW_O[pe]:EW_O[pe] + 8192].rearrange(
                        "p (t c) -> p t c", c=1024),
                        in_=ew[n].rearrange("(t p) c -> p t c", p=128)).then_inc(g_ew[pe], 16)
                    g.dma_start(out=ebrow[0:1, pe * D:(pe + 1) * D],
                                in_=eb[n:n + 1, :]).then_inc(g_ew[pe], 16)
                g.wait_ge(q_yt[0], 16 * 4)
                g.wait_ge(q_yt[1], 16 * 4)
                g.collective_compute("AllGather", ALU.bypass,
                                     replica_groups=[list(range(8))],
                                     ins=[yt_in.opt()], outs=[yt_out.opt()]
                                     ).then_inc(cc3, 1)
                g.wait_ge(q_st, 16)
                g.collective_compute("AllReduce", ALU.add,
                                     replica_groups=[[0, 1], [2, 3], [4, 5], [6, 7]],
                                     ins=[st_in.opt()], outs=[st_out2.opt()]
                                     ).then_inc(cc2, 1)
                g.wait_ge(t5y, NS * NCH)
                g.wait_ge(q_st, 16)

                for k in range(NT + 2):
                    if k < NT:
                        p = k % 2
                        if k >= 2:
                            g.wait_ge(g_7[p], 16 * (k // 2))
                        if k >= 4:
                            g.wait_ge(q_t7[p], 32 * ((k - 4) // 2 + 1))
                        g.dma_start(out=csm_ap(k),
                                    in_=sp_E[P7ORD[k]]).then_inc(g_7[p], 16)
                    if k >= 2:
                        j = k - 2
                        pj = j % 2
                        i2 = P7ORD[j]
                        if j >= 2:
                            g.wait_ge(v7x, j - 1)
                        if i2 < 8:
                            g.wait_ge(q_ya[i2 % 2], 16 * (12 + i2 // 2 + 1))
                        else:
                            g.wait_ge(q_ya[i2 % 2], 16 * ((i2 - 8) // 2 + 1))
                        g.dma_start(out=A3f[:, YAB2_F + 1024 * pj:
                                            YAB2_F + 1024 * (pj + 1)],
                                    in_=sp_ya[i2]).then_inc(q_y7[pj], 16)

    return nc


def kernel(L0_attn_x_seq, L0_attn_Wq, L0_attn_bq, L0_attn_Wk, L0_attn_bk,
           L0_attn_Wv, L0_attn_bv, L0_attn_gamma, L0_ffn_phi_w, L0_ffn_phi_b,
           L0_ffn_expert_w, L0_ffn_expert_b):
    x = np.asarray(L0_attn_x_seq)
    nc = build_nc()
    eye = np.eye(128, dtype=np.float32)
    causal = np.ascontiguousarray(np.tril(np.ones((128, 128), dtype=np.float32)).T)
    iota_col = np.arange(128, dtype=np.float32).reshape(128, 1)
    iota_row = np.arange(1056, dtype=np.float32).reshape(1, 1056)
    wqt = np.ascontiguousarray(np.asarray(L0_attn_Wq).T)
    wkt = np.ascontiguousarray(np.asarray(L0_attn_Wk).T)
    wvt = np.ascontiguousarray(np.asarray(L0_attn_Wv).T)
    phiwt = np.ascontiguousarray(np.asarray(L0_ffn_phi_w).T)
    in_maps = []
    for c in range(8):
        b, h = c // 2, c % 2
        in_maps.append({
            "xsh": x[b][h * TOK:(h + 1) * TOK],
            "wqt": wqt, "wkt": wkt, "wvt": wvt,
            "bq": np.asarray(L0_attn_bq).reshape(1, D),
            "bk": np.asarray(L0_attn_bk).reshape(1, D),
            "bv": np.asarray(L0_attn_bv).reshape(1, D),
            "gam": np.asarray(L0_attn_gamma, dtype=np.float32).reshape(1, 1),
            "phiwt": phiwt,
            "phib": np.asarray(L0_ffn_phi_b).reshape(1, SLOT),
            "ew": np.ascontiguousarray(np.asarray(L0_ffn_expert_w)[c * NEL:(c + 1) * NEL]),
            "eb": np.ascontiguousarray(np.asarray(L0_ffn_expert_b)[c * NEL:(c + 1) * NEL]),
            "eye": eye, "causal": causal,
            "iota_col": iota_col, "iota_row": iota_row,
            "flag": np.array([[float(h)]], dtype=np.float32),
        })
    res = run_bass_kernel_spmd(nc, in_maps, core_ids=list(range(8)))
    full = np.empty((B, S, D), dtype=np.float32)
    for c in range(8):
        b, h = c // 2, c % 2
        full[b, h * TOK:(h + 1) * TOK] = res.results[c]["out"]
    return full


# version 14 (final)
# speedup vs baseline: 1.2743x; 1.0351x over previous
import numpy as np
from contextlib import ExitStack
import concourse.bass as bass
import concourse.mybir as mybir
from concourse.bass_utils import run_bass_kernel_spmd

F32 = mybir.dt.float32
BF16 = mybir.dt.bfloat16
AF = mybir.ActivationFunctionType
ALU = mybir.AluOpType
AX = mybir.AxisListType

B, S, D = 4, 8192, 1024
TOK = 4096
NT = 32
DT = 8
CH = 512
NCH = TOK // CH
NS = CH // 128
SLOT = 2048
NEL = 8
VW = 1026
NUMW = 1025
EPS_ATTN = 1e-6
EPS_LN = 1e-5
GSL = ((0, 512), (512, 1024), (1024, 1025))


def build_nc():
    nc = bass.Bass("TRN2", target_bir_lowering=False)

    def din(name, shape):
        return nc.dram_tensor(name, list(shape), F32, kind="ExternalInput").ap()

    xsh = din("xsh", [TOK, D])
    wqt = din("wqt", [D, D]); wkt = din("wkt", [D, D]); wvt = din("wvt", [D, D])
    bq = din("bq", [1, D]); bk = din("bk", [1, D]); bv = din("bv", [1, D])
    gam = din("gam", [1, 1])
    phiwt_d = din("phiwt", [D, SLOT]); phib = din("phib", [1, SLOT])
    ew = din("ew", [NEL, D, D]); eb = din("eb", [NEL, D])
    eye_d = din("eye", [128, 128])
    causal_d = din("causal", [128, 128])
    iotac_d = din("iota_col", [128, 1])
    iotar_d = din("iota_row", [1, 1056])
    flag_d = din("flag", [1, 1])
    out_d = nc.dram_tensor("out", [TOK, D], F32, kind="ExternalOutput").ap()

    sp_xn = nc.dram_tensor("sp_xn", [NT, 128, D], BF16).ap()
    sp_E = nc.dram_tensor("sp_E", [NT, 128, SLOT], BF16).ap()
    sp_dsm = nc.dram_tensor("sp_dsm", [NT, 128, SLOT], BF16).ap()
    sp_q = nc.dram_tensor("sp_q", [NCH, 128, DT * 512], BF16).ap()
    sp_k = nc.dram_tensor("sp_k", [NCH, 128, DT * 512], BF16).ap()
    sp_kh = nc.dram_tensor("sp_kh", [NCH, 128, NS * D], BF16).ap()
    sp_v = nc.dram_tensor("sp_v", [NCH, 128, NS * VW], BF16).ap()
    sp_ya = nc.dram_tensor("sp_ya", [NT, 128, D], F32).ap()
    lng_dr = nc.dram_tensor("lng_dr", [1, 1], F32).ap()
    ar_in = nc.dram_tensor("cc_ar_in", [SLOT, D], F32).ap()
    rs_out = nc.dram_tensor("cc_rs_out", [256, D], F32).ap()
    st_in = nc.dram_tensor("cc_st_in", [DT, 128, VW], BF16).ap()
    st_out2 = nc.dram_tensor("cc_st_out2", [DT, 128, VW], BF16).ap()
    yt_in = nc.dram_tensor("cc_yt_in", [256, D], BF16).ap()
    yt_out = nc.dram_tensor("cc_yt_out", [SLOT, D], BF16).ap()

    with ExitStack() as stack:
        def sbt(name, shape, dt_):
            return stack.enter_context(nc.sbuf_tensor(name, list(shape), dt_))
        eye_f = sbt("eye_f", [128, 128], F32)
        eye_b = sbt("eye_b", [128, 128], BF16)
        causal = sbt("sb_causal", [128, 128], F32)
        iotac = sbt("sb_iotac", [128, 1], F32)
        iotaI = sbt("iotaI", [128, 512], F32)
        gam_s = sbt("gam_s", [1, 1], F32)
        lng = sbt("lng", [1, 1], F32)
        lng_bc = sbt("lng_bc", [128, 1], F32)
        nlng_bc = sbt("nlng_bc", [128, 1], F32)
        fcol = sbt("fcol", [128, 1], F32)
        g512c = sbt("g512c", [128, 1], F32)
        ccs = sbt("ccs", [128, 2], F32)
        asc = sbt("asc", [128, 4], F32)
        ksc = sbt("ksc", [128, 4], F32)
        bcols = sbt("bcols", [128, 8], F32)
        gIbf = sbt("gIbf", [128, 128], BF16)
        gammaI = sbt("gammaI", [128, 512], F32)
        ones_r = sbt("ones_r", [1, 512], BF16)
        stat = sbt("stat", [128, 16], F32)
        negmax = sbt("negmax", [128, 2], F32)
        denD = sbt("denD", [128, 128], F32)
        denC = sbt("denC", [128, 64], F32)
        nfcol = sbt("nfcol", [128, 1], F32)
        att_sb = sbt("att_sb", [128, 2048], BF16)
        ycol = sbt("ycol", [128, 4], F32)
        A1 = sbt("A1", [128, 32768], BF16)
        A2 = sbt("A2", [128, 49664], BF16)
        A3 = sbt("A3", [128, 16416], BF16)
        PS1 = stack.enter_context(nc.psum_tensor("PS1", [128, 2048], F32))
        PS2 = stack.enter_context(nc.psum_tensor("PS2", [128, 2048], F32))

        A1b = A1[:]
        A1f = A1[:].bitcast(F32)
        A2f = A2[:].bitcast(F32)
        A3f = A3[:].bitcast(F32)

        bq_r = A2[0:1, 45056:46080]
        bk_r = A2[0:1, 46080:47104]
        bv_r = A2[0:1, 47104:48128]
        phib_r = A3[0:1, 14336:16384]
        ebrow = A2[0:1, 47120:49168]

        xnT = A1b
        xn4 = A1b
        SINIT_O = 16500
        YA_O = 12544
        W3_O = 0
        PHIT_O = 24576
        CB = 12288 + NS * VW
        CB_O = [0, CB]
        DSM_O = 40960
        EW_O = [4096, 20488]
        CSM_O2 = [12288, 14336]
        CST_O2 = [28688, 30736]
        YTL_O = [4096, 20488]
        P7ORD = list(range(8, 32)) + list(range(8))
        YAB2_F = 0
        OUT2_F = 2048
        XO_F = 16392
        SQ_F = 20480
        SQ6_F = 21512
        XS_F = 21512
        XC_F = 14592
        TANH_O = 36880
        XTT_O = 38928
        YTS_O = 40976
        CSM_O = [0, 2048]
        CST_O = [4096, 6144]
        YAB_F = 4096
        OUT_F = 6144
        X_F = [0, 1024]
        XN_O = [4096, 5120]
        E_O = [6144, 8192]
        DS_O = [10240, 12288]
        P3Q, P3K, P3H, P3V = 0, 4096, 8192, 12288
        ST_O = [0, DT * VW]

        def sm(name):
            return nc.alloc_semaphore(name)

        def sm2(name):
            return [nc.alloc_semaphore(name + "0"), nc.alloc_semaphore(name + "1")]

        q_c = sm("q_c")
        g_eye = sm("g_eye"); g_bias = sm("g_bias"); g_w = sm("g_w"); g_phi = sm("g_phi")
        s_pro = sm("s_pro"); v_pro = sm("v_pro")
        q_x = sm2("q_x"); s1a = sm("s1a"); v1s = sm("v1s"); s1b = sm("s1b")
        t1 = sm("t1"); v1e = sm("v1e"); q_xn = sm2("q_xn")
        t2 = sm("t2"); v2m = sm("v2m"); s2e = sm("s2e"); v2d = sm("v2d")
        q_E = sm2("q_E"); q_D = sm2("q_D")
        t3 = sm("t3"); t3h = sm("t3h"); v3q = sm("v3q"); s3k = sm("s3k")
        v3v = sm("v3v"); s3h = sm("s3h")
        q_3q = sm("q_3q"); q_3k = sm("q_3k"); q_3h = sm("q_3h"); q_3v = sm("q_3v")
        q_r4 = sm("q_r4"); g_d4 = sm2("g_d4"); t4 = sm("t4"); v4 = sm("v4")
        q_4 = sm2("q_4"); cc1 = sm("cc1")
        q_5 = sm2("q_5"); v5i = sm("v5i"); t5u = sm("t5u"); sv5u = sm("sv5u")
        t5a = sm("t5a"); v5a = sm("v5a"); t5y = sm("t5y"); v5y = sm("v5y")
        s5y = sm("s5y"); q_ya = sm2("q_ya"); q_st = sm("q_st"); cc2 = sm("cc2")
        q_si = sm("q_si"); v5c = sm("v5c"); t5c = sm("t5c"); v5n = sm("v5n")
        q_rs = sm("q_rs"); g_ew = sm2("g_ew"); v6 = sm("v6"); s6 = sm("s6")
        t6 = sm("t6"); v6e = sm("v6e"); q_yt = sm2("q_yt"); cc3 = sm("cc3")
        q_ytl = sm("q_ytl")
        g_7 = sm2("g_7"); q_7 = sm2("q_7"); v7c = sm("v7c"); t7t = sm("t7t")
        v7t = sm("v7t"); t7y = sm("t7y"); v7o = sm("v7o"); q_out = sm2("q_out")
        vd = sm("vd"); sd = sm("sd")
        q_t1 = sm2("q_t1"); gd = sm("gd")
        q_t7 = sm2("q_t7"); gd7 = sm("gd7"); v7a = sm("v7a"); q_y7 = sm2("q_y7"); v7x = sm("v7x")
        q_x5 = sm2("q_x5"); q_xc = sm("q_xc"); q_c5 = sm2("q_c5")

        def csm_ap(k):
            m = k % 4
            if m < 2:
                return A2[:, CSM_O2[m]:CSM_O2[m] + 2048]
            off = 8208 + (m - 2) * 2048
            return A3[:, off:off + 2048]

        def psb(which, lo, hi, bf=False):
            t_ = PS1 if which == 0 else PS2
            ap = t_[:, lo:hi]
            return ap.bitcast(BF16) if bf else ap

        with nc.Block() as blk:

            @blk.tensor
            def _(t):
                t.wait_ge(g_eye, 16)
                t.wait_ge(g_phi, 16)
                t.wait_ge(g_bias, 16 * 4)
                t.wait_ge(v_pro, 13)
                for i in range(NT):
                    p = i % 2
                    t.wait_ge(q_t1[p], 16 * (i // 2 + 1))
                    if i >= 2:
                        t.wait_ge(s2e, i - 1)
                    log = (PS1 if p == 0 else PS2)[:, 0:2048]
                    for step in range(DT):
                        lhs = xnT[:, step * 4096 + i * 128: step * 4096 + (i + 1) * 128]
                        for cs in range(4):
                            t.matmul(log[:, cs * 512:(cs + 1) * 512], lhs,
                                     A2[:, PHIT_O + step * 2048 + cs * 512:
                                        PHIT_O + step * 2048 + (cs + 1) * 512],
                                     start=(step == 0), stop=False)
                    for cs in range(4):
                        ins = t.matmul(log[:, cs * 512:(cs + 1) * 512], ones_r[0:1, 0:128],
                                       phib_r[0:1, cs * 512:(cs + 1) * 512],
                                       start=False, stop=True)
                        if cs == 3:
                            ins.then_inc(t2, 1)

                t.wait_ge(g_w, 16 * 12)
                t.wait_ge(s2e, NT)
                for c in range(NCH):
                    for j in range(16):
                        proj = 0 if j < 8 else 1
                        e = j % 8
                        pq = j % 2
                        if j >= 2:
                            jp = j - 2
                            if jp < 8:
                                t.wait_ge(v3q, c * 8 + jp + 1)
                            else:
                                t.wait_ge(s3k, c * 8 + (jp - 8) + 1)
                        elif c > 0:
                            t.wait_ge(s3k, (c - 1) * 8 + 6 + j + 1)
                        woff = W3_O + proj * 8192
                        brow = bq_r if proj == 0 else bk_r
                        ps = PS1[:, 512 * pq:512 * (pq + 1)]
                        for step in range(DT):
                            t.matmul(ps, A2[:, woff + step * 1024 + e * 128:
                                            woff + step * 1024 + (e + 1) * 128],
                                     xnT[:, step * 4096 + c * 512:
                                         step * 4096 + (c + 1) * 512],
                                     start=(step == 0), stop=False)
                        t.matmul(ps, brow[0:1, e * 128:(e + 1) * 128],
                                 ones_r[0:1, 0:512], start=False,
                                 stop=True).then_inc(t3, 1)
                    for g in range(8):
                        sbt_, cs = g // 2, g % 2
                        pv = g % 2
                        if g >= 2:
                            t.wait_ge(v3v, c * 8 + (g - 2) + 1)
                        elif c > 0:
                            t.wait_ge(v3v, (c - 1) * 8 + 6 + g + 1)
                        ps = PS1[:, 1024 + 512 * pv:1024 + 512 * (pv + 1)]
                        for step in range(DT):
                            t.matmul(ps, xnT[:, step * 4096 + c * 512 + sbt_ * 128:
                                             step * 4096 + c * 512 + (sbt_ + 1) * 128],
                                     A2[:, W3_O + 16384 + step * 1024 + cs * 512:
                                        W3_O + 16384 + step * 1024 + (cs + 1) * 512],
                                     start=(step == 0), stop=False)
                        t.matmul(ps, ones_r[0:1, 0:128],
                                 bv_r[0:1, cs * 512:(cs + 1) * 512],
                                 start=False, stop=True).then_inc(t3, 1)
                    t.wait_ge(s3k, 8 * (c + 1))
                    for sbt_ in range(NS):
                        ptk = sbt_ % 2
                        if c * 4 + sbt_ >= 2:
                            t.wait_ge(s3h, c * 4 + sbt_ - 1)
                        trb = psb(1, 512 * ptk, 512 * (ptk + 1), bf=True)
                        for dtl in range(DT):
                            ins = t.transpose(trb[:, dtl * 128:(dtl + 1) * 128],
                                              A3[:, P3K + dtl * 512 + sbt_ * 128:
                                                 P3K + dtl * 512 + (sbt_ + 1) * 128],
                                              eye_b[:])
                            if dtl == DT - 1:
                                ins.then_inc(t3h, 1)

                t.wait_ge(v3q, 64); t.wait_ge(s3k, 64); t.wait_ge(v3v, 64)
                t.wait_ge(q_r4, 16)
                for sl in range(16):
                    p = sl % 2
                    t.wait_ge(g_d4[p], 16 * (sl // 2 + 1))
                    if sl >= 2:
                        t.wait_ge(v4, sl - 1)
                    ps = PS1[:, 1024 * p:1024 * (p + 1)]
                    dsm_b = A2[:, DSM_O + 4096 * p:DSM_O + 4096 * (p + 1)]
                    for tt in range(NT):
                        for cs in range(2):
                            ins = t.matmul(ps[:, cs * 512:(cs + 1) * 512],
                                           dsm_b[:, tt * 128:(tt + 1) * 128],
                                           xn4[:, tt * 1024 + cs * 512:
                                               tt * 1024 + (cs + 1) * 512],
                                           start=(tt == 0), stop=(tt == NT - 1))
                            if tt == NT - 1 and cs == 1:
                                ins.then_inc(t4, 1)

                t.wait_ge(v4, 16)
                t.wait_ge(s3h, 32)
                t.wait_ge(v5i, 2)
                t.wait_ge(v_pro, 12)
                for c in range(NCH):
                    bc = CB_O[c % 2]
                    sp = ST_O[c % 2]
                    t.wait_ge(q_5[c % 2], 16 * 4 * (c // 2 + 1))
                    if c in (1, 2):
                        t.wait_ge(v5y, 4 * c)
                    elif c >= 3:
                        t.wait_ge(s5y, 4 * (c - 2))
                    for dk in range(DT):
                        pu = dk % 2
                        if c * 8 + dk >= 2:
                            t.wait_ge(sv5u, c * 8 + dk - 1)
                        ps = PS1 if pu == 0 else PS2
                        for lo, hi in GSL:
                            t.matmul(ps[:, lo:hi], gIbf[:],
                                     A3[:, sp + dk * VW + lo: sp + dk * VW + hi],
                                     start=True, stop=False)
                        for sbt_ in range(NS):
                            lhs = A2[:, bc + 8192 + sbt_ * 1024 + dk * 128:
                                     bc + 8192 + sbt_ * 1024 + (dk + 1) * 128]
                            for gi, (lo, hi) in enumerate(GSL):
                                ins = t.matmul(ps[:, lo:hi], lhs,
                                               A2[:, bc + 12288 + sbt_ * VW + lo:
                                                  bc + 12288 + sbt_ * VW + hi],
                                               start=False, stop=(sbt_ == NS - 1))
                                if sbt_ == NS - 1 and gi == 2:
                                    ins.then_inc(t5u, 1)
                    for a_ in range(NS):
                        pa = a_ % 2
                        if c * 4 + a_ >= 2:
                            t.wait_ge(v5a, c * 4 + a_ - 1)
                        ps = (PS1 if pa == 0 else PS2)[:, 1536:2048]
                        for step in range(DT):
                            ins = t.matmul(ps, A2[:, bc + 4096 + step * 512 + a_ * 128:
                                                  bc + 4096 + step * 512 + (a_ + 1) * 128],
                                           A2[:, bc + step * 512:bc + (step + 1) * 512],
                                           start=(step == 0), stop=(step == DT - 1))
                            if step == DT - 1:
                                ins.then_inc(t5a, 1)
                    t.wait_ge(sv5u, 8 * (c + 1))
                    for b_ in range(NS):
                        py = b_ % 2
                        yidx = c * NS + b_
                        if yidx >= 2:
                            prev = yidx - 2
                            if prev < 2 * NS:
                                t.wait_ge(v5y, prev + 1)
                            else:
                                t.wait_ge(s5y, prev - 2 * NS + 1)
                        ps = PS1 if py == 0 else PS2
                        for step in range(DT):
                            lhs = A2[:, bc + step * 512 + b_ * 128:
                                     bc + step * 512 + (b_ + 1) * 128]
                            for lo, hi in GSL:
                                t.matmul(ps[:, lo:hi], lhs,
                                         A3[:, sp + step * VW + lo: sp + step * VW + hi],
                                         start=(step == 0), stop=False)
                        for a_ in range(b_ + 1):
                            t.wait_ge(v5a, c * 4 + a_ + 1)
                            for gi, (lo, hi) in enumerate(GSL):
                                ins = t.matmul(ps[:, lo:hi],
                                               att_sb[:, a_ * 512 + b_ * 128:
                                                  a_ * 512 + (b_ + 1) * 128],
                                               A2[:, bc + 12288 + a_ * VW + lo:
                                                  bc + 12288 + a_ * VW + hi],
                                               start=False, stop=(a_ == b_))
                                if a_ == b_ and gi == 2:
                                    ins.then_inc(t5y, 1)

                t.wait_ge(t5y, NS * NCH)
                t.wait_ge(v5a, 4 * NCH)
                for i6 in range(2):
                    t.wait_ge(s6, 2 + i6 + 1)
                    trb = psb(i6, 1536, 2048, bf=True)
                    for dtl in range(DT):
                        ins = t.transpose(trb[:, dtl * 128:(dtl + 1) * 128],
                                          A2[:, TANH_O + i6 * 1024 + dtl * 128:
                                             TANH_O + i6 * 1024 + (dtl + 1) * 128],
                                          eye_b[:])
                        if dtl == DT - 1:
                            ins.then_inc(t6, 1)
                for n in range(NEL):
                    pe = n % 2
                    t.wait_ge(g_ew[pe], 16 * 2 * (n // 2 + 1))
                    t.wait_ge(v6e, 2)
                    if n >= 1:
                        t.wait_ge(v6e, 2 + 2 * n)
                    for cs in range(2):
                        ps = psb(cs, 1536, 2048)
                        for step in range(DT):
                            t.matmul(ps[0:32, :],
                                     A2[:, XTT_O + step * 256 + n * 32:
                                        XTT_O + step * 256 + n * 32 + 32],
                                     A2[:, EW_O[pe] + step * 1024 + cs * 512:
                                        EW_O[pe] + step * 1024 + (cs + 1) * 512],
                                     start=(step == 0), stop=False)
                        ins = t.matmul(ps[0:32, :],
                                       ones_r[0:1, 0:32],
                                       ebrow[0:1, pe * D + cs * 512:pe * D + (cs + 1) * 512],
                                       start=False, stop=True)
                        if cs == 1:
                            ins.then_inc(t6, 1)

                t.wait_ge(q_ytl, 32)
                t.wait_ge(v6e, 18)
                for k, i in enumerate(P7ORD):
                    p = k % 2
                    if 16 <= k < 24:
                        idx = k - 16
                        cc = idx // NS
                        b_ = idx % NS
                        if idx == 0:
                            t.wait_ge(q_si, 16)
                            t.wait_ge(v5c, 2)
                        else:
                            t.wait_ge(v5n, idx)
                        for step in range(DT):
                            lhs = A2[:, CB_O[cc] + step * 512 + b_ * 128:
                                     CB_O[cc] + step * 512 + (b_ + 1) * 128]
                            for gi, (lo, hi) in enumerate(GSL):
                                ins = t.matmul(PS2[:, lo:hi], lhs,
                                               A1b[:, SINIT_O + step * VW + lo:
                                                   SINIT_O + step * VW + hi],
                                               start=(step == 0), stop=(step == DT - 1))
                                if step == DT - 1 and gi == 2:
                                    ins.then_inc(t5c, 1)
                    t.wait_ge(q_t7[p], 32 * (k // 2 + 1))
                    if k >= 2:
                        t.wait_ge(v7x, k - 1)
                    ps = PS1[:, 1024 * p:1024 * (p + 1)]
                    for sl in range(16):
                        yoff = (YTL_O[0] + sl * 1024 if sl < 8
                                else YTL_O[1] + (sl - 8) * 1024)
                        for cs in range(2):
                            ins = t.matmul(ps[:, cs * 512:(cs + 1) * 512],
                                           A2[:, CST_O2[p] + sl * 128:
                                              CST_O2[p] + (sl + 1) * 128],
                                           A2[:, yoff + cs * 512:
                                              yoff + (cs + 1) * 512],
                                           start=(sl == 0), stop=(sl == 15))
                            if sl == 15 and cs == 1:
                                ins.then_inc(t7y, 1)

            @blk.vector
            def _(v):
                nv = [0]

                def V(inst):
                    inst.then_inc(vd, 1)
                    nv[0] += 1

                def VB():
                    v.wait_ge(vd, nv[0])

                v.wait_ge(q_c, 16 * 8)
                v.tensor_scalar_mul(nlng_bc[:], lng_bc[:], -1.0).then_inc(v_pro, 1)
                for a_ in range(4):
                    v.tensor_scalar_mul(bcols[:, a_:a_ + 1], lng_bc[:],
                                        float(-(128 * a_ + 1))).then_inc(v_pro, 1)
                for s_ in range(4):
                    v.tensor_scalar_mul(bcols[:, 4 + s_:5 + s_], lng_bc[:],
                                        float(511 - 128 * s_)).then_inc(v_pro, 1)
                v.wait_ge(s_pro, 3)
                v.tensor_mul(ccs[:, 1:2], fcol[:], g512c[:]).then_inc(v_pro, 1)
                v.tensor_copy(ccs[:, 0:1], fcol[:]).then_inc(v_pro, 1)
                v.tensor_scalar_mul(gIbf[:], eye_f[:], g512c[:]).then_inc(v_pro, 1)
                v.memset(ones_r[:], 1.0).then_inc(v_pro, 1)
                v.tensor_scalar_mul(nfcol[:], fcol[:], -1.0).then_inc(v_pro, 1)
                v.wait_ge(v_pro, 14)
                v.tensor_scalar_add(nfcol[:], nfcol[:], 1.0).then_inc(v_pro, 1)

                for i in range(NT + 2):
                    if i < NT:
                        p = i % 2
                        v.wait_ge(q_x[p], 16 * (i // 2 + 1))
                        xt = A3f[:, X_F[p]:X_F[p] + 1024]
                        V(v.reduce_sum(stat[:, p * 8:p * 8 + 1], xt, axis=AX.X))
                        v.wait_ge(s1a, i + 1)
                        VB()
                        V(v.tensor_scalar_mul(stat[:, p * 8:p * 8 + 1],
                                              stat[:, p * 8:p * 8 + 1], -1.0 / D))
                        V(v.tensor_scalar_mul(stat[:, p * 8 + 2:p * 8 + 3],
                                              stat[:, p * 8 + 1:p * 8 + 2], 1.0 / D))
                        VB()
                        V(v.tensor_mul(stat[:, p * 8 + 3:p * 8 + 4],
                                       stat[:, p * 8:p * 8 + 1],
                                       stat[:, p * 8:p * 8 + 1]))
                        VB()
                        V(v.tensor_sub(stat[:, p * 8 + 2:p * 8 + 3],
                                       stat[:, p * 8 + 2:p * 8 + 3],
                                       stat[:, p * 8 + 3:p * 8 + 4]))
                        VB()
                        v.tensor_scalar_add(stat[:, p * 8 + 2:p * 8 + 3],
                                            stat[:, p * 8 + 2:p * 8 + 3],
                                            EPS_LN).then_inc(v1s, 1)
                    if i >= 2:
                        j = i - 2
                        pj = j % 2
                        v.wait_ge(s2e, j + 1)
                        if j >= 2:
                            v.wait_ge(v2d, j - 1)
                        E3 = A3[:, E_O[pj]:E_O[pj] + 2048].rearrange(
                            "p (n q) -> p n q", q=32)
                        V(v.reduce_sum(denD[:, 64 * pj:64 * pj + 64], E3, axis=AX.X))
                        VB()
                        v.reciprocal(denD[:, 64 * pj:64 * pj + 64],
                                     denD[:, 64 * pj:64 * pj + 64]).then_inc(gd, 1)

                v.wait_ge(v2d, NT)
                v.wait_ge(q_D[0], 16 * 16)
                v.wait_ge(q_D[1], 16 * 16)
                v.wait_ge(s_pro, 11)
                for c in range(NCH):
                    for e in range(DT):
                        pq = e % 2
                        v.wait_ge(t3, c * 24 + e + 1)
                        if c >= 1 and e < 2:
                            v.wait_ge(q_3q, 16 * c)
                        ps = PS1[:, 512 * pq:512 * (pq + 1)]
                        v.scalar_tensor_tensor(
                            A3[:, P3Q + e * 512:P3Q + (e + 1) * 512], ps, 0.0,
                            gammaI[:], ALU.max, ALU.mult).then_inc(v3q, 1)
                    if c == 0:
                        v.wait_ge(q_D[0], 16 * 16)
                        v.wait_ge(q_D[1], 16 * 16)
                        v.wait_ge(q_E[0], 16 * 16)
                        v.wait_ge(q_E[1], 16 * 16)
                        for sbt_ in range(NS):
                            V(v.memset(A3[:, P3V + sbt_ * VW + 1024:
                                          P3V + sbt_ * VW + 1025], 1.0))
                            V(v.memset(A3[:, P3V + sbt_ * VW + 1025:
                                          P3V + sbt_ * VW + 1026], 0.0))
                    for g in range(8):
                        sbt_, cs = g // 2, g % 2
                        pv = g % 2
                        v.wait_ge(t3, c * 24 + 16 + g + 1)
                        if c >= 1 and g < 2:
                            v.wait_ge(q_3v, 16 * c)
                        ps = PS1[:, 1024 + 512 * pv:1024 + 512 * (pv + 1)]
                        v.tensor_copy(A3[:, P3V + sbt_ * VW + cs * 512:
                                         P3V + sbt_ * VW + cs * 512 + 512],
                                      ps).then_inc(v3v, 1)

                for sl in range(16):
                    p = sl % 2
                    v.wait_ge(t4, sl + 1)
                    if sl >= 2:
                        v.wait_ge(q_4[p], 16 * ((sl - 2) // 2 + 1))
                    ps = PS1[:, 1024 * p:1024 * (p + 1)]
                    v.tensor_copy(A3f[:, 1024 * p:1024 * (p + 1)], ps).then_inc(v4, 1)

                v.wait_ge(q_4[0], 16 * 8)
                v.wait_ge(q_4[1], 16 * 8)
                v.memset(A3[:, ST_O[0]:ST_O[0] + DT * VW], 0.0).then_inc(v5i, 1)
                v.memset(A3[:, ST_O[1]:ST_O[1] + DT * VW], 0.0).then_inc(v5i, 1)

                for c in range(NCH):
                    for a_ in range(NS):
                        v.wait_ge(t5a, c * 4 + a_ + 1)
                        if c >= 1:
                            v.wait_ge(t5y, 4 * c)
                        pa = a_ % 2
                        ps = (PS1 if pa == 0 else PS2)[:, 1536:2048]
                        row = att_sb[:, a_ * 512:(a_ + 1) * 512]
                        V(v.tensor_scalar_mul(row, ps, asc[:, a_:a_ + 1]))
                        VB()
                        v.scalar_tensor_tensor(
                            att_sb[:, a_ * 512 + a_ * 128:a_ * 512 + (a_ + 1) * 128],
                            ps[:, a_ * 128:(a_ + 1) * 128], asc[:, a_:a_ + 1],
                            causal[:], ALU.mult, ALU.mult).then_inc(v5a, 1)
                    for b_ in range(NS):
                        py = b_ % 2
                        yidx = c * NS + b_
                        v.wait_ge(t5y, yidx + 1)
                        ps = PS1 if py == 0 else PS2
                        if c < 2:
                            v.tensor_copy(A1f[:, yidx * NUMW:(yidx + 1) * NUMW],
                                          ps[:, 0:NUMW]).then_inc(v5y, 1)
                        else:
                            if yidx >= 10:
                                v.wait_ge(s5y, yidx - 9)
                            V(v.tensor_scalar_add(ycol[:, py:py + 1], ps[:, 1024:1025],
                                                  EPS_ATTN))
                            VB()
                            v.reciprocal(ycol[:, 2 + py:3 + py],
                                         ycol[:, py:py + 1]).then_inc(v5y, 1)
                            v.wait_ge(q_x5[py], 16 * ((yidx - 8) // 2 + 1))
                            v.wait_ge(v5y, yidx + 1)
                            if yidx >= 10:
                                v.wait_ge(q_ya[py], 16 * ((yidx - 10) // 2 + 1))
                            VB()
                            v.scalar_tensor_tensor(
                                A1f[:, YA_O + 1024 * py:YA_O + 1024 * (py + 1)],
                                ps[:, 0:1024], ycol[:, 2 + py:3 + py],
                                A2f[:, XS_F + 1024 * py:XS_F + 1024 * (py + 1)],
                                ALU.mult, ALU.add).then_inc(s5y, 1)

                v.wait_ge(q_rs, 16)
                for i6 in range(2):
                    xo = A2f[:, XO_F + 1024 * i6:XO_F + 1024 * (i6 + 1)]
                    v.tensor_scalar(xo, xo, -33000.0, 65000.0,
                                    ALU.max, ALU.min).then_inc(v6, 1)
                    v.wait_ge(v6, 2 * i6 + 1)
                    V(v.reduce_sum(stat[:, 8 * i6:8 * i6 + 1], xo, axis=AX.X))
                    v.wait_ge(s6, i6 + 1)
                    VB()
                    V(v.tensor_scalar_mul(stat[:, 8 * i6:8 * i6 + 1],
                                          stat[:, 8 * i6:8 * i6 + 1], 1.0 / D))
                    V(v.tensor_scalar_mul(stat[:, 8 * i6 + 2:8 * i6 + 3],
                                          stat[:, 8 * i6 + 1:8 * i6 + 2], 1.0 / D))
                    VB()
                    V(v.tensor_mul(stat[:, 8 * i6 + 3:8 * i6 + 4],
                                   stat[:, 8 * i6:8 * i6 + 1], stat[:, 8 * i6:8 * i6 + 1]))
                    VB()
                    V(v.tensor_sub(stat[:, 8 * i6 + 2:8 * i6 + 3],
                                   stat[:, 8 * i6 + 2:8 * i6 + 3],
                                   stat[:, 8 * i6 + 3:8 * i6 + 4]))
                    VB()
                    V(v.tensor_scalar_add(stat[:, 8 * i6 + 2:8 * i6 + 3],
                                          stat[:, 8 * i6 + 2:8 * i6 + 3], EPS_LN))
                    VB()
                    v.reciprocal(stat[:, 8 * i6 + 3:8 * i6 + 4],
                                 stat[:, 8 * i6 + 2:8 * i6 + 3]).then_inc(v6, 1)
                for i6 in range(2):
                    v.wait_ge(t6, i6 + 1)
                    trb = psb(i6, 1536, 2048, bf=True)
                    src_ = trb.rearrange("p (t c) -> p t c", c=128)
                    dst = A2[:, XTT_O:XTT_O + 2048].rearrange(
                        "p (t c) -> p t c", c=256)[:, :, i6 * 128:(i6 + 1) * 128]
                    v.tensor_copy(dst, src_).then_inc(v6e, 1)
                for n in range(NEL):
                    pe = n % 2
                    v.wait_ge(t6, 2 + n + 1)
                    if n >= 2:
                        v.wait_ge(q_yt[pe], 16 * ((n - 2) // 2 + 1))
                    for cs in range(2):
                        ps = psb(cs, 1536, 2048)[0:32, :]
                        v.tensor_copy(A2[0:32, YTS_O + pe * D + cs * 512:
                                         YTS_O + pe * D + (cs + 1) * 512],
                                      ps).then_inc(v6e, 1)

                for cc in range(2):
                    v.wait_ge(q_c5[cc], 16)
                    v.wait_ge(v_pro, 11)
                    qb = A2[:, CB_O[cc]:CB_O[cc] + 4096]
                    v.tensor_scalar_mul(qb, qb, ccs[:, cc:cc + 1]).then_inc(v5c, 1)

                for k in range(NT + 2):
                    if k < NT:
                        p = k % 2
                        v.wait_ge(g_7[p], 16 * (k // 2 + 1))
                        E3 = csm_ap(k).rearrange("p (n q) -> p n q", q=32)
                        E3t = E3.transpose([0, 2, 1])
                        V(v.reduce_sum(denC[:, 32 * p:32 * p + 32], E3t, axis=AX.X))
                        VB()
                        V(v.reciprocal(denC[:, 32 * p:32 * p + 32],
                                       denC[:, 32 * p:32 * p + 32]))
                        rC = denC[:, 32 * p:32 * p + 32].unsqueeze(1).broadcast_to(
                            [128, 64, 32])
                        VB()
                        v.tensor_mul(E3, E3, rC).then_inc(v7c, 1)
                    if 16 <= k < 24:
                        idx = k - 16
                        pyc = idx % 2
                        v.wait_ge(t5c, idx + 1)
                        nh = A1f[:, idx * NUMW:(idx + 1) * NUMW]
                        V(v.tensor_add(nh, nh, PS2[:, 0:NUMW]))
                        VB()
                        V(v.tensor_scalar_add(ycol[:, pyc:pyc + 1], nh[:, 1024:1025],
                                              EPS_ATTN))
                        VB()
                        v.reciprocal(ycol[:, 2 + pyc:3 + pyc],
                                     ycol[:, pyc:pyc + 1]).then_inc(v5n, 1)
                        v.wait_ge(q_xc, 16 * (idx + 1))
                        v.wait_ge(v5n, idx + 1)
                        if idx < 2:
                            v.wait_ge(q_ya[pyc], 16 * 12)
                        else:
                            v.wait_ge(q_ya[pyc], 16 * (12 + (idx - 2) // 2 + 1))
                        VB()
                        v.scalar_tensor_tensor(
                            A1f[:, YA_O + 1024 * pyc:YA_O + 1024 * (pyc + 1)],
                            nh[:, 0:1024], ycol[:, 2 + pyc:3 + pyc],
                            A1f[:, XC_F:XC_F + 1024],
                            ALU.mult, ALU.add).then_inc(s5y, 1)
                    if k >= 2:
                        j = k - 2
                        pj = j % 2
                        v.wait_ge(t7y, j + 1)
                        v.wait_ge(q_y7[pj], 16 * (j // 2 + 1))
                        if j >= 2:
                            v.wait_ge(q_out[pj], 16 * ((j - 2) // 2 + 1))
                        ps = PS1[:, 1024 * pj:1024 * (pj + 1)]
                        ob = A3f[:, OUT2_F + 1024 * pj:OUT2_F + 1024 * (pj + 1)]
                        v.tensor_add(ob, ps,
                                     A3f[:, YAB2_F + 1024 * pj:
                                         YAB2_F + 1024 * (pj + 1)]
                                     ).then_inc(v7x, 1)

            @blk.scalar
            def _(s):
                ns = [0]

                def SC(inst):
                    inst.then_inc(sd, 1)
                    ns[0] += 1

                def SB():
                    s.wait_ge(sd, ns[0])

                s.dma_start(out=A3f[:, X_F[0]:X_F[0] + 1024],
                            in_=xsh[0:128, :]).then_inc(q_x[0], 16)
                s.dma_start(out=A3f[:, X_F[1]:X_F[1] + 1024],
                            in_=xsh[128:256, :]).then_inc(q_x[1], 16)
                s.wait_ge(q_c, 16 * 6)
                s.activation(lng[:], gam_s[:], AF.Ln).then_inc(s_pro, 1)
                s.wait_ge(q_c, 16 * 8)
                s.activation(gammaI[:], iotaI[:], AF.Exp,
                             scale=lng_bc[:]).then_inc(s_pro, 1)
                s.activation(g512c[:], lng_bc[:], AF.Exp, scale=512.0).then_inc(s_pro, 1)
                s.wait_ge(v_pro, 9)
                for a_ in range(4):
                    s.activation(asc[:, a_:a_ + 1], iotac[:], AF.Exp,
                                 scale=nlng_bc[:], bias=bcols[:, a_:a_ + 1]).then_inc(s_pro, 1)
                for s_ in range(4):
                    s.activation(ksc[:, s_:s_ + 1], iotac[:], AF.Exp,
                                 scale=nlng_bc[:], bias=bcols[:, 4 + s_:5 + s_]).then_inc(s_pro, 1)

                s.wait_ge(q_x[0], 16)
                s.activation(A2f[:, SQ_F:SQ_F + 1024], A3f[:, X_F[0]:X_F[0] + 1024],
                             AF.Square,
                             accum_out=stat[:, 1:2]).then_inc(s1a, 1)
                for i in range(NT + 3):
                    if 2 <= i < NT + 2:
                        j = i - 2
                        pj = j % 2
                        s.wait_ge(t2, j + 1)
                        if j >= 2:
                            s.wait_ge(v2d, j - 1)
                            s.wait_ge(q_E[pj], 16 * ((j - 2) // 2 + 1))
                        log = (PS1 if pj == 0 else PS2)[:, 0:2048]
                        s.activation(A3[:, E_O[pj]:E_O[pj] + 2048], log,
                                     AF.Exp).then_inc(s2e, 1)
                    if i < NT:
                        p = i % 2
                        s.wait_ge(v1s, i + 1)
                        xt = A3f[:, X_F[p]:X_F[p] + 1024]
                        SC(s.activation(stat[:, p * 8 + 3:p * 8 + 4],
                                        stat[:, p * 8 + 2:p * 8 + 3], AF.Ln))
                        SB()
                        SC(s.activation(stat[:, p * 8 + 4:p * 8 + 5],
                                        stat[:, p * 8 + 3:p * 8 + 4], AF.Exp,
                                        scale=-0.5))
                        SB()
                        SC(s.mul(stat[:, p * 8 + 5:p * 8 + 6], stat[:, p * 8:p * 8 + 1],
                                 stat[:, p * 8 + 4:p * 8 + 5]))
                        if i >= 2:
                            s.wait_ge(q_xn[p], 16 * ((i - 2) // 2 + 1))
                            s.wait_ge(q_t1[p], 16 * ((i - 2) // 2 + 1))
                        SB()
                        s.activation(A3[:, XN_O[p]:XN_O[p] + 1024], xt, AF.Identity,
                                     scale=stat[:, p * 8 + 4:p * 8 + 5],
                                     bias=stat[:, p * 8 + 5:p * 8 + 6]).then_inc(s1b, 1)
                    if i + 1 < NT:
                        p1 = (i + 1) % 2
                        s.wait_ge(q_x[p1], 16 * ((i + 1) // 2 + 1))
                        s.activation(A2f[:, SQ_F + 1024 * p1:SQ_F + 1024 * (p1 + 1)],
                                     A3f[:, X_F[p1]:X_F[p1] + 1024], AF.Square,
                                     accum_out=stat[:, p1 * 8 + 1:p1 * 8 + 2]
                                     ).then_inc(s1a, 1)
                    if i + 2 < NT:
                        p = i % 2
                        s.wait_ge(s1b, i + 1)
                        s.dma_start(out=A3f[:, X_F[p]:X_F[p] + 1024],
                                    in_=xsh[(i + 2) * 128:(i + 3) * 128, :]
                                    ).then_inc(q_x[p], 16)
                    if i >= 3:
                        j = i - 3
                        pj = j % 2
                        s.wait_ge(s2e, j + 1)
                        s.dma_start(out=sp_E[j], in_=A3[:, E_O[pj]:E_O[pj] + 2048]
                                    ).then_inc(q_E[pj], 16)

                s.wait_ge(q_E[0], 16 * 16)
                s.wait_ge(q_E[1], 16 * 16)
                s.wait_ge(q_xn[0], 16 * 16)
                s.wait_ge(q_xn[1], 16 * 16)
                s.wait_ge(v2d, NT)
                s.wait_ge(q_D[0], 16 * 16)
                s.wait_ge(q_D[1], 16 * 16)
                s.wait_ge(s_pro, 11)
                for c in range(NCH):
                    for e in range(DT):
                        pq = e % 2
                        s.wait_ge(t3, c * 24 + 8 + e + 1)
                        if c >= 1 and e < 2:
                            s.wait_ge(q_3k, 16 * c)
                        ps = PS1[:, 512 * pq:512 * (pq + 1)]
                        s.activation(A3[:, P3K + e * 512:P3K + (e + 1) * 512], ps,
                                     AF.Relu).then_inc(s3k, 1)
                    for sbt_ in range(NS):
                        ptk = sbt_ % 2
                        s.wait_ge(t3h, c * 4 + sbt_ + 1)
                        if c >= 1 and sbt_ < 2:
                            s.wait_ge(q_3h, 16 * c)
                        trb = psb(1, 512 * ptk, 512 * (ptk + 1), bf=True)
                        s.mul(A3[:, P3H + sbt_ * 1024:P3H + (sbt_ + 1) * 1024], trb,
                              ksc[:, sbt_:sbt_ + 1]).then_inc(s3h, 1)

                for c in range(NCH):
                    sq_ = ST_O[(c + 1) % 2]
                    if c == NCH - 1:
                        s.wait_ge(v_pro, 15)
                    for dk in range(DT):
                        pu = dk % 2
                        s.wait_ge(t5u, c * 8 + dk + 1)
                        ps = PS1 if pu == 0 else PS2
                        if c == NCH - 1:
                            s.activation(A3[:, sq_ + dk * VW:sq_ + dk * VW + NUMW],
                                         ps[:, 0:NUMW], AF.Identity,
                                         scale=nfcol[:]).then_inc(sv5u, 1)
                        else:
                            s.copy(A3[:, sq_ + dk * VW:sq_ + dk * VW + NUMW],
                                   ps[:, 0:NUMW]).then_inc(sv5u, 1)

                for i6 in range(2):
                    s.wait_ge(v6, 2 * i6 + 1)
                    xo = A2f[:, XO_F + 1024 * i6:XO_F + 1024 * (i6 + 1)]
                    s.activation(A2f[:, SQ6_F + 1024 * i6:SQ6_F + 1024 * (i6 + 1)], xo,
                                 AF.Square,
                                 accum_out=stat[:, 8 * i6 + 1:8 * i6 + 2]).then_inc(s6, 1)
                for i6 in range(2):
                    s.wait_ge(v6, 2 * i6 + 2)
                    xo = A2f[:, XO_F + 1024 * i6:XO_F + 1024 * (i6 + 1)]
                    SC(s.activation(stat[:, 8 * i6 + 4:8 * i6 + 5],
                                    stat[:, 8 * i6 + 3:8 * i6 + 4], AF.Sqrt))
                    SB()
                    SC(s.mul(stat[:, 8 * i6 + 5:8 * i6 + 6], stat[:, 8 * i6:8 * i6 + 1],
                             stat[:, 8 * i6 + 4:8 * i6 + 5]))
                    SB()
                    SC(s.mul(stat[:, 8 * i6 + 5:8 * i6 + 6],
                             stat[:, 8 * i6 + 5:8 * i6 + 6], -1.0))
                    SB()
                    s.activation(A2[:, TANH_O + i6 * 1024:TANH_O + (i6 + 1) * 1024], xo,
                                 AF.Tanh, scale=stat[:, 8 * i6 + 4:8 * i6 + 5],
                                 bias=stat[:, 8 * i6 + 5:8 * i6 + 6]).then_inc(s6, 1)


                s.wait_ge(cc3, 1)
                s.wait_ge(t6, 10)
                s.dma_start(out=A2[:, YTL_O[1]:YTL_O[1] + 8192].rearrange(
                    "p (t c) -> p t c", c=1024),
                    in_=yt_out.rearrange("(t p) c -> p t c", p=128)[:, 8:16, :]
                    ).then_inc(q_ytl, 16)
                s.wait_ge(t5y, NS * NCH)
                for k in range(NT):
                    p = k % 2
                    s.wait_ge(v7c, k + 1)
                    if k >= 2:
                        s.wait_ge(t7y, k - 1)
                    csm = csm_ap(k)
                    for half in range(2):
                        src_ = csm[:, half * 1024:(half + 1) * 1024
                                   ].rearrange("p (t c) -> p t c", c=128)
                        dst = A2[:, CST_O2[p] + half * 1024:CST_O2[p] + (half + 1) * 1024
                                 ].rearrange("p (t c) -> p t c", c=128)
                        s.dma_start_transpose(dst, src_).then_inc(q_t7[p], 16)


            @blk.sync
            def _(sy):
                sy.dma_start(out=eye_f[:], in_=eye_d).then_inc(q_c, 16)
                sy.dma_start(out=causal[:], in_=causal_d).then_inc(q_c, 16)
                sy.dma_start(out=iotac[:], in_=iotac_d).then_inc(q_c, 16)
                sy.dma_start(out=gam_s[:], in_=gam).then_inc(q_c, 16)
                sy.dma_start(out=fcol[:], in_=flag_d[0:1, 0:1].partition_broadcast(128)
                             ).then_inc(q_c, 16)
                sy.dma_start(out=iotaI[:], in_=iotar_d[0:1, 1:513].partition_broadcast(128)
                             ).then_inc(q_c, 16)
                sy.wait_ge(s_pro, 1)
                sy.dma_start(out=lng_dr, in_=lng[:]).then_inc(q_c, 16)
                sy.wait_ge(q_c, 16 * 7)
                sy.dma_start(out=lng_bc[:], in_=lng_dr[0:1, 0:1].partition_broadcast(128)
                             ).then_inc(q_c, 16)

                for j in range(NT):
                    pp = j % 2
                    sy.wait_ge(s1b, j + 1)
                    sy.dma_start_transpose(
                        A1b.rearrange("p (t c) -> p t c", c=4096
                                      )[:, :, j * 128:(j + 1) * 128],
                        A3[:, XN_O[pp]:XN_O[pp] + 1024].rearrange(
                            "p (t c) -> p t c", c=128)).then_inc(q_t1[pp], 16)
                    sy.dma_start(out=sp_xn[j], in_=A3[:, XN_O[pp]:XN_O[pp] + 1024]
                                 ).then_inc(q_xn[pp], 16)

                for c in range(NCH):
                    sy.wait_ge(v3q, 8 * (c + 1))
                    sy.dma_start(out=sp_q[c], in_=A3[:, P3Q:P3Q + 4096]).then_inc(q_3q, 16)
                    sy.wait_ge(s3k, 8 * (c + 1))
                    sy.dma_start(out=sp_k[c], in_=A3[:, P3K:P3K + 4096]).then_inc(q_3k, 16)
                    sy.wait_ge(s3h, 4 * (c + 1))
                    sy.dma_start(out=sp_kh[c], in_=A3[:, P3H:P3H + 4096]).then_inc(q_3h, 16)
                    sy.wait_ge(v3v, 8 * (c + 1))
                    sy.dma_start(out=sp_v[c], in_=A3[:, P3V:P3V + NS * VW]).then_inc(q_3v, 16)

                sy.wait_ge(q_3q, 16 * 8); sy.wait_ge(q_3k, 16 * 8)
                sy.wait_ge(q_3h, 16 * 8); sy.wait_ge(q_3v, 16 * 8)
                sy.wait_ge(q_xn[0], 16 * 16); sy.wait_ge(q_xn[1], 16 * 16)
                sy.wait_ge(t3h, 4 * NCH)
                sy.dma_start(out=xn4[:, 0:32768].rearrange("p (t c) -> p t c", c=1024),
                             in_=sp_xn.transpose([1, 0, 2])).then_inc(q_r4, 16)
                for sl in range(16):
                    p = sl % 2
                    sy.wait_ge(v4, sl + 1)
                    sy.dma_start(out=ar_in[sl * 128:(sl + 1) * 128, :],
                                 in_=A3f[:, 1024 * p:1024 * (p + 1)]).then_inc(q_4[p], 16)

                def _x5_load(c_):
                    for b_ in range(NS):
                        yidx = c_ * NS + b_
                        py = b_ % 2
                        if yidx >= 10:
                            sy.wait_ge(s5y, yidx - 9)
                        sy.dma_start(out=A2f[:, XS_F + 1024 * py:XS_F + 1024 * (py + 1)],
                                     in_=xsh[yidx * 128:(yidx + 1) * 128, :]
                                     ).then_inc(q_x5[py], 16)

                def _ya_store(c_):
                    for b_ in range(NS):
                        yidx = c_ * NS + b_
                        py = b_ % 2
                        sy.wait_ge(s5y, (yidx - 2 * NS) + 1)
                        sy.dma_start(out=sp_ya[yidx],
                                     in_=A1f[:, YA_O + 1024 * py:YA_O + 1024 * (py + 1)]
                                     ).then_inc(q_ya[py], 16)
                for c in range(NCH):
                    if 2 <= c - 1:
                        _ya_store(c - 1)
                    if c >= 2:
                        _x5_load(c)
                    if c == 4:
                        sy.wait_ge(cc1, 1)
                        sy.dma_start(out=A2f[:, XO_F:XO_F + 2048].rearrange(
                            "p (t c) -> p t c", c=1024),
                            in_=rs_out.rearrange("(t p) c -> p t c", p=128)
                            ).then_inc(q_rs, 16)
                _ya_store(NCH - 1)
                sy.wait_ge(sv5u, 8 * NCH)
                sy.dma_start(out=st_in.transpose([1, 0, 2]),
                             in_=A3[:, ST_O[0]:ST_O[0] + DT * VW].rearrange(
                                 "p (t c) -> p t c", c=VW)).then_inc(q_st, 16)
                for cc in range(2):
                    sy.wait_ge(t5y, NS * NCH)
                    sy.dma_start(out=A2[:, CB_O[cc]:CB_O[cc] + 4096],
                                 in_=sp_q[cc]).then_inc(q_c5[cc], 16)
                for n in range(NEL):
                    pe = n % 2
                    sy.wait_ge(v6e, 2 + 2 * (n + 1))
                    sy.dma_start(out=yt_in[n * 32:(n + 1) * 32, :],
                                 in_=A2[0:32, YTS_O + pe * D:YTS_O + (pe + 1) * D]
                                 ).then_inc(q_yt[pe], 16)
                sy.wait_ge(cc3, 1)
                sy.wait_ge(t6, 10)
                sy.dma_start(out=A2[:, YTL_O[0]:YTL_O[0] + 8192].rearrange(
                    "p (t c) -> p t c", c=1024),
                    in_=yt_out.rearrange("(t p) c -> p t c", p=128)[:, 0:8, :]
                    ).then_inc(q_ytl, 16)

                for k, i in enumerate(P7ORD):
                    pj = k % 2
                    if k == 12:
                        sy.wait_ge(cc2, 1)
                        sy.dma_start(out=A1b[:, SINIT_O:SINIT_O + DT * VW].rearrange(
                            "p (t c) -> p t c", c=VW),
                            in_=st_out2.transpose([1, 0, 2])).then_inc(q_si, 16)
                    if 14 <= k < 22:
                        idx = k - 14
                        if idx >= 1:
                            sy.wait_ge(s5y, NS * (NCH - 2) + idx)
                        sy.dma_start(out=A1f[:, XC_F:XC_F + 1024],
                                     in_=xsh[idx * 128:(idx + 1) * 128, :]
                                     ).then_inc(q_xc, 16)
                    if 15 <= k < 23:
                        idx = k - 15
                        pyc = idx % 2
                        sy.wait_ge(s5y, NS * (NCH - 2) + idx + 1)
                        sy.dma_start(out=sp_ya[idx],
                                     in_=A1f[:, YA_O + 1024 * pyc:YA_O + 1024 * (pyc + 1)]
                                     ).then_inc(q_ya[pyc], 16)
                    sy.wait_ge(v7x, k + 1)
                    sy.dma_start(out=out_d[i * 128:(i + 1) * 128, :],
                                 in_=A3f[:, OUT2_F + 1024 * pj:OUT2_F + 1024 * (pj + 1)]
                                 ).then_inc(q_out[pj], 16)
                sy.wait_ge(q_out[0], 16 * 16)
                sy.wait_ge(q_out[1], 16 * 16)

            @blk.gpsimd
            def _(g):
                g.dma_start(out=eye_b[:], in_=eye_d).then_inc(g_eye, 16)
                g.dma_start(out=bq_r, in_=bq).then_inc(g_bias, 16)
                g.dma_start(out=bk_r, in_=bk).then_inc(g_bias, 16)
                g.dma_start(out=bv_r, in_=bv).then_inc(g_bias, 16)
                g.dma_start(out=phib_r, in_=phib).then_inc(g_bias, 16)
                g.dma_start(out=A2[:, PHIT_O:PHIT_O + 16384].rearrange(
                    "p (t c) -> p t c", c=2048),
                    in_=phiwt_d.rearrange("(t p) c -> p t c", p=128)).then_inc(g_phi, 16)

                for i in range(NT):
                    p = i % 2
                    g.wait_ge(gd, i + 1)
                    if i >= 2:
                        g.wait_ge(q_D[p], 16 * ((i - 2) // 2 + 1))
                    E3g = A3[:, E_O[p]:E_O[p] + 2048].rearrange("p (n q) -> p n q", q=32)
                    d3g = A3[:, DS_O[p]:DS_O[p] + 2048].rearrange("p (n q) -> p n q", q=32)
                    rDg = denD[:, 64 * p:64 * p + 64].unsqueeze(2).broadcast_to(
                        [128, 64, 32])
                    g.tensor_mul(d3g, E3g, rDg).then_inc(v2d, 1)
                    g.wait_ge(v2d, i + 1)
                    g.dma_start(out=sp_dsm[i], in_=A3[:, DS_O[p]:DS_O[p] + 2048]
                                ).then_inc(q_D[p], 16)
                    if 4 <= i < 28 and i % 2 == 0:
                        pidx = (i - 4) // 2
                        wi, qq = pidx // 4, pidx % 4
                        wsrc = (wqt, wkt, wvt)[wi]
                        g.dma_start(out=A2[:, W3_O + wi * 8192 + qq * 2048:
                                           W3_O + wi * 8192 + (qq + 1) * 2048
                                           ].rearrange("p (t c) -> p t c", c=1024),
                                    in_=wsrc.rearrange("(t p) c -> p t c", p=128
                                                       )[:, qq * 2:(qq + 1) * 2, :]
                                    ).then_inc(g_w, 16)

                g.wait_ge(q_D[0], 16 * 16)
                g.wait_ge(q_D[1], 16 * 16)
                g.wait_ge(t3h, 4 * NCH)
                for sl in range(16):
                    p = sl % 2
                    if sl >= 2:
                        g.wait_ge(t4, sl - 1)
                    src = sp_dsm.transpose([1, 0, 2])[:, :, sl * 128:(sl + 1) * 128]
                    g.dma_start(out=A2[:, DSM_O + 4096 * p:DSM_O + 4096 * (p + 1)
                                       ].rearrange("p (t c) -> p t c", c=128),
                                in_=src).then_inc(g_d4[p], 16)
                g.wait_ge(q_4[0], 16 * 8)
                g.wait_ge(q_4[1], 16 * 8)
                g.collective_compute("ReduceScatter", ALU.add,
                                     replica_groups=[list(range(8))],
                                     ins=[ar_in.opt()], outs=[rs_out.opt()]
                                     ).then_inc(cc1, 1)
                for c in range(NCH):
                    bc = CB_O[c % 2]
                    if c >= 2:
                        g.wait_ge(t5y, NS * (c - 1))
                    g.dma_start(out=A2[:, bc:bc + 4096], in_=sp_q[c]).then_inc(q_5[c % 2], 16)
                    g.dma_start(out=A2[:, bc + 4096:bc + 8192],
                                 in_=sp_k[c]).then_inc(q_5[c % 2], 16)
                    g.dma_start(out=A2[:, bc + 8192:bc + 12288],
                                 in_=sp_kh[c]).then_inc(q_5[c % 2], 16)
                    g.dma_start(out=A2[:, bc + 12288:bc + 12288 + NS * VW],
                                 in_=sp_v[c]).then_inc(q_5[c % 2], 16)
                for n in range(NEL):
                    pe = n % 2
                    if n == 0:
                        g.wait_ge(t5y, NS * (NCH - 1))
                    elif n == 1:
                        g.wait_ge(t5y, NS * NCH)
                    else:
                        g.wait_ge(t6, 2 + (n - 2) + 1)
                    g.dma_start(out=A2[:, EW_O[pe]:EW_O[pe] + 8192].rearrange(
                        "p (t c) -> p t c", c=1024),
                        in_=ew[n].rearrange("(t p) c -> p t c", p=128)).then_inc(g_ew[pe], 16)
                    g.dma_start(out=ebrow[0:1, pe * D:(pe + 1) * D],
                                in_=eb[n:n + 1, :]).then_inc(g_ew[pe], 16)
                g.wait_ge(q_yt[0], 16 * 4)
                g.wait_ge(q_yt[1], 16 * 4)
                g.collective_compute("AllGather", ALU.bypass,
                                     replica_groups=[list(range(8))],
                                     ins=[yt_in.opt()], outs=[yt_out.opt()]
                                     ).then_inc(cc3, 1)
                g.wait_ge(q_st, 16)
                g.collective_compute("AllReduce", ALU.add,
                                     replica_groups=[[0, 1], [2, 3], [4, 5], [6, 7]],
                                     ins=[st_in.opt()], outs=[st_out2.opt()]
                                     ).then_inc(cc2, 1)
                g.wait_ge(t5y, NS * NCH)
                g.wait_ge(q_st, 16)

                for k in range(NT + 2):
                    if k < NT:
                        p = k % 2
                        if k >= 2:
                            g.wait_ge(g_7[p], 16 * (k // 2))
                        if k >= 4:
                            g.wait_ge(q_t7[p], 32 * ((k - 4) // 2 + 1))
                        g.dma_start(out=csm_ap(k),
                                    in_=sp_E[P7ORD[k]]).then_inc(g_7[p], 16)
                    if k >= 2:
                        j = k - 2
                        pj = j % 2
                        i2 = P7ORD[j]
                        if j >= 2:
                            g.wait_ge(v7x, j - 1)
                        if i2 < 8:
                            g.wait_ge(q_ya[i2 % 2], 16 * (12 + i2 // 2 + 1))
                        else:
                            g.wait_ge(q_ya[i2 % 2], 16 * ((i2 - 8) // 2 + 1))
                        g.dma_start(out=A3f[:, YAB2_F + 1024 * pj:
                                            YAB2_F + 1024 * (pj + 1)],
                                    in_=sp_ya[i2]).then_inc(q_y7[pj], 16)

    return nc


def kernel(L0_attn_x_seq, L0_attn_Wq, L0_attn_bq, L0_attn_Wk, L0_attn_bk,
           L0_attn_Wv, L0_attn_bv, L0_attn_gamma, L0_ffn_phi_w, L0_ffn_phi_b,
           L0_ffn_expert_w, L0_ffn_expert_b):
    x = np.asarray(L0_attn_x_seq)
    nc = build_nc()
    eye = np.eye(128, dtype=np.float32)
    causal = np.ascontiguousarray(np.tril(np.ones((128, 128), dtype=np.float32)).T)
    iota_col = np.arange(128, dtype=np.float32).reshape(128, 1)
    iota_row = np.arange(1056, dtype=np.float32).reshape(1, 1056)
    wqt = np.ascontiguousarray(np.asarray(L0_attn_Wq).T)
    wkt = np.ascontiguousarray(np.asarray(L0_attn_Wk).T)
    wvt = np.ascontiguousarray(np.asarray(L0_attn_Wv).T)
    phiwt = np.ascontiguousarray(np.asarray(L0_ffn_phi_w).T)
    in_maps = []
    for c in range(8):
        b, h = c // 2, c % 2
        in_maps.append({
            "xsh": x[b][h * TOK:(h + 1) * TOK],
            "wqt": wqt, "wkt": wkt, "wvt": wvt,
            "bq": np.asarray(L0_attn_bq).reshape(1, D),
            "bk": np.asarray(L0_attn_bk).reshape(1, D),
            "bv": np.asarray(L0_attn_bv).reshape(1, D),
            "gam": np.asarray(L0_attn_gamma, dtype=np.float32).reshape(1, 1),
            "phiwt": phiwt,
            "phib": np.asarray(L0_ffn_phi_b).reshape(1, SLOT),
            "ew": np.ascontiguousarray(np.asarray(L0_ffn_expert_w)[c * NEL:(c + 1) * NEL]),
            "eb": np.ascontiguousarray(np.asarray(L0_ffn_expert_b)[c * NEL:(c + 1) * NEL]),
            "eye": eye, "causal": causal,
            "iota_col": iota_col, "iota_row": iota_row,
            "flag": np.array([[float(h)]], dtype=np.float32),
        })
    res = run_bass_kernel_spmd(nc, in_maps, core_ids=list(range(8)))
    full = np.empty((B, S, D), dtype=np.float32)
    for c in range(8):
        b, h = c // 2, c % 2
        full[b, h * TOK:(h + 1) * TOK] = res.results[c]["out"]
    return full
